# version 01
# baseline (speedup 1.0000x reference)
import math
from contextlib import ExitStack

import numpy as np
import ml_dtypes

import concourse.bass as bass
import concourse.mybir as mybir
from concourse.bass_utils import run_bass_kernel_spmd

F32 = mybir.dt.float32
BF16 = mybir.dt.bfloat16
I32 = mybir.dt.int32
AF = mybir.ActivationFunctionType
ALU = mybir.AluOpType
AX = mybir.AxisListType

B, S, D = 2, 8192, 2048
NH, HD = 16, 128
TPC = 2048
CHUNK = 1024
NTT = 4
TT = 512
KC = D // 128

H1, D1, BS1 = 32, 64, 256
NL = 32
F0 = 5632
NE, R0 = 8, 16
LORA_SCALE = 2.0
NFC = F0 // 128

NEG_BIG = -1.0e30
ENGINES = ("sync", "tensor", "vector", "scalar", "gpsimd")


def _chunks_of(core):
    j = core % 4
    return (j, 7 - j)


def _own_token_index(core):
    ja, jb = _chunks_of(core)
    return np.r_[ja * CHUNK:(ja + 1) * CHUNK, jb * CHUNK:(jb + 1) * CHUNK]


def _rope_tables():
    inv_freq = 1.0 / 10000.0 ** (np.arange(0, HD, 2, dtype=np.float64) / HD)
    t = np.arange(S, dtype=np.float64)
    freq = np.einsum("s,f->sf", t, inv_freq)
    emb = np.concatenate([freq, freq], axis=-1)
    A = np.sin(emb).T.astype(np.float32)
    Bt = np.cos(emb).T.astype(np.float32)
    return A, Bt


def _l1_decay():
    base = 1.0 / 2.0 ** (8.0 / H1)
    rate = base ** (np.arange(H1, dtype=np.float64) + 1.0)
    factor = 1.0 + 1e-5
    slope = rate * factor
    r = np.arange(BS1, dtype=np.float64) + 1.0
    qd = np.exp(-slope[:, None] * r[None, :])
    kd = np.exp(-slope[:, None] * (BS1 - r[None, :]))
    dd = r[:, None] - r[None, :]
    ddm = np.where(dd >= 0, np.exp(-slope[:, None, None] * dd[None]), 0.0)
    bd = np.exp(-slope * BS1)
    return (qd.astype(np.float32), kd.astype(np.float32),
            ddm.astype(np.float32), bd)


def _oc_tiles(w):
    din, n = w.shape
    kcn = din // 128
    t = w.reshape(kcn, 128, n // 128, 128)
    return np.ascontiguousarray(t.transpose(2, 1, 0, 3).reshape(n // 128, 128, kcn * 128))


def _pack_kc_free(m):
    din, t = m.shape
    kcn = din // 128
    return np.ascontiguousarray(
        m.reshape(kcn, 128, t).transpose(1, 0, 2).reshape(128, kcn * t))


def _cols2d(v):
    return np.ascontiguousarray(v.reshape(-1, 128).T)


class Rec:

    def __init__(self, nc, es):
        self.nc = nc
        self.es = es
        self.ops = {k: [] for k in ENGINES}
        self.sems = {}

    def sem(self, name):
        h = self.es.enter_context(self.nc.semaphore(name))
        s = [h, 0]
        self.sems[name] = s
        return name

    def bump(self, name, k=16):
        self.sems[name][1] += k
        return self.sems[name][1]

    def cur(self, name):
        return self.sems[name][1]

    def h(self, name):
        return self.sems[name][0]

    def add(self, eng, fn):
        self.ops[eng].append(fn)

    def wait(self, eng, name, val):
        h = self.sems[name][0]
        self.ops[eng].append(lambda e, h=h, v=val: e.wait_ge(h, v))

    def run(self, block):
        o = self.ops

        @block.sync
        def _(q):
            for f in o["sync"]:
                f(q)

        @block.tensor
        def _(t):
            for f in o["tensor"]:
                f(t)

        @block.vector
        def _(v):
            for f in o["vector"]:
                f(v)

        @block.scalar
        def _(s):
            for f in o["scalar"]:
                f(s)

        @block.gpsimd
        def _(g):
            for f in o["gpsimd"]:
                f(g)


class Arena:

    DTS = {F32: 4, BF16: 2, I32: 4}

    def __init__(self, nc, es, name, nbytes):
        self.hdl = es.enter_context(nc.sbuf_tensor(name, [128, nbytes], mybir.dt.uint8))
        self.base = self.hdl.ap()
        self.off = 0
        self.nbytes = nbytes

    def tile(self, p, w, dt):
        sz = w * self.DTS[dt]
        off = (self.off + 31) & ~31
        assert off + sz <= self.nbytes, f"arena overflow {off + sz} > {self.nbytes}"
        self.off = off + sz
        return self.base[0:p, off:off + sz].bitcast(dt)

    def mark(self):
        return self.off

    def reset(self, mark):
        self.off = mark


def prep_launch1(inp):
    A, Bt = _rope_tables()
    x = np.asarray(inp["L0_attn_x"])
    x_fm = np.ascontiguousarray(x.transpose(0, 2, 1))

    w_qkv = np.asarray(inp["L0_attn_w_qkv"])
    b_qkv = np.asarray(inp["L0_attn_b_qkv"])
    wqk = np.empty((D, 2 * D), np.float32)
    bqk = np.empty(2 * D, np.float32)
    for h in range(NH):
        wqk[:, 256 * h:256 * h + 128] = w_qkv[:, 384 * h:384 * h + 128]
        wqk[:, 256 * h + 128:256 * h + 256] = w_qkv[:, 384 * h + 128:384 * h + 256]
        bqk[256 * h:256 * h + 128] = b_qkv[384 * h:384 * h + 128]
        bqk[256 * h + 128:256 * h + 256] = b_qkv[384 * h + 128:384 * h + 256]
    wv = np.concatenate([w_qkv[:, 384 * h + 256:384 * h + 384] for h in range(NH)], 1)
    bv = np.concatenate([b_qkv[384 * h + 256:384 * h + 384] for h in range(NH)])

    prot = np.zeros((128, 128), np.float32)
    prot[np.arange(64), np.arange(64) + 64] = 1.0
    prot[np.arange(64) + 64, np.arange(64)] = -1.0
    wqk_t = _oc_tiles(wqk)
    wv_t = _oc_tiles(wv).reshape(4, 4, 128, KC * 128)
    wv_t = _oc_tiles(wv).reshape(16, 128, KC * 128)
    wv_g = np.empty((4, 128, KC * 512), np.float32)
    for vc in range(4):
        for sub in range(4):
            for kc in range(KC):
                wv_g[vc][:, kc * 512 + sub * 128:kc * 512 + (sub + 1) * 128] = \
                    wv_t[vc * 4 + sub][:, kc * 128:(kc + 1) * 128]

    pc = []
    for c in range(8):
        b = c // 4
        ti = _own_token_index(c)
        pc.append({
            "x_p": _pack_kc_free(np.ascontiguousarray(x_fm[b][:, ti])),
            "wqk": wqk_t,
            "wv": wv_g,
            "bqk": _cols2d(bqk),
            "bv": np.ascontiguousarray(bv[None, :]),
            "ropeA": np.ascontiguousarray(A[:, ti]),
            "ropeB": np.ascontiguousarray(Bt[:, ti]),
            "Prot": prot,
        })
    return pc


def build_launch1():
    nc = bass.Bass("TRN2", target_bir_lowering=False)
    ap = {
        "x_p": nc.dram_tensor("x_p", [128, KC * TPC], F32, kind="ExternalInput").ap(),
        "wqk": nc.dram_tensor("wqk", [32, 128, KC * 128], F32, kind="ExternalInput").ap(),
        "wv": nc.dram_tensor("wv", [4, 128, KC * 512], F32, kind="ExternalInput").ap(),
        "bqk": nc.dram_tensor("bqk", [128, 32], F32, kind="ExternalInput").ap(),
        "bv": nc.dram_tensor("bv", [1, D], F32, kind="ExternalInput").ap(),
        "ropeA": nc.dram_tensor("ropeA", [128, TPC], F32, kind="ExternalInput").ap(),
        "ropeB": nc.dram_tensor("ropeB", [128, TPC], F32, kind="ExternalInput").ap(),
        "Prot": nc.dram_tensor("Prot", [128, 128], F32, kind="ExternalInput").ap(),
        "Qr": nc.dram_tensor("Qr", [D, TPC], BF16, kind="ExternalOutput").ap(),
        "Kr": nc.dram_tensor("Kr", [D, TPC], BF16, kind="ExternalOutput").ap(),
        "Vt": nc.dram_tensor("Vt", [TPC, D], BF16, kind="ExternalOutput").ap(),
    }

    es = ExitStack()
    with es:
        sb = lambda name, shape, dt: es.enter_context(nc.sbuf_tensor(name, shape, dt))
        psum = lambda name, shape: es.enter_context(nc.psum_tensor(name, shape, F32))
        x_bf = sb("x_bf", [128, KC * TPC], BF16)
        ropeA = sb("ropeA_sb", [128, TPC], F32)
        ropeB = sb("ropeB_sb", [128, TPC], F32)
        bqk_sb = sb("bqk_sb", [128, 32], F32)
        bv_sb = sb("bv_sb", [1, D], F32)
        bvb = sb("bvb_sb", [128, D], F32)
        ones_row = sb("ones_row", [1, 128], F32)
        wqk_sb = [sb(f"wqk{i}", [128, KC * 128], BF16) for i in range(2)]
        wv_sb = [sb(f"wv{i}", [128, KC * 512], BF16) for i in range(2)]
        qkraw = [sb(f"qkraw{i}", [128, TT], BF16) for i in range(2)]
        prot_sb = sb("prot_sb", [128, 128], BF16)
        ma = [sb(f"ma{i}", [128, TT], F32) for i in range(2)]
        qk_bf = [sb(f"qkbf{i}", [128, TT], BF16) for i in range(2)]
        v_bf = [sb(f"vbf{i}", [128, TT], BF16) for i in range(2)]
        qk_ps = [psum(f"qkps{i}", [128, TT]) for i in range(2)]
        rot_ps = [psum(f"rotps{i}", [128, TT]) for i in range(2)]
        v_ps = [psum(f"vps{i}", [128, TT]) for i in range(2)]
        bvb_ps = psum("bvbps", [128, TT])

        r = Rec(nc, es)
        for s in ("in", "w0", "w1", "v0", "v1", "pe", "ev", "rp", "st", "c", "gin"):
            r.sem(s)
        block = es.enter_context(nc.Block())

        r.add("gpsimd", lambda g: g.dma_start(out=x_bf[:], in_=ap["x_p"])
              .then_inc(r.h("gin"), 16))
        r.bump("gin")
        r.add("gpsimd", lambda g: g.dma_start(out=prot_sb[:], in_=ap["Prot"])
              .then_inc(r.h("gin"), 16))
        r.bump("gin")
        for t, dst in (("ropeA", ropeA), ("ropeB", ropeB), ("bqk", bqk_sb),
                       ("bv", bv_sb)):
            r.add("sync", lambda q, t=t, dst=dst: q.dma_start(
                out=dst[:], in_=ap[t]).then_inc(r.h("in"), 16))
            r.bump("in")
        in_done = r.cur("in")
        gin_done = r.cur("gin")

        r.wait("vector", "in", in_done)
        r.wait("vector", "gin", gin_done)
        r.add("vector", lambda v: v.memset(ones_row[:], 1.0).then_inc(r.h("c"), 1))
        ones_done = r.bump("c", 1)
        r.wait("tensor", "c", ones_done)
        bvb_ev = None
        for i in range(4):
            if i >= 1:
                r.wait("tensor", "c", bvb_ev)
            r.add("tensor", lambda t, i=i: t.matmul(
                bvb_ps[:], ones_row[:],
                bv_sb[:, i * 512:(i + 1) * 512],
                start=True, stop=True).then_inc(r.h("pe"), 1))
            bvb_mm = r.bump("pe", 1)
            r.wait("vector", "pe", bvb_mm)
            r.add("vector", lambda v, i=i: v.tensor_copy(
                bvb[:, i * 512:(i + 1) * 512], bvb_ps[:]).then_inc(r.h("c"), 1))
            bvb_ev = r.bump("c", 1)

        ev_done, rp_done, st_done, pe_after_oc = {}, {}, {}, {}
        for oc in range(32):
            wbuf = wqk_sb[oc % 2]
            if oc >= 2:
                r.wait("gpsimd", "pe", pe_after_oc[oc - 2])
            wsem = "w0" if oc % 2 == 0 else "w1"
            r.add("gpsimd", lambda g, oc=oc, wbuf=wbuf, wsem=wsem: g.dma_start(
                out=wbuf[:], in_=ap["wqk"][oc]).then_inc(r.h(wsem), 16))
            w_done = r.bump(wsem)
            for ti in range(NTT):
                idx = oc * NTT + ti
                pbuf = qk_ps[idx % 2]
                r.wait("tensor", wsem, w_done)
                if idx >= 2:
                    r.wait("tensor", "ev", ev_done[idx - 2])
                if idx == 0:
                    r.wait("tensor", "in", in_done)
                    r.wait("tensor", "gin", gin_done)
                for kc in range(KC):
                    last = kc == KC - 1
                    def mm(t, oc=oc, ti=ti, kc=kc, pbuf=pbuf, wbuf=wbuf, last=last):
                        i = t.matmul(pbuf[:], wbuf[:, kc * 128:(kc + 1) * 128],
                                     x_bf[:, kc * TPC + ti * TT: kc * TPC + ti * TT + TT],
                                     start=(kc == 0), stop=last)
                        if last:
                            i.then_inc(r.h("pe"), 1)
                    r.add("tensor", mm)
                mm_done = r.bump("pe", 1)
                rbuf = qkraw[idx % 2]
                r.wait("scalar", "pe", mm_done)
                if idx >= 2:
                    r.wait("scalar", "rp", rp_done[idx - 2])
                r.add("scalar", lambda s, oc=oc, pbuf=pbuf, rbuf=rbuf: s.activation(
                    rbuf[:], pbuf[:], AF.Identity,
                    bias=bqk_sb[:, oc:oc + 1]).then_inc(r.h("ev"), 1))
                ev_done[idx] = r.bump("ev", 1)
                rotp = rot_ps[idx % 2]
                r.wait("tensor", "ev", ev_done[idx])
                if idx >= 2:
                    r.wait("tensor", "rp", rp_done[idx - 2])
                r.add("tensor", lambda t, rbuf=rbuf, rotp=rotp: t.matmul(
                    rotp[:], prot_sb[:], rbuf[:], start=True, stop=True)
                    .then_inc(r.h("pe"), 1))
                rot_done = r.bump("pe", 1)
                tok0 = ti * TT
                mabuf, obuf = ma[idx % 2], qk_bf[idx % 2]
                r.wait("vector", "ev", ev_done[idx])
                if idx == 0:
                    r.wait("vector", "in", in_done)
                if idx >= 2:
                    r.wait("vector", "st", st_done[idx - 2])
                r.add("vector", lambda v, rbuf=rbuf, mabuf=mabuf, tok0=tok0:
                      v.tensor_mul(mabuf[:], rbuf[:], ropeA[:, tok0:tok0 + TT]))
                r.wait("vector", "pe", rot_done)
                r.add("vector", lambda v, rotp=rotp, mabuf=mabuf, tok0=tok0:
                      v.tensor_mul(rotp[:], rotp[:], ropeB[:, tok0:tok0 + TT]))
                r.add("vector", lambda v, mabuf=mabuf, rotp=rotp, obuf=obuf:
                      v.tensor_add(obuf[:], mabuf[:], rotp[:])
                      .then_inc(r.h("rp"), 1))
                rp_done[idx] = r.bump("rp", 1)
                h = oc // 2
                dst = ap["Qr"] if oc % 2 == 0 else ap["Kr"]
                r.wait("sync", "rp", rp_done[idx])
                r.add("sync", lambda q, dst=dst, h=h, tok0=tok0, obuf=obuf:
                      q.dma_start(out=dst[128 * h:128 * h + 128, tok0:tok0 + TT],
                                  in_=obuf[:]).then_inc(r.h("st"), 16))
                st_done[idx] = r.bump("st")
            pe_after_oc[oc] = r.cur("pe")

        vev_done, vst_done, pe_after_vc = {}, {}, {}
        for vc in range(4):
            wbuf = wv_sb[vc % 2]
            if vc >= 2:
                r.wait("gpsimd", "pe", pe_after_vc[vc - 2])
            wsem = "v0" if vc % 2 == 0 else "v1"
            r.add("gpsimd", lambda g, vc=vc, wbuf=wbuf, wsem=wsem: g.dma_start(
                out=wbuf[:], in_=ap["wv"][vc]).then_inc(r.h(wsem), 16))
            w_done = r.bump(wsem)
            for tt in range(TPC // 128):
                idx = vc * (TPC // 128) + tt
                pbuf = v_ps[idx % 2]
                r.wait("tensor", wsem, w_done)
                if idx >= 2:
                    r.wait("tensor", "c", vev_done[idx - 2])
                for kc in range(KC):
                    last = kc == KC - 1
                    def mm(t, vc=vc, tt=tt, kc=kc, pbuf=pbuf, wbuf=wbuf, last=last):
                        i = t.matmul(pbuf[:],
                                     x_bf[:, kc * TPC + tt * 128: kc * TPC + (tt + 1) * 128],
                                     wbuf[:, kc * 512:(kc + 1) * 512],
                                     start=(kc == 0), stop=last)
                        if last:
                            i.then_inc(r.h("pe"), 1)
                    r.add("tensor", mm)
                mm_done = r.bump("pe", 1)
                obuf = v_bf[idx % 2]
                r.wait("vector", "pe", mm_done)
                if idx == 0:
                    r.wait("vector", "c", bvb_ev)
                if idx >= 2:
                    r.wait("vector", "st", vst_done[idx - 2])
                r.add("vector", lambda v, vc=vc, pbuf=pbuf, obuf=obuf:
                      v.tensor_add(obuf[:], pbuf[:], bvb[:, vc * 512:(vc + 1) * 512])
                      .then_inc(r.h("c"), 1))
                vev_done[idx] = r.bump("c", 1)
                r.wait("sync", "c", vev_done[idx])
                r.add("sync", lambda q, vc=vc, tt=tt, obuf=obuf: q.dma_start(
                    out=ap["Vt"][tt * 128:(tt + 1) * 128, vc * 512:(vc + 1) * 512],
                    in_=obuf[:]).then_inc(r.h("st"), 16))
                vst_done[idx] = r.bump("st")
            pe_after_vc[vc] = r.cur("pe")

        r.wait("sync", "st", r.cur("st"))
        r.run(block)

    return nc


def kernel(**inputs):
    raise NotImplementedError("launches 2 and 3 under construction")


def prep_launch2(inp, l1out):
    x = np.asarray(inp["L0_attn_x"])
    x_fm = np.ascontiguousarray(x.transpose(0, 2, 1))

    w_o = np.asarray(inp["L0_attn_w_o"])
    b_o = np.asarray(inp["L0_attn_b_o"])
    wo_t = _oc_tiles(w_o)

    KhA = np.zeros((2, 8 * 128, NH * CHUNK), ml_dtypes.bfloat16)
    VhA = np.zeros((2, 8 * 128, NH * CHUNK), ml_dtypes.bfloat16)
    QhA = np.zeros((8, 2 * 128, NH * CHUNK), ml_dtypes.bfloat16)
    for c in range(8):
        b = c // 4
        ja, jb = _chunks_of(c)
        Kr, Vt, Qr = l1out[c]["Kr"], l1out[c]["Vt"], l1out[c]["Qr"]
        for qc, ch in ((0, ja), (1, jb)):
            for h in range(NH):
                kt = Kr[128 * h:128 * h + 128, qc * CHUNK:(qc + 1) * CHUNK]
                KhA[b, ch * 128:(ch + 1) * 128, h * CHUNK:(h + 1) * CHUNK] = kt
                QhA[c, qc * 128:(qc + 1) * 128, h * CHUNK:(h + 1) * CHUNK] = \
                    Qr[128 * h:128 * h + 128, qc * CHUNK:(qc + 1) * CHUNK]
                vblk = Vt[qc * CHUNK:(qc + 1) * CHUNK, 128 * h:128 * h + 128]
                VhA[b, ch * 128:(ch + 1) * 128, h * CHUNK:(h + 1) * CHUNK] = \
                    vblk.reshape(8, 128, 128).transpose(1, 0, 2).reshape(128, CHUNK)

    masks = np.zeros((4, 128, 512), np.float32)
    for m in range(4):
        for p in range(128):
            masks[m, p, 128 * m + p + 1:] = NEG_BIG

    ident = np.eye(128, dtype=np.float32)
    pp = np.arange(128, dtype=np.int32)
    ffn = prep_ffn0(inp)
    pc = []
    for c in range(8):
        b = c // 4
        ja, jb = _chunks_of(c)
        plan = [(ja, 0), (jb, 1)]
        plan += [(kv, 0) for kv in range(ja)]
        plan += [(kv, 1) for kv in range(jb)]
        assert len(plan) == 9
        kvidx = np.stack(
            [(kv * 128 + pp) * 2 + hf for kv, _ in plan for hf in range(2)], 1)
        qhidx = np.stack(
            [(qc * 128 + pp) * 2 + hf for _, qc in plan for hf in range(2)], 1)
        nidx = np.stack([qc * 128 + pp for _, qc in plan], 1)
        oidx = np.zeros((128, 9 * NH), np.int32)
        for u, (_, qc) in enumerate(plan):
            for h in range(NH):
                oidx[:, u * NH + h] = qc * NH * 128 + h * 128 + pp
        pc.append({
            "x_p": _pack_kc_free(np.ascontiguousarray(x_fm[b][:, _own_token_index(c)])),
            "QhA": np.ascontiguousarray(
                QhA[c].reshape(2 * 128, 2, 8 * CHUNK).reshape(2 * 128 * 2, 8 * CHUNK)),
            "KhA": np.ascontiguousarray(
                KhA[b].reshape(8 * 128, 2, 8 * CHUNK).reshape(8 * 128 * 2, 8 * CHUNK)),
            "VhA": np.ascontiguousarray(
                VhA[b].reshape(8 * 128, 2, 8 * CHUNK).reshape(8 * 128 * 2, 8 * CHUNK)),
            "kvidx": kvidx, "qhidx": qhidx, "nidx": nidx, "oidx": oidx,
            "masks": masks, "ident": ident,
            "wo": wo_t, "bo": _cols2d(b_o),
            **ffn,
        })
    return pc


def build_launch2(debug=False, stages=3):
    nc = bass.Bass("TRN2", target_bir_lowering=False)
    ap = {
        "x_p": nc.dram_tensor("x_p", [128, KC * TPC], F32, kind="ExternalInput").ap(),
        "QhA": nc.dram_tensor("QhA", [2 * 256, NH * CHUNK // 2], BF16, kind="ExternalInput").ap(),
        "KhA": nc.dram_tensor("KhA", [8 * 256, NH * CHUNK // 2], BF16, kind="ExternalInput").ap(),
        "VhA": nc.dram_tensor("VhA", [8 * 256, NH * CHUNK // 2], BF16, kind="ExternalInput").ap(),
        "kvidx": nc.dram_tensor("kvidx", [128, 18], I32, kind="ExternalInput").ap(),
        "qhidx": nc.dram_tensor("qhidx", [128, 18], I32, kind="ExternalInput").ap(),
        "nidx": nc.dram_tensor("nidx", [128, 9], I32, kind="ExternalInput").ap(),
        "oidx": nc.dram_tensor("oidx", [128, 9 * NH], I32, kind="ExternalInput").ap(),
        "masks": nc.dram_tensor("masks", [4, 128, 512], F32, kind="ExternalInput").ap(),
        "ident": nc.dram_tensor("ident", [128, 128], F32, kind="ExternalInput").ap(),
        "wo": nc.dram_tensor("wo", [KC, 128, KC * 128], F32, kind="ExternalInput").ap(),
        "bo": nc.dram_tensor("bo", [128, KC], F32, kind="ExternalInput").ap(),
        "x1": nc.dram_tensor("x1", [128, KC * TPC], F32,
                             kind="ExternalOutput" if debug else "Internal").ap(),
        "x2": nc.dram_tensor("x2", [128, KC * TPC], F32, kind="ExternalOutput").ap(),
        "Odram": nc.dram_tensor("Odram", [2, NH, 128, 1032], F32,
                                kind="ExternalOutput" if debug else "Internal").ap(),
        "gw": nc.dram_tensor("gw", [128, KC * 8], F32, kind="ExternalInput").ap(),
        "w13": nc.dram_tensor("w13", [NFC, 128, 2 * KC * 128], F32, kind="ExternalInput").ap(),
        "b13": nc.dram_tensor("b13", [NFC, 48, NE * 128], F32, kind="ExternalInput").ap(),
        "a13": nc.dram_tensor("a13", [NE, 128, KC * 64], F32, kind="ExternalInput").ap(),
        "a2t": nc.dram_tensor("a2t", [128, NE * NFC * 16], F32, kind="ExternalInput").ap(),
        "b2so": nc.dram_tensor("b2so", [128, 2 * KC * 128], F32, kind="ExternalInput").ap(),
        "w2fc": nc.dram_tensor("w2fc", [NFC, 128, KC * 128], F32, kind="ExternalInput").ap(),
    }

    es = ExitStack()
    with es:
        sb = lambda name, shape, dt: es.enter_context(nc.sbuf_tensor(name, shape, dt))
        PB = [es.enter_context(nc.psum_tensor(f"bank{i}", [128, 512], F32))
              for i in range(8)]
        AR = Arena(nc, es, "arena", 188 * 1024)
        r = Rec(nc, es)
        for sname in ("in", "pe", "ev", "rp", "st", "c", "acc", "gin",
                      "pA", "pB", "gA", "gB", "gN",
                      "oA", "oB", "xA", "xB", "wA", "wB",
                      "c0s", "c1s", "aA", "aB", "fA", "fB", "bA", "bB",
                      "dA", "dB", "wrS", "wrT"):
            r.sem(sname)
        block = es.enter_context(nc.Block())

        m0 = AR.mark()
        fl = emit_flash(nc, es, r, ap, sb, PB, AR)
        AR.reset(m0)
        if stages >= 2:
            w = emit_wo_ln(nc, es, r, ap, sb, PB, fl, AR)
            fl["x1_done"] = w["x1_done"]
        AR.reset(m0)
        if stages >= 3:
            emit_ffn0(nc, es, r, ap, sb, PB, fl, AR)
        r.wait("sync", "st", r.cur("st"))
        r.wait("gpsimd", "acc", r.cur("acc"))
        r.run(block)
    return nc


def emit_flash(nc, es, r, ap, sb, PB, AR):
    from concourse.bass import IndirectOffsetOnAxis
    ones_row = sb("f_onesr", [1, 128], F32)
    ones_col = sb("f_onesc", [128, 1], F32)

    masks_sb = AR.tile(128, 4 * 512, BF16)
    ident_bf = sb("f_identbf", [128, 128], BF16)
    ident_f32 = sb("f_identf", [128, 128], F32)
    kvidx_sb = sb("f_kvidx", [128, 18], I32)
    qhidx_sb = sb("f_qhidx", [128, 18], I32)
    nidx_sb = sb("f_nidx", [128, 9], I32)
    oidx_sb = sb("f_oidx", [128, 9 * NH], I32)
    negb_qc = [sb(f"f_negb{i}", [128, 128], F32) for i in range(2)]
    negb_u = sb("f_negbu", [128, 128], F32)
    maxk_sb = sb("f_maxk", [1, NH], F32)


    scl = sb("f_scl", [128, 1], F32)
    bcol = sb("f_bcol", [128, 1], F32)
    HGRP = NH // 2 * CHUNK
    kt_big = [AR.tile(128, HGRP, BF16) for i in range(2)]
    vt_big = [AR.tile(128, HGRP, BF16) for i in range(2)]
    qt_big = [AR.tile(128, HGRP, BF16) for i in range(2)]
    knorm = AR.tile(1, 8 * CHUNK, F32)
    k2 = AR.tile(128, CHUNK, F32)
    qnrow = AR.tile(1, CHUNK, F32)
    zeros_t = AR.tile(128, 1032, F32)
    oacc = [AR.tile(128, 1032, F32) for i in range(2)]
    p_sb = AR.tile(128, CHUNK, BF16)
    pt_sb = AR.tile(128, CHUNK, BF16)
    den_sb = sb("f_den", [128, 2], F32)
    negb_dram = nc.dram_tensor("negb_dram", [2 * 128, 128], F32, kind="Internal").ap()

    for t, dst in (("kvidx", kvidx_sb), ("qhidx", qhidx_sb), ("nidx", nidx_sb),
                   ("oidx", oidx_sb), ("ident", ident_f32)):
        r.add("sync", lambda q, t=t, dst=dst: q.dma_start(
            out=dst[:], in_=ap[t]).then_inc(r.h("in"), 16))
        r.bump("in")
    r.add("gpsimd", lambda g: g.dma_start(out=ident_bf[:], in_=ap["ident"])
          .then_inc(r.h("gin"), 16))
    r.bump("gin")
    for m in range(4):
        r.add("gpsimd", lambda g, m=m: g.dma_start(
            out=masks_sb[:, m * 512:(m + 1) * 512], in_=ap["masks"][m])
            .then_inc(r.h("gin"), 16))
        r.bump("gin")
    in_done = r.cur("in")
    gin_done = r.cur("gin")
    r.add("vector", lambda v: v.memset(ones_row[:], 1.0))
    r.add("vector", lambda v: v.memset(ones_col[:], 1.0))
    r.add("vector", lambda v: v.memset(zeros_t[:], 0.0).then_inc(r.h("c"), 1))
    zi = r.bump("c", 1)

    r.wait("gpsimd", "c", zi)
    for qc in range(2):
        for h in range(NH):
            r.add("gpsimd", lambda g, qc=qc, h=h: g.dma_start(
                out=ap["Odram"][qc, h], in_=zeros_t[:]).then_inc(r.h("acc"), 16))
            r.bump("acc")
    ozero_done = r.cur("acc")

    pre_done = {}
    knmm_prev = 0
    r.wait("gpsimd", "in", in_done)
    r.wait("gpsimd", "gin", gin_done)
    for h in range(NH):
        for ch in range(8):
            idx = h * 8 + ch
            buf = kt_big[idx % 2]
            if idx >= 2:
                r.wait("gpsimd", "c", pre_done[idx - 2])
            ksem = "pA" if idx % 2 == 0 else "pB"
            r.add("gpsimd", lambda g, h=h, ch=ch, buf=buf, ksem=ksem: g.dma_start(
                out=buf[:, 0:CHUNK],
                in_=ap["KhA"].rearrange("(r two) n -> r two n", two=2)
                    [ch * 128:(ch + 1) * 128, h // 8,
                     (h % 8) * CHUNK:(h % 8 + 1) * CHUNK])
                .then_inc(r.h(ksem), 16))
            kld = r.bump(ksem)
            r.wait("scalar", ksem, kld)
            if idx >= 2:
                r.wait("scalar", "c", pre_done[idx - 2])
            if idx >= 1:
                r.wait("scalar", "pe", knmm_prev)
            r.add("scalar", lambda s, buf=buf: s.activation(
                k2[:], buf[:, 0:CHUNK], AF.Square).then_inc(r.h("ev"), 1))
            sq_done = r.bump("ev", 1)
            r.wait("tensor", "ev", sq_done)
            if idx >= 1:
                r.wait("tensor", "c", pre_done[idx - 1])
            for t4 in range(2):
                bank = PB[5 + t4]
                r.add("tensor", lambda t, t4=t4, bank=bank: t.matmul(
                    bank[0:1, :], ones_col[:], k2[:, t4 * 512:(t4 + 1) * 512],
                    start=True, stop=True).then_inc(r.h("pe"), 1))
                knmm = r.bump("pe", 1)
                r.wait("vector", "pe", knmm)
                r.add("vector", lambda v, ch=ch, t4=t4, bank=bank: v.tensor_copy(
                    knorm[0:1, ch * CHUNK + t4 * 512: ch * CHUNK + (t4 + 1) * 512],
                    bank[0:1, :]).then_inc(r.h("c"), 1))
                pre_done[idx] = r.bump("c", 1)
            knmm_prev = knmm
        r.wait("vector", "c", r.cur("c"))
        r.add("vector", lambda v, h=h: v.reduce_max(
            maxk_sb[0:1, h:h + 1], knorm[0:1, :], axis=AX.X).then_inc(r.h("c"), 1))
        pre_done[h * 8 + 7] = r.bump("c", 1)
    maxk_done = r.cur("c")

    nb_done = {}
    cmm_prev = 0
    qn_prev = 0
    bc_prev = 0
    r.wait("tensor", "c", maxk_done)
    for h in range(NH):
        if bc_prev:
            r.wait("tensor", "c", bc_prev)
        r.add("tensor", lambda t, h=h: t.matmul(
            PB[7][:, 0:1], ones_row[:], maxk_sb[0:1, h:h + 1],
            start=True, stop=True).then_inc(r.h("pe"), 1))
        mk = r.bump("pe", 1)
        r.wait("vector", "pe", mk)
        r.add("vector", lambda v: v.tensor_copy(scl[:], PB[7][:, 0:1])
              .then_inc(r.h("c"), 1))
        sc_done = r.bump("c", 1)
        for qc in range(2):
            idx = h * 2 + qc
            buf = qt_big[idx % 2]
            if idx >= 2:
                r.wait("gpsimd", "c", nb_done[idx - 2])
            else:
                r.wait("gpsimd", "c", maxk_done)
            ksem = "pA" if idx % 2 == 0 else "pB"
            r.add("gpsimd", lambda g, h=h, qc=qc, buf=buf, ksem=ksem: g.dma_start(
                out=buf[:, 0:CHUNK],
                in_=ap["QhA"].rearrange("(r two) n -> r two n", two=2)
                    [qc * 128:(qc + 1) * 128, h // 8,
                     (h % 8) * CHUNK:(h % 8 + 1) * CHUNK])
                .then_inc(r.h(ksem), 16))
            qld = r.bump(ksem)
            r.wait("scalar", ksem, qld)
            if idx >= 2:
                r.wait("scalar", "c", nb_done[idx - 2])
            r.wait("scalar", "pe", knmm_prev)
            r.add("scalar", lambda s, buf=buf: s.activation(
                k2[:], buf[:, 0:CHUNK], AF.Square).then_inc(r.h("ev"), 1))
            q2_done = r.bump("ev", 1)
            r.wait("tensor", "ev", q2_done)
            r.wait("tensor", "c", qn_prev)
            for t4 in range(2):
                bank = PB[5 + t4]
                r.add("tensor", lambda t, t4=t4, bank=bank: t.matmul(
                    bank[0:1, :], ones_col[:], k2[:, t4 * 512:(t4 + 1) * 512],
                    start=True, stop=True).then_inc(r.h("pe"), 1))
                qmm = r.bump("pe", 1)
                r.wait("vector", "pe", qmm)
                if cmm_prev:
                    r.wait("vector", "pe", cmm_prev)
                r.add("vector", lambda v, t4=t4, bank=bank: v.tensor_copy(
                    qnrow[0:1, t4 * 512:(t4 + 1) * 512], bank[0:1, :])
                    .then_inc(r.h("c"), 1))
                r.bump("c", 1)
            knmm_prev = qmm
            qn_prev = r.cur("c")
            qn_done = r.cur("c")
            r.wait("tensor", "c", qn_done)
            for rr in range(8):
                if bc_prev:
                    r.wait("tensor", "c", bc_prev)
                r.add("tensor", lambda t, rr=rr: t.matmul(
                    PB[7][:, 1:2], qnrow[0:1, rr * 128:(rr + 1) * 128],
                    ones_row[0:1, 0:1],
                    start=True, stop=True).then_inc(r.h("pe"), 1))
                cmm = r.bump("pe", 1)
                cmm_prev = cmm
                r.wait("vector", "pe", cmm)
                r.wait("vector", "c", sc_done)
                r.wait("vector", "c", r.cur("c"))
                r.add("vector", lambda v: v.tensor_mul(
                    bcol[:], PB[7][:, 1:2], scl[:]).then_inc(r.h("c"), 1))
                bc_done = r.bump("c", 1)
                bc_prev = bc_done
                r.wait("scalar", "c", bc_done)
                r.add("scalar", lambda s: s.activation(
                    bcol[:], bcol[:], AF.Sqrt, scale=1.0 / 128.0)
                    .then_inc(r.h("ev"), 1))
                sq2 = r.bump("ev", 1)
                r.wait("vector", "ev", sq2)
                r.add("vector", lambda v, qc=qc, h=h, rr=rr: v.tensor_scalar_mul(
                    negb_qc[qc][:, h * 8 + rr: h * 8 + rr + 1], bcol[:], -1.0)
                    .then_inc(r.h("c"), 1))
                nb_done[idx] = r.bump("c", 1)
    negb_done = r.cur("c")
    r.wait("sync", "c", negb_done)
    for qc in range(2):
        r.add("sync", lambda q, qc=qc: q.dma_start(
            out=negb_dram[qc * 128:(qc + 1) * 128, :], in_=negb_qc[qc][:])
            .then_inc(r.h("st"), 16))
        r.bump("st")
    negb_st = r.cur("st")

    half_pe_done = {}
    oacc_dma_done = {}
    s_free = {0: 0, 1: 0}
    pt_ps_free = {0: 0, 1: 0}
    p_free = {0: 0, 1: 0}
    pt_sb_free = {0: 0, 1: 0}
    o_free = 0
    den_free = {0: 0, 1: 0}
    negbu_used = 0
    r.wait("gpsimd", "st", negb_st)
    for u in range(9):
        diag = u < 2
        if negbu_used:
            r.wait("gpsimd", "ev", negbu_used)
        r.add("gpsimd", lambda g, u=u: g.indirect_dma_start(
            negb_u[:], None, negb_dram,
            IndirectOffsetOnAxis(nidx_sb[:, u:u + 1], 0)).then_inc(r.h("gN"), 16))
        nb_ld = r.bump("gN")
        for half in range(2):
            hit = u * 2 + half
            kb, vb, qb = kt_big[hit % 2], vt_big[hit % 2], qt_big[hit % 2]
            if hit >= 2:
                r.wait("gpsimd", "pe", half_pe_done[hit - 2])
            uc = u * 2 + half

            gsem = "gA" if hit % 2 == 0 else "gB"

            def lds(g, uc=uc, kb=kb, vb=vb, qb=qb, gsem=gsem):
                g.indirect_dma_start(kb[:], None, ap["KhA"],
                                     IndirectOffsetOnAxis(kvidx_sb[:, uc:uc + 1], 0)) \
                    .then_inc(r.h(gsem), 16)
                g.indirect_dma_start(vb[:], None, ap["VhA"],
                                     IndirectOffsetOnAxis(kvidx_sb[:, uc:uc + 1], 0)) \
                    .then_inc(r.h(gsem), 16)
                g.indirect_dma_start(qb[:], None, ap["QhA"],
                                     IndirectOffsetOnAxis(qhidx_sb[:, uc:uc + 1], 0)) \
                    .then_inc(r.h(gsem), 16)
            r.add("gpsimd", lds)
            ld_done = r.bump(gsem, 48)

            r.wait("tensor", gsem, ld_done)
            if half == 0:
                r.wait("tensor", "gN", nb_ld)
            for hh in range(NH // 2):
                h = half * (NH // 2) + hh
                it = u * NH + h
                ob = oacc[it % 2]
                hb = hh * CHUNK
                exp_evs = {}
                av_done = 0
                for rr in range(8):
                    khs = [0] if (diag and rr < 4) else [0, 1]
                    for kh in khs:
                        bank = PB[kh]
                        if s_free[kh]:
                            r.wait("tensor", "ev", s_free[kh])
                        r.add("tensor", lambda t, qb=qb, kb=kb, hb=hb, rr=rr, kh=kh, bank=bank:
                              t.matmul(bank[:], qb[:, hb + rr * 128: hb + (rr + 1) * 128],
                                       kb[:, hb + kh * 512: hb + (kh + 1) * 512],
                                       start=True, stop=True).then_inc(r.h("pe"), 1))
                        qk_done = r.bump("pe", 1)
                        masked = diag and kh == rr // 4
                        if masked:
                            r.wait("vector", "pe", qk_done)
                            r.add("vector", lambda v, rr=rr, bank=bank:
                                  v.tensor_add(bank[:], bank[:],
                                               masks_sb[:, (rr % 4) * 512:(rr % 4 + 1) * 512])
                                  .then_inc(r.h("rp"), 1))
                            sem_nm, sem_v = "rp", r.bump("rp", 1)
                        else:
                            sem_nm, sem_v = "pe", qk_done
                        r.wait("scalar", sem_nm, sem_v)
                        if hh == 0 and rr == 0 and kh == khs[0]:
                            r.wait("scalar", "gN", nb_ld)
                        if p_free[kh]:
                            r.wait("scalar", "pe", p_free[kh])
                        if den_free[kh]:
                            r.wait("scalar", "c", den_free[kh])
                        r.add("scalar", lambda s, h=h, rr=rr, kh=kh, bank=bank:
                              s.activation(p_sb[:, kh * 512:(kh + 1) * 512], bank[:],
                                           AF.Exp, scale=1.0 / math.sqrt(128.0),
                                           bias=negb_u[:, h * 8 + rr: h * 8 + rr + 1],
                                           accum_out=den_sb[:, kh:kh + 1])
                              .then_inc(r.h("ev"), 1))
                        s_free[kh] = exp_evs[(rr, kh)] = r.bump("ev", 1)
                        r.wait("tensor", "ev", exp_evs[(rr, kh)])
                        if pt_ps_free[kh]:
                            r.wait("tensor", "c", pt_ps_free[kh])
                        ptbank = PB[2 + kh].ap().bitcast(BF16)
                        for t4 in range(4):
                            def tr(t, kh=kh, t4=t4, ptbank=ptbank):
                                i = t.transpose(
                                    ptbank[:, t4 * 128:(t4 + 1) * 128],
                                    p_sb[:, kh * 512 + t4 * 128: kh * 512 + (t4 + 1) * 128],
                                    ident_bf[:])
                                if t4 == 3:
                                    i.then_inc(r.h("pe"), 1)
                            r.add("tensor", tr)
                        p_free[kh] = tr_done = r.bump("pe", 1)
                        r.wait("vector", "pe", tr_done)
                        if pt_sb_free[kh]:
                            r.wait("vector", "pe", pt_sb_free[kh])
                        r.add("vector", lambda v, kh=kh, ptbank=ptbank: v.tensor_copy(
                            pt_sb[:, kh * 512:(kh + 1) * 512], ptbank[:, 0:512])
                            .then_inc(r.h("c"), 1))
                        pt_ps_free[kh] = ptev = r.bump("c", 1)
                        r.wait("tensor", "c", ptev)
                        if kh == khs[0] and o_free:
                            r.wait("tensor", "c", o_free)
                        for t4 in range(4):
                            first = kh == khs[0] and t4 == 0
                            last = kh == khs[-1] and t4 == 3
                            def av(t, kh=kh, t4=t4, vb=vb, hb=hb, first=first, last=last):
                                i = t.matmul(PB[4][:, 0:128],
                                             pt_sb[:, kh * 512 + t4 * 128: kh * 512 + (t4 + 1) * 128],
                                             vb[:, hb + (kh * 4 + t4) * 128: hb + (kh * 4 + t4 + 1) * 128],
                                             start=first, stop=last)
                                if last:
                                    i.then_inc(r.h("pe"), 1)
                            r.add("tensor", av)
                        if kh == khs[-1]:
                            pt_sb_free[0] = pt_sb_free[1] = av_done = r.bump("pe", 1)
                            r.wait("vector", "ev", exp_evs[(rr, khs[0])])
                            if len(khs) == 2:
                                r.wait("vector", "ev", exp_evs[(rr, 1)])
                            if it >= 2:
                                r.wait("vector", "acc", oacc_dma_done[it - 2])
                            else:
                                r.wait("vector", "acc", ozero_done)
                            if len(khs) == 1:
                                r.add("vector", lambda v, rr=rr, ob=ob: v.tensor_copy(
                                    ob[:, 1024 + rr:1024 + rr + 1], den_sb[:, 0:1])
                                    .then_inc(r.h("c"), 1))
                            else:
                                r.add("vector", lambda v, rr=rr, ob=ob: v.tensor_add(
                                    ob[:, 1024 + rr:1024 + rr + 1], den_sb[:, 0:1],
                                    den_sb[:, 1:2]).then_inc(r.h("c"), 1))
                            den_free[0] = den_free[1] = r.bump("c", 1)
                    r.wait("vector", "pe", av_done)
                    r.add("vector", lambda v, rr=rr, ob=ob: v.tensor_copy(
                        ob[:, rr * 128:(rr + 1) * 128], PB[4][:, 0:128])
                        .then_inc(r.h("c"), 1))
                    o_free = r.bump("c", 1)
                oacc_filled = o_free
                r.wait("gpsimd", "c", oacc_filled)
                r.wait("gpsimd", "c", den_free[0])
                if it >= 1:
                    r.wait("gpsimd", "acc", oacc_dma_done[it - 1])
                else:
                    r.wait("gpsimd", "acc", ozero_done)
                r.add("gpsimd", lambda g, u=u, h=h, ob=ob: g.indirect_dma_start(
                    ap["Odram"].rearrange("a h p t -> (a h p) t"),
                    IndirectOffsetOnAxis(oidx_sb[:, u * NH + h: u * NH + h + 1], 0),
                    ob[:], None, compute_op=ALU.add).then_inc(r.h("acc"), 16))
                oacc_dma_done[it] = r.bump("acc")
            half_pe_done[hit] = r.cur("pe")
        negbu_used = max(s_free[0], s_free[1])

    return {
        "ones_row": ones_row, "ones_col": ones_col, "ident_bf": ident_bf,
        "ident_f32": ident_f32, "flash_acc_done": r.cur("acc"),
        "flash_pe_done": r.cur("pe"), "negb_done": negb_done,
    }


def emit_ln_epilogue(nc, r, sb_tiles, y_full, x1_slices, PB, s1b, s2b,
                     tok_w=512, nchunks=KC, eps=1e-5, dim=None):
    dim = dim or (nchunks * 128)
    ones_row = sb_tiles["ones_row"]
    m_row, m2_row = sb_tiles["m_row"], sb_tiles["m2_row"]
    var_row, rstd_row, mr_row = sb_tiles["var_row"], sb_tiles["rstd_row"], sb_tiles["mr_row"]
    tmp = sb_tiles["tmp"]

    r.add("scalar", lambda s: s.activation(
        m_row[:], PB[s1b][0:1, 0:tok_w], AF.Identity, scale=1.0 / dim)
        .then_inc(r.h("ev"), 1))
    mdone = r.bump("ev", 1)
    r.wait("vector", "ev", mdone)
    r.add("vector", lambda v: v.tensor_mul(m2_row[:], m_row[:], m_row[:])
          .then_inc(r.h("c"), 1))
    r.wait("vector", "c", r.bump("c", 1))
    r.add("vector", lambda v: v.scalar_tensor_tensor(
        var_row[:], PB[s2b][0:1, 0:tok_w], 1.0 / dim, m2_row[:],
        ALU.mult, ALU.subtract).then_inc(r.h("c"), 1))
    vdone = r.bump("c", 1)
    r.wait("scalar", "c", vdone)
    r.add("scalar", lambda s: s.activation(
        var_row[:], var_row[:], AF.Sqrt, bias=sb_tiles["eps_col"][:])
        .then_inc(r.h("ev"), 1))
    sdone = r.bump("ev", 1)
    r.wait("vector", "ev", sdone)
    r.add("vector", lambda v: v.reciprocal(rstd_row[:], var_row[:])
          .then_inc(r.h("c"), 1))
    r.wait("vector", "c", r.bump("c", 1))
    r.add("vector", lambda v: v.tensor_mul(mr_row[:], m_row[:], rstd_row[:])
          .then_inc(r.h("c"), 1))
    rows_done = r.bump("c", 1)
    r.wait("tensor", "c", rows_done)
    r.add("tensor", lambda t: t.matmul(PB[s1b][:, 0:tok_w], ones_row[:],
                                       rstd_row[:], start=True, stop=True)
          .then_inc(r.h("pe"), 1))
    r.add("tensor", lambda t: t.matmul(PB[s2b][:, 0:tok_w], ones_row[:],
                                       mr_row[:], start=True, stop=True)
          .then_inc(r.h("pe"), 1))
    r.bump("pe", 1)
    bc_done = r.bump("pe", 1)
    r.wait("vector", "pe", bc_done)
    for oc in range(nchunks):
        r.wait("vector", "c", r.cur("c"))
        r.add("vector", lambda v, oc=oc: v.tensor_mul(
            tmp[:], y_full[:, oc * tok_w:(oc + 1) * tok_w], PB[s1b][:, 0:tok_w])
            .then_inc(r.h("c"), 1))
        r.wait("vector", "c", r.bump("c", 1))
        r.add("vector", lambda v, oc=oc: v.tensor_sub(
            x1_slices(oc), tmp[:], PB[s2b][:, 0:tok_w]).then_inc(r.h("c"), 1))
        r.bump("c", 1)
    return r.cur("c")


def emit_wo_ln(nc, es, r, ap, sb, PB, fl, AR):
    ones_row, ones_col = fl["ones_row"], fl["ones_col"]
    ident_bf = fl["ident_bf"]
    ot_sb = AR.tile(128, NH * TPC, BF16)
    od_sb = [AR.tile(128, 1032, F32) for i in range(2)]
    recip_sb = sb("w_recip", [128, 8], F32)
    obf = sb("w_obf", [128, 128], BF16)
    wo_sb = [AR.tile(128, KC * 128, BF16) for i in range(2)]
    bo_sb = sb("w_bo", [128, KC], F32)
    x_sl = [AR.tile(128, TT, F32) for i in range(2)]
    y_full = AR.tile(128, KC * TT, F32)
    sq_bf = AR.tile(128, TT, BF16)
    y_bf = AR.tile(128, TT, BF16)
    ones_col_bf = sb("w_onescbf", [128, 1], BF16)
    lnt = {
        "ones_row": ones_row,
        "m_row": AR.tile(1, TT, F32), "m2_row": AR.tile(1, TT, F32),
        "var_row": AR.tile(1, TT, F32), "rstd_row": AR.tile(1, TT, F32),
        "mr_row": AR.tile(1, TT, F32), "tmp": AR.tile(128, TT, F32),
        "eps_col": sb("w_eps", [1, 1], F32),
    }

    r.add("vector", lambda v: v.memset(lnt["eps_col"][:], 1e-5))
    r.add("vector", lambda v: v.memset(ones_col_bf[:], 1.0).then_inc(r.h("c"), 1))
    onesbf_done = r.bump("c", 1)
    r.add("sync", lambda q: q.dma_start(out=bo_sb[:], in_=ap["bo"])
          .then_inc(r.h("in"), 16))
    bo_done = r.bump("in")

    flash_done = fl["flash_acc_done"]
    r.wait("sync", "acc", flash_done)
    r.wait("vector", "pe", fl["flash_pe_done"])
    r.wait("vector", "acc", flash_done)
    ot_done = {}
    od_used = {}
    obf_free = 0
    tr_free = 0
    for qc in range(2):
        for h in range(NH):
            it = qc * NH + h
            buf = od_sb[it % 2]
            if it >= 2:
                r.wait("sync", "ev", od_used[it - 2])
            osem = "oA" if it % 2 == 0 else "oB"
            r.add("sync", lambda q, qc=qc, h=h, buf=buf, osem=osem: q.dma_start(
                out=buf[:], in_=ap["Odram"][qc, h]).then_inc(r.h(osem), 16))
            old = r.bump(osem)
            r.wait("vector", osem, old)
            r.add("vector", lambda v, buf=buf: v.tensor_scalar_add(
                recip_sb[:], buf[:, 1024:1032], 1e-30).then_inc(r.h("c"), 1))
            r.wait("vector", "c", r.bump("c", 1))
            r.add("vector", lambda v: v.reciprocal(recip_sb[:], recip_sb[:])
                  .then_inc(r.h("c"), 1))
            rec_done = r.bump("c", 1)
            for rr in range(8):
                r.wait("scalar", "c", rec_done)
                if obf_free:
                    r.wait("scalar", "pe", obf_free)
                r.add("scalar", lambda s, buf=buf, rr=rr: s.activation(
                    obf[:], buf[:, rr * 128:(rr + 1) * 128], AF.Identity,
                    scale=recip_sb[:, rr:rr + 1]).then_inc(r.h("ev"), 1))
                ob_done = r.bump("ev", 1)
                od_used[it] = ob_done
                r.wait("tensor", "ev", ob_done)
                if tr_free:
                    r.wait("tensor", "c", tr_free)
                r.add("tensor", lambda t: t.transpose(
                    PB[2].ap().bitcast(BF16)[:, 0:128], obf[:], ident_bf[:])
                    .then_inc(r.h("pe"), 1))
                obf_free = trd = r.bump("pe", 1)
                r.wait("vector", "pe", trd)
                col = h * TPC + qc * CHUNK + rr * 128
                r.add("vector", lambda v, col=col: v.tensor_copy(
                    ot_sb[:, col:col + 128], PB[2].ap().bitcast(BF16)[:, 0:128])
                    .then_inc(r.h("c"), 1))
                tr_free = r.bump("c", 1)
            ot_done[it] = tr_free
    ot_all = tr_free

    st_prev = 0
    wo_used, x_used, y_done_all = {}, {}, {}
    for ti in range(NTT):
        y_done = {}
        s12_done = 0
        for oc in range(KC):
            gidx = ti * KC + oc
            wbuf = wo_sb[gidx % 2]
            if gidx >= 2:
                r.wait("gpsimd", "pe", wo_used[gidx - 2])
            wsem = "wA" if gidx % 2 == 0 else "wB"
            r.add("gpsimd", lambda g, oc=oc, wbuf=wbuf, wsem=wsem: g.dma_start(
                out=wbuf[:], in_=ap["wo"][oc]).then_inc(r.h(wsem), 16))
            wdone = r.bump(wsem)
            xbuf = x_sl[gidx % 2]
            if gidx >= 2:
                r.wait("sync", "c", x_used[gidx - 2])
            xsem = "xA" if gidx % 2 == 0 else "xB"
            r.add("sync", lambda q, oc=oc, ti=ti, xbuf=xbuf, xsem=xsem: q.dma_start(
                out=xbuf[:], in_=ap["x_p"][:, oc * TPC + ti * TT: oc * TPC + ti * TT + TT])
                .then_inc(r.h(xsem), 16))
            xdone = r.bump(xsem)
            bank = PB[gidx % 2]
            r.wait("tensor", wsem, wdone)
            if gidx == 0:
                r.wait("tensor", "c", ot_all)
            if gidx >= 2:
                r.wait("tensor", "c", y_done_all.get(gidx - 2, 0))
            for kc in range(KC):
                last = kc == KC - 1
                def mm(t, oc=oc, ti=ti, kc=kc, wbuf=wbuf, bank=bank, last=last):
                    i = t.matmul(bank[:], wbuf[:, kc * 128:(kc + 1) * 128],
                                 ot_sb[:, kc * TPC + ti * TT: kc * TPC + ti * TT + TT],
                                 start=(kc == 0), stop=last)
                    if last:
                        i.then_inc(r.h("pe"), 1)
                r.add("tensor", mm)
            wo_used[gidx] = mm_done = r.bump("pe", 1)
            r.wait("vector", "pe", mm_done)
            r.wait("vector", xsem, xdone)
            if gidx == 0:
                r.wait("vector", "in", bo_done)
            if st_prev:
                r.wait("vector", "st", st_prev)
            r.add("vector", lambda v, oc=oc, bank=bank, xbuf=xbuf:
                  v.scalar_tensor_tensor(
                      y_full[:, oc * TT:(oc + 1) * TT], bank[:],
                      bo_sb[:, oc:oc + 1], xbuf[:], ALU.add, ALU.add)
                  .then_inc(r.h("c"), 1))
            x_used[gidx] = y_done_all[gidx] = yd = r.bump("c", 1)
            r.wait("scalar", "c", yd)
            if s12_done:
                r.wait("scalar", "pe", s12_done)
            r.add("scalar", lambda s, oc=oc: s.activation(
                y_bf[:], y_full[:, oc * TT:(oc + 1) * TT], AF.Identity)
                .then_inc(r.h("ev"), 1))
            r.bump("ev", 1)
            r.add("scalar", lambda s, oc=oc: s.activation(
                sq_bf[:], y_full[:, oc * TT:(oc + 1) * TT], AF.Square)
                .then_inc(r.h("ev"), 1))
            sqd = r.bump("ev", 1)
            r.wait("tensor", "ev", sqd)
            if gidx == 0:
                r.wait("tensor", "c", onesbf_done)
            r.add("tensor", lambda t, oc=oc: t.matmul(
                PB[6][0:1, 0:TT], ones_col_bf[:], y_bf[:],
                start=(oc == 0), stop=(oc == KC - 1)))
            def mm2(t, oc=oc):
                i = t.matmul(PB[7][0:1, 0:TT], ones_col_bf[:], sq_bf[:],
                             start=(oc == 0), stop=(oc == KC - 1))
                i.then_inc(r.h("pe"), 1)
            r.add("tensor", mm2)
            s12_done = r.bump("pe", 1)
        r.wait("scalar", "pe", s12_done)
        r.wait("vector", "pe", s12_done)
        emit_ln_epilogue(nc, r, lnt, y_full,
                         lambda oc: y_full[:, oc * TT:(oc + 1) * TT], PB, 6, 7)
        ln_done = r.cur("c")
        r.wait("sync", "c", ln_done)
        for oc in range(KC):
            r.add("sync", lambda q, oc=oc, ti=ti: q.dma_start(
                out=ap["x1"][:, oc * TPC + ti * TT: oc * TPC + ti * TT + TT],
                in_=y_full[:, oc * TT:(oc + 1) * TT]).then_inc(r.h("st"), 16))
            r.bump("st")
        st_prev = r.cur("st")
    return {"x1_done": st_prev}


def prep_ffn0(inp):
    gate_w = np.asarray(inp["L0_ffn_gate_w"])
    w1 = np.asarray(inp["L0_ffn_w1"])
    w3 = np.asarray(inp["L0_ffn_w3"])
    w2 = np.asarray(inp["L0_ffn_w2"])
    a1 = np.asarray(inp["L0_ffn_a1"])
    b1 = np.asarray(inp["L0_ffn_b1"])
    a3 = np.asarray(inp["L0_ffn_a3"])
    b3 = np.asarray(inp["L0_ffn_b3"])
    a2 = np.asarray(inp["L0_ffn_a2"])
    b2 = np.asarray(inp["L0_ffn_b2"])

    gw = _pack_kc_free(np.ascontiguousarray(gate_w.T))
    w1t = _oc_tiles(np.ascontiguousarray(w1.T))
    w3t = _oc_tiles(np.ascontiguousarray(w3.T))
    w13 = np.concatenate([w1t, w3t], axis=2)
    b13 = np.zeros((NFC, 48, NE * 128), np.float32)
    for fc in range(NFC):
        for e in range(NE):
            b13[fc, 0:16, e * 128:(e + 1) * 128] = b1[e].T[:, fc * 128:(fc + 1) * 128]
            b13[fc, 32:48, e * 128:(e + 1) * 128] = b3[e].T[:, fc * 128:(fc + 1) * 128]
    a13 = np.zeros((NE, 128, KC * 64), np.float32)
    for e in range(NE):
        for kc in range(KC):
            a13[e][:, kc * 64:kc * 64 + 16] = a1[e][:, kc * 128:(kc + 1) * 128].T
            a13[e][:, kc * 64 + 32:kc * 64 + 48] = a3[e][:, kc * 128:(kc + 1) * 128].T
    a2t = np.zeros((NE, NFC, 128, 16), np.float32)
    for e in range(NE):
        for fc in range(NFC):
            a2t[e, fc] = a2[e][:, fc * 128:(fc + 1) * 128].T
    b2s = np.zeros((2, 128, D), np.float32)
    for e in range(NE):
        g = e // 2
        b2s[e % 2, 32 * g:32 * g + 16, :] = b2[e].T
    b2so = np.stack([_oc_tiles(b2s[0]), _oc_tiles(b2s[1])])
    w2t = _oc_tiles(np.ascontiguousarray(w2.T))
    w2fc = np.zeros((NFC, 128, KC * 128), np.float32)
    for fc in range(NFC):
        for oc in range(KC):
            w2fc[fc][:, oc * 128:(oc + 1) * 128] = w2t[oc][:, fc * 128:(fc + 1) * 128]
    a2th = np.ascontiguousarray(
        a2t.transpose(2, 0, 1, 3).reshape(128, NE * NFC * 16))
    b2sh = np.ascontiguousarray(
        b2so.transpose(2, 0, 1, 3).reshape(128, 2 * KC * 128))
    return {"gw": gw, "w13": w13, "b13": b13, "a13": a13, "a2t": a2th,
            "b2so": b2sh, "w2fc": w2fc}


def emit_ffn0(nc, es, r, ap, sb, PB, fl, AR):
    ones_row, ones_col, ident_bf, ident_f32 = (
        fl["ones_row"], fl["ones_col"], fl["ident_bf"], fl["ident_f32"])
    x1bf = AR.tile(128, KC * TT, BF16)
    acc = AR.tile(128, NFC * TT, BF16)
    y_full = AR.tile(128, KC * TT, F32)
    w13_sb = [AR.tile(128, 2 * KC * 128, BF16) for _ in range(2)]
    w2_sb = [AR.tile(128, KC * 128, BF16) for _ in range(2)]
    a13_sb = [AR.tile(128, 2 * KC * 64, BF16) for _ in range(2)]
    a2t_sb = AR.tile(128, NE * NFC * 16, BF16)
    b2s_sb = AR.tile(128, 2 * KC * 128, BF16)
    gw_sb = AR.tile(128, KC * 8, BF16)
    wb_sb = AR.tile(128, NE * TT, BF16)
    u_sb = AR.tile(64, NE * TT, BF16)
    b13_sb = [AR.tile(48, NE * 128, BF16) for _ in range(2)]
    zbf = AR.tile(128, 2 * TT, BF16)
    lg = sb("n_lg", [8, TT], F32)
    ltok = sb("n_ltok", [128, 8], F32)
    m1 = sb("n_m1", [128, 1], F32)
    ecol = sb("n_e", [128, 8], F32)
    ind1 = sb("n_ind1", [128, 8], F32)
    em = sb("n_em", [128, 8], F32)
    m2c = sb("n_m2", [128, 1], F32)
    wun = sb("n_wun", [128, 8], F32)
    dcol = sb("n_den", [128, 1], F32)
    wtok = sb("n_wtok", [128, 8], F32)
    wrow = sb("n_wrow", [8, TT], F32)
    wrow0 = [sb(f"n_wrow0{i}", [1, TT], F32) for i in range(2)]
    silu_sb = AR.tile(128, TT, BF16)
    act_sb = AR.tile(128, TT, BF16)
    t2_sb = AR.tile(128, TT, BF16)
    x_sl = [AR.tile(128, TT, F32) for i in range(2)]
    sqn_bf = AR.tile(128, TT, BF16)
    yn_bf = AR.tile(128, TT, BF16)
    lnt = {
        "ones_row": ones_row,
        "m_row": AR.tile(1, TT, F32), "m2_row": AR.tile(1, TT, F32),
        "var_row": AR.tile(1, TT, F32), "rstd_row": AR.tile(1, TT, F32),
        "mr_row": AR.tile(1, TT, F32), "tmp": AR.tile(128, TT, F32),
        "eps_col": sb("n_eps", [1, 1], F32),
    }
    ocbf = sb("n_ocbf", [128, 1], BF16)
    zrow_bf = sb("n_zrow", [1, 640], BF16)

    def vw(val):
        r.wait("vector", "c", val)

    def vadd(fn):
        r.add("vector", lambda v, fn=fn: fn(v).then_inc(r.h("c"), 1))
        return r.bump("c", 1)

    def sadd(fn):
        r.add("scalar", lambda s, fn=fn: fn(s).then_inc(r.h("ev"), 1))
        return r.bump("ev", 1)

    r.add("vector", lambda v: v.memset(lnt["eps_col"][:], 1e-5))
    r.add("vector", lambda v: v.memset(zrow_bf[:], 0.0))
    onesbf2 = vadd(lambda v: v.memset(ocbf[:], 1.0))

    r.wait("gpsimd", "st", fl["x1_done"])
    r.add("gpsimd", lambda g: g.dma_start(out=gw_sb[:], in_=ap["gw"])
          .then_inc(r.h("gin"), 16))
    r.bump("gin")
    r.add("gpsimd", lambda g: g.dma_start(
        out=b2s_sb[:], in_=ap["b2so"]).then_inc(r.h("gin"), 16))
    r.bump("gin")
    r.add("gpsimd", lambda g: g.dma_start(
        out=a2t_sb[:], in_=ap["a2t"]).then_inc(r.h("gin"), 16))
    r.bump("gin")
    ffn_gin = r.cur("gin")

    x1_done = fl["x1_done"]
    prev_store = 0
    y_ln_prev = 0
    x_used, w13_used, b13_used, a13_used, w2_used = {}, {}, {}, {}, {}
    x1bf_used = {}
    for ti in range(NTT):
        xs = "c0s" if ti % 2 == 0 else "c1s"
        if ti >= 2:
            r.wait("gpsimd", "pe", x1bf_used[ti - 2])
        if ti == 0:
            r.wait("gpsimd", "st", x1_done)
        r.add("gpsimd", lambda g, ti=ti, xs=xs: g.dma_start(
            out=x1bf[:].rearrange("p (a t) -> p a t", a=KC),
            in_=ap["x1"].rearrange("p (a t) -> p a t", a=KC)
            [:, :, ti * TT:(ti + 1) * TT]).then_inc(r.h(xs), 16))
        x1bf_ld = r.bump(xs)
        r.wait("tensor", xs, x1bf_ld)
        if ti == 0:
            r.wait("tensor", "gin", ffn_gin)
        if ti >= 1:
            r.wait("tensor", "c", y_ln_prev)
        for kc in range(KC):
            def mmg(t, kc=kc, last=kc == KC - 1):
                i = t.matmul(PB[0][0:8, :], gw_sb[:, kc * 8:(kc + 1) * 8],
                             x1bf[:, kc * TT:(kc + 1) * TT],
                             start=kc == 0, stop=last)
                if last:
                    i.then_inc(r.h("pe"), 1)
            r.add("tensor", mmg)
        gmm = r.bump("pe", 1)
        r.wait("vector", "pe", gmm)
        vw(r.cur("c"))
        lg_ev = vadd(lambda v: v.tensor_copy(lg[:], PB[0][0:8, :]))
        wrow_parts = []
        for sfour in range(4):
            r.wait("tensor", "c", lg_ev)
            if wrow_parts:
                r.wait("tensor", "c", wrow_parts[-1])
            r.add("tensor", lambda t, sfour=sfour: t.transpose(
                PB[1][:, 0:8], lg[:, sfour * 128:(sfour + 1) * 128],
                ident_f32[0:8, 0:8]).then_inc(r.h("pe"), 1))
            ltr = r.bump("pe", 1)
            r.wait("vector", "pe", ltr)
            vw(r.cur("c"))
            lt_ev = vadd(lambda v: v.tensor_copy(ltok[:], PB[1][:, 0:8]))
            vw(lt_ev)
            e1 = vadd(lambda v: v.reduce_max(m1[:], ltok[:], axis=AX.X))
            vw(e1)
            e2 = vadd(lambda v: v.tensor_scalar_mul(m1[:], m1[:], -1.0))
            r.wait("scalar", "c", e2)
            e3 = sadd(lambda s: s.activation(ecol[:], ltok[:], AF.Exp, bias=m1[:]))
            r.wait("vector", "ev", e3)
            e4 = vadd(lambda v: v.tensor_single_scalar(ind1[:], ecol[:], 1.0, ALU.is_ge))
            vw(e4)
            e5 = vadd(lambda v: v.tensor_scalar(ind1[:], ind1[:], -1.0, 1.0,
                                                ALU.mult, ALU.add))
            vw(e5)
            e6 = vadd(lambda v: v.tensor_mul(em[:], ecol[:], ind1[:]))
            vw(e6)
            e7 = vadd(lambda v: v.reduce_max(m2c[:], em[:], axis=AX.X))
            vw(e7)
            e8 = vadd(lambda v: v.tensor_single_scalar(wun[:], ecol[:], m2c[:], ALU.is_ge))
            vw(e8)
            e9 = vadd(lambda v: v.tensor_mul(wun[:], wun[:], ecol[:]))
            vw(e9)
            e10 = vadd(lambda v: v.reduce_sum(dcol[:], wun[:], axis=AX.X))
            vw(e10)
            e11 = vadd(lambda v: v.reciprocal(dcol[:], dcol[:]))
            vw(e11)
            e12 = vadd(lambda v: v.tensor_scalar_mul(wtok[:], wun[:], dcol[:]))
            r.wait("tensor", "c", e12)
            r.add("tensor", lambda t, sfour=sfour: t.transpose(
                PB[1][0:8, 0:128], wtok[:], ident_f32[:]).then_inc(r.h("pe"), 1))
            wtr = r.bump("pe", 1)
            r.wait("vector", "pe", wtr)
            vw(r.cur("c"))
            wrow_parts.append(vadd(lambda v, sfour=sfour: v.tensor_copy(
                wrow[:, sfour * 128:(sfour + 1) * 128], PB[1][0:8, 0:128])))
        wb_evs = []
        wmm_by_buf = {0: 0, 1: 0}
        for e in range(NE):
            wsel = "wrS" if e % 2 == 0 else "wrT"
            wbuf0 = wrow0[e % 2]
            r.wait("sync", "c", wrow_parts[-1])
            if wmm_by_buf[e % 2]:
                r.wait("sync", "pe", wmm_by_buf[e % 2])
            r.add("sync", lambda q, e=e, wbuf0=wbuf0: q.dma_start(
                out=wbuf0[:], in_=wrow[e:e + 1, :]).then_inc(
                r.h("wrS" if e % 2 == 0 else "wrT"), 16))
            wr0 = r.bump(wsel)
            r.wait("tensor", wsel, wr0)
            if wb_evs:
                r.wait("tensor", "c", wb_evs[-1])
            r.add("tensor", lambda t, wbuf0=wbuf0: t.matmul(
                PB[0][:, 0:TT], ones_row[:], wbuf0[:],
                start=True, stop=True).then_inc(r.h("pe"), 1))
            wmm = r.bump("pe", 1)
            wmm_by_buf[e % 2] = wmm
            r.wait("vector", "pe", wmm)
            vw(r.cur("c"))
            wb_evs.append(vadd(lambda v, e=e: v.tensor_copy(
                wb_sb[:, e * TT:(e + 1) * TT], PB[0][:, 0:TT])))
        wb_done = wb_evs[-1]

        for p in range(4):
            pidx = ti * 4 + p
            abuf = a13_sb[pidx % 2]
            asem = "aA" if pidx % 2 == 0 else "aB"
            if pidx >= 2:
                r.wait("gpsimd", "pe", a13_used[pidx - 2])

            def lda(g, p=p, abuf=abuf, asem=asem):
                g.dma_start(out=abuf[:, 0:KC * 64], in_=ap["a13"][2 * p]) \
                    .then_inc(r.h(asem), 16)
                g.dma_start(out=abuf[:, KC * 64:2 * KC * 64], in_=ap["a13"][2 * p + 1]) \
                    .then_inc(r.h(asem), 16)
            r.add("gpsimd", lda)
            ald = r.bump(asem, 32)
            r.wait("tensor", asem, ald)
            if p == 0 and ti >= 1:
                r.wait("tensor", "c", y_ln_prev)
            if pidx >= 1:
                r.wait("tensor", "c", u_ev_prev)
            for e_in in range(2):
                for kc in range(KC):
                    def mmu(t, p=p, e_in=e_in, kc=kc, abuf=abuf,
                            last=kc == KC - 1):
                        i = t.matmul(
                            PB[2][64 * e_in:64 * e_in + 64, :],
                            abuf[:, e_in * KC * 64 + kc * 64: e_in * KC * 64 + (kc + 1) * 64],
                            x1bf[:, kc * TT:(kc + 1) * TT],
                            start=kc == 0, stop=last,
                            tile_position=(0, 64 * e_in))
                        if last:
                            i.then_inc(r.h("pe"), 1)
                    r.add("tensor", mmu)
                r.bump("pe", 1)
            umm = r.cur("pe")
            a13_used[pidx] = umm
            r.wait("vector", "pe", umm)
            vw(r.cur("c"))
            vadd(lambda v, p=p: v.tensor_scalar_mul(
                u_sb[:, (2 * p) * TT:(2 * p + 1) * TT], PB[2][0:64, :], LORA_SCALE))
            vw(r.cur("c"))
            u_ev_prev = vadd(lambda v, p=p: v.tensor_scalar_mul(
                u_sb[:, (2 * p + 1) * TT:(2 * p + 2) * TT], PB[2][64:128, :], LORA_SCALE))
        u_done = u_ev_prev

        if ti >= 1:
            r.wait("tensor", "c", y_ln_prev)
        r.wait("tensor", "c", u_done)
        r.wait("tensor", "c", onesbf2)
        for zb in (7, 2):
            def zmm(t, zb=zb):
                t.matmul(PB[zb][:], zrow_bf[0:1, 0:128], zrow_bf[0:1, 128:640],
                         start=True, stop=False).then_inc(r.h("pe"), 1)
            r.add("tensor", zmm)
            r.bump("pe", 1)
        zclear = r.cur("pe")

        h_free = {3: 0, 5: 0, 4: 0, 6: 0}
        base_free = {3: 0, 4: 0}
        t2_used = 0
        silu_free = 0
        act_free = 0
        prev_accv = 0
        for fc in range(NFC):
            fidx = ti * NFC + fc
            wbuf = w13_sb[fidx % 2]
            fsem = "fA" if fidx % 2 == 0 else "fB"
            if fidx >= 2:
                r.wait("gpsimd", "pe", w13_used[fidx - 2])
            r.add("gpsimd", lambda g, fc=fc, wbuf=wbuf, fsem=fsem: g.dma_start(
                out=wbuf[:], in_=ap["w13"][fc]).then_inc(r.h(fsem), 16))
            wld = r.bump(fsem)
            bbuf = b13_sb[fidx % 2]
            bsem = "bA" if fidx % 2 == 0 else "bB"
            if fidx >= 2:
                r.wait("gpsimd", "pe", b13_used[fidx - 2])
            r.add("gpsimd", lambda g, fc=fc, bbuf=bbuf, bsem=bsem: g.dma_start(
                out=bbuf[:], in_=ap["b13"][fc]).then_inc(r.h(bsem), 16))
            bld = r.bump(bsem)
            r.wait("tensor", fsem, wld)
            for bi, bank in ((0, 3), (1, 4)):
                if base_free[bank]:
                    r.wait("tensor", "ev", base_free[bank])
                for kc in range(KC):
                    def mmb(t, bi=bi, kc=kc, bank=bank, wbuf=wbuf,
                            last=kc == KC - 1):
                        i = t.matmul(
                            PB[bank][:],
                            wbuf[:, bi * KC * 128 + kc * 128: bi * KC * 128 + (kc + 1) * 128],
                            x1bf[:, kc * TT:(kc + 1) * TT],
                            start=kc == 0, stop=last)
                        if last:
                            i.then_inc(r.h("pe"), 1)
                    r.add("tensor", mmb)
                r.bump("pe", 1)
            bmm = r.cur("pe")
            r.wait("scalar", "pe", bmm)
            if silu_free:
                r.wait("scalar", "c", silu_free)
            b1ev = sadd(lambda s: s.activation(silu_sb[:], PB[3][:], AF.Identity))
            b3ev = sadd(lambda s: s.activation(act_sb[:], PB[4][:], AF.Identity))
            base_free[3] = base_free[4] = b3ev
            for e in range(NE):
                h1b = 3 if e % 2 == 0 else 5
                h3b = 4 if e % 2 == 0 else 6
                r.wait("tensor", "ev", b3ev)
                r.wait("tensor", bsem, bld)
                if fc == 0 and e <= 1 and ti >= 1:
                    r.wait("tensor", "c", y_ln_prev)
                if h_free[h1b]:
                    r.wait("tensor", "ev", h_free[h1b])
                if e == 0:
                    r.wait("tensor", "c", u_done)
                    r.wait("tensor", "c", wb_done)

                def mmh(t, e=e, h1b=h1b, h3b=h3b, bbuf=bbuf):
                    t.matmul(PB[h1b][:], ident_bf[:], silu_sb[:],
                             start=True, stop=False)
                    i = t.matmul(PB[h1b][:], bbuf[0:16, e * 128:(e + 1) * 128],
                                 u_sb[0:16, e * TT:(e + 1) * TT],
                                 start=False, stop=True)
                    i.then_inc(r.h("pe"), 1)
                    t.matmul(PB[h3b][:], ident_bf[:], act_sb[:],
                             start=True, stop=False)
                    i2 = t.matmul(PB[h3b][:], bbuf[32:48, e * 128:(e + 1) * 128],
                                  u_sb[32:48, e * TT:(e + 1) * TT],
                                  start=False, stop=True)
                    i2.then_inc(r.h("pe"), 1)
                r.add("tensor", mmh)
                r.bump("pe", 1)
                hmm_done = r.bump("pe", 1)
                r.wait("scalar", "pe", hmm_done)
                if t2_used:
                    r.wait("scalar", "pe", t2_used)
                if prev_accv:
                    r.wait("scalar", "c", prev_accv)
                sl = sadd(lambda s, h1b=h1b: s.activation(
                    t2_sb[:], PB[h1b][:], AF.Silu))
                h_free[h1b] = sl
                r.wait("vector", "ev", sl)
                vw(r.cur("c"))
                a1v = vadd(lambda v, h3b=h3b: v.tensor_mul(
                    t2_sb[:], t2_sb[:], PB[h3b][:]))
                h_free[h3b] = a1v
                act_free = a1v
                vw(a1v)
                t2v = vadd(lambda v, e=e: v.tensor_mul(
                    t2_sb[:], t2_sb[:], wb_sb[:, e * TT:(e + 1) * TT]))
                vw(t2v)
                if e == 0:
                    accv = vadd(lambda v, fc=fc: v.tensor_copy(
                        acc[:, fc * TT:(fc + 1) * TT], t2_sb[:]))
                else:
                    accv = vadd(lambda v, fc=fc: v.tensor_add(
                        acc[:, fc * TT:(fc + 1) * TT],
                        acc[:, fc * TT:(fc + 1) * TT], t2_sb[:]))
                zbank = 7 if e % 2 == 0 else 2
                grp = e // 2
                r.wait("tensor", "c", t2v)

                def mmz(t, e=e, fc=fc, zbank=zbank, grp=grp):
                    t.matmul(PB[zbank][32 * grp:32 * grp + 16, :],
                             a2t_sb[:, (e * NFC + fc) * 16:(e * NFC + fc + 1) * 16],
                             t2_sb[:],
                             start=False, stop=False,
                             tile_position=(0, 32 * grp)).then_inc(r.h("pe"), 1)
                r.add("tensor", mmz)
                zmm = r.bump("pe", 1)
                t2_used = zmm
                prev_accv = accv
                silu_free = accv
            w13_used[fidx] = r.cur("pe")
            b13_used[fidx] = r.cur("pe")
        for zb in (7, 2):
            def zcl(t, zb=zb):
                t.matmul(PB[zb][:], zrow_bf[0:1, 0:128], zrow_bf[0:1, 128:640],
                         start=False, stop=True).then_inc(r.h("pe"), 1)
            r.add("tensor", zcl)
            r.bump("pe", 1)
        r.wait("vector", "pe", r.cur("pe"))
        vw(r.cur("c"))
        zev1 = vadd(lambda v: v.tensor_scalar_mul(zbf[:, 0:TT], PB[7][:], LORA_SCALE))
        vw(zev1)
        zev2 = vadd(lambda v: v.tensor_scalar_mul(zbf[:, TT:2 * TT], PB[2][:], LORA_SCALE))
        z_done = zev2

        down_mm = {}
        for ocg in range(2):
            for fc in range(NFC):
                fidx2 = (ti * 2 + ocg) * NFC + fc
                wbuf = w2_sb[fidx2 % 2]
                wsem2 = "dA" if fidx2 % 2 == 0 else "dB"
                if fidx2 >= 2:
                    r.wait("gpsimd", "pe", w2_used[fidx2 - 2])
                r.add("gpsimd", lambda g, fc=fc, wbuf=wbuf, wsem2=wsem2: g.dma_start(
                    out=wbuf[:], in_=ap["w2fc"][fc]).then_inc(r.h(wsem2), 16))
                w2ld = r.bump(wsem2)
                r.wait("tensor", wsem2, w2ld)
                if fc == 0:
                    r.wait("tensor", "c", z_done)
                    if ocg == 0 and ti >= 1:
                        r.wait("tensor", "c", y_ln_done)
                    if ocg == 1:
                        r.wait("tensor", "c", ybank_free)
                for oc8 in range(8):
                    oc = ocg * 8 + oc8

                    def mmd(t, fc=fc, oc=oc, oc8=oc8, wbuf=wbuf,
                            inc=oc8 == 7):
                        i = t.matmul(PB[oc8][:],
                                     wbuf[:, oc * 128:(oc + 1) * 128],
                                     acc[:, fc * TT:(fc + 1) * TT],
                                     start=fc == 0, stop=False)
                        if inc:
                            i.then_inc(r.h("pe"), 1)
                    r.add("tensor", mmd)
                w2_used[fidx2] = r.bump("pe", 1)
            for oc8 in range(8):
                oc = ocg * 8 + oc8

                def mmb2(t, oc=oc, oc8=oc8):
                    t.matmul(PB[oc8][:], b2s_sb[:, oc * 128:(oc + 1) * 128],
                             zbf[:, 0:TT], start=False, stop=False)
                    t.matmul(PB[oc8][:],
                             b2s_sb[:, (KC + oc) * 128:(KC + oc + 1) * 128],
                             zbf[:, TT:2 * TT], start=False, stop=True) \
                        .then_inc(r.h("pe"), 1)
                r.add("tensor", mmb2)
                down_mm[oc] = r.bump("pe", 1)
            for oc8 in range(8):
                oc = ocg * 8 + oc8
                gx = ti * KC + oc
                xbuf = x_sl[gx % 2]
                xsem = "xA" if gx % 2 == 0 else "xB"
                if gx >= 2:
                    r.wait("sync", "c", x_used[gx - 2])
                r.add("sync", lambda q, oc=oc, ti=ti, xbuf=xbuf, xsem=xsem: q.dma_start(
                    out=xbuf[:],
                    in_=ap["x1"][:, oc * TPC + ti * TT: oc * TPC + ti * TT + TT])
                    .then_inc(r.h(xsem), 16))
                xd = r.bump(xsem)
                r.wait("vector", "pe", down_mm[oc])
                r.wait("vector", xsem, xd)
                if prev_store:
                    r.wait("vector", "st", prev_store)
                vw(r.cur("c"))
                x_used[gx] = vadd(lambda v, oc=oc, oc8=oc8, xbuf=xbuf: v.tensor_add(
                    y_full[:, oc * TT:(oc + 1) * TT], PB[oc8][:], xbuf[:]))
            ybank_free = r.cur("c")
        y_done = r.cur("c")
        s12p = 0
        for oc in range(KC):
            r.wait("scalar", "c", y_done)
            if s12p:
                r.wait("scalar", "pe", s12p)
            sadd(lambda s, oc=oc: s.activation(
                yn_bf[:], y_full[:, oc * TT:(oc + 1) * TT], AF.Identity))
            sqd = sadd(lambda s, oc=oc: s.activation(
                sqn_bf[:], y_full[:, oc * TT:(oc + 1) * TT], AF.Square))
            r.wait("tensor", "ev", sqd)
            if oc == 0:
                r.wait("tensor", "c", onesbf2)
            r.add("tensor", lambda t, oc=oc: t.matmul(
                PB[5][0:1, 0:TT], ocbf[:], yn_bf[:],
                start=oc == 0, stop=oc == KC - 1))

            def mm2n(t, oc=oc):
                i = t.matmul(PB[6][0:1, 0:TT], ocbf[:], sqn_bf[:],
                             start=oc == 0, stop=oc == KC - 1)
                i.then_inc(r.h("pe"), 1)
            r.add("tensor", mm2n)
            s12p = r.bump("pe", 1)
        r.wait("scalar", "pe", s12p)
        r.wait("vector", "pe", s12p)
        emit_ln_epilogue(nc, r, lnt, y_full,
                         lambda oc: y_full[:, oc * TT:(oc + 1) * TT], PB, 5, 6)
        y_ln_done = r.cur("c")
        r.wait("sync", "c", y_ln_done)
        for oc in range(KC):
            r.add("sync", lambda q, oc=oc, ti=ti: q.dma_start(
                out=ap["x2"][:, oc * TPC + ti * TT: oc * TPC + ti * TT + TT],
                in_=y_full[:, oc * TT:(oc + 1) * TT]).then_inc(r.h("st"), 16))
            r.bump("st")
        prev_store = r.cur("st")
        y_ln_prev = y_ln_done
        x1bf_used[ti] = r.cur("pe")
    return {"x2_done": prev_store}


def prep_launch2b(inp, x2_by_core):
    qkv_w = np.asarray(inp["L1_attn_qkv_w"])
    gate_w = np.asarray(inp["L1_attn_gate_w"])
    qkv_wT = np.ascontiguousarray(qkv_w.T)
    wqk1 = np.empty((D, 2 * H1 * D1), np.float32)
    wv1 = np.empty((D, H1 * D1), np.float32)
    for h in range(H1):
        wqk1[:, 128 * h:128 * h + 64] = qkv_wT[:, 192 * h:192 * h + 64]
        wqk1[:, 128 * h + 64:128 * h + 128] = qkv_wT[:, 192 * h + 64:192 * h + 128]
        wv1[:, 64 * h:64 * h + 64] = qkv_wT[:, 192 * h + 128:192 * h + 192]
    wqk1_t = _oc_tiles(wqk1)
    wv1_t = _oc_tiles(wv1).reshape(16, 128, KC * 128)
    wv1_g = np.empty((4, 128, KC * 512), np.float32)
    for vc in range(4):
        for sub in range(4):
            for kc in range(KC):
                wv1_g[vc][:, kc * 512 + sub * 128:kc * 512 + (sub + 1) * 128] = \
                    wv1_t[vc * 4 + sub][:, kc * 128:(kc + 1) * 128]
    gwt = _oc_tiles(np.ascontiguousarray(gate_w.T))

    qd, kd, ddm, bd = _l1_decay()
    qd_e = np.ascontiguousarray(np.broadcast_to(qd[:, None, :], (H1, 64, BS1)))
    kd_e = np.ascontiguousarray(np.broadcast_to(kd[:, None, :], (H1, 64, BS1)))
    ddTf = ddm.transpose(0, 2, 1)
    ddT = np.ascontiguousarray(
        ddTf.reshape(H1, 2, 128, BS1).transpose(0, 2, 1, 3).reshape(H1, 128, 2 * BS1))
    ident = np.eye(128, dtype=np.float32)

    pc = []
    for c in range(8):
        pc.append({
            "x2": x2_by_core[c],
            "wqk1": wqk1_t, "wv1": wv1_g, "gwt": gwt,
            "qd": qd_e, "kd": kd_e, "ddT": ddT, "ident": ident,
        })
    return pc


def build_launch2b():
    nc = bass.Bass("TRN2", target_bir_lowering=False)
    ap = {
        "x2": nc.dram_tensor("x2", [128, KC * TPC], F32, kind="ExternalInput").ap(),
        "wqk1": nc.dram_tensor("wqk1", [32, 128, KC * 128], F32, kind="ExternalInput").ap(),
        "wv1": nc.dram_tensor("wv1", [4, 128, KC * 512], F32, kind="ExternalInput").ap(),
        "gwt": nc.dram_tensor("gwt", [16, 128, KC * 128], F32, kind="ExternalInput").ap(),
        "qd": nc.dram_tensor("qd", [H1, 64, BS1], F32, kind="ExternalInput").ap(),
        "kd": nc.dram_tensor("kd", [H1, 64, BS1], F32, kind="ExternalInput").ap(),
        "ddT": nc.dram_tensor("ddT", [H1, 128, 2 * BS1], F32, kind="ExternalInput").ap(),
        "ident": nc.dram_tensor("ident", [128, 128], F32, kind="ExternalInput").ap(),
        "qp": nc.dram_tensor("qp", [H1 * 64, TPC], BF16, kind="ExternalOutput").ap(),
        "Opart": nc.dram_tensor("Opart", [H1 * 64, TPC], F32, kind="ExternalOutput").ap(),
        "Cch": nc.dram_tensor("Cch", [2, H1, 64, 64], F32, kind="ExternalOutput").ap(),
        "gate": nc.dram_tensor("gate", [D, TPC], BF16, kind="ExternalOutput").ap(),
        "qk1": nc.dram_tensor("qk1", [2 * H1 * 64, TPC], BF16, kind="Internal").ap(),
        "v1": nc.dram_tensor("v1", [TPC, H1 * 64], BF16, kind="Internal").ap(),
    }

    es = ExitStack()
    with es:
        sb = lambda name, shape, dt: es.enter_context(nc.sbuf_tensor(name, shape, dt))
        PB = [es.enter_context(nc.psum_tensor(f"bank{i}", [128, 512], F32))
              for i in range(8)]
        r = Rec(nc, es)
        for sname in ("in", "pe", "ev", "rp", "st", "c", "gin",
                      "w0", "w1", "v0", "v1s", "s0", "s1",
                      "q0", "q1", "k0", "k1", "dd0", "dd1", "vb0", "vb1"):
            r.sem(sname)
        block = es.enter_context(nc.Block())

        x_bf = sb("x_bf", [128, KC * TPC], BF16)
        ident_bf = sb("ident_bf", [128, 128], BF16)
        wbuf2 = [sb(f"wb{i}", [128, KC * 128], BF16) for i in range(2)]
        wvbuf = [sb(f"wvb_{i}", [128, KC * 512], BF16) for i in range(2)]
        out_bf = [sb(f"obf{i}", [128, TT], BF16) for i in range(2)]

        r.add("gpsimd", lambda g: g.dma_start(out=x_bf[:], in_=ap["x2"])
              .then_inc(r.h("gin"), 16))
        r.bump("gin")
        r.add("gpsimd", lambda g: g.dma_start(out=ident_bf[:], in_=ap["ident"])
              .then_inc(r.h("gin"), 16))
        r.bump("gin")
        gin_done = r.cur("gin")

        ev_done, st_done, pe_after = {}, {}, {}
        gidx = 0
        for kind, nocs, wsrc, dst in (("qk", 32, "wqk1", "qk1"),
                                      ("gate", 16, "gwt", "gate")):
            for oc in range(nocs):
                wbuf = wbuf2[gidx % 2]
                wsem = "w0" if gidx % 2 == 0 else "w1"
                if gidx >= 2:
                    r.wait("gpsimd", "pe", pe_after[gidx - 2])
                r.add("gpsimd", lambda g, wsrc=wsrc, oc=oc, wbuf=wbuf, wsem=wsem:
                      g.dma_start(out=wbuf[:], in_=ap[wsrc][oc])
                      .then_inc(r.h(wsem), 16))
                w_done = r.bump(wsem)
                for ti in range(NTT):
                    idx = gidx * NTT + ti
                    pbuf = PB[idx % 2]
                    r.wait("tensor", wsem, w_done)
                    if idx >= 2:
                        r.wait("tensor", "ev", ev_done[idx - 2])
                    if idx == 0:
                        r.wait("tensor", "gin", gin_done)
                    for kc in range(KC):
                        def mm(t, oc=oc, ti=ti, kc=kc, pbuf=pbuf, wbuf=wbuf,
                               last=kc == KC - 1):
                            i = t.matmul(pbuf[:], wbuf[:, kc * 128:(kc + 1) * 128],
                                         x_bf[:, kc * TPC + ti * TT: kc * TPC + ti * TT + TT],
                                         start=kc == 0, stop=last)
                            if last:
                                i.then_inc(r.h("pe"), 1)
                        r.add("tensor", mm)
                    mm_done = r.bump("pe", 1)
                    obuf = out_bf[idx % 2]
                    r.wait("scalar", "pe", mm_done)
                    ssem = "s0" if idx % 2 == 0 else "s1"
                    if idx >= 2:
                        r.wait("scalar", ssem, st_done[idx - 2])
                    fn = AF.Silu if kind == "qk" else AF.Sigmoid
                    r.add("scalar", lambda s, pbuf=pbuf, obuf=obuf, fn=fn:
                          s.activation(obuf[:], pbuf[:], fn).then_inc(r.h("ev"), 1))
                    ev_done[idx] = r.bump("ev", 1)
                    r.wait("sync", "ev", ev_done[idx])
                    r.add("sync", lambda q, dst=dst, oc=oc, ti=ti, obuf=obuf, ssem=ssem:
                          q.dma_start(out=ap[dst][128 * oc:128 * (oc + 1),
                                                  ti * TT:(ti + 1) * TT],
                                      in_=obuf[:]).then_inc(r.h(ssem), 16))
                    st_done[idx] = r.bump(ssem)
                pe_after[gidx] = r.cur("pe")
                gidx += 1
        vev, vst, vpe = {}, {}, {}
        for vc in range(4):
            wbuf = wvbuf[vc % 2]
            wsem = "v0" if vc % 2 == 0 else "v1s"
            if vc >= 2:
                r.wait("gpsimd", "pe", vpe[vc - 2])
            r.add("gpsimd", lambda g, vc=vc, wbuf=wbuf, wsem=wsem: g.dma_start(
                out=wbuf[:], in_=ap["wv1"][vc]).then_inc(r.h(wsem), 16))
            w_done = r.bump(wsem)
            for tt in range(TPC // 128):
                idx = vc * (TPC // 128) + tt
                pbuf = PB[2 + idx % 2]
                r.wait("tensor", wsem, w_done)
                if idx >= 2:
                    r.wait("tensor", "ev", vev[idx - 2])
                for kc in range(KC):
                    def mmv(t, vc=vc, tt=tt, kc=kc, pbuf=pbuf, wbuf=wbuf,
                            last=kc == KC - 1):
                        i = t.matmul(pbuf[:],
                                     x_bf[:, kc * TPC + tt * 128: kc * TPC + (tt + 1) * 128],
                                     wbuf[:, kc * 512:(kc + 1) * 512],
                                     start=kc == 0, stop=last)
                        if last:
                            i.then_inc(r.h("pe"), 1)
                    r.add("tensor", mmv)
                mm_done = r.bump("pe", 1)
                obuf = out_bf[idx % 2]
                r.wait("scalar", "pe", mm_done)
                ssem = "s0" if idx % 2 == 0 else "s1"
                if idx >= 2:
                    r.wait("scalar", ssem, vst[idx - 2])
                else:
                    r.wait("scalar", ssem, r.cur(ssem))
                r.add("scalar", lambda s, pbuf=pbuf, obuf=obuf:
                      s.activation(obuf[:], pbuf[:], AF.Silu).then_inc(r.h("ev"), 1))
                vev[idx] = r.bump("ev", 1)
                r.wait("sync", "ev", vev[idx])
                r.add("sync", lambda q, vc=vc, tt=tt, obuf=obuf, ssem=ssem: q.dma_start(
                    out=ap["v1"][tt * 128:(tt + 1) * 128, vc * 512:(vc + 1) * 512],
                    in_=obuf[:]).then_inc(r.h(ssem), 16))
                vst[idx] = r.bump(ssem)
            vpe[vc] = r.cur("pe")
        proj_s0 = r.cur("s0")
        proj_s1 = r.cur("s1")
        projections_pe = r.cur("pe")

        qtile = [sb(f"qt{i}", [64, CHUNK], BF16) for i in range(2)]
        ktile = [sb(f"kt{i}", [64, CHUNK], BF16) for i in range(2)]
        vtile = [sb(f"vt{i}", [128, 8 * 64], BF16) for i in range(2)]
        qd_sb = [sb(f"qd{i}", [64, BS1], F32) for i in range(2)]
        kd_sb = [sb(f"kd{i}", [64, BS1], F32) for i in range(2)]
        dd_sb = [sb(f"dd{i}", [128, 2 * BS1], F32) for i in range(2)]
        qp_st = sb("qp_st", [64, CHUNK], BF16)
        op_st = sb("op_st", [64, CHUNK], F32)
        kp_bf = sb("kp_bf", [64, BS1], BF16)
        ktr = sb("ktr", [128, 2 * 64], BF16)
        p1_sb = sb("p1_sb", [128, 2 * BS1], BF16)
        s_f32 = sb("s_f32", [64, 64], F32)
        s_bf = sb("s_bf", [64, 64], BF16)
        cch_st = sb("cch_st", [64, 64], F32)

        def vadd(fn):
            r.add("vector", lambda v, fn=fn: fn(v).then_inc(r.h("c"), 1))
            return r.bump("c", 1)

        r.wait("sync", "s0", proj_s0)
        r.wait("sync", "s1", proj_s1)
        hdec_used = {}
        qk_used = {}
        dd_done = {}
        for h in range(H1):
            hb = h % 2
            dsem = "dd0" if hb == 0 else "dd1"
            if h >= 2:
                r.wait("sync", "pe", hdec_used[h - 2])

            def ldd(q, h=h, hb=hb, dsem=dsem):
                q.dma_start(out=qd_sb[hb][:], in_=ap["qd"][h]).then_inc(r.h(dsem), 16)
                q.dma_start(out=kd_sb[hb][:], in_=ap["kd"][h]).then_inc(r.h(dsem), 16)
                q.dma_start(out=dd_sb[hb][:], in_=ap["ddT"][h]).then_inc(r.h(dsem), 16)
            r.add("sync", ldd)
            dd_done[h] = r.bump(dsem, 48)
            for chos in range(2):
                it = h * 2 + chos
                tb = it % 2
                qsem = "q0" if tb == 0 else "q1"
                vsem = "vb0" if tb == 0 else "vb1"
                if it >= 2:
                    r.wait("sync", "pe", qk_used[it - 2])

                def ldq(q, h=h, chos=chos, tb=tb, qsem=qsem, vsem=vsem):
                    q.dma_start(out=qtile[tb][:],
                                in_=ap["qk1"][128 * h:128 * h + 64,
                                              chos * CHUNK:(chos + 1) * CHUNK]) \
                        .then_inc(r.h(qsem), 16)
                    q.dma_start(out=ktile[tb][:],
                                in_=ap["qk1"][128 * h + 64:128 * h + 128,
                                              chos * CHUNK:(chos + 1) * CHUNK]) \
                        .then_inc(r.h(qsem), 16)
                    q.dma_start(out=vtile[tb][:].rearrange("p (sq f) -> p sq f", sq=8),
                                in_=ap["v1"][chos * CHUNK:(chos + 1) * CHUNK,
                                             64 * h:64 * (h + 1)]
                                .rearrange("(sq p) f -> p sq f", p=128)) \
                        .then_inc(r.h(vsem), 16)
                r.add("sync", ldq)
                q_done = r.bump(qsem, 32)
                v_done = r.bump(vsem)
                r.wait("vector", qsem, q_done)
                r.wait("vector", vsem, v_done)
                r.wait("vector", dsem, dd_done[h])
                if it >= 1:
                    r.wait("vector", "st", qp_op_st)
                vw_c = vadd(lambda v: v.memset(s_f32[:], 0.0))
                sbf_c = vadd(lambda v: v.memset(s_bf[:], 0.0))
                if it == 0:
                    p_prev = {0: 0, 1: 0}
                    o_ev_c = 0
                    s_upd_c = 0
                for blk in range(4):
                    q_sl = qtile[tb][:, blk * BS1:(blk + 1) * BS1]
                    k_sl = ktile[tb][:, blk * BS1:(blk + 1) * BS1]
                    if blk >= 1:
                        r.wait("vector", "pe", inter_done)
                    qp_c = vadd(lambda v, q_sl=q_sl, hb=hb, blk=blk: v.tensor_mul(
                        qp_st[:, blk * BS1:(blk + 1) * BS1], q_sl, qd_sb[hb][:]))
                    kp_c = vadd(lambda v, k_sl=k_sl, hb=hb: v.tensor_mul(
                        kp_bf[:], k_sl, kd_sb[hb][:]))
                    r.wait("tensor", "c", kp_c)
                    if blk >= 1 or it >= 1:
                        r.wait("tensor", "c", ktr_ev if it + blk else 0)
                    for ci in range(2):
                        def trk(t, ci=ci):
                            i = t.transpose(
                                PB[4].ap().bitcast(BF16)[0:128, ci * 64:(ci + 1) * 64],
                                kp_bf[:, ci * 128:(ci + 1) * 128], ident_bf[0:64, 0:64])
                            if ci == 1:
                                i.then_inc(r.h("pe"), 1)
                        r.add("tensor", trk)
                    ktr_pe = r.bump("pe", 1)
                    r.wait("vector", "pe", ktr_pe)
                    vw2 = r.cur("c")
                    r.wait("vector", "c", vw2)
                    ktr_ev = vadd(lambda v: v.tensor_copy(
                        ktr[:], PB[4].ap().bitcast(BF16)[0:128, 0:128]))
                    r.wait("tensor", "c", ktr_ev)
                    if s_upd_c:
                        r.wait("tensor", "c", s_upd_c)
                    for ci in range(2):
                        def mmc(t, ci=ci, blk=blk, tb=tb):
                            i = t.matmul(PB[2][0:64, 0:64],
                                         ktr[:, ci * 64:(ci + 1) * 64],
                                         vtile[tb][:, (2 * blk + ci) * 64:(2 * blk + ci + 1) * 64],
                                         start=ci == 0, stop=ci == 1)
                            if ci == 1:
                                i.then_inc(r.h("pe"), 1)
                        r.add("tensor", mmc)
                    c_mm = r.bump("pe", 1)
                    p_evs = []
                    for ci in range(2):
                        bank = PB[ci]
                        if p_prev[ci]:
                            r.wait("tensor", "c", p_prev[ci])
                        def mma(t, ci=ci, bank=bank, k_sl=k_sl, q_sl=q_sl):
                            t.matmul(bank[0:128, 0:BS1],
                                     k_sl[:, ci * 128:(ci + 1) * 128], q_sl,
                                     start=True, stop=True).then_inc(r.h("pe"), 1)
                        r.add("tensor", mma)
                        aw_mm = r.bump("pe", 1)
                        r.wait("vector", "pe", aw_mm)
                        r.wait("vector", "c", r.cur("c"))
                        p_evs.append(vadd(lambda v, ci=ci, bank=bank, hb=hb: v.tensor_mul(
                            p1_sb[:, ci * BS1:(ci + 1) * BS1], bank[0:128, 0:BS1],
                            dd_sb[hb][:, ci * BS1:(ci + 1) * BS1])))
                    p_prev = {0: p_evs[0], 1: p_evs[1]}
                    r.wait("tensor", "c", p_evs[1])
                    if o_ev_c:
                        r.wait("tensor", "c", o_ev_c)
                    r.wait("tensor", "c", sbf_c)

                    def mmav(t, blk=blk, tb=tb):
                        t.matmul(PB[3][0:64, 0:BS1],
                                 vtile[tb][:, (2 * blk) * 64:(2 * blk + 1) * 64],
                                 p1_sb[:, 0:BS1], start=True, stop=False)
                        t.matmul(PB[3][0:64, 0:BS1],
                                 vtile[tb][:, (2 * blk + 1) * 64:(2 * blk + 2) * 64],
                                 p1_sb[:, BS1:2 * BS1], start=False, stop=False)
                        t.matmul(PB[3][0:64, 0:BS1], s_bf[:],
                                 qp_st[:, blk * BS1:(blk + 1) * BS1],
                                 start=False, stop=True).then_inc(r.h("pe"), 1)
                    r.add("tensor", mmav)
                    inter_done = r.bump("pe", 1)
                    r.wait("vector", "pe", inter_done)
                    r.wait("vector", "c", r.cur("c"))
                    o_ev_c = vadd(lambda v, blk=blk: v.tensor_copy(
                        op_st[:, blk * BS1:(blk + 1) * BS1], PB[3][0:64, 0:BS1]))
                    r.wait("vector", "pe", c_mm)
                    s_upd_c = vadd(lambda v, h=h: v.scalar_tensor_tensor(
                        s_f32[:], s_f32[:], float(BD1[h]), PB[2][0:64, 0:64],
                        ALU.mult, ALU.add))
                    r.wait("vector", "c", s_upd_c)
                    sbf_c = vadd(lambda v: v.tensor_copy(s_bf[:], s_f32[:]))
                qk_used[it] = r.cur("pe")
                hdec_used[h] = r.cur("pe")
                r.wait("sync", "c", sbf_c)
                r.wait("sync", "c", o_ev_c)

                def stq(q, h=h, chos=chos):
                    q.dma_start(out=ap["qp"][64 * h:64 * (h + 1),
                                             chos * CHUNK:(chos + 1) * CHUNK],
                                in_=qp_st[:]).then_inc(r.h("st"), 16)
                    q.dma_start(out=ap["Opart"][64 * h:64 * (h + 1),
                                                chos * CHUNK:(chos + 1) * CHUNK],
                                in_=op_st[:]).then_inc(r.h("st"), 16)
                    q.dma_start(out=ap["Cch"][chos, h], in_=s_f32[:]) \
                        .then_inc(r.h("st"), 16)
                r.add("sync", stq)
                qp_op_st = r.bump("st", 48)
        r.wait("sync", "st", r.cur("st"))
        r.run(block)
    return nc


BD1 = _l1_decay()[3]


def prep_launch3(inp, l2out, l2bout):
    out_w = np.asarray(inp["L1_attn_out_w"])
    norm_w = np.asarray(inp["L1_attn_norm_w"])
    wq_gate = np.asarray(inp["L1_ffn_wq_gate"])
    s_gate = np.asarray(inp["L1_ffn_s_gate"])
    wq_up = np.asarray(inp["L1_ffn_wq_up"])
    s_up = np.asarray(inp["L1_ffn_s_up"])
    wq_down = np.asarray(inp["L1_ffn_wq_down"])
    s_down = np.asarray(inp["L1_ffn_s_down"])

    outw_t = _oc_tiles(np.ascontiguousarray(out_w.T))
    wgT = _oc_tiles(np.ascontiguousarray(wq_gate.T.astype(np.int32)))
    wuT = _oc_tiles(np.ascontiguousarray(wq_up.T.astype(np.int32)))
    wdT = np.ascontiguousarray(wq_down.T.astype(np.int32))
    wdt_oc = _oc_tiles(wdT)
    wdfc = np.zeros((NFC, 128, KC * 128), np.int32)
    for fc in range(NFC):
        for oc in range(KC):
            wdfc[fc][:, oc * 128:(oc + 1) * 128] = wdt_oc[oc][:, fc * 128:(fc + 1) * 128]

    _, _, _, bd = _l1_decay()
    CgT = np.zeros((2, 64, 8 * H1 * 64), np.float32)
    for cc in range(8):
        b = cc // 4
        for chos in range(2):
            absj = _chunks_of(cc)[chos]
            for h in range(H1):
                CgT[b][:, (absj * H1 + h) * 64:(absj * H1 + h + 1) * 64] = \
                    l2bout[cc]["Cch"][chos, h]
    pc = []
    for c in range(8):
        b = c // 4
        wcols = np.zeros((64, 2 * 8 * H1), np.float32)
        for oc2 in range(2):
            C = _chunks_of(c)[oc2]
            for j in range(8):
                if j < C:
                    for h in range(H1):
                        wcols[:, oc2 * 8 * H1 + j * H1 + h] = bd[h] ** (4 * (C - 1 - j))
        pc.append({
            "x2": l2out[c]["x2"],
            "Opart": l2bout[c]["Opart"],
            "qp": l2bout[c]["qp"],
            "gate": l2bout[c]["gate"],
            "Cg": CgT[b],
            "wcols": wcols,
            "nwp": _cols2d(norm_w),
            "outw": outw_t,
            "wgT": wgT, "wuT": wuT, "wdfc": wdfc,
            "sg": _cols2d(s_gate), "su": _cols2d(s_up), "sd": _cols2d(s_down),
        })
    return pc


def build_launch3():
    nc = bass.Bass("TRN2", target_bir_lowering=False)
    ap = {
        "x2": nc.dram_tensor("x2", [128, KC * TPC], F32, kind="ExternalInput").ap(),
        "Opart": nc.dram_tensor("Opart", [H1 * 64, TPC], F32, kind="ExternalInput").ap(),
        "qp": nc.dram_tensor("qp", [H1 * 64, TPC], BF16, kind="ExternalInput").ap(),
        "gate": nc.dram_tensor("gate", [D, TPC], BF16, kind="ExternalInput").ap(),
        "Cg": nc.dram_tensor("Cg", [64, 8 * H1 * 64], F32, kind="ExternalInput").ap(),
        "wcols": nc.dram_tensor("wcols", [64, 2 * 8 * H1], F32, kind="ExternalInput").ap(),
        "nwp": nc.dram_tensor("nwp", [128, KC], F32, kind="ExternalInput").ap(),
        "outw": nc.dram_tensor("outw", [KC, 128, KC * 128], F32, kind="ExternalInput").ap(),
        "wgT": nc.dram_tensor("wgT", [NFC, 128, KC * 128], I32, kind="ExternalInput").ap(),
        "wuT": nc.dram_tensor("wuT", [NFC, 128, KC * 128], I32, kind="ExternalInput").ap(),
        "wdfc": nc.dram_tensor("wdfc", [NFC, 128, KC * 128], I32, kind="ExternalInput").ap(),
        "sg": nc.dram_tensor("sg", [128, NFC], F32, kind="ExternalInput").ap(),
        "su": nc.dram_tensor("su", [128, NFC], F32, kind="ExternalInput").ap(),
        "sd": nc.dram_tensor("sd", [128, KC], F32, kind="ExternalInput").ap(),
        "out": nc.dram_tensor("out", [128, KC * TPC], F32, kind="ExternalOutput").ap(),
        "x3": nc.dram_tensor("x3", [128, KC * TPC], F32, kind="Internal").ap(),
        "gu_d": nc.dram_tensor("gu_d", [F0, TPC], BF16, kind="Internal").ap(),
    }

    es = ExitStack()
    with es:
        sb = lambda name, shape, dt: es.enter_context(nc.sbuf_tensor(name, shape, dt))
        PB = [es.enter_context(nc.psum_tensor(f"bank{i}", [128, 512], F32))
              for i in range(8)]
        AR = Arena(nc, es, "arena", 168 * 1024)
        r = Rec(nc, es)
        for sname in ("in", "pe", "ev", "rp", "st", "c", "gin",
                      "oA", "oB", "qA", "qB", "gA", "gB", "wA", "wB",
                      "xA", "xB", "fA", "fB", "uA", "uB", "dA", "dB",
                      "x3A", "x3B", "r3A", "r3B"):
            r.sem(sname)
        block = es.enter_context(nc.Block())

        ones_row = sb("ones_row", [1, 128], F32)
        ones_col = sb("ones_col", [128, 1], F32)
        ones_col_bf = sb("ones_cbf", [128, 1], BF16)
        nw_sb = sb("nw_sb", [128, KC], F32)
        sg_sb = sb("sg_sb", [128, NFC], F32)
        su_sb = sb("su_sb", [128, NFC], F32)
        sd_sb = sb("sd_sb", [128, KC], F32)
        lnt = {
            "ones_row": ones_row,
            "m_row": sb("m_row", [1, TT], F32), "m2_row": sb("m2_row", [1, TT], F32),
            "var_row": sb("var_row", [1, TT], F32), "rstd_row": sb("rstd_row", [1, TT], F32),
            "mr_row": sb("mr_row", [1, TT], F32), "tmp": sb("tmpt", [128, TT], F32),
            "eps_col": sb("eps_col", [1, 1], F32),
        }

        def vadd(fn):
            r.add("vector", lambda v, fn=fn: fn(v).then_inc(r.h("c"), 1))
            return r.bump("c", 1)

        def sadd(fn):
            r.add("scalar", lambda s, fn=fn: fn(s).then_inc(r.h("ev"), 1))
            return r.bump("ev", 1)

        r.add("vector", lambda v: v.memset(ones_row[:], 1.0))
        r.add("vector", lambda v: v.memset(ones_col[:], 1.0))
        r.add("vector", lambda v: v.memset(lnt["eps_col"][:], 1e-5))
        ones_done = vadd(lambda v: v.memset(ones_col_bf[:], 1.0))
        for t, dst in (("nwp", nw_sb), ("sg", sg_sb), ("su", su_sb), ("sd", sd_sb)):
            r.add("sync", lambda q, t=t, dst=dst: q.dma_start(
                out=dst[:], in_=ap[t]).then_inc(r.h("in"), 16))
            r.bump("in")
        const_in = r.cur("in")

        _, _, _, bd = _l1_decay()

        m0 = AR.mark()
        cg_sb = AR.tile(128, 8 * H1 * 64, F32)
        wc_sb = sb("wc_sb", [128, 2 * 8 * H1], F32)
        accd = sb("accd", [128, 64], F32)
        inc_sb = AR.tile(128, 2 * KC * 4 * 128, BF16)
        r.add("sync", lambda q: q.dma_start(out=cg_sb[0:64, :], in_=ap["Cg"])
              .then_inc(r.h("in"), 16))
        r.bump("in")
        r.add("sync", lambda q: q.dma_start(out=cg_sb[64:128, :], in_=ap["Cg"])
              .then_inc(r.h("in"), 16))
        r.bump("in")
        r.add("vector", lambda v: v.memset(inc_sb[:], 0.0).then_inc(r.h("c"), 1))
        r.bump("c", 1)
        r.add("sync", lambda q: q.dma_start(out=wc_sb[0:64, :], in_=ap["wcols"])
              .then_inc(r.h("in"), 16))
        r.bump("in")
        r.add("sync", lambda q: q.dma_start(out=wc_sb[64:128, :], in_=ap["wcols"])
              .then_inc(r.h("in"), 16))
        cg_in = r.bump("in")
        r.wait("vector", "in", cg_in)
        for oc2 in range(2):
            for h in range(H1):
                sub = h % 2
                arow = accd[64 * sub:64 * sub + 64, :]
                for j in range(8):
                    col = oc2 * 8 * H1 + j * H1 + h
                    src = cg_sb[64 * sub:64 * sub + 64,
                                (j * H1 + h) * 64:(j * H1 + h + 1) * 64]
                    scl_ap = wc_sb[64 * sub:64 * sub + 64, col:col + 1]
                    if j == 0:
                        vadd(lambda v, src=src, scl_ap=scl_ap, arow=arow:
                             v.tensor_scalar_mul(arow, src, scl_ap))
                    else:
                        vadd(lambda v, src=src, scl_ap=scl_ap, arow=arow:
                             v.scalar_tensor_tensor(
                                 arow, src, scl_ap, arow, ALU.mult, ALU.add))
                    r.wait("vector", "c", r.cur("c"))
                for il in range(4):
                    hp = h // 2
                    off = ((oc2 * KC + hp) * 4 + il) * 128 + sub * 64
                    vadd(lambda v, off=off, h=h, il=il, sub=sub, arow=arow:
                         v.tensor_scalar_mul(
                             inc_sb[64 * sub:64 * sub + 64, off:off + 64],
                             arow, float(bd[h] ** il)))
                    r.wait("vector", "c", r.cur("c"))
        inc_done = r.cur("c")

        attn_pk = AR.tile(128, KC * BS1, F32)
        attng = AR.tile(128, KC * BS1, BF16)
        sq64 = AR.tile(128, BS1, BF16)
        ybf256 = AR.tile(128, BS1, BF16)
        sqbf256 = AR.tile(128, BS1, BF16)
        y256 = AR.tile(128, KC * BS1, F32)
        op_t = [AR.tile(128, BS1, F32) for _ in range(2)]
        qp_t = [AR.tile(128, BS1, BF16) for _ in range(2)]
        gate16 = AR.tile(128, KC * BS1, BF16)
        ow_t = [AR.tile(128, KC * 128, BF16) for _ in range(2)]
        x2_t = [AR.tile(128, BS1, F32) for _ in range(2)]
        lnt256 = {
            "ones_row": ones_row,
            "m_row": sb("m_row2", [1, BS1], F32), "m2_row": sb("m2_row2", [1, BS1], F32),
            "var_row": sb("var_row2", [1, BS1], F32),
            "rstd_row": sb("rstd_row2", [1, BS1], F32),
            "mr_row": sb("mr_row2", [1, BS1], F32), "tmp": sb("tmpt2", [128, BS1], F32),
            "eps_col": lnt["eps_col"],
        }
        hp_used, x3st = {}, {}
        prev_x3_st = 0
        prev_ow = {}
        x2_used, y_prev = {}, {}
        gate_used = {}
        for chos in range(2):
            for blk in range(4):
                tix = chos * 4 + blk
                tok0 = chos * CHUNK + blk * BS1
                s2_mm = 0
                gsem = "gA" if tix % 2 == 0 else "gB"
                if tix >= 2:
                    r.wait("sync", "c", gate_used[tix - 2])
                r.add("sync", lambda q, tok0=tok0, gsem=gsem: q.dma_start(
                    out=gate16[:].rearrange("p (a t) -> p a t", a=KC),
                    in_=ap["gate"].rearrange("(a p) t -> p a t", p=128)
                    [:, :, tok0:tok0 + BS1]).then_inc(r.h(gsem), 16))
                g_ld = r.bump(gsem)
                for hp in range(KC):
                    it = tix * KC + hp
                    ob, qb = op_t[it % 2], qp_t[it % 2]
                    osem = "oA" if it % 2 == 0 else "oB"
                    qsem = "qA" if it % 2 == 0 else "qB"
                    if it >= 2:
                        r.wait("sync", "c", hp_used[it - 2])

                    def ldh(q, hp=hp, tok0=tok0, ob=ob, qb=qb,
                            osem=osem, qsem=qsem):
                        q.dma_start(out=ob[:], in_=ap["Opart"][128 * hp:128 * (hp + 1),
                                                               tok0:tok0 + BS1]) \
                            .then_inc(r.h(osem), 16)
                        q.dma_start(out=qb[:], in_=ap["qp"][128 * hp:128 * (hp + 1),
                                                            tok0:tok0 + BS1]) \
                            .then_inc(r.h(qsem), 16)
                    r.add("sync", ldh)
                    o_ld = r.bump(osem)
                    q_ld = r.bump(qsem)
                    r.wait("tensor", qsem, q_ld)
                    if it == 0:
                        r.wait("tensor", "c", inc_done)
                    if it >= 1:
                        r.wait("tensor", "c", attn_ev)
                    off = ((chos * KC + hp) * 4 + blk) * 128

                    def mmc(t, off=off, qb=qb):
                        t.matmul(PB[0][:, 0:BS1], inc_sb[:, off:off + 128],
                                 qb[:], start=True, stop=True) \
                            .then_inc(r.h("pe"), 1)
                    r.add("tensor", mmc)
                    corr_mm = r.bump("pe", 1)
                    r.wait("vector", "pe", corr_mm)
                    r.wait("vector", osem, o_ld)
                    r.wait("vector", "c", r.cur("c"))
                    attn_ev = vadd(lambda v, hp=hp, ob=ob: v.tensor_add(
                        attn_pk[:, hp * BS1:(hp + 1) * BS1], ob[:], PB[0][:, 0:BS1]))
                    r.wait("scalar", "c", attn_ev)
                    if s2_mm:
                        r.wait("scalar", "pe", s2_mm)
                    sqe = sadd(lambda s, hp=hp: s.activation(
                        sq64[:], attn_pk[:, hp * BS1:(hp + 1) * BS1], AF.Square))
                    r.wait("tensor", "ev", sqe)
                    if it == 0:
                        r.wait("tensor", "c", ones_done)

                    def mms(t, hp=hp, tix=tix):
                        t.matmul(PB[1][0:1, 0:BS1], ones_col_bf[:], sq64[:],
                                 start=hp == 0, stop=hp == KC - 1) \
                            .then_inc(r.h("pe"), 1)
                    r.add("tensor", mms)
                    s2_mm = r.bump("pe", 1)
                    hp_used[it] = attn_ev
                r.wait("scalar", "pe", s2_mm)
                rr1 = sadd(lambda s: s.activation(
                    lnt256["var_row"][:], PB[1][0:1, 0:BS1], AF.Identity,
                    scale=1.0 / D))
                r.wait("scalar", "ev", rr1)
                rr2 = sadd(lambda s: s.activation(
                    lnt256["var_row"][:], lnt256["var_row"][:], AF.Sqrt,
                    bias=lnt["eps_col"][:]))
                r.wait("vector", "ev", rr2)
                r.wait("vector", "c", r.cur("c"))
                rr3 = vadd(lambda v: v.reciprocal(
                    lnt256["rstd_row"][:], lnt256["var_row"][:]))
                r.wait("tensor", "c", rr3)
                r.add("tensor", lambda t: t.matmul(
                    PB[1][:, 0:BS1], ones_row[:], lnt256["rstd_row"][:],
                    start=True, stop=True).then_inc(r.h("pe"), 1))
                rbc = r.bump("pe", 1)
                r.wait("vector", "pe", rbc)
                ag_ev = 0
                for hp in range(KC):
                    r.wait("vector", "c", r.cur("c"))
                    t1 = vadd(lambda v, hp=hp: v.tensor_mul(
                        lnt256["tmp"][:], attn_pk[:, hp * BS1:(hp + 1) * BS1],
                        PB[1][:, 0:BS1]))
                    r.wait("vector", "c", t1)
                    if hp == 0:
                        r.wait("vector", gsem, g_ld)
                    ag_ev = vadd(lambda v, hp=hp: v.scalar_tensor_tensor(
                        attng[:, hp * BS1:(hp + 1) * BS1], lnt256["tmp"][:],
                        nw_sb[:, hp:hp + 1],
                        gate16[:, hp * BS1:(hp + 1) * BS1], ALU.mult, ALU.mult))
                gate_used[tix] = ag_ev
                y_evs = 0
                s12_mm = 0
                for oc in range(KC):
                    it2 = tix * KC + oc
                    wb = ow_t[it2 % 2]
                    wsem = "wA" if it2 % 2 == 0 else "wB"
                    if it2 >= 2:
                        r.wait("gpsimd", "pe", prev_ow[it2 - 2])
                    r.add("gpsimd", lambda g, oc=oc, wb=wb, wsem=wsem: g.dma_start(
                        out=wb[:], in_=ap["outw"][oc]).then_inc(r.h(wsem), 16))
                    owl = r.bump(wsem)
                    xb = x2_t[it2 % 2]
                    xsem = "xA" if it2 % 2 == 0 else "xB"
                    if it2 >= 2:
                        r.wait("sync", "c", x2_used[it2 - 2])
                    r.add("sync", lambda q, oc=oc, tok0=tok0, xb=xb, xsem=xsem:
                          q.dma_start(out=xb[:],
                                      in_=ap["x2"][:, oc * TPC + tok0: oc * TPC + tok0 + BS1])
                          .then_inc(r.h(xsem), 16))
                    x2l = r.bump(xsem)
                    bank = PB[2 + it2 % 2]
                    r.wait("tensor", wsem, owl)
                    r.wait("tensor", "c", ag_ev)
                    if it2 >= 2:
                        r.wait("tensor", "c", y_prev.get(it2 - 2, 0))
                    for kc in range(KC):
                        def mmo(t, oc=oc, kc=kc, wb=wb, bank=bank,
                                last=kc == KC - 1):
                            i = t.matmul(bank[:, 0:BS1],
                                         wb[:, kc * 128:(kc + 1) * 128],
                                         attng[:, kc * BS1:(kc + 1) * BS1],
                                         start=kc == 0, stop=last)
                            if last:
                                i.then_inc(r.h("pe"), 1)
                        r.add("tensor", mmo)
                    prev_ow[it2] = omm = r.bump("pe", 1)
                    r.wait("vector", "pe", omm)
                    r.wait("vector", xsem, x2l)
                    if tix >= 1 and oc == 0:
                        psem, pval = x3st[tix - 1]
                        r.wait("vector", psem, pval)
                    r.wait("vector", "c", r.cur("c"))
                    x2_used[it2] = y_prev[it2] = vadd(
                        lambda v, oc=oc, bank=bank, xb=xb: v.tensor_add(
                            y256[:, oc * BS1:(oc + 1) * BS1], bank[:, 0:BS1], xb[:]))
                    r.wait("scalar", "c", y_prev[it2])
                    if s12_mm:
                        r.wait("scalar", "pe", s12_mm)
                    sadd(lambda s, oc=oc: s.activation(
                        ybf256[:], y256[:, oc * BS1:(oc + 1) * BS1], AF.Identity))
                    sq2e = sadd(lambda s, oc=oc: s.activation(
                        sqbf256[:], y256[:, oc * BS1:(oc + 1) * BS1], AF.Square))
                    r.wait("tensor", "ev", sq2e)
                    r.add("tensor", lambda t, oc=oc: t.matmul(
                        PB[4][0:1, 0:BS1], ones_col_bf[:], ybf256[:],
                        start=oc == 0, stop=oc == KC - 1))

                    def mm2o(t, oc=oc):
                        t.matmul(PB[5][0:1, 0:BS1], ones_col_bf[:], sqbf256[:],
                                 start=oc == 0, stop=oc == KC - 1) \
                            .then_inc(r.h("pe"), 1)
                    r.add("tensor", mm2o)
                    s12_mm = r.bump("pe", 1)
                r.wait("scalar", "pe", s12_mm)
                r.wait("vector", "pe", s12_mm)
                emit_ln_epilogue(nc, r, lnt256, y256,
                                 lambda oc: y256[:, oc * BS1:(oc + 1) * BS1],
                                 PB, 4, 5, tok_w=BS1)
                x3_ln = r.cur("c")
                r.wait("sync", "c", x3_ln)
                x3sem = "x3A" if tix % 2 == 0 else "x3B"
                for oc in range(KC):
                    r.add("sync", lambda q, oc=oc, tok0=tok0, x3sem=x3sem: q.dma_start(
                        out=ap["x3"][:, oc * TPC + tok0: oc * TPC + tok0 + BS1],
                        in_=y256[:, oc * BS1:(oc + 1) * BS1]).then_inc(r.h(x3sem), 16))
                    r.bump(x3sem)
                x3st[tix] = (x3sem, r.cur(x3sem))
        x3_all_A = r.cur("x3A")
        x3_all_B = r.cur("x3B")
        AR.reset(m0)

        x3bf = AR.tile(128, KC * TT, BF16)
        gu = AR.tile(128, NFC * TT, BF16)
        yf = AR.tile(128, KC * TT, F32)
        wq_sb = [AR.tile(128, KC * 128, BF16) for _ in range(4)]
        wd_sb = [AR.tile(128, KC * 128, BF16) for _ in range(2)]
        x3_sl = [AR.tile(128, TT, F32) for _ in range(2)]
        silu1 = AR.tile(128, TT, BF16)
        ybff = AR.tile(128, TT, BF16)
        sqbff = AR.tile(128, TT, BF16)

        x3u, wqu, wdu, x3su = {}, {}, {}, {}
        guq_mm, dmm_done = {}, {}
        prev_out_st = 0
        yp = {}
        for ti in range(NTT):
            csem = "uA" if ti % 2 == 0 else "uB"
            if ti >= 2:
                r.wait("gpsimd", "pe", x3u[ti - 2])
            if ti == 0:
                r.wait("gpsimd", "x3A", x3_all_A)
                r.wait("gpsimd", "x3B", x3_all_B)
                r.wait("gpsimd", "pe", r.cur("pe"))
                r.wait("gpsimd", "c", r.cur("c"))
                r.wait("gpsimd", "ev", r.cur("ev"))
            r.add("gpsimd", lambda g, ti=ti, csem=csem: g.dma_start(
                out=x3bf[:].rearrange("p (a t) -> p a t", a=KC),
                in_=ap["x3"].rearrange("p (a t) -> p a t", a=KC)
                [:, :, ti * TT:(ti + 1) * TT]).then_inc(r.h(csem), 16))
            x3ld = r.bump(csem)
            gu_evs = {}
            sle_evs = {}
            for fc in range(NFC):
                fi = ti * NFC + fc
                wg = wq_sb[2 * (fi % 2)]
                wu = wq_sb[2 * (fi % 2) + 1]
                fsem = "fA" if fi % 2 == 0 else "fB"
                if fi >= 2:
                    r.wait("gpsimd", "pe", guq_mm[fi - 2])

                def ldw(g, fc=fc, wg=wg, wu=wu, fsem=fsem):
                    g.dma_start(out=wg[:], in_=ap["wgT"][fc]).then_inc(r.h(fsem), 16)
                    g.dma_start(out=wu[:], in_=ap["wuT"][fc]).then_inc(r.h(fsem), 16)
                r.add("gpsimd", ldw)
                wld = r.bump(fsem, 32)
                r.wait("vector", fsem, wld)
                if fi >= 2:
                    r.wait("vector", "pe", guq_mm.get(fi - 2, 0))
                r.wait("vector", "c", r.cur("c"))
                d1 = vadd(lambda v, wg=wg: v.tensor_scalar_sub(wg[:], wg[:], 128.0))
                r.wait("vector", "c", d1)
                d2 = vadd(lambda v, wu=wu: v.tensor_scalar_sub(wu[:], wu[:], 128.0))
                r.wait("tensor", "c", d2)
                r.wait("tensor", csem, x3ld)
                if fi >= 1:
                    r.wait("tensor", "ev", sle_evs.get(fi - 1, 0))
                    r.wait("tensor", "c", gu_evs.get(fi - 1, 0))
                for bi, wz in ((0, wg), (1, wu)):
                    bank = PB[bi]
                    for kc in range(KC):
                        def mmq(t, kc=kc, wz=wz, bank=bank, last=kc == KC - 1):
                            i = t.matmul(bank[:], wz[:, kc * 128:(kc + 1) * 128],
                                         x3bf[:, kc * TT:(kc + 1) * TT],
                                         start=kc == 0, stop=last)
                            if last:
                                i.then_inc(r.h("pe"), 1)
                        r.add("tensor", mmq)
                    r.bump("pe", 1)
                guq_mm[fi] = qmm = r.cur("pe")
                r.wait("scalar", "pe", qmm)
                if fi >= 1:
                    r.wait("scalar", "c", gu_v)
                sle = sadd(lambda s, fc=fc: s.activation(
                    silu1[:], PB[0][:], AF.Silu, scale=sg_sb[:, fc:fc + 1]))
                sle_evs[fi] = sle
                r.wait("vector", "ev", sle)
                r.wait("vector", "c", r.cur("c"))
                gu_v = vadd(lambda v, fc=fc: v.scalar_tensor_tensor(
                    gu[:, fc * TT:(fc + 1) * TT], PB[1][:], su_sb[:, fc:fc + 1],
                    silu1[:], ALU.mult, ALU.mult))
                gu_evs[fi] = gu_v
                wqu[fi] = gu_v
            down_mm = {}
            ybank_free2 = 0
            for ocg in range(2):
                for fc in range(NFC):
                    fi2 = (ti * 2 + ocg) * NFC + fc
                    wd = wd_sb[fi2 % 2]
                    dsem = "dA" if fi2 % 2 == 0 else "dB"
                    if fi2 >= 2:
                        r.wait("gpsimd", "pe", dmm_done[fi2 - 2])
                    r.add("gpsimd", lambda g, fc=fc, wd=wd, dsem=dsem: g.dma_start(
                        out=wd[:], in_=ap["wdfc"][fc]).then_inc(r.h(dsem), 16))
                    dld = r.bump(dsem)
                    r.wait("vector", dsem, dld)
                    if fi2 >= 2:
                        r.wait("vector", "pe", dmm_done.get(fi2 - 2, 0))
                    r.wait("vector", "c", r.cur("c"))
                    d3 = vadd(lambda v, wd=wd: v.tensor_scalar_sub(wd[:], wd[:], 128.0))
                    wdu[fi2] = d3
                    r.wait("tensor", "c", d3)
                    if fc == 0:
                        r.wait("tensor", "c", gu_v)
                        if ocg == 1:
                            r.wait("tensor", "c", ybank_free2)
                        elif ti >= 1:
                            r.wait("tensor", "c", ln_done_prev)
                    for oc8 in range(8):
                        oc = ocg * 8 + oc8

                        def mmd2(t, fc=fc, oc=oc, oc8=oc8, wd=wd,
                                 inc=oc8 == 7, last=fc == NFC - 1):
                            i = t.matmul(PB[oc8][:],
                                         wd[:, oc * 128:(oc + 1) * 128],
                                         gu[:, fc * TT:(fc + 1) * TT],
                                         start=fc == 0, stop=last)
                            if inc:
                                i.then_inc(r.h("pe"), 1)
                        r.add("tensor", mmd2)
                    dmm_done[fi2] = r.bump("pe", 1)
                dend = r.cur("pe")
                for oc8 in range(8):
                    oc = ocg * 8 + oc8
                    gx = ti * KC + oc
                    xb = x3_sl[gx % 2]
                    xsem2 = "r3A" if gx % 2 == 0 else "r3B"
                    if gx >= 2:
                        r.wait("sync", "c", x3su[gx - 2])
                    r.add("sync", lambda q, oc=oc, ti=ti, xb=xb, xsem2=xsem2:
                          q.dma_start(out=xb[:],
                                      in_=ap["x3"][:, oc * TPC + ti * TT: oc * TPC + ti * TT + TT])
                          .then_inc(r.h(xsem2), 16))
                    x3l = r.bump(xsem2)
                    r.wait("vector", "pe", dend)
                    r.wait("vector", xsem2, x3l)
                    if prev_out_st:
                        r.wait("vector", "st", prev_out_st)
                    r.wait("vector", "c", r.cur("c"))
                    x3su[gx] = yp[gx] = vadd(
                        lambda v, oc=oc, oc8=oc8, xb=xb: v.scalar_tensor_tensor(
                            yf[:, oc * TT:(oc + 1) * TT], PB[oc8][:],
                            sd_sb[:, oc:oc + 1], xb[:], ALU.mult, ALU.add))
                ybank_free2 = r.cur("c")
            y_done2 = r.cur("c")
            s12f = 0
            for oc in range(KC):
                r.wait("scalar", "c", y_done2)
                if s12f:
                    r.wait("scalar", "pe", s12f)
                sadd(lambda s, oc=oc: s.activation(
                    ybff[:], yf[:, oc * TT:(oc + 1) * TT], AF.Identity))
                sqf = sadd(lambda s, oc=oc: s.activation(
                    sqbff[:], yf[:, oc * TT:(oc + 1) * TT], AF.Square))
                r.wait("tensor", "ev", sqf)
                if oc == 0 and ti == 0:
                    r.wait("tensor", "c", ones_done)
                r.add("tensor", lambda t, oc=oc: t.matmul(
                    PB[4][0:1, 0:TT], ones_col_bf[:], ybff[:],
                    start=oc == 0, stop=oc == KC - 1))

                def mm2f(t, oc=oc):
                    t.matmul(PB[5][0:1, 0:TT], ones_col_bf[:], sqbff[:],
                             start=oc == 0, stop=oc == KC - 1) \
                        .then_inc(r.h("pe"), 1)
                r.add("tensor", mm2f)
                s12f = r.bump("pe", 1)
            r.wait("scalar", "pe", s12f)
            r.wait("vector", "pe", s12f)
            emit_ln_epilogue(nc, r, lnt, yf,
                             lambda oc: yf[:, oc * TT:(oc + 1) * TT], PB, 4, 5)
            ln_done_prev = r.cur("c")
            r.wait("sync", "c", ln_done_prev)
            for oc in range(KC):
                r.add("sync", lambda q, oc=oc, ti=ti: q.dma_start(
                    out=ap["out"][:, oc * TPC + ti * TT: oc * TPC + ti * TT + TT],
                    in_=yf[:, oc * TT:(oc + 1) * TT]).then_inc(r.h("st"), 16))
                r.bump("st")
            prev_out_st = r.cur("st")
            x3u[ti] = r.cur("pe")
        r.wait("sync", "st", r.cur("st"))
        r.run(block)
    return nc


def kernel(**inputs):
    pc1 = prep_launch1(inputs)
    res1 = run_bass_kernel_spmd(build_launch1(), pc1, core_ids=list(range(8)))
    l1out = [res1.results[c] for c in range(8)]

    pc2 = prep_launch2(inputs, l1out)
    res2 = run_bass_kernel_spmd(build_launch2(debug=False, stages=3), pc2,
                                core_ids=list(range(8)))
    l2out = [res2.results[c] for c in range(8)]

    pc2b = prep_launch2b(inputs, [l2out[c]["x2"] for c in range(8)])
    res2b = run_bass_kernel_spmd(build_launch2b(), pc2b, core_ids=list(range(8)))
    l2bout = [res2b.results[c] for c in range(8)]

    pc3 = prep_launch3(inputs, l2out, l2bout)
    res3 = run_bass_kernel_spmd(build_launch3(), pc3, core_ids=list(range(8)))

    out_fm = np.zeros((B, D, S), np.float32)
    for c in range(8):
        b = c // 4
        ti = _own_token_index(c)
        o = res3.results[c]["out"]
        out_fm[b][:, ti] = o.reshape(128, KC, TPC).transpose(1, 0, 2).reshape(D, TPC)
    return np.ascontiguousarray(out_fm.transpose(0, 2, 1))


# version 02
# speedup vs baseline: 1.0033x; 1.0033x over previous
import math
from contextlib import ExitStack

import numpy as np
import ml_dtypes

import concourse.bass as bass
import concourse.mybir as mybir
from concourse.bass_utils import run_bass_kernel_spmd

F32 = mybir.dt.float32
BF16 = mybir.dt.bfloat16
I32 = mybir.dt.int32
AF = mybir.ActivationFunctionType
ALU = mybir.AluOpType
AX = mybir.AxisListType

B, S, D = 2, 8192, 2048
NH, HD = 16, 128
TPC = 2048
CHUNK = 1024
NTT = 4
TT = 512
KC = D // 128

H1, D1, BS1 = 32, 64, 256
NL = 32
F0 = 5632
NE, R0 = 8, 16
LORA_SCALE = 2.0
NFC = F0 // 128

NEG_BIG = -1.0e30
ENGINES = ("sync", "tensor", "vector", "scalar", "gpsimd")


def _chunks_of(core):
    j = core % 4
    return (j, 7 - j)


def _own_token_index(core):
    ja, jb = _chunks_of(core)
    return np.r_[ja * CHUNK:(ja + 1) * CHUNK, jb * CHUNK:(jb + 1) * CHUNK]


def _rope_tables():
    inv_freq = 1.0 / 10000.0 ** (np.arange(0, HD, 2, dtype=np.float64) / HD)
    t = np.arange(S, dtype=np.float64)
    freq = np.einsum("s,f->sf", t, inv_freq)
    emb = np.concatenate([freq, freq], axis=-1)
    A = np.sin(emb).T.astype(np.float32)
    Bt = np.cos(emb).T.astype(np.float32)
    return A, Bt


def _l1_decay():
    base = 1.0 / 2.0 ** (8.0 / H1)
    rate = base ** (np.arange(H1, dtype=np.float64) + 1.0)
    factor = 1.0 + 1e-5
    slope = rate * factor
    r = np.arange(BS1, dtype=np.float64) + 1.0
    qd = np.exp(-slope[:, None] * r[None, :])
    kd = np.exp(-slope[:, None] * (BS1 - r[None, :]))
    dd = r[:, None] - r[None, :]
    ddm = np.where(dd >= 0, np.exp(-slope[:, None, None] * dd[None]), 0.0)
    bd = np.exp(-slope * BS1)
    return (qd.astype(np.float32), kd.astype(np.float32),
            ddm.astype(np.float32), bd)


def _oc_tiles(w):
    din, n = w.shape
    kcn = din // 128
    t = w.reshape(kcn, 128, n // 128, 128)
    return np.ascontiguousarray(t.transpose(2, 1, 0, 3).reshape(n // 128, 128, kcn * 128))


def _pack_kc_free(m):
    din, t = m.shape
    kcn = din // 128
    return np.ascontiguousarray(
        m.reshape(kcn, 128, t).transpose(1, 0, 2).reshape(128, kcn * t))


def _cols2d(v):
    return np.ascontiguousarray(v.reshape(-1, 128).T)


class Rec:

    def __init__(self, nc, es):
        self.nc = nc
        self.es = es
        self.ops = {k: [] for k in ENGINES}
        self.sems = {}

    def sem(self, name):
        h = self.es.enter_context(self.nc.semaphore(name))
        s = [h, 0]
        self.sems[name] = s
        return name

    def bump(self, name, k=16):
        self.sems[name][1] += k
        return self.sems[name][1]

    def cur(self, name):
        return self.sems[name][1]

    def h(self, name):
        return self.sems[name][0]

    def add(self, eng, fn):
        self.ops[eng].append(fn)

    def wait(self, eng, name, val):
        h = self.sems[name][0]
        self.ops[eng].append(lambda e, h=h, v=val: e.wait_ge(h, v))

    def run(self, block):
        o = self.ops

        @block.sync
        def _(q):
            for f in o["sync"]:
                f(q)

        @block.tensor
        def _(t):
            for f in o["tensor"]:
                f(t)

        @block.vector
        def _(v):
            for f in o["vector"]:
                f(v)

        @block.scalar
        def _(s):
            for f in o["scalar"]:
                f(s)

        @block.gpsimd
        def _(g):
            for f in o["gpsimd"]:
                f(g)


class Arena:

    DTS = {F32: 4, BF16: 2, I32: 4}

    def __init__(self, nc, es, name, nbytes):
        self.hdl = es.enter_context(nc.sbuf_tensor(name, [128, nbytes], mybir.dt.uint8))
        self.base = self.hdl.ap()
        self.off = 0
        self.nbytes = nbytes

    def tile(self, p, w, dt):
        sz = w * self.DTS[dt]
        off = (self.off + 31) & ~31
        assert off + sz <= self.nbytes, f"arena overflow {off + sz} > {self.nbytes}"
        self.off = off + sz
        return self.base[0:p, off:off + sz].bitcast(dt)

    def mark(self):
        return self.off

    def reset(self, mark):
        self.off = mark


def prep_launch1(inp):
    A, Bt = _rope_tables()
    x = np.asarray(inp["L0_attn_x"])
    x_fm = np.ascontiguousarray(x.transpose(0, 2, 1))

    w_qkv = np.asarray(inp["L0_attn_w_qkv"])
    b_qkv = np.asarray(inp["L0_attn_b_qkv"])
    wqk = np.empty((D, 2 * D), np.float32)
    bqk = np.empty(2 * D, np.float32)
    for h in range(NH):
        wqk[:, 256 * h:256 * h + 128] = w_qkv[:, 384 * h:384 * h + 128]
        wqk[:, 256 * h + 128:256 * h + 256] = w_qkv[:, 384 * h + 128:384 * h + 256]
        bqk[256 * h:256 * h + 128] = b_qkv[384 * h:384 * h + 128]
        bqk[256 * h + 128:256 * h + 256] = b_qkv[384 * h + 128:384 * h + 256]
    wv = np.concatenate([w_qkv[:, 384 * h + 256:384 * h + 384] for h in range(NH)], 1)
    bv = np.concatenate([b_qkv[384 * h + 256:384 * h + 384] for h in range(NH)])

    prot = np.zeros((128, 128), np.float32)
    prot[np.arange(64), np.arange(64) + 64] = 1.0
    prot[np.arange(64) + 64, np.arange(64)] = -1.0
    wqk_t = _oc_tiles(wqk)
    wv_t = _oc_tiles(wv).reshape(4, 4, 128, KC * 128)
    wv_t = _oc_tiles(wv).reshape(16, 128, KC * 128)
    wv_g = np.empty((4, 128, KC * 512), np.float32)
    for vc in range(4):
        for sub in range(4):
            for kc in range(KC):
                wv_g[vc][:, kc * 512 + sub * 128:kc * 512 + (sub + 1) * 128] = \
                    wv_t[vc * 4 + sub][:, kc * 128:(kc + 1) * 128]

    pc = []
    for c in range(8):
        b = c // 4
        ti = _own_token_index(c)
        pc.append({
            "x_p": _pack_kc_free(np.ascontiguousarray(x_fm[b][:, ti])),
            "wqk": wqk_t,
            "wv": wv_g,
            "bqk": _cols2d(bqk),
            "bv": np.ascontiguousarray(bv[None, :]),
            "ropeA": np.ascontiguousarray(A[:, ti]),
            "ropeB": np.ascontiguousarray(Bt[:, ti]),
            "Prot": prot,
        })
    return pc


def build_launch1():
    nc = bass.Bass("TRN2", target_bir_lowering=False)
    ap = {
        "x_p": nc.dram_tensor("x_p", [128, KC * TPC], F32, kind="ExternalInput").ap(),
        "wqk": nc.dram_tensor("wqk", [32, 128, KC * 128], F32, kind="ExternalInput").ap(),
        "wv": nc.dram_tensor("wv", [4, 128, KC * 512], F32, kind="ExternalInput").ap(),
        "bqk": nc.dram_tensor("bqk", [128, 32], F32, kind="ExternalInput").ap(),
        "bv": nc.dram_tensor("bv", [1, D], F32, kind="ExternalInput").ap(),
        "ropeA": nc.dram_tensor("ropeA", [128, TPC], F32, kind="ExternalInput").ap(),
        "ropeB": nc.dram_tensor("ropeB", [128, TPC], F32, kind="ExternalInput").ap(),
        "Prot": nc.dram_tensor("Prot", [128, 128], F32, kind="ExternalInput").ap(),
        "Qr": nc.dram_tensor("Qr", [D, TPC], BF16, kind="ExternalOutput").ap(),
        "Kr": nc.dram_tensor("Kr", [D, TPC], BF16, kind="ExternalOutput").ap(),
        "Vt": nc.dram_tensor("Vt", [TPC, D], BF16, kind="ExternalOutput").ap(),
    }

    es = ExitStack()
    with es:
        sb = lambda name, shape, dt: es.enter_context(nc.sbuf_tensor(name, shape, dt))
        psum = lambda name, shape: es.enter_context(nc.psum_tensor(name, shape, F32))
        x_bf = sb("x_bf", [128, KC * TPC], BF16)
        ropeA = sb("ropeA_sb", [128, TPC], F32)
        ropeB = sb("ropeB_sb", [128, TPC], F32)
        bqk_sb = sb("bqk_sb", [128, 32], F32)
        bv_sb = sb("bv_sb", [1, D], F32)
        bvb = sb("bvb_sb", [128, D], F32)
        ones_row = sb("ones_row", [1, 128], F32)
        wqk_sb = [sb(f"wqk{i}", [128, KC * 128], BF16) for i in range(2)]
        wv_sb = [sb(f"wv{i}", [128, KC * 512], BF16) for i in range(2)]
        qkraw = [sb(f"qkraw{i}", [128, TT], BF16) for i in range(2)]
        prot_sb = sb("prot_sb", [128, 128], BF16)
        ma = [sb(f"ma{i}", [128, TT], F32) for i in range(2)]
        qk_bf = [sb(f"qkbf{i}", [128, TT], BF16) for i in range(2)]
        v_bf = [sb(f"vbf{i}", [128, TT], BF16) for i in range(2)]
        qk_ps = [psum(f"qkps{i}", [128, TT]) for i in range(2)]
        rot_ps = [psum(f"rotps{i}", [128, TT]) for i in range(2)]
        v_ps = [psum(f"vps{i}", [128, TT]) for i in range(2)]
        bvb_ps = psum("bvbps", [128, TT])

        r = Rec(nc, es)
        for s in ("in", "w0", "w1", "v0", "v1", "pe", "ev", "rp", "st", "c", "gin"):
            r.sem(s)
        block = es.enter_context(nc.Block())

        r.add("gpsimd", lambda g: g.dma_start(out=x_bf[:], in_=ap["x_p"])
              .then_inc(r.h("gin"), 16))
        r.bump("gin")
        r.add("gpsimd", lambda g: g.dma_start(out=prot_sb[:], in_=ap["Prot"])
              .then_inc(r.h("gin"), 16))
        r.bump("gin")
        for t, dst in (("ropeA", ropeA), ("ropeB", ropeB), ("bqk", bqk_sb),
                       ("bv", bv_sb)):
            r.add("sync", lambda q, t=t, dst=dst: q.dma_start(
                out=dst[:], in_=ap[t]).then_inc(r.h("in"), 16))
            r.bump("in")
        in_done = r.cur("in")
        gin_done = r.cur("gin")

        r.wait("vector", "in", in_done)
        r.wait("vector", "gin", gin_done)
        r.add("vector", lambda v: v.memset(ones_row[:], 1.0).then_inc(r.h("c"), 1))
        ones_done = r.bump("c", 1)
        r.wait("tensor", "c", ones_done)
        bvb_ev = None
        for i in range(4):
            if i >= 1:
                r.wait("tensor", "c", bvb_ev)
            r.add("tensor", lambda t, i=i: t.matmul(
                bvb_ps[:], ones_row[:],
                bv_sb[:, i * 512:(i + 1) * 512],
                start=True, stop=True).then_inc(r.h("pe"), 1))
            bvb_mm = r.bump("pe", 1)
            r.wait("vector", "pe", bvb_mm)
            r.add("vector", lambda v, i=i: v.tensor_copy(
                bvb[:, i * 512:(i + 1) * 512], bvb_ps[:]).then_inc(r.h("c"), 1))
            bvb_ev = r.bump("c", 1)

        ev_done, rp_done, st_done, pe_after_oc = {}, {}, {}, {}
        for oc in range(32):
            wbuf = wqk_sb[oc % 2]
            if oc >= 2:
                r.wait("gpsimd", "pe", pe_after_oc[oc - 2])
            wsem = "w0" if oc % 2 == 0 else "w1"
            r.add("gpsimd", lambda g, oc=oc, wbuf=wbuf, wsem=wsem: g.dma_start(
                out=wbuf[:], in_=ap["wqk"][oc]).then_inc(r.h(wsem), 16))
            w_done = r.bump(wsem)
            for ti in range(NTT):
                idx = oc * NTT + ti
                pbuf = qk_ps[idx % 2]
                r.wait("tensor", wsem, w_done)
                if idx >= 2:
                    r.wait("tensor", "ev", ev_done[idx - 2])
                if idx == 0:
                    r.wait("tensor", "in", in_done)
                    r.wait("tensor", "gin", gin_done)
                for kc in range(KC):
                    last = kc == KC - 1
                    def mm(t, oc=oc, ti=ti, kc=kc, pbuf=pbuf, wbuf=wbuf, last=last):
                        i = t.matmul(pbuf[:], wbuf[:, kc * 128:(kc + 1) * 128],
                                     x_bf[:, kc * TPC + ti * TT: kc * TPC + ti * TT + TT],
                                     start=(kc == 0), stop=last)
                        if last:
                            i.then_inc(r.h("pe"), 1)
                    r.add("tensor", mm)
                mm_done = r.bump("pe", 1)
                rbuf = qkraw[idx % 2]
                r.wait("scalar", "pe", mm_done)
                if idx >= 2:
                    r.wait("scalar", "rp", rp_done[idx - 2])
                r.add("scalar", lambda s, oc=oc, pbuf=pbuf, rbuf=rbuf: s.activation(
                    rbuf[:], pbuf[:], AF.Identity,
                    bias=bqk_sb[:, oc:oc + 1]).then_inc(r.h("ev"), 1))
                ev_done[idx] = r.bump("ev", 1)
                rotp = rot_ps[idx % 2]
                r.wait("tensor", "ev", ev_done[idx])
                if idx >= 2:
                    r.wait("tensor", "rp", rp_done[idx - 2])
                r.add("tensor", lambda t, rbuf=rbuf, rotp=rotp: t.matmul(
                    rotp[:], prot_sb[:], rbuf[:], start=True, stop=True)
                    .then_inc(r.h("pe"), 1))
                rot_done = r.bump("pe", 1)
                tok0 = ti * TT
                mabuf, obuf = ma[idx % 2], qk_bf[idx % 2]
                r.wait("vector", "ev", ev_done[idx])
                if idx == 0:
                    r.wait("vector", "in", in_done)
                if idx >= 2:
                    r.wait("vector", "st", st_done[idx - 2])
                r.add("vector", lambda v, rbuf=rbuf, mabuf=mabuf, tok0=tok0:
                      v.tensor_mul(mabuf[:], rbuf[:], ropeA[:, tok0:tok0 + TT]))
                r.wait("vector", "pe", rot_done)
                r.add("vector", lambda v, rotp=rotp, mabuf=mabuf, tok0=tok0:
                      v.tensor_mul(rotp[:], rotp[:], ropeB[:, tok0:tok0 + TT]))
                r.add("vector", lambda v, mabuf=mabuf, rotp=rotp, obuf=obuf:
                      v.tensor_add(obuf[:], mabuf[:], rotp[:])
                      .then_inc(r.h("rp"), 1))
                rp_done[idx] = r.bump("rp", 1)
                h = oc // 2
                dst = ap["Qr"] if oc % 2 == 0 else ap["Kr"]
                r.wait("sync", "rp", rp_done[idx])
                r.add("sync", lambda q, dst=dst, h=h, tok0=tok0, obuf=obuf:
                      q.dma_start(out=dst[128 * h:128 * h + 128, tok0:tok0 + TT],
                                  in_=obuf[:]).then_inc(r.h("st"), 16))
                st_done[idx] = r.bump("st")
            pe_after_oc[oc] = r.cur("pe")

        vev_done, vst_done, pe_after_vc = {}, {}, {}
        for vc in range(4):
            wbuf = wv_sb[vc % 2]
            if vc >= 2:
                r.wait("gpsimd", "pe", pe_after_vc[vc - 2])
            wsem = "v0" if vc % 2 == 0 else "v1"
            r.add("gpsimd", lambda g, vc=vc, wbuf=wbuf, wsem=wsem: g.dma_start(
                out=wbuf[:], in_=ap["wv"][vc]).then_inc(r.h(wsem), 16))
            w_done = r.bump(wsem)
            for tt in range(TPC // 128):
                idx = vc * (TPC // 128) + tt
                pbuf = v_ps[idx % 2]
                r.wait("tensor", wsem, w_done)
                if idx >= 2:
                    r.wait("tensor", "c", vev_done[idx - 2])
                for kc in range(KC):
                    last = kc == KC - 1
                    def mm(t, vc=vc, tt=tt, kc=kc, pbuf=pbuf, wbuf=wbuf, last=last):
                        i = t.matmul(pbuf[:],
                                     x_bf[:, kc * TPC + tt * 128: kc * TPC + (tt + 1) * 128],
                                     wbuf[:, kc * 512:(kc + 1) * 512],
                                     start=(kc == 0), stop=last)
                        if last:
                            i.then_inc(r.h("pe"), 1)
                    r.add("tensor", mm)
                mm_done = r.bump("pe", 1)
                obuf = v_bf[idx % 2]
                r.wait("vector", "pe", mm_done)
                if idx == 0:
                    r.wait("vector", "c", bvb_ev)
                if idx >= 2:
                    r.wait("vector", "st", vst_done[idx - 2])
                r.add("vector", lambda v, vc=vc, pbuf=pbuf, obuf=obuf:
                      v.tensor_add(obuf[:], pbuf[:], bvb[:, vc * 512:(vc + 1) * 512])
                      .then_inc(r.h("c"), 1))
                vev_done[idx] = r.bump("c", 1)
                r.wait("sync", "c", vev_done[idx])
                r.add("sync", lambda q, vc=vc, tt=tt, obuf=obuf: q.dma_start(
                    out=ap["Vt"][tt * 128:(tt + 1) * 128, vc * 512:(vc + 1) * 512],
                    in_=obuf[:]).then_inc(r.h("st"), 16))
                vst_done[idx] = r.bump("st")
            pe_after_vc[vc] = r.cur("pe")

        r.wait("sync", "st", r.cur("st"))
        r.run(block)

    return nc


def kernel(**inputs):
    raise NotImplementedError("launches 2 and 3 under construction")


def prep_launch2(inp, l1out):
    x = np.asarray(inp["L0_attn_x"])
    x_fm = np.ascontiguousarray(x.transpose(0, 2, 1))

    w_o = np.asarray(inp["L0_attn_w_o"])
    b_o = np.asarray(inp["L0_attn_b_o"])
    wo_t = _oc_tiles(w_o)

    KhA = np.zeros((2, 8 * 128, NH * CHUNK), ml_dtypes.bfloat16)
    VhA = np.zeros((2, 8 * 128, NH * CHUNK), ml_dtypes.bfloat16)
    QhA = np.zeros((8, 2 * 128, NH * CHUNK), ml_dtypes.bfloat16)
    for c in range(8):
        b = c // 4
        ja, jb = _chunks_of(c)
        Kr, Vt, Qr = l1out[c]["Kr"], l1out[c]["Vt"], l1out[c]["Qr"]
        for qc, ch in ((0, ja), (1, jb)):
            for h in range(NH):
                kt = Kr[128 * h:128 * h + 128, qc * CHUNK:(qc + 1) * CHUNK]
                KhA[b, ch * 128:(ch + 1) * 128, h * CHUNK:(h + 1) * CHUNK] = kt
                QhA[c, qc * 128:(qc + 1) * 128, h * CHUNK:(h + 1) * CHUNK] = \
                    Qr[128 * h:128 * h + 128, qc * CHUNK:(qc + 1) * CHUNK]
                vblk = Vt[qc * CHUNK:(qc + 1) * CHUNK, 128 * h:128 * h + 128]
                VhA[b, ch * 128:(ch + 1) * 128, h * CHUNK:(h + 1) * CHUNK] = \
                    vblk.reshape(8, 128, 128).transpose(1, 0, 2).reshape(128, CHUNK)

    masks = np.zeros((4, 128, 512), np.float32)
    for m in range(4):
        for p in range(128):
            masks[m, p, 128 * m + p + 1:] = NEG_BIG

    ident = np.eye(128, dtype=np.float32)
    pp = np.arange(128, dtype=np.int32)
    ffn = prep_ffn0(inp)
    l1a = prep_launch2b(inp, [None] * 8)
    for d in l1a:
        d.pop("x2", None)
        d.pop("ident", None)
    pc = []
    for c in range(8):
        b = c // 4
        ja, jb = _chunks_of(c)
        plan = [(ja, 0), (jb, 1)]
        plan += [(kv, 0) for kv in range(ja)]
        plan += [(kv, 1) for kv in range(jb)]
        assert len(plan) == 9
        kvidx = np.stack(
            [(kv * 128 + pp) * 2 + hf for kv, _ in plan for hf in range(2)], 1)
        qhidx = np.stack(
            [(qc * 128 + pp) * 2 + hf for _, qc in plan for hf in range(2)], 1)
        nidx = np.stack([qc * 128 + pp for _, qc in plan], 1)
        oidx = np.zeros((128, 9 * NH), np.int32)
        for u, (_, qc) in enumerate(plan):
            for h in range(NH):
                oidx[:, u * NH + h] = qc * NH * 128 + h * 128 + pp
        pc.append({
            "x_p": _pack_kc_free(np.ascontiguousarray(x_fm[b][:, _own_token_index(c)])),
            "QhA": np.ascontiguousarray(
                QhA[c].reshape(2 * 128, 2, 8 * CHUNK).reshape(2 * 128 * 2, 8 * CHUNK)),
            "KhA": np.ascontiguousarray(
                KhA[b].reshape(8 * 128, 2, 8 * CHUNK).reshape(8 * 128 * 2, 8 * CHUNK)),
            "VhA": np.ascontiguousarray(
                VhA[b].reshape(8 * 128, 2, 8 * CHUNK).reshape(8 * 128 * 2, 8 * CHUNK)),
            "kvidx": kvidx, "qhidx": qhidx, "nidx": nidx, "oidx": oidx,
            "masks": masks, "ident": ident,
            "wo": wo_t, "bo": _cols2d(b_o),
            **ffn, **l1a[c],
        })
    return pc


def build_launch2(debug=False, stages=4):
    nc = bass.Bass("TRN2", target_bir_lowering=False)
    ap = {
        "x_p": nc.dram_tensor("x_p", [128, KC * TPC], F32, kind="ExternalInput").ap(),
        "QhA": nc.dram_tensor("QhA", [2 * 256, NH * CHUNK // 2], BF16, kind="ExternalInput").ap(),
        "KhA": nc.dram_tensor("KhA", [8 * 256, NH * CHUNK // 2], BF16, kind="ExternalInput").ap(),
        "VhA": nc.dram_tensor("VhA", [8 * 256, NH * CHUNK // 2], BF16, kind="ExternalInput").ap(),
        "kvidx": nc.dram_tensor("kvidx", [128, 18], I32, kind="ExternalInput").ap(),
        "qhidx": nc.dram_tensor("qhidx", [128, 18], I32, kind="ExternalInput").ap(),
        "nidx": nc.dram_tensor("nidx", [128, 9], I32, kind="ExternalInput").ap(),
        "oidx": nc.dram_tensor("oidx", [128, 9 * NH], I32, kind="ExternalInput").ap(),
        "masks": nc.dram_tensor("masks", [4, 128, 512], F32, kind="ExternalInput").ap(),
        "ident": nc.dram_tensor("ident", [128, 128], F32, kind="ExternalInput").ap(),
        "wo": nc.dram_tensor("wo", [KC, 128, KC * 128], F32, kind="ExternalInput").ap(),
        "bo": nc.dram_tensor("bo", [128, KC], F32, kind="ExternalInput").ap(),
        "x1": nc.dram_tensor("x1", [128, KC * TPC], F32,
                             kind="ExternalOutput" if debug else "Internal").ap(),
        "x2": nc.dram_tensor("x2", [128, KC * TPC], F32, kind="ExternalOutput").ap(),
        "Odram": nc.dram_tensor("Odram", [2, NH, 128, 1032], F32,
                                kind="ExternalOutput" if debug else "Internal").ap(),
        "gw": nc.dram_tensor("gw", [128, KC * 8], F32, kind="ExternalInput").ap(),
        "w13": nc.dram_tensor("w13", [NFC, 128, 2 * KC * 128], F32, kind="ExternalInput").ap(),
        "b13": nc.dram_tensor("b13", [NFC, 48, NE * 128], F32, kind="ExternalInput").ap(),
        "a13": nc.dram_tensor("a13", [NE, 128, KC * 64], F32, kind="ExternalInput").ap(),
        "a2t": nc.dram_tensor("a2t", [128, NE * NFC * 16], F32, kind="ExternalInput").ap(),
        "b2so": nc.dram_tensor("b2so", [128, 2 * KC * 128], F32, kind="ExternalInput").ap(),
        "w2fc": nc.dram_tensor("w2fc", [NFC, 128, KC * 128], F32, kind="ExternalInput").ap(),
        "wqk1": nc.dram_tensor("wqk1", [32, 128, KC * 128], F32, kind="ExternalInput").ap(),
        "wv1": nc.dram_tensor("wv1", [4, 128, KC * 512], F32, kind="ExternalInput").ap(),
        "gwt": nc.dram_tensor("gwt", [16, 128, KC * 128], F32, kind="ExternalInput").ap(),
        "qd": nc.dram_tensor("qd", [H1, 64, BS1], F32, kind="ExternalInput").ap(),
        "kd": nc.dram_tensor("kd", [H1, 64, BS1], F32, kind="ExternalInput").ap(),
        "ddT": nc.dram_tensor("ddT", [H1, 128, 2 * BS1], F32, kind="ExternalInput").ap(),
        "qp": nc.dram_tensor("qp", [H1 * 64, TPC], BF16, kind="ExternalOutput").ap(),
        "Opart": nc.dram_tensor("Opart", [H1 * 64, TPC], F32, kind="ExternalOutput").ap(),
        "Cch": nc.dram_tensor("Cch", [2, H1, 64, 64], F32, kind="ExternalOutput").ap(),
        "gate": nc.dram_tensor("gate", [D, TPC], BF16, kind="ExternalOutput").ap(),
        "qk1": nc.dram_tensor("qk1", [2 * H1 * 64, TPC], BF16, kind="Internal").ap(),
        "v1": nc.dram_tensor("v1", [TPC, H1 * 64], BF16, kind="Internal").ap(),
    }

    es = ExitStack()
    with es:
        sb = lambda name, shape, dt: es.enter_context(nc.sbuf_tensor(name, shape, dt))
        PB = [es.enter_context(nc.psum_tensor(f"bank{i}", [128, 512], F32))
              for i in range(8)]
        AR = Arena(nc, es, "arena", 188 * 1024)
        r = Rec(nc, es)
        for sname in ("in", "pe", "ev", "rp", "st", "c", "acc", "gin",
                      "pA", "pB", "gA", "gB", "gN",
                      "oA", "oB", "xA", "xB", "wA", "wB",
                      "c0s", "c1s", "aA", "aB", "fA", "fB", "bA", "bB",
                      "dA", "dB", "wrS", "wrT",
                      "w0", "w1", "v0", "v1s", "s0", "s1",
                      "q0", "q1", "k0", "k1", "dd0", "dd1", "vb0", "vb1"):
            r.sem(sname)
        block = es.enter_context(nc.Block())

        m0 = AR.mark()
        fl = emit_flash(nc, es, r, ap, sb, PB, AR)
        AR.reset(m0)
        if stages >= 2:
            w = emit_wo_ln(nc, es, r, ap, sb, PB, fl, AR)
            fl["x1_done"] = w["x1_done"]
        AR.reset(m0)
        if stages >= 3:
            f3 = emit_ffn0(nc, es, r, ap, sb, PB, fl, AR)
        AR.reset(m0)
        if stages >= 4:
            emit_l1a(nc, es, r, ap, sb, PB, AR, f3["x2_done"],
                     fl["ident_bf"], fl["ident_f32"])
        r.wait("sync", "st", r.cur("st"))
        for pn in ("s0", "s1"):
            if pn in r.sems:
                r.wait("sync", pn, r.cur(pn))
        r.wait("gpsimd", "acc", r.cur("acc"))
        r.run(block)
    return nc


def emit_flash(nc, es, r, ap, sb, PB, AR):
    from concourse.bass import IndirectOffsetOnAxis
    ones_row = sb("f_onesr", [1, 128], F32)
    ones_col = sb("f_onesc", [128, 1], F32)

    masks_sb = AR.tile(128, 4 * 512, BF16)
    ident_bf = sb("f_identbf", [128, 128], BF16)
    ident_f32 = sb("f_identf", [128, 128], F32)
    kvidx_sb = sb("f_kvidx", [128, 18], I32)
    qhidx_sb = sb("f_qhidx", [128, 18], I32)
    nidx_sb = sb("f_nidx", [128, 9], I32)
    oidx_sb = sb("f_oidx", [128, 9 * NH], I32)
    negb_qc = [sb(f"f_negb{i}", [128, 128], F32) for i in range(2)]
    negb_u = sb("f_negbu", [128, 128], F32)
    maxk_sb = sb("f_maxk", [1, NH], F32)


    scl = sb("f_scl", [128, 1], F32)
    bcol = sb("f_bcol", [128, 1], F32)
    HGRP = NH // 2 * CHUNK
    kt_big = [AR.tile(128, HGRP, BF16) for i in range(2)]
    vt_big = [AR.tile(128, HGRP, BF16) for i in range(2)]
    qt_big = [AR.tile(128, HGRP, BF16) for i in range(2)]
    knorm = AR.tile(1, 8 * CHUNK, F32)
    k2 = AR.tile(128, CHUNK, F32)
    qnrow = AR.tile(1, CHUNK, F32)
    zeros_t = AR.tile(128, 1032, F32)
    oacc = [AR.tile(128, 1032, F32) for i in range(2)]
    p_sb = AR.tile(128, CHUNK, BF16)
    pt_sb = AR.tile(128, CHUNK, BF16)
    den_sb = sb("f_den", [128, 2], F32)
    negb_dram = nc.dram_tensor("negb_dram", [2 * 128, 128], F32, kind="Internal").ap()

    for t, dst in (("kvidx", kvidx_sb), ("qhidx", qhidx_sb), ("nidx", nidx_sb),
                   ("oidx", oidx_sb), ("ident", ident_f32)):
        r.add("sync", lambda q, t=t, dst=dst: q.dma_start(
            out=dst[:], in_=ap[t]).then_inc(r.h("in"), 16))
        r.bump("in")
    r.add("gpsimd", lambda g: g.dma_start(out=ident_bf[:], in_=ap["ident"])
          .then_inc(r.h("gin"), 16))
    r.bump("gin")
    for m in range(4):
        r.add("gpsimd", lambda g, m=m: g.dma_start(
            out=masks_sb[:, m * 512:(m + 1) * 512], in_=ap["masks"][m])
            .then_inc(r.h("gin"), 16))
        r.bump("gin")
    in_done = r.cur("in")
    gin_done = r.cur("gin")
    r.add("vector", lambda v: v.memset(ones_row[:], 1.0))
    r.add("vector", lambda v: v.memset(ones_col[:], 1.0))
    r.add("vector", lambda v: v.memset(zeros_t[:], 0.0).then_inc(r.h("c"), 1))
    zi = r.bump("c", 1)

    r.wait("gpsimd", "c", zi)
    for qc in range(2):
        for h in range(NH):
            r.add("gpsimd", lambda g, qc=qc, h=h: g.dma_start(
                out=ap["Odram"][qc, h], in_=zeros_t[:]).then_inc(r.h("acc"), 16))
            r.bump("acc")
    ozero_done = r.cur("acc")

    pre_done = {}
    knmm_prev = 0
    r.wait("gpsimd", "in", in_done)
    r.wait("gpsimd", "gin", gin_done)
    for h in range(NH):
        for ch in range(8):
            idx = h * 8 + ch
            buf = kt_big[idx % 2]
            if idx >= 2:
                r.wait("gpsimd", "c", pre_done[idx - 2])
            ksem = "pA" if idx % 2 == 0 else "pB"
            r.add("gpsimd", lambda g, h=h, ch=ch, buf=buf, ksem=ksem: g.dma_start(
                out=buf[:, 0:CHUNK],
                in_=ap["KhA"].rearrange("(r two) n -> r two n", two=2)
                    [ch * 128:(ch + 1) * 128, h // 8,
                     (h % 8) * CHUNK:(h % 8 + 1) * CHUNK])
                .then_inc(r.h(ksem), 16))
            kld = r.bump(ksem)
            r.wait("scalar", ksem, kld)
            if idx >= 2:
                r.wait("scalar", "c", pre_done[idx - 2])
            if idx >= 1:
                r.wait("scalar", "pe", knmm_prev)
            r.add("scalar", lambda s, buf=buf: s.activation(
                k2[:], buf[:, 0:CHUNK], AF.Square).then_inc(r.h("ev"), 1))
            sq_done = r.bump("ev", 1)
            r.wait("tensor", "ev", sq_done)
            if idx >= 1:
                r.wait("tensor", "c", pre_done[idx - 1])
            for t4 in range(2):
                bank = PB[5 + t4]
                r.add("tensor", lambda t, t4=t4, bank=bank: t.matmul(
                    bank[0:1, :], ones_col[:], k2[:, t4 * 512:(t4 + 1) * 512],
                    start=True, stop=True).then_inc(r.h("pe"), 1))
                knmm = r.bump("pe", 1)
                r.wait("vector", "pe", knmm)
                r.add("vector", lambda v, ch=ch, t4=t4, bank=bank: v.tensor_copy(
                    knorm[0:1, ch * CHUNK + t4 * 512: ch * CHUNK + (t4 + 1) * 512],
                    bank[0:1, :]).then_inc(r.h("c"), 1))
                pre_done[idx] = r.bump("c", 1)
            knmm_prev = knmm
        r.wait("vector", "c", r.cur("c"))
        r.add("vector", lambda v, h=h: v.reduce_max(
            maxk_sb[0:1, h:h + 1], knorm[0:1, :], axis=AX.X).then_inc(r.h("c"), 1))
        pre_done[h * 8 + 7] = r.bump("c", 1)
    maxk_done = r.cur("c")

    nb_done = {}
    cmm_prev = 0
    qn_prev = 0
    bc_prev = 0
    r.wait("tensor", "c", maxk_done)
    for h in range(NH):
        if bc_prev:
            r.wait("tensor", "c", bc_prev)
        r.add("tensor", lambda t, h=h: t.matmul(
            PB[7][:, 0:1], ones_row[:], maxk_sb[0:1, h:h + 1],
            start=True, stop=True).then_inc(r.h("pe"), 1))
        mk = r.bump("pe", 1)
        r.wait("vector", "pe", mk)
        r.add("vector", lambda v: v.tensor_copy(scl[:], PB[7][:, 0:1])
              .then_inc(r.h("c"), 1))
        sc_done = r.bump("c", 1)
        for qc in range(2):
            idx = h * 2 + qc
            buf = qt_big[idx % 2]
            if idx >= 2:
                r.wait("gpsimd", "c", nb_done[idx - 2])
            else:
                r.wait("gpsimd", "c", maxk_done)
            ksem = "pA" if idx % 2 == 0 else "pB"
            r.add("gpsimd", lambda g, h=h, qc=qc, buf=buf, ksem=ksem: g.dma_start(
                out=buf[:, 0:CHUNK],
                in_=ap["QhA"].rearrange("(r two) n -> r two n", two=2)
                    [qc * 128:(qc + 1) * 128, h // 8,
                     (h % 8) * CHUNK:(h % 8 + 1) * CHUNK])
                .then_inc(r.h(ksem), 16))
            qld = r.bump(ksem)
            r.wait("scalar", ksem, qld)
            if idx >= 2:
                r.wait("scalar", "c", nb_done[idx - 2])
            r.wait("scalar", "pe", knmm_prev)
            r.add("scalar", lambda s, buf=buf: s.activation(
                k2[:], buf[:, 0:CHUNK], AF.Square).then_inc(r.h("ev"), 1))
            q2_done = r.bump("ev", 1)
            r.wait("tensor", "ev", q2_done)
            r.wait("tensor", "c", qn_prev)
            for t4 in range(2):
                bank = PB[5 + t4]
                r.add("tensor", lambda t, t4=t4, bank=bank: t.matmul(
                    bank[0:1, :], ones_col[:], k2[:, t4 * 512:(t4 + 1) * 512],
                    start=True, stop=True).then_inc(r.h("pe"), 1))
                qmm = r.bump("pe", 1)
                r.wait("vector", "pe", qmm)
                if cmm_prev:
                    r.wait("vector", "pe", cmm_prev)
                r.add("vector", lambda v, t4=t4, bank=bank: v.tensor_copy(
                    qnrow[0:1, t4 * 512:(t4 + 1) * 512], bank[0:1, :])
                    .then_inc(r.h("c"), 1))
                r.bump("c", 1)
            knmm_prev = qmm
            qn_prev = r.cur("c")
            qn_done = r.cur("c")
            r.wait("tensor", "c", qn_done)
            for rr in range(8):
                if bc_prev:
                    r.wait("tensor", "c", bc_prev)
                r.add("tensor", lambda t, rr=rr: t.matmul(
                    PB[7][:, 1:2], qnrow[0:1, rr * 128:(rr + 1) * 128],
                    ones_row[0:1, 0:1],
                    start=True, stop=True).then_inc(r.h("pe"), 1))
                cmm = r.bump("pe", 1)
                cmm_prev = cmm
                r.wait("vector", "pe", cmm)
                r.wait("vector", "c", sc_done)
                r.wait("vector", "c", r.cur("c"))
                r.add("vector", lambda v: v.tensor_mul(
                    bcol[:], PB[7][:, 1:2], scl[:]).then_inc(r.h("c"), 1))
                bc_done = r.bump("c", 1)
                bc_prev = bc_done
                r.wait("scalar", "c", bc_done)
                r.add("scalar", lambda s: s.activation(
                    bcol[:], bcol[:], AF.Sqrt, scale=1.0 / 128.0)
                    .then_inc(r.h("ev"), 1))
                sq2 = r.bump("ev", 1)
                r.wait("vector", "ev", sq2)
                r.add("vector", lambda v, qc=qc, h=h, rr=rr: v.tensor_scalar_mul(
                    negb_qc[qc][:, h * 8 + rr: h * 8 + rr + 1], bcol[:], -1.0)
                    .then_inc(r.h("c"), 1))
                nb_done[idx] = r.bump("c", 1)
    negb_done = r.cur("c")
    r.wait("sync", "c", negb_done)
    for qc in range(2):
        r.add("sync", lambda q, qc=qc: q.dma_start(
            out=negb_dram[qc * 128:(qc + 1) * 128, :], in_=negb_qc[qc][:])
            .then_inc(r.h("st"), 16))
        r.bump("st")
    negb_st = r.cur("st")

    half_pe_done = {}
    oacc_dma_done = {}
    s_free = {0: 0, 1: 0}
    pt_ps_free = {0: 0, 1: 0}
    p_free = {0: 0, 1: 0}
    pt_sb_free = {0: 0, 1: 0}
    o_free = 0
    den_free = {0: 0, 1: 0}
    negbu_used = 0
    r.wait("gpsimd", "st", negb_st)
    for u in range(9):
        diag = u < 2
        if negbu_used:
            r.wait("gpsimd", "ev", negbu_used)
        r.add("gpsimd", lambda g, u=u: g.indirect_dma_start(
            negb_u[:], None, negb_dram,
            IndirectOffsetOnAxis(nidx_sb[:, u:u + 1], 0)).then_inc(r.h("gN"), 16))
        nb_ld = r.bump("gN")
        for half in range(2):
            hit = u * 2 + half
            kb, vb, qb = kt_big[hit % 2], vt_big[hit % 2], qt_big[hit % 2]
            if hit >= 2:
                r.wait("gpsimd", "pe", half_pe_done[hit - 2])
            uc = u * 2 + half

            gsem = "gA" if hit % 2 == 0 else "gB"

            def lds(g, uc=uc, kb=kb, vb=vb, qb=qb, gsem=gsem):
                g.indirect_dma_start(kb[:], None, ap["KhA"],
                                     IndirectOffsetOnAxis(kvidx_sb[:, uc:uc + 1], 0)) \
                    .then_inc(r.h(gsem), 16)
                g.indirect_dma_start(vb[:], None, ap["VhA"],
                                     IndirectOffsetOnAxis(kvidx_sb[:, uc:uc + 1], 0)) \
                    .then_inc(r.h(gsem), 16)
                g.indirect_dma_start(qb[:], None, ap["QhA"],
                                     IndirectOffsetOnAxis(qhidx_sb[:, uc:uc + 1], 0)) \
                    .then_inc(r.h(gsem), 16)
            r.add("gpsimd", lds)
            ld_done = r.bump(gsem, 48)

            r.wait("tensor", gsem, ld_done)
            if half == 0:
                r.wait("tensor", "gN", nb_ld)
            for hh in range(NH // 2):
                h = half * (NH // 2) + hh
                it = u * NH + h
                ob = oacc[it % 2]
                hb = hh * CHUNK
                exp_evs = {}
                av_done = 0
                for rr in range(8):
                    khs = [0] if (diag and rr < 4) else [0, 1]
                    for kh in khs:
                        bank = PB[kh]
                        if s_free[kh]:
                            r.wait("tensor", "ev", s_free[kh])
                        r.add("tensor", lambda t, qb=qb, kb=kb, hb=hb, rr=rr, kh=kh, bank=bank:
                              t.matmul(bank[:], qb[:, hb + rr * 128: hb + (rr + 1) * 128],
                                       kb[:, hb + kh * 512: hb + (kh + 1) * 512],
                                       start=True, stop=True).then_inc(r.h("pe"), 1))
                        qk_done = r.bump("pe", 1)
                        masked = diag and kh == rr // 4
                        if masked:
                            r.wait("vector", "pe", qk_done)
                            r.add("vector", lambda v, rr=rr, bank=bank:
                                  v.tensor_add(bank[:], bank[:],
                                               masks_sb[:, (rr % 4) * 512:(rr % 4 + 1) * 512])
                                  .then_inc(r.h("rp"), 1))
                            sem_nm, sem_v = "rp", r.bump("rp", 1)
                        else:
                            sem_nm, sem_v = "pe", qk_done
                        r.wait("scalar", sem_nm, sem_v)
                        if hh == 0 and rr == 0 and kh == khs[0]:
                            r.wait("scalar", "gN", nb_ld)
                        if p_free[kh]:
                            r.wait("scalar", "pe", p_free[kh])
                        if den_free[kh]:
                            r.wait("scalar", "c", den_free[kh])
                        r.add("scalar", lambda s, h=h, rr=rr, kh=kh, bank=bank:
                              s.activation(p_sb[:, kh * 512:(kh + 1) * 512], bank[:],
                                           AF.Exp, scale=1.0 / math.sqrt(128.0),
                                           bias=negb_u[:, h * 8 + rr: h * 8 + rr + 1],
                                           accum_out=den_sb[:, kh:kh + 1])
                              .then_inc(r.h("ev"), 1))
                        s_free[kh] = exp_evs[(rr, kh)] = r.bump("ev", 1)
                        r.wait("tensor", "ev", exp_evs[(rr, kh)])
                        if pt_ps_free[kh]:
                            r.wait("tensor", "c", pt_ps_free[kh])
                        ptbank = PB[2 + kh].ap().bitcast(BF16)
                        for t4 in range(4):
                            def tr(t, kh=kh, t4=t4, ptbank=ptbank):
                                i = t.transpose(
                                    ptbank[:, t4 * 128:(t4 + 1) * 128],
                                    p_sb[:, kh * 512 + t4 * 128: kh * 512 + (t4 + 1) * 128],
                                    ident_bf[:])
                                if t4 == 3:
                                    i.then_inc(r.h("pe"), 1)
                            r.add("tensor", tr)
                        p_free[kh] = tr_done = r.bump("pe", 1)
                        r.wait("vector", "pe", tr_done)
                        if pt_sb_free[kh]:
                            r.wait("vector", "pe", pt_sb_free[kh])
                        r.add("vector", lambda v, kh=kh, ptbank=ptbank: v.tensor_copy(
                            pt_sb[:, kh * 512:(kh + 1) * 512], ptbank[:, 0:512])
                            .then_inc(r.h("c"), 1))
                        pt_ps_free[kh] = ptev = r.bump("c", 1)
                        r.wait("tensor", "c", ptev)
                        if kh == khs[0] and o_free:
                            r.wait("tensor", "c", o_free)
                        for t4 in range(4):
                            first = kh == khs[0] and t4 == 0
                            last = kh == khs[-1] and t4 == 3
                            def av(t, kh=kh, t4=t4, vb=vb, hb=hb, first=first, last=last):
                                i = t.matmul(PB[4][:, 0:128],
                                             pt_sb[:, kh * 512 + t4 * 128: kh * 512 + (t4 + 1) * 128],
                                             vb[:, hb + (kh * 4 + t4) * 128: hb + (kh * 4 + t4 + 1) * 128],
                                             start=first, stop=last)
                                if last:
                                    i.then_inc(r.h("pe"), 1)
                            r.add("tensor", av)
                        if kh == khs[-1]:
                            pt_sb_free[0] = pt_sb_free[1] = av_done = r.bump("pe", 1)
                            r.wait("vector", "ev", exp_evs[(rr, khs[0])])
                            if len(khs) == 2:
                                r.wait("vector", "ev", exp_evs[(rr, 1)])
                            if it >= 2:
                                r.wait("vector", "acc", oacc_dma_done[it - 2])
                            else:
                                r.wait("vector", "acc", ozero_done)
                            if len(khs) == 1:
                                r.add("vector", lambda v, rr=rr, ob=ob: v.tensor_copy(
                                    ob[:, 1024 + rr:1024 + rr + 1], den_sb[:, 0:1])
                                    .then_inc(r.h("c"), 1))
                            else:
                                r.add("vector", lambda v, rr=rr, ob=ob: v.tensor_add(
                                    ob[:, 1024 + rr:1024 + rr + 1], den_sb[:, 0:1],
                                    den_sb[:, 1:2]).then_inc(r.h("c"), 1))
                            den_free[0] = den_free[1] = r.bump("c", 1)
                    r.wait("vector", "pe", av_done)
                    r.add("vector", lambda v, rr=rr, ob=ob: v.tensor_copy(
                        ob[:, rr * 128:(rr + 1) * 128], PB[4][:, 0:128])
                        .then_inc(r.h("c"), 1))
                    o_free = r.bump("c", 1)
                oacc_filled = o_free
                r.wait("gpsimd", "c", oacc_filled)
                r.wait("gpsimd", "c", den_free[0])
                if it >= 1:
                    r.wait("gpsimd", "acc", oacc_dma_done[it - 1])
                else:
                    r.wait("gpsimd", "acc", ozero_done)
                r.add("gpsimd", lambda g, u=u, h=h, ob=ob: g.indirect_dma_start(
                    ap["Odram"].rearrange("a h p t -> (a h p) t"),
                    IndirectOffsetOnAxis(oidx_sb[:, u * NH + h: u * NH + h + 1], 0),
                    ob[:], None, compute_op=ALU.add).then_inc(r.h("acc"), 16))
                oacc_dma_done[it] = r.bump("acc")
            half_pe_done[hit] = r.cur("pe")
        negbu_used = max(s_free[0], s_free[1])

    return {
        "ones_row": ones_row, "ones_col": ones_col, "ident_bf": ident_bf,
        "ident_f32": ident_f32, "flash_acc_done": r.cur("acc"),
        "flash_pe_done": r.cur("pe"), "negb_done": negb_done,
    }


def emit_ln_epilogue(nc, r, sb_tiles, y_full, x1_slices, PB, s1b, s2b,
                     tok_w=512, nchunks=KC, eps=1e-5, dim=None):
    dim = dim or (nchunks * 128)
    ones_row = sb_tiles["ones_row"]
    m_row, m2_row = sb_tiles["m_row"], sb_tiles["m2_row"]
    var_row, rstd_row, mr_row = sb_tiles["var_row"], sb_tiles["rstd_row"], sb_tiles["mr_row"]
    tmp = sb_tiles["tmp"]

    r.add("scalar", lambda s: s.activation(
        m_row[:], PB[s1b][0:1, 0:tok_w], AF.Identity, scale=1.0 / dim)
        .then_inc(r.h("ev"), 1))
    mdone = r.bump("ev", 1)
    r.wait("vector", "ev", mdone)
    r.add("vector", lambda v: v.tensor_mul(m2_row[:], m_row[:], m_row[:])
          .then_inc(r.h("c"), 1))
    r.wait("vector", "c", r.bump("c", 1))
    r.add("vector", lambda v: v.scalar_tensor_tensor(
        var_row[:], PB[s2b][0:1, 0:tok_w], 1.0 / dim, m2_row[:],
        ALU.mult, ALU.subtract).then_inc(r.h("c"), 1))
    vdone = r.bump("c", 1)
    r.wait("scalar", "c", vdone)
    r.add("scalar", lambda s: s.activation(
        var_row[:], var_row[:], AF.Sqrt, bias=sb_tiles["eps_col"][:])
        .then_inc(r.h("ev"), 1))
    sdone = r.bump("ev", 1)
    r.wait("vector", "ev", sdone)
    r.add("vector", lambda v: v.reciprocal(rstd_row[:], var_row[:])
          .then_inc(r.h("c"), 1))
    r.wait("vector", "c", r.bump("c", 1))
    r.add("vector", lambda v: v.tensor_mul(mr_row[:], m_row[:], rstd_row[:])
          .then_inc(r.h("c"), 1))
    rows_done = r.bump("c", 1)
    r.wait("tensor", "c", rows_done)
    r.add("tensor", lambda t: t.matmul(PB[s1b][:, 0:tok_w], ones_row[:],
                                       rstd_row[:], start=True, stop=True)
          .then_inc(r.h("pe"), 1))
    r.add("tensor", lambda t: t.matmul(PB[s2b][:, 0:tok_w], ones_row[:],
                                       mr_row[:], start=True, stop=True)
          .then_inc(r.h("pe"), 1))
    r.bump("pe", 1)
    bc_done = r.bump("pe", 1)
    r.wait("vector", "pe", bc_done)
    for oc in range(nchunks):
        r.wait("vector", "c", r.cur("c"))
        r.add("vector", lambda v, oc=oc: v.tensor_mul(
            tmp[:], y_full[:, oc * tok_w:(oc + 1) * tok_w], PB[s1b][:, 0:tok_w])
            .then_inc(r.h("c"), 1))
        r.wait("vector", "c", r.bump("c", 1))
        r.add("vector", lambda v, oc=oc: v.tensor_sub(
            x1_slices(oc), tmp[:], PB[s2b][:, 0:tok_w]).then_inc(r.h("c"), 1))
        r.bump("c", 1)
    return r.cur("c")


def emit_wo_ln(nc, es, r, ap, sb, PB, fl, AR):
    ones_row, ones_col = fl["ones_row"], fl["ones_col"]
    ident_bf = fl["ident_bf"]
    ot_sb = AR.tile(128, NH * TPC, BF16)
    od_sb = [AR.tile(128, 1032, F32) for i in range(2)]
    recip_sb = sb("w_recip", [128, 8], F32)
    obf = sb("w_obf", [128, 128], BF16)
    wo_sb = [AR.tile(128, KC * 128, BF16) for i in range(2)]
    bo_sb = sb("w_bo", [128, KC], F32)
    x_sl = [AR.tile(128, TT, F32) for i in range(2)]
    y_full = AR.tile(128, KC * TT, F32)
    sq_bf = AR.tile(128, TT, BF16)
    y_bf = AR.tile(128, TT, BF16)
    ones_col_bf = sb("w_onescbf", [128, 1], BF16)
    lnt = {
        "ones_row": ones_row,
        "m_row": AR.tile(1, TT, F32), "m2_row": AR.tile(1, TT, F32),
        "var_row": AR.tile(1, TT, F32), "rstd_row": AR.tile(1, TT, F32),
        "mr_row": AR.tile(1, TT, F32), "tmp": AR.tile(128, TT, F32),
        "eps_col": sb("w_eps", [1, 1], F32),
    }

    r.add("vector", lambda v: v.memset(lnt["eps_col"][:], 1e-5))
    r.add("vector", lambda v: v.memset(ones_col_bf[:], 1.0).then_inc(r.h("c"), 1))
    onesbf_done = r.bump("c", 1)
    r.add("sync", lambda q: q.dma_start(out=bo_sb[:], in_=ap["bo"])
          .then_inc(r.h("in"), 16))
    bo_done = r.bump("in")

    flash_done = fl["flash_acc_done"]
    r.wait("sync", "acc", flash_done)
    r.wait("vector", "pe", fl["flash_pe_done"])
    r.wait("vector", "acc", flash_done)
    ot_done = {}
    od_used = {}
    obf_free = 0
    tr_free = 0
    for qc in range(2):
        for h in range(NH):
            it = qc * NH + h
            buf = od_sb[it % 2]
            if it >= 2:
                r.wait("sync", "ev", od_used[it - 2])
            osem = "oA" if it % 2 == 0 else "oB"
            r.add("sync", lambda q, qc=qc, h=h, buf=buf, osem=osem: q.dma_start(
                out=buf[:], in_=ap["Odram"][qc, h]).then_inc(r.h(osem), 16))
            old = r.bump(osem)
            r.wait("vector", osem, old)
            r.add("vector", lambda v, buf=buf: v.tensor_scalar_add(
                recip_sb[:], buf[:, 1024:1032], 1e-30).then_inc(r.h("c"), 1))
            r.wait("vector", "c", r.bump("c", 1))
            r.add("vector", lambda v: v.reciprocal(recip_sb[:], recip_sb[:])
                  .then_inc(r.h("c"), 1))
            rec_done = r.bump("c", 1)
            for rr in range(8):
                r.wait("scalar", "c", rec_done)
                if obf_free:
                    r.wait("scalar", "pe", obf_free)
                r.add("scalar", lambda s, buf=buf, rr=rr: s.activation(
                    obf[:], buf[:, rr * 128:(rr + 1) * 128], AF.Identity,
                    scale=recip_sb[:, rr:rr + 1]).then_inc(r.h("ev"), 1))
                ob_done = r.bump("ev", 1)
                od_used[it] = ob_done
                r.wait("tensor", "ev", ob_done)
                if tr_free:
                    r.wait("tensor", "c", tr_free)
                r.add("tensor", lambda t: t.transpose(
                    PB[2].ap().bitcast(BF16)[:, 0:128], obf[:], ident_bf[:])
                    .then_inc(r.h("pe"), 1))
                obf_free = trd = r.bump("pe", 1)
                r.wait("vector", "pe", trd)
                col = h * TPC + qc * CHUNK + rr * 128
                r.add("vector", lambda v, col=col: v.tensor_copy(
                    ot_sb[:, col:col + 128], PB[2].ap().bitcast(BF16)[:, 0:128])
                    .then_inc(r.h("c"), 1))
                tr_free = r.bump("c", 1)
            ot_done[it] = tr_free
    ot_all = tr_free

    st_prev = 0
    wo_used, x_used, y_done_all = {}, {}, {}
    for ti in range(NTT):
        y_done = {}
        s12_done = 0
        for oc in range(KC):
            gidx = ti * KC + oc
            wbuf = wo_sb[gidx % 2]
            if gidx >= 2:
                r.wait("gpsimd", "pe", wo_used[gidx - 2])
            wsem = "wA" if gidx % 2 == 0 else "wB"
            r.add("gpsimd", lambda g, oc=oc, wbuf=wbuf, wsem=wsem: g.dma_start(
                out=wbuf[:], in_=ap["wo"][oc]).then_inc(r.h(wsem), 16))
            wdone = r.bump(wsem)
            xbuf = x_sl[gidx % 2]
            if gidx >= 2:
                r.wait("sync", "c", x_used[gidx - 2])
            xsem = "xA" if gidx % 2 == 0 else "xB"
            r.add("sync", lambda q, oc=oc, ti=ti, xbuf=xbuf, xsem=xsem: q.dma_start(
                out=xbuf[:], in_=ap["x_p"][:, oc * TPC + ti * TT: oc * TPC + ti * TT + TT])
                .then_inc(r.h(xsem), 16))
            xdone = r.bump(xsem)
            bank = PB[gidx % 2]
            r.wait("tensor", wsem, wdone)
            if gidx == 0:
                r.wait("tensor", "c", ot_all)
            if gidx >= 2:
                r.wait("tensor", "c", y_done_all.get(gidx - 2, 0))
            for kc in range(KC):
                last = kc == KC - 1
                def mm(t, oc=oc, ti=ti, kc=kc, wbuf=wbuf, bank=bank, last=last):
                    i = t.matmul(bank[:], wbuf[:, kc * 128:(kc + 1) * 128],
                                 ot_sb[:, kc * TPC + ti * TT: kc * TPC + ti * TT + TT],
                                 start=(kc == 0), stop=last)
                    if last:
                        i.then_inc(r.h("pe"), 1)
                r.add("tensor", mm)
            wo_used[gidx] = mm_done = r.bump("pe", 1)
            r.wait("vector", "pe", mm_done)
            r.wait("vector", xsem, xdone)
            if gidx == 0:
                r.wait("vector", "in", bo_done)
            if st_prev:
                r.wait("vector", "st", st_prev)
            r.add("vector", lambda v, oc=oc, bank=bank, xbuf=xbuf:
                  v.scalar_tensor_tensor(
                      y_full[:, oc * TT:(oc + 1) * TT], bank[:],
                      bo_sb[:, oc:oc + 1], xbuf[:], ALU.add, ALU.add)
                  .then_inc(r.h("c"), 1))
            x_used[gidx] = y_done_all[gidx] = yd = r.bump("c", 1)
            r.wait("scalar", "c", yd)
            if s12_done:
                r.wait("scalar", "pe", s12_done)
            r.add("scalar", lambda s, oc=oc: s.activation(
                y_bf[:], y_full[:, oc * TT:(oc + 1) * TT], AF.Identity)
                .then_inc(r.h("ev"), 1))
            r.bump("ev", 1)
            r.add("scalar", lambda s, oc=oc: s.activation(
                sq_bf[:], y_full[:, oc * TT:(oc + 1) * TT], AF.Square)
                .then_inc(r.h("ev"), 1))
            sqd = r.bump("ev", 1)
            r.wait("tensor", "ev", sqd)
            if gidx == 0:
                r.wait("tensor", "c", onesbf_done)
            r.add("tensor", lambda t, oc=oc: t.matmul(
                PB[6][0:1, 0:TT], ones_col_bf[:], y_bf[:],
                start=(oc == 0), stop=(oc == KC - 1)))
            def mm2(t, oc=oc):
                i = t.matmul(PB[7][0:1, 0:TT], ones_col_bf[:], sq_bf[:],
                             start=(oc == 0), stop=(oc == KC - 1))
                i.then_inc(r.h("pe"), 1)
            r.add("tensor", mm2)
            s12_done = r.bump("pe", 1)
        r.wait("scalar", "pe", s12_done)
        r.wait("vector", "pe", s12_done)
        emit_ln_epilogue(nc, r, lnt, y_full,
                         lambda oc: y_full[:, oc * TT:(oc + 1) * TT], PB, 6, 7)
        ln_done = r.cur("c")
        r.wait("sync", "c", ln_done)
        for oc in range(KC):
            r.add("sync", lambda q, oc=oc, ti=ti: q.dma_start(
                out=ap["x1"][:, oc * TPC + ti * TT: oc * TPC + ti * TT + TT],
                in_=y_full[:, oc * TT:(oc + 1) * TT]).then_inc(r.h("st"), 16))
            r.bump("st")
        st_prev = r.cur("st")
    return {"x1_done": st_prev}


def prep_ffn0(inp):
    gate_w = np.asarray(inp["L0_ffn_gate_w"])
    w1 = np.asarray(inp["L0_ffn_w1"])
    w3 = np.asarray(inp["L0_ffn_w3"])
    w2 = np.asarray(inp["L0_ffn_w2"])
    a1 = np.asarray(inp["L0_ffn_a1"])
    b1 = np.asarray(inp["L0_ffn_b1"])
    a3 = np.asarray(inp["L0_ffn_a3"])
    b3 = np.asarray(inp["L0_ffn_b3"])
    a2 = np.asarray(inp["L0_ffn_a2"])
    b2 = np.asarray(inp["L0_ffn_b2"])

    gw = _pack_kc_free(np.ascontiguousarray(gate_w.T))
    w1t = _oc_tiles(np.ascontiguousarray(w1.T))
    w3t = _oc_tiles(np.ascontiguousarray(w3.T))
    w13 = np.concatenate([w1t, w3t], axis=2)
    b13 = np.zeros((NFC, 48, NE * 128), np.float32)
    for fc in range(NFC):
        for e in range(NE):
            b13[fc, 0:16, e * 128:(e + 1) * 128] = b1[e].T[:, fc * 128:(fc + 1) * 128]
            b13[fc, 32:48, e * 128:(e + 1) * 128] = b3[e].T[:, fc * 128:(fc + 1) * 128]
    a13 = np.zeros((NE, 128, KC * 64), np.float32)
    for e in range(NE):
        for kc in range(KC):
            a13[e][:, kc * 64:kc * 64 + 16] = a1[e][:, kc * 128:(kc + 1) * 128].T
            a13[e][:, kc * 64 + 32:kc * 64 + 48] = a3[e][:, kc * 128:(kc + 1) * 128].T
    a2t = np.zeros((NE, NFC, 128, 16), np.float32)
    for e in range(NE):
        for fc in range(NFC):
            a2t[e, fc] = a2[e][:, fc * 128:(fc + 1) * 128].T
    b2s = np.zeros((2, 128, D), np.float32)
    for e in range(NE):
        g = e // 2
        b2s[e % 2, 32 * g:32 * g + 16, :] = b2[e].T
    b2so = np.stack([_oc_tiles(b2s[0]), _oc_tiles(b2s[1])])
    w2t = _oc_tiles(np.ascontiguousarray(w2.T))
    w2fc = np.zeros((NFC, 128, KC * 128), np.float32)
    for fc in range(NFC):
        for oc in range(KC):
            w2fc[fc][:, oc * 128:(oc + 1) * 128] = w2t[oc][:, fc * 128:(fc + 1) * 128]
    a2th = np.ascontiguousarray(
        a2t.transpose(2, 0, 1, 3).reshape(128, NE * NFC * 16))
    b2sh = np.ascontiguousarray(
        b2so.transpose(2, 0, 1, 3).reshape(128, 2 * KC * 128))
    return {"gw": gw, "w13": w13, "b13": b13, "a13": a13, "a2t": a2th,
            "b2so": b2sh, "w2fc": w2fc}


def emit_ffn0(nc, es, r, ap, sb, PB, fl, AR):
    ones_row, ones_col, ident_bf, ident_f32 = (
        fl["ones_row"], fl["ones_col"], fl["ident_bf"], fl["ident_f32"])
    x1bf = AR.tile(128, KC * TT, BF16)
    acc = AR.tile(128, NFC * TT, BF16)
    y_full = AR.tile(128, KC * TT, F32)
    w13_sb = [AR.tile(128, 2 * KC * 128, BF16) for _ in range(2)]
    w2_sb = [AR.tile(128, KC * 128, BF16) for _ in range(2)]
    a13_sb = [AR.tile(128, 2 * KC * 64, BF16) for _ in range(2)]
    a2t_sb = AR.tile(128, NE * NFC * 16, BF16)
    b2s_sb = AR.tile(128, 2 * KC * 128, BF16)
    gw_sb = AR.tile(128, KC * 8, BF16)
    wb_sb = AR.tile(128, NE * TT, BF16)
    u_sb = AR.tile(64, NE * TT, BF16)
    b13_sb = [AR.tile(48, NE * 128, BF16) for _ in range(2)]
    zbf = AR.tile(128, 2 * TT, BF16)
    lg = sb("n_lg", [8, TT], F32)
    ltok = sb("n_ltok", [128, 8], F32)
    m1 = sb("n_m1", [128, 1], F32)
    ecol = sb("n_e", [128, 8], F32)
    ind1 = sb("n_ind1", [128, 8], F32)
    em = sb("n_em", [128, 8], F32)
    m2c = sb("n_m2", [128, 1], F32)
    wun = sb("n_wun", [128, 8], F32)
    dcol = sb("n_den", [128, 1], F32)
    wtok = sb("n_wtok", [128, 8], F32)
    wrow = sb("n_wrow", [8, TT], F32)
    wrow0 = [sb(f"n_wrow0{i}", [1, TT], F32) for i in range(2)]
    silu_sb = AR.tile(128, TT, BF16)
    act_sb = AR.tile(128, TT, BF16)
    t2_sb = AR.tile(128, TT, BF16)
    x_sl = [AR.tile(128, TT, F32) for i in range(2)]
    sqn_bf = AR.tile(128, TT, BF16)
    yn_bf = AR.tile(128, TT, BF16)
    lnt = {
        "ones_row": ones_row,
        "m_row": AR.tile(1, TT, F32), "m2_row": AR.tile(1, TT, F32),
        "var_row": AR.tile(1, TT, F32), "rstd_row": AR.tile(1, TT, F32),
        "mr_row": AR.tile(1, TT, F32), "tmp": AR.tile(128, TT, F32),
        "eps_col": sb("n_eps", [1, 1], F32),
    }
    ocbf = sb("n_ocbf", [128, 1], BF16)
    zrow_bf = sb("n_zrow", [1, 640], BF16)

    def vw(val):
        r.wait("vector", "c", val)

    def vadd(fn):
        r.add("vector", lambda v, fn=fn: fn(v).then_inc(r.h("c"), 1))
        return r.bump("c", 1)

    def sadd(fn):
        r.add("scalar", lambda s, fn=fn: fn(s).then_inc(r.h("ev"), 1))
        return r.bump("ev", 1)

    r.add("vector", lambda v: v.memset(lnt["eps_col"][:], 1e-5))
    r.add("vector", lambda v: v.memset(zrow_bf[:], 0.0))
    onesbf2 = vadd(lambda v: v.memset(ocbf[:], 1.0))

    r.wait("gpsimd", "st", fl["x1_done"])
    r.add("gpsimd", lambda g: g.dma_start(out=gw_sb[:], in_=ap["gw"])
          .then_inc(r.h("gin"), 16))
    r.bump("gin")
    r.add("gpsimd", lambda g: g.dma_start(
        out=b2s_sb[:], in_=ap["b2so"]).then_inc(r.h("gin"), 16))
    r.bump("gin")
    r.add("gpsimd", lambda g: g.dma_start(
        out=a2t_sb[:], in_=ap["a2t"]).then_inc(r.h("gin"), 16))
    r.bump("gin")
    ffn_gin = r.cur("gin")

    x1_done = fl["x1_done"]
    prev_store = 0
    y_ln_prev = 0
    x_used, w13_used, b13_used, a13_used, w2_used = {}, {}, {}, {}, {}
    x1bf_used = {}
    for ti in range(NTT):
        xs = "c0s" if ti % 2 == 0 else "c1s"
        if ti >= 2:
            r.wait("gpsimd", "pe", x1bf_used[ti - 2])
        if ti == 0:
            r.wait("gpsimd", "st", x1_done)
        r.add("gpsimd", lambda g, ti=ti, xs=xs: g.dma_start(
            out=x1bf[:].rearrange("p (a t) -> p a t", a=KC),
            in_=ap["x1"].rearrange("p (a t) -> p a t", a=KC)
            [:, :, ti * TT:(ti + 1) * TT]).then_inc(r.h(xs), 16))
        x1bf_ld = r.bump(xs)
        r.wait("tensor", xs, x1bf_ld)
        if ti == 0:
            r.wait("tensor", "gin", ffn_gin)
        if ti >= 1:
            r.wait("tensor", "c", y_ln_prev)
        for kc in range(KC):
            def mmg(t, kc=kc, last=kc == KC - 1):
                i = t.matmul(PB[0][0:8, :], gw_sb[:, kc * 8:(kc + 1) * 8],
                             x1bf[:, kc * TT:(kc + 1) * TT],
                             start=kc == 0, stop=last)
                if last:
                    i.then_inc(r.h("pe"), 1)
            r.add("tensor", mmg)
        gmm = r.bump("pe", 1)
        r.wait("vector", "pe", gmm)
        vw(r.cur("c"))
        lg_ev = vadd(lambda v: v.tensor_copy(lg[:], PB[0][0:8, :]))
        wrow_parts = []
        for sfour in range(4):
            r.wait("tensor", "c", lg_ev)
            if wrow_parts:
                r.wait("tensor", "c", wrow_parts[-1])
            r.add("tensor", lambda t, sfour=sfour: t.transpose(
                PB[1][:, 0:8], lg[:, sfour * 128:(sfour + 1) * 128],
                ident_f32[0:8, 0:8]).then_inc(r.h("pe"), 1))
            ltr = r.bump("pe", 1)
            r.wait("vector", "pe", ltr)
            vw(r.cur("c"))
            lt_ev = vadd(lambda v: v.tensor_copy(ltok[:], PB[1][:, 0:8]))
            vw(lt_ev)
            e1 = vadd(lambda v: v.reduce_max(m1[:], ltok[:], axis=AX.X))
            vw(e1)
            e2 = vadd(lambda v: v.tensor_scalar_mul(m1[:], m1[:], -1.0))
            r.wait("scalar", "c", e2)
            e3 = sadd(lambda s: s.activation(ecol[:], ltok[:], AF.Exp, bias=m1[:]))
            r.wait("vector", "ev", e3)
            e4 = vadd(lambda v: v.tensor_single_scalar(ind1[:], ecol[:], 1.0, ALU.is_ge))
            vw(e4)
            e5 = vadd(lambda v: v.tensor_scalar(ind1[:], ind1[:], -1.0, 1.0,
                                                ALU.mult, ALU.add))
            vw(e5)
            e6 = vadd(lambda v: v.tensor_mul(em[:], ecol[:], ind1[:]))
            vw(e6)
            e7 = vadd(lambda v: v.reduce_max(m2c[:], em[:], axis=AX.X))
            vw(e7)
            e8 = vadd(lambda v: v.tensor_single_scalar(wun[:], ecol[:], m2c[:], ALU.is_ge))
            vw(e8)
            e9 = vadd(lambda v: v.tensor_mul(wun[:], wun[:], ecol[:]))
            vw(e9)
            e10 = vadd(lambda v: v.reduce_sum(dcol[:], wun[:], axis=AX.X))
            vw(e10)
            e11 = vadd(lambda v: v.reciprocal(dcol[:], dcol[:]))
            vw(e11)
            e12 = vadd(lambda v: v.tensor_scalar_mul(wtok[:], wun[:], dcol[:]))
            r.wait("tensor", "c", e12)
            r.add("tensor", lambda t, sfour=sfour: t.transpose(
                PB[1][0:8, 0:128], wtok[:], ident_f32[:]).then_inc(r.h("pe"), 1))
            wtr = r.bump("pe", 1)
            r.wait("vector", "pe", wtr)
            vw(r.cur("c"))
            wrow_parts.append(vadd(lambda v, sfour=sfour: v.tensor_copy(
                wrow[:, sfour * 128:(sfour + 1) * 128], PB[1][0:8, 0:128])))
        wb_evs = []
        wmm_by_buf = {0: 0, 1: 0}
        for e in range(NE):
            wsel = "wrS" if e % 2 == 0 else "wrT"
            wbuf0 = wrow0[e % 2]
            r.wait("sync", "c", wrow_parts[-1])
            if wmm_by_buf[e % 2]:
                r.wait("sync", "pe", wmm_by_buf[e % 2])
            r.add("sync", lambda q, e=e, wbuf0=wbuf0: q.dma_start(
                out=wbuf0[:], in_=wrow[e:e + 1, :]).then_inc(
                r.h("wrS" if e % 2 == 0 else "wrT"), 16))
            wr0 = r.bump(wsel)
            r.wait("tensor", wsel, wr0)
            if wb_evs:
                r.wait("tensor", "c", wb_evs[-1])
            r.add("tensor", lambda t, wbuf0=wbuf0: t.matmul(
                PB[0][:, 0:TT], ones_row[:], wbuf0[:],
                start=True, stop=True).then_inc(r.h("pe"), 1))
            wmm = r.bump("pe", 1)
            wmm_by_buf[e % 2] = wmm
            r.wait("vector", "pe", wmm)
            vw(r.cur("c"))
            wb_evs.append(vadd(lambda v, e=e: v.tensor_copy(
                wb_sb[:, e * TT:(e + 1) * TT], PB[0][:, 0:TT])))
        wb_done = wb_evs[-1]

        for p in range(4):
            pidx = ti * 4 + p
            abuf = a13_sb[pidx % 2]
            asem = "aA" if pidx % 2 == 0 else "aB"
            if pidx >= 2:
                r.wait("gpsimd", "pe", a13_used[pidx - 2])

            def lda(g, p=p, abuf=abuf, asem=asem):
                g.dma_start(out=abuf[:, 0:KC * 64], in_=ap["a13"][2 * p]) \
                    .then_inc(r.h(asem), 16)
                g.dma_start(out=abuf[:, KC * 64:2 * KC * 64], in_=ap["a13"][2 * p + 1]) \
                    .then_inc(r.h(asem), 16)
            r.add("gpsimd", lda)
            ald = r.bump(asem, 32)
            r.wait("tensor", asem, ald)
            if p == 0 and ti >= 1:
                r.wait("tensor", "c", y_ln_prev)
            if pidx >= 1:
                r.wait("tensor", "c", u_ev_prev)
            for e_in in range(2):
                for kc in range(KC):
                    def mmu(t, p=p, e_in=e_in, kc=kc, abuf=abuf,
                            last=kc == KC - 1):
                        i = t.matmul(
                            PB[2][64 * e_in:64 * e_in + 64, :],
                            abuf[:, e_in * KC * 64 + kc * 64: e_in * KC * 64 + (kc + 1) * 64],
                            x1bf[:, kc * TT:(kc + 1) * TT],
                            start=kc == 0, stop=last,
                            tile_position=(0, 64 * e_in))
                        if last:
                            i.then_inc(r.h("pe"), 1)
                    r.add("tensor", mmu)
                r.bump("pe", 1)
            umm = r.cur("pe")
            a13_used[pidx] = umm
            r.wait("vector", "pe", umm)
            vw(r.cur("c"))
            vadd(lambda v, p=p: v.tensor_scalar_mul(
                u_sb[:, (2 * p) * TT:(2 * p + 1) * TT], PB[2][0:64, :], LORA_SCALE))
            vw(r.cur("c"))
            u_ev_prev = vadd(lambda v, p=p: v.tensor_scalar_mul(
                u_sb[:, (2 * p + 1) * TT:(2 * p + 2) * TT], PB[2][64:128, :], LORA_SCALE))
        u_done = u_ev_prev

        if ti >= 1:
            r.wait("tensor", "c", y_ln_prev)
        r.wait("tensor", "c", u_done)
        r.wait("tensor", "c", onesbf2)
        for zb in (7, 2):
            def zmm(t, zb=zb):
                t.matmul(PB[zb][:], zrow_bf[0:1, 0:128], zrow_bf[0:1, 128:640],
                         start=True, stop=False).then_inc(r.h("pe"), 1)
            r.add("tensor", zmm)
            r.bump("pe", 1)
        zclear = r.cur("pe")

        h_free = {3: 0, 5: 0, 4: 0, 6: 0}
        base_free = {3: 0, 4: 0}
        t2_used = 0
        silu_free = 0
        act_free = 0
        prev_accv = 0
        for fc in range(NFC):
            fidx = ti * NFC + fc
            wbuf = w13_sb[fidx % 2]
            fsem = "fA" if fidx % 2 == 0 else "fB"
            if fidx >= 2:
                r.wait("gpsimd", "pe", w13_used[fidx - 2])
            r.add("gpsimd", lambda g, fc=fc, wbuf=wbuf, fsem=fsem: g.dma_start(
                out=wbuf[:], in_=ap["w13"][fc]).then_inc(r.h(fsem), 16))
            wld = r.bump(fsem)
            bbuf = b13_sb[fidx % 2]
            bsem = "bA" if fidx % 2 == 0 else "bB"
            if fidx >= 2:
                r.wait("gpsimd", "pe", b13_used[fidx - 2])
            r.add("gpsimd", lambda g, fc=fc, bbuf=bbuf, bsem=bsem: g.dma_start(
                out=bbuf[:], in_=ap["b13"][fc]).then_inc(r.h(bsem), 16))
            bld = r.bump(bsem)
            r.wait("tensor", fsem, wld)
            for bi, bank in ((0, 3), (1, 4)):
                if base_free[bank]:
                    r.wait("tensor", "ev", base_free[bank])
                for kc in range(KC):
                    def mmb(t, bi=bi, kc=kc, bank=bank, wbuf=wbuf,
                            last=kc == KC - 1):
                        i = t.matmul(
                            PB[bank][:],
                            wbuf[:, bi * KC * 128 + kc * 128: bi * KC * 128 + (kc + 1) * 128],
                            x1bf[:, kc * TT:(kc + 1) * TT],
                            start=kc == 0, stop=last)
                        if last:
                            i.then_inc(r.h("pe"), 1)
                    r.add("tensor", mmb)
                r.bump("pe", 1)
            bmm = r.cur("pe")
            r.wait("scalar", "pe", bmm)
            if silu_free:
                r.wait("scalar", "c", silu_free)
            b1ev = sadd(lambda s: s.activation(silu_sb[:], PB[3][:], AF.Identity))
            b3ev = sadd(lambda s: s.activation(act_sb[:], PB[4][:], AF.Identity))
            base_free[3] = base_free[4] = b3ev
            for e in range(NE):
                h1b = 3 if e % 2 == 0 else 5
                h3b = 4 if e % 2 == 0 else 6
                r.wait("tensor", "ev", b3ev)
                r.wait("tensor", bsem, bld)
                if fc == 0 and e <= 1 and ti >= 1:
                    r.wait("tensor", "c", y_ln_prev)
                if h_free[h1b]:
                    r.wait("tensor", "ev", h_free[h1b])
                if e == 0:
                    r.wait("tensor", "c", u_done)
                    r.wait("tensor", "c", wb_done)

                def mmh(t, e=e, h1b=h1b, h3b=h3b, bbuf=bbuf):
                    t.matmul(PB[h1b][:], ident_bf[:], silu_sb[:],
                             start=True, stop=False)
                    i = t.matmul(PB[h1b][:], bbuf[0:16, e * 128:(e + 1) * 128],
                                 u_sb[0:16, e * TT:(e + 1) * TT],
                                 start=False, stop=True)
                    i.then_inc(r.h("pe"), 1)
                    t.matmul(PB[h3b][:], ident_bf[:], act_sb[:],
                             start=True, stop=False)
                    i2 = t.matmul(PB[h3b][:], bbuf[32:48, e * 128:(e + 1) * 128],
                                  u_sb[32:48, e * TT:(e + 1) * TT],
                                  start=False, stop=True)
                    i2.then_inc(r.h("pe"), 1)
                r.add("tensor", mmh)
                r.bump("pe", 1)
                hmm_done = r.bump("pe", 1)
                r.wait("scalar", "pe", hmm_done)
                if t2_used:
                    r.wait("scalar", "pe", t2_used)
                if prev_accv:
                    r.wait("scalar", "c", prev_accv)
                sl = sadd(lambda s, h1b=h1b: s.activation(
                    t2_sb[:], PB[h1b][:], AF.Silu))
                h_free[h1b] = sl
                r.wait("vector", "ev", sl)
                vw(r.cur("c"))
                a1v = vadd(lambda v, h3b=h3b: v.tensor_mul(
                    t2_sb[:], t2_sb[:], PB[h3b][:]))
                h_free[h3b] = a1v
                act_free = a1v
                vw(a1v)
                t2v = vadd(lambda v, e=e: v.tensor_mul(
                    t2_sb[:], t2_sb[:], wb_sb[:, e * TT:(e + 1) * TT]))
                vw(t2v)
                if e == 0:
                    accv = vadd(lambda v, fc=fc: v.tensor_copy(
                        acc[:, fc * TT:(fc + 1) * TT], t2_sb[:]))
                else:
                    accv = vadd(lambda v, fc=fc: v.tensor_add(
                        acc[:, fc * TT:(fc + 1) * TT],
                        acc[:, fc * TT:(fc + 1) * TT], t2_sb[:]))
                zbank = 7 if e % 2 == 0 else 2
                grp = e // 2
                r.wait("tensor", "c", t2v)

                def mmz(t, e=e, fc=fc, zbank=zbank, grp=grp):
                    t.matmul(PB[zbank][32 * grp:32 * grp + 16, :],
                             a2t_sb[:, (e * NFC + fc) * 16:(e * NFC + fc + 1) * 16],
                             t2_sb[:],
                             start=False, stop=False,
                             tile_position=(0, 32 * grp)).then_inc(r.h("pe"), 1)
                r.add("tensor", mmz)
                zmm = r.bump("pe", 1)
                t2_used = zmm
                prev_accv = accv
                silu_free = accv
            w13_used[fidx] = r.cur("pe")
            b13_used[fidx] = r.cur("pe")
        for zb in (7, 2):
            def zcl(t, zb=zb):
                t.matmul(PB[zb][:], zrow_bf[0:1, 0:128], zrow_bf[0:1, 128:640],
                         start=False, stop=True).then_inc(r.h("pe"), 1)
            r.add("tensor", zcl)
            r.bump("pe", 1)
        r.wait("vector", "pe", r.cur("pe"))
        vw(r.cur("c"))
        zev1 = vadd(lambda v: v.tensor_scalar_mul(zbf[:, 0:TT], PB[7][:], LORA_SCALE))
        vw(zev1)
        zev2 = vadd(lambda v: v.tensor_scalar_mul(zbf[:, TT:2 * TT], PB[2][:], LORA_SCALE))
        z_done = zev2

        down_mm = {}
        for ocg in range(2):
            for fc in range(NFC):
                fidx2 = (ti * 2 + ocg) * NFC + fc
                wbuf = w2_sb[fidx2 % 2]
                wsem2 = "dA" if fidx2 % 2 == 0 else "dB"
                if fidx2 >= 2:
                    r.wait("gpsimd", "pe", w2_used[fidx2 - 2])
                r.add("gpsimd", lambda g, fc=fc, wbuf=wbuf, wsem2=wsem2: g.dma_start(
                    out=wbuf[:], in_=ap["w2fc"][fc]).then_inc(r.h(wsem2), 16))
                w2ld = r.bump(wsem2)
                r.wait("tensor", wsem2, w2ld)
                if fc == 0:
                    r.wait("tensor", "c", z_done)
                    if ocg == 0 and ti >= 1:
                        r.wait("tensor", "c", y_ln_done)
                    if ocg == 1:
                        r.wait("tensor", "c", ybank_free)
                for oc8 in range(8):
                    oc = ocg * 8 + oc8

                    def mmd(t, fc=fc, oc=oc, oc8=oc8, wbuf=wbuf,
                            inc=oc8 == 7):
                        i = t.matmul(PB[oc8][:],
                                     wbuf[:, oc * 128:(oc + 1) * 128],
                                     acc[:, fc * TT:(fc + 1) * TT],
                                     start=fc == 0, stop=False)
                        if inc:
                            i.then_inc(r.h("pe"), 1)
                    r.add("tensor", mmd)
                w2_used[fidx2] = r.bump("pe", 1)
            for oc8 in range(8):
                oc = ocg * 8 + oc8

                def mmb2(t, oc=oc, oc8=oc8):
                    t.matmul(PB[oc8][:], b2s_sb[:, oc * 128:(oc + 1) * 128],
                             zbf[:, 0:TT], start=False, stop=False)
                    t.matmul(PB[oc8][:],
                             b2s_sb[:, (KC + oc) * 128:(KC + oc + 1) * 128],
                             zbf[:, TT:2 * TT], start=False, stop=True) \
                        .then_inc(r.h("pe"), 1)
                r.add("tensor", mmb2)
                down_mm[oc] = r.bump("pe", 1)
            for oc8 in range(8):
                oc = ocg * 8 + oc8
                gx = ti * KC + oc
                xbuf = x_sl[gx % 2]
                xsem = "xA" if gx % 2 == 0 else "xB"
                if gx >= 2:
                    r.wait("sync", "c", x_used[gx - 2])
                r.add("sync", lambda q, oc=oc, ti=ti, xbuf=xbuf, xsem=xsem: q.dma_start(
                    out=xbuf[:],
                    in_=ap["x1"][:, oc * TPC + ti * TT: oc * TPC + ti * TT + TT])
                    .then_inc(r.h(xsem), 16))
                xd = r.bump(xsem)
                r.wait("vector", "pe", down_mm[oc])
                r.wait("vector", xsem, xd)
                if prev_store:
                    r.wait("vector", "st", prev_store)
                vw(r.cur("c"))
                x_used[gx] = vadd(lambda v, oc=oc, oc8=oc8, xbuf=xbuf: v.tensor_add(
                    y_full[:, oc * TT:(oc + 1) * TT], PB[oc8][:], xbuf[:]))
            ybank_free = r.cur("c")
        y_done = r.cur("c")
        s12p = 0
        for oc in range(KC):
            r.wait("scalar", "c", y_done)
            if s12p:
                r.wait("scalar", "pe", s12p)
            sadd(lambda s, oc=oc: s.activation(
                yn_bf[:], y_full[:, oc * TT:(oc + 1) * TT], AF.Identity))
            sqd = sadd(lambda s, oc=oc: s.activation(
                sqn_bf[:], y_full[:, oc * TT:(oc + 1) * TT], AF.Square))
            r.wait("tensor", "ev", sqd)
            if oc == 0:
                r.wait("tensor", "c", onesbf2)
            r.add("tensor", lambda t, oc=oc: t.matmul(
                PB[5][0:1, 0:TT], ocbf[:], yn_bf[:],
                start=oc == 0, stop=oc == KC - 1))

            def mm2n(t, oc=oc):
                i = t.matmul(PB[6][0:1, 0:TT], ocbf[:], sqn_bf[:],
                             start=oc == 0, stop=oc == KC - 1)
                i.then_inc(r.h("pe"), 1)
            r.add("tensor", mm2n)
            s12p = r.bump("pe", 1)
        r.wait("scalar", "pe", s12p)
        r.wait("vector", "pe", s12p)
        emit_ln_epilogue(nc, r, lnt, y_full,
                         lambda oc: y_full[:, oc * TT:(oc + 1) * TT], PB, 5, 6)
        y_ln_done = r.cur("c")
        r.wait("sync", "c", y_ln_done)
        for oc in range(KC):
            r.add("sync", lambda q, oc=oc, ti=ti: q.dma_start(
                out=ap["x2"][:, oc * TPC + ti * TT: oc * TPC + ti * TT + TT],
                in_=y_full[:, oc * TT:(oc + 1) * TT]).then_inc(r.h("st"), 16))
            r.bump("st")
        prev_store = r.cur("st")
        y_ln_prev = y_ln_done
        x1bf_used[ti] = r.cur("pe")
    return {"x2_done": prev_store}


def prep_launch2b(inp, x2_by_core):
    qkv_w = np.asarray(inp["L1_attn_qkv_w"])
    gate_w = np.asarray(inp["L1_attn_gate_w"])
    qkv_wT = np.ascontiguousarray(qkv_w.T)
    wqk1 = np.empty((D, 2 * H1 * D1), np.float32)
    wv1 = np.empty((D, H1 * D1), np.float32)
    for h in range(H1):
        wqk1[:, 128 * h:128 * h + 64] = qkv_wT[:, 192 * h:192 * h + 64]
        wqk1[:, 128 * h + 64:128 * h + 128] = qkv_wT[:, 192 * h + 64:192 * h + 128]
        wv1[:, 64 * h:64 * h + 64] = qkv_wT[:, 192 * h + 128:192 * h + 192]
    wqk1_t = _oc_tiles(wqk1)
    wv1_t = _oc_tiles(wv1).reshape(16, 128, KC * 128)
    wv1_g = np.empty((4, 128, KC * 512), np.float32)
    for vc in range(4):
        for sub in range(4):
            for kc in range(KC):
                wv1_g[vc][:, kc * 512 + sub * 128:kc * 512 + (sub + 1) * 128] = \
                    wv1_t[vc * 4 + sub][:, kc * 128:(kc + 1) * 128]
    gwt = _oc_tiles(np.ascontiguousarray(gate_w.T))

    qd, kd, ddm, bd = _l1_decay()
    qd_e = np.ascontiguousarray(np.broadcast_to(qd[:, None, :], (H1, 64, BS1)))
    kd_e = np.ascontiguousarray(np.broadcast_to(kd[:, None, :], (H1, 64, BS1)))
    ddTf = ddm.transpose(0, 2, 1)
    ddT = np.ascontiguousarray(
        ddTf.reshape(H1, 2, 128, BS1).transpose(0, 2, 1, 3).reshape(H1, 128, 2 * BS1))
    ident = np.eye(128, dtype=np.float32)

    pc = []
    for c in range(8):
        d = {
            "wqk1": wqk1_t, "wv1": wv1_g, "gwt": gwt,
            "qd": qd_e, "kd": kd_e, "ddT": ddT, "ident": ident,
        }
        if x2_by_core[c] is not None:
            d["x2"] = x2_by_core[c]
        pc.append(d)
    return pc


def emit_l1a(nc, es, r, ap, sb, PB, AR, entry_st, ident_bf, ident_f32):
    x_bf = AR.tile(128, KC * TPC, BF16)
    wbuf2 = [AR.tile(128, KC * 128, BF16) for i in range(2)]
    wvbuf = [AR.tile(128, KC * 512, BF16) for i in range(2)]
    out_bf = [AR.tile(128, TT, BF16) for i in range(2)]

    r.wait("gpsimd", "st", entry_st)
    r.wait("gpsimd", "pe", r.cur("pe"))
    r.wait("gpsimd", "c", r.cur("c"))
    r.wait("gpsimd", "ev", r.cur("ev"))
    r.add("gpsimd", lambda g: g.dma_start(out=x_bf[:], in_=ap["x2"])
          .then_inc(r.h("gin"), 16))
    r.bump("gin")
    gin_done = r.cur("gin")

    ev_done, st_done, pe_after = {}, {}, {}
    gidx = 0
    for kind, nocs, wsrc, dst in (("qk", 32, "wqk1", "qk1"),
                                  ("gate", 16, "gwt", "gate")):
        for oc in range(nocs):
            wbuf = wbuf2[gidx % 2]
            wsem = "w0" if gidx % 2 == 0 else "w1"
            if gidx >= 2:
                r.wait("gpsimd", "pe", pe_after[gidx - 2])
            r.add("gpsimd", lambda g, wsrc=wsrc, oc=oc, wbuf=wbuf, wsem=wsem:
                  g.dma_start(out=wbuf[:], in_=ap[wsrc][oc])
                  .then_inc(r.h(wsem), 16))
            w_done = r.bump(wsem)
            for ti in range(NTT):
                idx = gidx * NTT + ti
                pbuf = PB[idx % 2]
                r.wait("tensor", wsem, w_done)
                if idx >= 2:
                    r.wait("tensor", "ev", ev_done[idx - 2])
                if idx == 0:
                    r.wait("tensor", "gin", gin_done)
                for kc in range(KC):
                    def mm(t, oc=oc, ti=ti, kc=kc, pbuf=pbuf, wbuf=wbuf,
                           last=kc == KC - 1):
                        i = t.matmul(pbuf[:], wbuf[:, kc * 128:(kc + 1) * 128],
                                     x_bf[:, kc * TPC + ti * TT: kc * TPC + ti * TT + TT],
                                     start=kc == 0, stop=last)
                        if last:
                            i.then_inc(r.h("pe"), 1)
                    r.add("tensor", mm)
                mm_done = r.bump("pe", 1)
                obuf = out_bf[idx % 2]
                r.wait("scalar", "pe", mm_done)
                ssem = "s0" if idx % 2 == 0 else "s1"
                if idx >= 2:
                    r.wait("scalar", ssem, st_done[idx - 2])
                fn = AF.Silu if kind == "qk" else AF.Sigmoid
                r.add("scalar", lambda s, pbuf=pbuf, obuf=obuf, fn=fn:
                      s.activation(obuf[:], pbuf[:], fn).then_inc(r.h("ev"), 1))
                ev_done[idx] = r.bump("ev", 1)
                r.wait("sync", "ev", ev_done[idx])
                r.add("sync", lambda q, dst=dst, oc=oc, ti=ti, obuf=obuf, ssem=ssem:
                      q.dma_start(out=ap[dst][128 * oc:128 * (oc + 1),
                                              ti * TT:(ti + 1) * TT],
                                  in_=obuf[:]).then_inc(r.h(ssem), 16))
                st_done[idx] = r.bump(ssem)
            pe_after[gidx] = r.cur("pe")
            gidx += 1
    vev, vst, vpe = {}, {}, {}
    for vc in range(4):
        wbuf = wvbuf[vc % 2]
        wsem = "v0" if vc % 2 == 0 else "v1s"
        if vc >= 2:
            r.wait("gpsimd", "pe", vpe[vc - 2])
        r.add("gpsimd", lambda g, vc=vc, wbuf=wbuf, wsem=wsem: g.dma_start(
            out=wbuf[:], in_=ap["wv1"][vc]).then_inc(r.h(wsem), 16))
        w_done = r.bump(wsem)
        for tt in range(TPC // 128):
            idx = vc * (TPC // 128) + tt
            pbuf = PB[2 + idx % 2]
            r.wait("tensor", wsem, w_done)
            if idx >= 2:
                r.wait("tensor", "ev", vev[idx - 2])
            for kc in range(KC):
                def mmv(t, vc=vc, tt=tt, kc=kc, pbuf=pbuf, wbuf=wbuf,
                        last=kc == KC - 1):
                    i = t.matmul(pbuf[:],
                                 x_bf[:, kc * TPC + tt * 128: kc * TPC + (tt + 1) * 128],
                                 wbuf[:, kc * 512:(kc + 1) * 512],
                                 start=kc == 0, stop=last)
                    if last:
                        i.then_inc(r.h("pe"), 1)
                r.add("tensor", mmv)
            mm_done = r.bump("pe", 1)
            obuf = out_bf[idx % 2]
            r.wait("scalar", "pe", mm_done)
            ssem = "s0" if idx % 2 == 0 else "s1"
            if idx >= 2:
                r.wait("scalar", ssem, vst[idx - 2])
            else:
                r.wait("scalar", ssem, r.cur(ssem))
            r.add("scalar", lambda s, pbuf=pbuf, obuf=obuf:
                  s.activation(obuf[:], pbuf[:], AF.Silu).then_inc(r.h("ev"), 1))
            vev[idx] = r.bump("ev", 1)
            r.wait("sync", "ev", vev[idx])
            r.add("sync", lambda q, vc=vc, tt=tt, obuf=obuf, ssem=ssem: q.dma_start(
                out=ap["v1"][tt * 128:(tt + 1) * 128, vc * 512:(vc + 1) * 512],
                in_=obuf[:]).then_inc(r.h(ssem), 16))
            vst[idx] = r.bump(ssem)
        vpe[vc] = r.cur("pe")
    proj_s0 = r.cur("s0")
    proj_s1 = r.cur("s1")
    projections_pe = r.cur("pe")

    qtile = [AR.tile(64, CHUNK, BF16) for i in range(2)]
    ktile = [AR.tile(64, CHUNK, BF16) for i in range(2)]
    vtile = [AR.tile(128, 8 * 64, BF16) for i in range(2)]
    qd_sb = [AR.tile(64, BS1, F32) for i in range(2)]
    kd_sb = [AR.tile(64, BS1, F32) for i in range(2)]
    dd_sb = [AR.tile(128, 2 * BS1, F32) for i in range(2)]
    qp_st = AR.tile(64, CHUNK, BF16)
    op_st = AR.tile(64, CHUNK, F32)
    kp_bf = AR.tile(64, BS1, BF16)
    ktr = AR.tile(128, 2 * 64, BF16)
    p1_sb = AR.tile(128, 2 * BS1, BF16)
    s_f32 = AR.tile(64, 64, F32)
    s_bf = AR.tile(64, 64, BF16)
    cch_st = AR.tile(64, 64, F32)

    def vadd(fn):
        r.add("vector", lambda v, fn=fn: fn(v).then_inc(r.h("c"), 1))
        return r.bump("c", 1)

    r.wait("sync", "s0", proj_s0)
    r.wait("sync", "s1", proj_s1)
    hdec_used = {}
    qk_used = {}
    dd_done = {}
    for h in range(H1):
        hb = h % 2
        dsem = "dd0" if hb == 0 else "dd1"
        if h >= 2:
            r.wait("sync", "pe", hdec_used[h - 2])

        def ldd(q, h=h, hb=hb, dsem=dsem):
            q.dma_start(out=qd_sb[hb][:], in_=ap["qd"][h]).then_inc(r.h(dsem), 16)
            q.dma_start(out=kd_sb[hb][:], in_=ap["kd"][h]).then_inc(r.h(dsem), 16)
            q.dma_start(out=dd_sb[hb][:], in_=ap["ddT"][h]).then_inc(r.h(dsem), 16)
        r.add("sync", ldd)
        dd_done[h] = r.bump(dsem, 48)
        for chos in range(2):
            it = h * 2 + chos
            tb = it % 2
            qsem = "q0" if tb == 0 else "q1"
            vsem = "vb0" if tb == 0 else "vb1"
            if it >= 2:
                r.wait("sync", "pe", qk_used[it - 2])

            def ldq(q, h=h, chos=chos, tb=tb, qsem=qsem, vsem=vsem):
                q.dma_start(out=qtile[tb][:],
                            in_=ap["qk1"][128 * h:128 * h + 64,
                                          chos * CHUNK:(chos + 1) * CHUNK]) \
                    .then_inc(r.h(qsem), 16)
                q.dma_start(out=ktile[tb][:],
                            in_=ap["qk1"][128 * h + 64:128 * h + 128,
                                          chos * CHUNK:(chos + 1) * CHUNK]) \
                    .then_inc(r.h(qsem), 16)
                q.dma_start(out=vtile[tb][:].rearrange("p (sq f) -> p sq f", sq=8),
                            in_=ap["v1"][chos * CHUNK:(chos + 1) * CHUNK,
                                         64 * h:64 * (h + 1)]
                            .rearrange("(sq p) f -> p sq f", p=128)) \
                    .then_inc(r.h(vsem), 16)
            r.add("sync", ldq)
            q_done = r.bump(qsem, 32)
            v_done = r.bump(vsem)
            r.wait("vector", qsem, q_done)
            r.wait("vector", vsem, v_done)
            r.wait("vector", dsem, dd_done[h])
            if it >= 1:
                r.wait("vector", "st", qp_op_st)
            vw_c = vadd(lambda v: v.memset(s_f32[:], 0.0))
            sbf_c = vadd(lambda v: v.memset(s_bf[:], 0.0))
            if it == 0:
                p_prev = {0: 0, 1: 0}
                o_ev_c = 0
                s_upd_c = 0
            for blk in range(4):
                q_sl = qtile[tb][:, blk * BS1:(blk + 1) * BS1]
                k_sl = ktile[tb][:, blk * BS1:(blk + 1) * BS1]
                if blk >= 1:
                    r.wait("vector", "pe", inter_done)
                qp_c = vadd(lambda v, q_sl=q_sl, hb=hb, blk=blk: v.tensor_mul(
                    qp_st[:, blk * BS1:(blk + 1) * BS1], q_sl, qd_sb[hb][:]))
                kp_c = vadd(lambda v, k_sl=k_sl, hb=hb: v.tensor_mul(
                    kp_bf[:], k_sl, kd_sb[hb][:]))
                r.wait("tensor", "c", kp_c)
                if blk >= 1 or it >= 1:
                    r.wait("tensor", "c", ktr_ev if it + blk else 0)
                for ci in range(2):
                    def trk(t, ci=ci):
                        i = t.transpose(
                            PB[4].ap().bitcast(BF16)[0:128, ci * 64:(ci + 1) * 64],
                            kp_bf[:, ci * 128:(ci + 1) * 128], ident_bf[0:64, 0:64])
                        if ci == 1:
                            i.then_inc(r.h("pe"), 1)
                    r.add("tensor", trk)
                ktr_pe = r.bump("pe", 1)
                r.wait("vector", "pe", ktr_pe)
                vw2 = r.cur("c")
                r.wait("vector", "c", vw2)
                ktr_ev = vadd(lambda v: v.tensor_copy(
                    ktr[:], PB[4].ap().bitcast(BF16)[0:128, 0:128]))
                r.wait("tensor", "c", ktr_ev)
                if s_upd_c:
                    r.wait("tensor", "c", s_upd_c)
                for ci in range(2):
                    def mmc(t, ci=ci, blk=blk, tb=tb):
                        i = t.matmul(PB[2][0:64, 0:64],
                                     ktr[:, ci * 64:(ci + 1) * 64],
                                     vtile[tb][:, (2 * blk + ci) * 64:(2 * blk + ci + 1) * 64],
                                     start=ci == 0, stop=ci == 1)
                        if ci == 1:
                            i.then_inc(r.h("pe"), 1)
                    r.add("tensor", mmc)
                c_mm = r.bump("pe", 1)
                p_evs = []
                for ci in range(2):
                    bank = PB[ci]
                    if p_prev[ci]:
                        r.wait("tensor", "c", p_prev[ci])
                    def mma(t, ci=ci, bank=bank, k_sl=k_sl, q_sl=q_sl):
                        t.matmul(bank[0:128, 0:BS1],
                                 k_sl[:, ci * 128:(ci + 1) * 128], q_sl,
                                 start=True, stop=True).then_inc(r.h("pe"), 1)
                    r.add("tensor", mma)
                    aw_mm = r.bump("pe", 1)
                    r.wait("vector", "pe", aw_mm)
                    r.wait("vector", "c", r.cur("c"))
                    p_evs.append(vadd(lambda v, ci=ci, bank=bank, hb=hb: v.tensor_mul(
                        p1_sb[:, ci * BS1:(ci + 1) * BS1], bank[0:128, 0:BS1],
                        dd_sb[hb][:, ci * BS1:(ci + 1) * BS1])))
                p_prev = {0: p_evs[0], 1: p_evs[1]}
                r.wait("tensor", "c", p_evs[1])
                if o_ev_c:
                    r.wait("tensor", "c", o_ev_c)
                r.wait("tensor", "c", sbf_c)

                def mmav(t, blk=blk, tb=tb):
                    t.matmul(PB[3][0:64, 0:BS1],
                             vtile[tb][:, (2 * blk) * 64:(2 * blk + 1) * 64],
                             p1_sb[:, 0:BS1], start=True, stop=False)
                    t.matmul(PB[3][0:64, 0:BS1],
                             vtile[tb][:, (2 * blk + 1) * 64:(2 * blk + 2) * 64],
                             p1_sb[:, BS1:2 * BS1], start=False, stop=False)
                    t.matmul(PB[3][0:64, 0:BS1], s_bf[:],
                             qp_st[:, blk * BS1:(blk + 1) * BS1],
                             start=False, stop=True).then_inc(r.h("pe"), 1)
                r.add("tensor", mmav)
                inter_done = r.bump("pe", 1)
                r.wait("vector", "pe", inter_done)
                r.wait("vector", "c", r.cur("c"))
                o_ev_c = vadd(lambda v, blk=blk: v.tensor_copy(
                    op_st[:, blk * BS1:(blk + 1) * BS1], PB[3][0:64, 0:BS1]))
                r.wait("vector", "pe", c_mm)
                s_upd_c = vadd(lambda v, h=h: v.scalar_tensor_tensor(
                    s_f32[:], s_f32[:], float(BD1[h]), PB[2][0:64, 0:64],
                    ALU.mult, ALU.add))
                r.wait("vector", "c", s_upd_c)
                sbf_c = vadd(lambda v: v.tensor_copy(s_bf[:], s_f32[:]))
            qk_used[it] = r.cur("pe")
            hdec_used[h] = r.cur("pe")
            r.wait("sync", "c", sbf_c)
            r.wait("sync", "c", o_ev_c)

            def stq(q, h=h, chos=chos):
                q.dma_start(out=ap["qp"][64 * h:64 * (h + 1),
                                         chos * CHUNK:(chos + 1) * CHUNK],
                            in_=qp_st[:]).then_inc(r.h("st"), 16)
                q.dma_start(out=ap["Opart"][64 * h:64 * (h + 1),
                                            chos * CHUNK:(chos + 1) * CHUNK],
                            in_=op_st[:]).then_inc(r.h("st"), 16)
                q.dma_start(out=ap["Cch"][chos, h], in_=s_f32[:]) \
                    .then_inc(r.h("st"), 16)
            r.add("sync", stq)
            qp_op_st = r.bump("st", 48)

    return {"l1a_st": r.cur("st")}


def build_launch2b():
    nc = bass.Bass("TRN2", target_bir_lowering=False)
    ap = {
        "x2": nc.dram_tensor("x2", [128, KC * TPC], F32, kind="ExternalInput").ap(),
        "wqk1": nc.dram_tensor("wqk1", [32, 128, KC * 128], F32, kind="ExternalInput").ap(),
        "wv1": nc.dram_tensor("wv1", [4, 128, KC * 512], F32, kind="ExternalInput").ap(),
        "gwt": nc.dram_tensor("gwt", [16, 128, KC * 128], F32, kind="ExternalInput").ap(),
        "qd": nc.dram_tensor("qd", [H1, 64, BS1], F32, kind="ExternalInput").ap(),
        "kd": nc.dram_tensor("kd", [H1, 64, BS1], F32, kind="ExternalInput").ap(),
        "ddT": nc.dram_tensor("ddT", [H1, 128, 2 * BS1], F32, kind="ExternalInput").ap(),
        "ident": nc.dram_tensor("ident", [128, 128], F32, kind="ExternalInput").ap(),
        "qp": nc.dram_tensor("qp", [H1 * 64, TPC], BF16, kind="ExternalOutput").ap(),
        "Opart": nc.dram_tensor("Opart", [H1 * 64, TPC], F32, kind="ExternalOutput").ap(),
        "Cch": nc.dram_tensor("Cch", [2, H1, 64, 64], F32, kind="ExternalOutput").ap(),
        "gate": nc.dram_tensor("gate", [D, TPC], BF16, kind="ExternalOutput").ap(),
        "qk1": nc.dram_tensor("qk1", [2 * H1 * 64, TPC], BF16, kind="Internal").ap(),
        "v1": nc.dram_tensor("v1", [TPC, H1 * 64], BF16, kind="Internal").ap(),
    }

    es = ExitStack()
    with es:
        sb = lambda name, shape, dt: es.enter_context(nc.sbuf_tensor(name, shape, dt))
        PB = [es.enter_context(nc.psum_tensor(f"bank{i}", [128, 512], F32))
              for i in range(8)]
        r = Rec(nc, es)
        for sname in ("in", "pe", "ev", "rp", "st", "c", "gin",
                      "w0", "w1", "v0", "v1s", "s0", "s1",
                      "q0", "q1", "k0", "k1", "dd0", "dd1", "vb0", "vb1"):
            r.sem(sname)
        block = es.enter_context(nc.Block())

        x_bf = sb("x_bf", [128, KC * TPC], BF16)
        ident_bf = sb("ident_bf", [128, 128], BF16)
        wbuf2 = [sb(f"wb{i}", [128, KC * 128], BF16) for i in range(2)]
        wvbuf = [sb(f"wvb_{i}", [128, KC * 512], BF16) for i in range(2)]
        out_bf = [sb(f"obf{i}", [128, TT], BF16) for i in range(2)]

        r.add("gpsimd", lambda g: g.dma_start(out=x_bf[:], in_=ap["x2"])
              .then_inc(r.h("gin"), 16))
        r.bump("gin")
        r.add("gpsimd", lambda g: g.dma_start(out=ident_bf[:], in_=ap["ident"])
              .then_inc(r.h("gin"), 16))
        r.bump("gin")
        gin_done = r.cur("gin")

        ev_done, st_done, pe_after = {}, {}, {}
        gidx = 0
        for kind, nocs, wsrc, dst in (("qk", 32, "wqk1", "qk1"),
                                      ("gate", 16, "gwt", "gate")):
            for oc in range(nocs):
                wbuf = wbuf2[gidx % 2]
                wsem = "w0" if gidx % 2 == 0 else "w1"
                if gidx >= 2:
                    r.wait("gpsimd", "pe", pe_after[gidx - 2])
                r.add("gpsimd", lambda g, wsrc=wsrc, oc=oc, wbuf=wbuf, wsem=wsem:
                      g.dma_start(out=wbuf[:], in_=ap[wsrc][oc])
                      .then_inc(r.h(wsem), 16))
                w_done = r.bump(wsem)
                for ti in range(NTT):
                    idx = gidx * NTT + ti
                    pbuf = PB[idx % 2]
                    r.wait("tensor", wsem, w_done)
                    if idx >= 2:
                        r.wait("tensor", "ev", ev_done[idx - 2])
                    if idx == 0:
                        r.wait("tensor", "gin", gin_done)
                    for kc in range(KC):
                        def mm(t, oc=oc, ti=ti, kc=kc, pbuf=pbuf, wbuf=wbuf,
                               last=kc == KC - 1):
                            i = t.matmul(pbuf[:], wbuf[:, kc * 128:(kc + 1) * 128],
                                         x_bf[:, kc * TPC + ti * TT: kc * TPC + ti * TT + TT],
                                         start=kc == 0, stop=last)
                            if last:
                                i.then_inc(r.h("pe"), 1)
                        r.add("tensor", mm)
                    mm_done = r.bump("pe", 1)
                    obuf = out_bf[idx % 2]
                    r.wait("scalar", "pe", mm_done)
                    ssem = "s0" if idx % 2 == 0 else "s1"
                    if idx >= 2:
                        r.wait("scalar", ssem, st_done[idx - 2])
                    fn = AF.Silu if kind == "qk" else AF.Sigmoid
                    r.add("scalar", lambda s, pbuf=pbuf, obuf=obuf, fn=fn:
                          s.activation(obuf[:], pbuf[:], fn).then_inc(r.h("ev"), 1))
                    ev_done[idx] = r.bump("ev", 1)
                    r.wait("sync", "ev", ev_done[idx])
                    r.add("sync", lambda q, dst=dst, oc=oc, ti=ti, obuf=obuf, ssem=ssem:
                          q.dma_start(out=ap[dst][128 * oc:128 * (oc + 1),
                                                  ti * TT:(ti + 1) * TT],
                                      in_=obuf[:]).then_inc(r.h(ssem), 16))
                    st_done[idx] = r.bump(ssem)
                pe_after[gidx] = r.cur("pe")
                gidx += 1
        vev, vst, vpe = {}, {}, {}
        for vc in range(4):
            wbuf = wvbuf[vc % 2]
            wsem = "v0" if vc % 2 == 0 else "v1s"
            if vc >= 2:
                r.wait("gpsimd", "pe", vpe[vc - 2])
            r.add("gpsimd", lambda g, vc=vc, wbuf=wbuf, wsem=wsem: g.dma_start(
                out=wbuf[:], in_=ap["wv1"][vc]).then_inc(r.h(wsem), 16))
            w_done = r.bump(wsem)
            for tt in range(TPC // 128):
                idx = vc * (TPC // 128) + tt
                pbuf = PB[2 + idx % 2]
                r.wait("tensor", wsem, w_done)
                if idx >= 2:
                    r.wait("tensor", "ev", vev[idx - 2])
                for kc in range(KC):
                    def mmv(t, vc=vc, tt=tt, kc=kc, pbuf=pbuf, wbuf=wbuf,
                            last=kc == KC - 1):
                        i = t.matmul(pbuf[:],
                                     x_bf[:, kc * TPC + tt * 128: kc * TPC + (tt + 1) * 128],
                                     wbuf[:, kc * 512:(kc + 1) * 512],
                                     start=kc == 0, stop=last)
                        if last:
                            i.then_inc(r.h("pe"), 1)
                    r.add("tensor", mmv)
                mm_done = r.bump("pe", 1)
                obuf = out_bf[idx % 2]
                r.wait("scalar", "pe", mm_done)
                ssem = "s0" if idx % 2 == 0 else "s1"
                if idx >= 2:
                    r.wait("scalar", ssem, vst[idx - 2])
                else:
                    r.wait("scalar", ssem, r.cur(ssem))
                r.add("scalar", lambda s, pbuf=pbuf, obuf=obuf:
                      s.activation(obuf[:], pbuf[:], AF.Silu).then_inc(r.h("ev"), 1))
                vev[idx] = r.bump("ev", 1)
                r.wait("sync", "ev", vev[idx])
                r.add("sync", lambda q, vc=vc, tt=tt, obuf=obuf, ssem=ssem: q.dma_start(
                    out=ap["v1"][tt * 128:(tt + 1) * 128, vc * 512:(vc + 1) * 512],
                    in_=obuf[:]).then_inc(r.h(ssem), 16))
                vst[idx] = r.bump(ssem)
            vpe[vc] = r.cur("pe")
        proj_s0 = r.cur("s0")
        proj_s1 = r.cur("s1")
        projections_pe = r.cur("pe")

        qtile = [sb(f"qt{i}", [64, CHUNK], BF16) for i in range(2)]
        ktile = [sb(f"kt{i}", [64, CHUNK], BF16) for i in range(2)]
        vtile = [sb(f"vt{i}", [128, 8 * 64], BF16) for i in range(2)]
        qd_sb = [sb(f"qd{i}", [64, BS1], F32) for i in range(2)]
        kd_sb = [sb(f"kd{i}", [64, BS1], F32) for i in range(2)]
        dd_sb = [sb(f"dd{i}", [128, 2 * BS1], F32) for i in range(2)]
        qp_st = sb("qp_st", [64, CHUNK], BF16)
        op_st = sb("op_st", [64, CHUNK], F32)
        kp_bf = sb("kp_bf", [64, BS1], BF16)
        ktr = sb("ktr", [128, 2 * 64], BF16)
        p1_sb = sb("p1_sb", [128, 2 * BS1], BF16)
        s_f32 = sb("s_f32", [64, 64], F32)
        s_bf = sb("s_bf", [64, 64], BF16)
        cch_st = sb("cch_st", [64, 64], F32)

        def vadd(fn):
            r.add("vector", lambda v, fn=fn: fn(v).then_inc(r.h("c"), 1))
            return r.bump("c", 1)

        r.wait("sync", "s0", proj_s0)
        r.wait("sync", "s1", proj_s1)
        hdec_used = {}
        qk_used = {}
        dd_done = {}
        for h in range(H1):
            hb = h % 2
            dsem = "dd0" if hb == 0 else "dd1"
            if h >= 2:
                r.wait("sync", "pe", hdec_used[h - 2])

            def ldd(q, h=h, hb=hb, dsem=dsem):
                q.dma_start(out=qd_sb[hb][:], in_=ap["qd"][h]).then_inc(r.h(dsem), 16)
                q.dma_start(out=kd_sb[hb][:], in_=ap["kd"][h]).then_inc(r.h(dsem), 16)
                q.dma_start(out=dd_sb[hb][:], in_=ap["ddT"][h]).then_inc(r.h(dsem), 16)
            r.add("sync", ldd)
            dd_done[h] = r.bump(dsem, 48)
            for chos in range(2):
                it = h * 2 + chos
                tb = it % 2
                qsem = "q0" if tb == 0 else "q1"
                vsem = "vb0" if tb == 0 else "vb1"
                if it >= 2:
                    r.wait("sync", "pe", qk_used[it - 2])

                def ldq(q, h=h, chos=chos, tb=tb, qsem=qsem, vsem=vsem):
                    q.dma_start(out=qtile[tb][:],
                                in_=ap["qk1"][128 * h:128 * h + 64,
                                              chos * CHUNK:(chos + 1) * CHUNK]) \
                        .then_inc(r.h(qsem), 16)
                    q.dma_start(out=ktile[tb][:],
                                in_=ap["qk1"][128 * h + 64:128 * h + 128,
                                              chos * CHUNK:(chos + 1) * CHUNK]) \
                        .then_inc(r.h(qsem), 16)
                    q.dma_start(out=vtile[tb][:].rearrange("p (sq f) -> p sq f", sq=8),
                                in_=ap["v1"][chos * CHUNK:(chos + 1) * CHUNK,
                                             64 * h:64 * (h + 1)]
                                .rearrange("(sq p) f -> p sq f", p=128)) \
                        .then_inc(r.h(vsem), 16)
                r.add("sync", ldq)
                q_done = r.bump(qsem, 32)
                v_done = r.bump(vsem)
                r.wait("vector", qsem, q_done)
                r.wait("vector", vsem, v_done)
                r.wait("vector", dsem, dd_done[h])
                if it >= 1:
                    r.wait("vector", "st", qp_op_st)
                vw_c = vadd(lambda v: v.memset(s_f32[:], 0.0))
                sbf_c = vadd(lambda v: v.memset(s_bf[:], 0.0))
                if it == 0:
                    p_prev = {0: 0, 1: 0}
                    o_ev_c = 0
                    s_upd_c = 0
                for blk in range(4):
                    q_sl = qtile[tb][:, blk * BS1:(blk + 1) * BS1]
                    k_sl = ktile[tb][:, blk * BS1:(blk + 1) * BS1]
                    if blk >= 1:
                        r.wait("vector", "pe", inter_done)
                    qp_c = vadd(lambda v, q_sl=q_sl, hb=hb, blk=blk: v.tensor_mul(
                        qp_st[:, blk * BS1:(blk + 1) * BS1], q_sl, qd_sb[hb][:]))
                    kp_c = vadd(lambda v, k_sl=k_sl, hb=hb: v.tensor_mul(
                        kp_bf[:], k_sl, kd_sb[hb][:]))
                    r.wait("tensor", "c", kp_c)
                    if blk >= 1 or it >= 1:
                        r.wait("tensor", "c", ktr_ev if it + blk else 0)
                    for ci in range(2):
                        def trk(t, ci=ci):
                            i = t.transpose(
                                PB[4].ap().bitcast(BF16)[0:128, ci * 64:(ci + 1) * 64],
                                kp_bf[:, ci * 128:(ci + 1) * 128], ident_bf[0:64, 0:64])
                            if ci == 1:
                                i.then_inc(r.h("pe"), 1)
                        r.add("tensor", trk)
                    ktr_pe = r.bump("pe", 1)
                    r.wait("vector", "pe", ktr_pe)
                    vw2 = r.cur("c")
                    r.wait("vector", "c", vw2)
                    ktr_ev = vadd(lambda v: v.tensor_copy(
                        ktr[:], PB[4].ap().bitcast(BF16)[0:128, 0:128]))
                    r.wait("tensor", "c", ktr_ev)
                    if s_upd_c:
                        r.wait("tensor", "c", s_upd_c)
                    for ci in range(2):
                        def mmc(t, ci=ci, blk=blk, tb=tb):
                            i = t.matmul(PB[2][0:64, 0:64],
                                         ktr[:, ci * 64:(ci + 1) * 64],
                                         vtile[tb][:, (2 * blk + ci) * 64:(2 * blk + ci + 1) * 64],
                                         start=ci == 0, stop=ci == 1)
                            if ci == 1:
                                i.then_inc(r.h("pe"), 1)
                        r.add("tensor", mmc)
                    c_mm = r.bump("pe", 1)
                    p_evs = []
                    for ci in range(2):
                        bank = PB[ci]
                        if p_prev[ci]:
                            r.wait("tensor", "c", p_prev[ci])
                        def mma(t, ci=ci, bank=bank, k_sl=k_sl, q_sl=q_sl):
                            t.matmul(bank[0:128, 0:BS1],
                                     k_sl[:, ci * 128:(ci + 1) * 128], q_sl,
                                     start=True, stop=True).then_inc(r.h("pe"), 1)
                        r.add("tensor", mma)
                        aw_mm = r.bump("pe", 1)
                        r.wait("vector", "pe", aw_mm)
                        r.wait("vector", "c", r.cur("c"))
                        p_evs.append(vadd(lambda v, ci=ci, bank=bank, hb=hb: v.tensor_mul(
                            p1_sb[:, ci * BS1:(ci + 1) * BS1], bank[0:128, 0:BS1],
                            dd_sb[hb][:, ci * BS1:(ci + 1) * BS1])))
                    p_prev = {0: p_evs[0], 1: p_evs[1]}
                    r.wait("tensor", "c", p_evs[1])
                    if o_ev_c:
                        r.wait("tensor", "c", o_ev_c)
                    r.wait("tensor", "c", sbf_c)

                    def mmav(t, blk=blk, tb=tb):
                        t.matmul(PB[3][0:64, 0:BS1],
                                 vtile[tb][:, (2 * blk) * 64:(2 * blk + 1) * 64],
                                 p1_sb[:, 0:BS1], start=True, stop=False)
                        t.matmul(PB[3][0:64, 0:BS1],
                                 vtile[tb][:, (2 * blk + 1) * 64:(2 * blk + 2) * 64],
                                 p1_sb[:, BS1:2 * BS1], start=False, stop=False)
                        t.matmul(PB[3][0:64, 0:BS1], s_bf[:],
                                 qp_st[:, blk * BS1:(blk + 1) * BS1],
                                 start=False, stop=True).then_inc(r.h("pe"), 1)
                    r.add("tensor", mmav)
                    inter_done = r.bump("pe", 1)
                    r.wait("vector", "pe", inter_done)
                    r.wait("vector", "c", r.cur("c"))
                    o_ev_c = vadd(lambda v, blk=blk: v.tensor_copy(
                        op_st[:, blk * BS1:(blk + 1) * BS1], PB[3][0:64, 0:BS1]))
                    r.wait("vector", "pe", c_mm)
                    s_upd_c = vadd(lambda v, h=h: v.scalar_tensor_tensor(
                        s_f32[:], s_f32[:], float(BD1[h]), PB[2][0:64, 0:64],
                        ALU.mult, ALU.add))
                    r.wait("vector", "c", s_upd_c)
                    sbf_c = vadd(lambda v: v.tensor_copy(s_bf[:], s_f32[:]))
                qk_used[it] = r.cur("pe")
                hdec_used[h] = r.cur("pe")
                r.wait("sync", "c", sbf_c)
                r.wait("sync", "c", o_ev_c)

                def stq(q, h=h, chos=chos):
                    q.dma_start(out=ap["qp"][64 * h:64 * (h + 1),
                                             chos * CHUNK:(chos + 1) * CHUNK],
                                in_=qp_st[:]).then_inc(r.h("st"), 16)
                    q.dma_start(out=ap["Opart"][64 * h:64 * (h + 1),
                                                chos * CHUNK:(chos + 1) * CHUNK],
                                in_=op_st[:]).then_inc(r.h("st"), 16)
                    q.dma_start(out=ap["Cch"][chos, h], in_=s_f32[:]) \
                        .then_inc(r.h("st"), 16)
                r.add("sync", stq)
                qp_op_st = r.bump("st", 48)
        r.wait("sync", "st", r.cur("st"))
        r.run(block)
    return nc


BD1 = _l1_decay()[3]


def prep_launch3(inp, l2out, l2bout):
    out_w = np.asarray(inp["L1_attn_out_w"])
    norm_w = np.asarray(inp["L1_attn_norm_w"])
    wq_gate = np.asarray(inp["L1_ffn_wq_gate"])
    s_gate = np.asarray(inp["L1_ffn_s_gate"])
    wq_up = np.asarray(inp["L1_ffn_wq_up"])
    s_up = np.asarray(inp["L1_ffn_s_up"])
    wq_down = np.asarray(inp["L1_ffn_wq_down"])
    s_down = np.asarray(inp["L1_ffn_s_down"])

    outw_t = _oc_tiles(np.ascontiguousarray(out_w.T))
    wgT = _oc_tiles(np.ascontiguousarray(wq_gate.T.astype(np.int32)))
    wuT = _oc_tiles(np.ascontiguousarray(wq_up.T.astype(np.int32)))
    wdT = np.ascontiguousarray(wq_down.T.astype(np.int32))
    wdt_oc = _oc_tiles(wdT)
    wdfc = np.zeros((NFC, 128, KC * 128), np.int32)
    for fc in range(NFC):
        for oc in range(KC):
            wdfc[fc][:, oc * 128:(oc + 1) * 128] = wdt_oc[oc][:, fc * 128:(fc + 1) * 128]

    _, _, _, bd = _l1_decay()
    CgT = np.zeros((2, 64, 8 * H1 * 64), np.float32)
    for cc in range(8):
        b = cc // 4
        for chos in range(2):
            absj = _chunks_of(cc)[chos]
            for h in range(H1):
                CgT[b][:, (absj * H1 + h) * 64:(absj * H1 + h + 1) * 64] = \
                    l2bout[cc]["Cch"][chos, h]
    pc = []
    for c in range(8):
        b = c // 4
        wcols = np.zeros((64, 2 * 8 * H1), np.float32)
        for oc2 in range(2):
            C = _chunks_of(c)[oc2]
            for j in range(8):
                if j < C:
                    for h in range(H1):
                        wcols[:, oc2 * 8 * H1 + j * H1 + h] = bd[h] ** (4 * (C - 1 - j))
        pc.append({
            "x2": l2out[c]["x2"],
            "Opart": l2bout[c]["Opart"],
            "qp": l2bout[c]["qp"],
            "gate": l2bout[c]["gate"],
            "Cg": CgT[b],
            "wcols": wcols,
            "nwp": _cols2d(norm_w),
            "outw": outw_t,
            "wgT": wgT, "wuT": wuT, "wdfc": wdfc,
            "sg": _cols2d(s_gate), "su": _cols2d(s_up), "sd": _cols2d(s_down),
        })
    return pc


def build_launch3():
    nc = bass.Bass("TRN2", target_bir_lowering=False)
    ap = {
        "x2": nc.dram_tensor("x2", [128, KC * TPC], F32, kind="ExternalInput").ap(),
        "Opart": nc.dram_tensor("Opart", [H1 * 64, TPC], F32, kind="ExternalInput").ap(),
        "qp": nc.dram_tensor("qp", [H1 * 64, TPC], BF16, kind="ExternalInput").ap(),
        "gate": nc.dram_tensor("gate", [D, TPC], BF16, kind="ExternalInput").ap(),
        "Cg": nc.dram_tensor("Cg", [64, 8 * H1 * 64], F32, kind="ExternalInput").ap(),
        "wcols": nc.dram_tensor("wcols", [64, 2 * 8 * H1], F32, kind="ExternalInput").ap(),
        "nwp": nc.dram_tensor("nwp", [128, KC], F32, kind="ExternalInput").ap(),
        "outw": nc.dram_tensor("outw", [KC, 128, KC * 128], F32, kind="ExternalInput").ap(),
        "wgT": nc.dram_tensor("wgT", [NFC, 128, KC * 128], I32, kind="ExternalInput").ap(),
        "wuT": nc.dram_tensor("wuT", [NFC, 128, KC * 128], I32, kind="ExternalInput").ap(),
        "wdfc": nc.dram_tensor("wdfc", [NFC, 128, KC * 128], I32, kind="ExternalInput").ap(),
        "sg": nc.dram_tensor("sg", [128, NFC], F32, kind="ExternalInput").ap(),
        "su": nc.dram_tensor("su", [128, NFC], F32, kind="ExternalInput").ap(),
        "sd": nc.dram_tensor("sd", [128, KC], F32, kind="ExternalInput").ap(),
        "out": nc.dram_tensor("out", [128, KC * TPC], F32, kind="ExternalOutput").ap(),
        "x3": nc.dram_tensor("x3", [128, KC * TPC], F32, kind="Internal").ap(),
        "gu_d": nc.dram_tensor("gu_d", [F0, TPC], BF16, kind="Internal").ap(),
    }

    es = ExitStack()
    with es:
        sb = lambda name, shape, dt: es.enter_context(nc.sbuf_tensor(name, shape, dt))
        PB = [es.enter_context(nc.psum_tensor(f"bank{i}", [128, 512], F32))
              for i in range(8)]
        AR = Arena(nc, es, "arena", 168 * 1024)
        r = Rec(nc, es)
        for sname in ("in", "pe", "ev", "rp", "st", "c", "gin",
                      "oA", "oB", "qA", "qB", "gA", "gB", "wA", "wB",
                      "xA", "xB", "fA", "fB", "uA", "uB", "dA", "dB",
                      "x3A", "x3B", "r3A", "r3B"):
            r.sem(sname)
        block = es.enter_context(nc.Block())

        ones_row = sb("ones_row", [1, 128], F32)
        ones_col = sb("ones_col", [128, 1], F32)
        ones_col_bf = sb("ones_cbf", [128, 1], BF16)
        nw_sb = sb("nw_sb", [128, KC], F32)
        sg_sb = sb("sg_sb", [128, NFC], F32)
        su_sb = sb("su_sb", [128, NFC], F32)
        sd_sb = sb("sd_sb", [128, KC], F32)
        lnt = {
            "ones_row": ones_row,
            "m_row": sb("m_row", [1, TT], F32), "m2_row": sb("m2_row", [1, TT], F32),
            "var_row": sb("var_row", [1, TT], F32), "rstd_row": sb("rstd_row", [1, TT], F32),
            "mr_row": sb("mr_row", [1, TT], F32), "tmp": sb("tmpt", [128, TT], F32),
            "eps_col": sb("eps_col", [1, 1], F32),
        }

        def vadd(fn):
            r.add("vector", lambda v, fn=fn: fn(v).then_inc(r.h("c"), 1))
            return r.bump("c", 1)

        def sadd(fn):
            r.add("scalar", lambda s, fn=fn: fn(s).then_inc(r.h("ev"), 1))
            return r.bump("ev", 1)

        r.add("vector", lambda v: v.memset(ones_row[:], 1.0))
        r.add("vector", lambda v: v.memset(ones_col[:], 1.0))
        r.add("vector", lambda v: v.memset(lnt["eps_col"][:], 1e-5))
        ones_done = vadd(lambda v: v.memset(ones_col_bf[:], 1.0))
        for t, dst in (("nwp", nw_sb), ("sg", sg_sb), ("su", su_sb), ("sd", sd_sb)):
            r.add("sync", lambda q, t=t, dst=dst: q.dma_start(
                out=dst[:], in_=ap[t]).then_inc(r.h("in"), 16))
            r.bump("in")
        const_in = r.cur("in")

        _, _, _, bd = _l1_decay()

        m0 = AR.mark()
        cg_sb = AR.tile(128, 8 * H1 * 64, F32)
        wc_sb = sb("wc_sb", [128, 2 * 8 * H1], F32)
        accd = sb("accd", [128, 64], F32)
        inc_sb = AR.tile(128, 2 * KC * 4 * 128, BF16)
        r.add("sync", lambda q: q.dma_start(out=cg_sb[0:64, :], in_=ap["Cg"])
              .then_inc(r.h("in"), 16))
        r.bump("in")
        r.add("sync", lambda q: q.dma_start(out=cg_sb[64:128, :], in_=ap["Cg"])
              .then_inc(r.h("in"), 16))
        r.bump("in")
        r.add("vector", lambda v: v.memset(inc_sb[:], 0.0).then_inc(r.h("c"), 1))
        r.bump("c", 1)
        r.add("sync", lambda q: q.dma_start(out=wc_sb[0:64, :], in_=ap["wcols"])
              .then_inc(r.h("in"), 16))
        r.bump("in")
        r.add("sync", lambda q: q.dma_start(out=wc_sb[64:128, :], in_=ap["wcols"])
              .then_inc(r.h("in"), 16))
        cg_in = r.bump("in")
        r.wait("vector", "in", cg_in)
        for oc2 in range(2):
            for h in range(H1):
                sub = h % 2
                arow = accd[64 * sub:64 * sub + 64, :]
                for j in range(8):
                    col = oc2 * 8 * H1 + j * H1 + h
                    src = cg_sb[64 * sub:64 * sub + 64,
                                (j * H1 + h) * 64:(j * H1 + h + 1) * 64]
                    scl_ap = wc_sb[64 * sub:64 * sub + 64, col:col + 1]
                    if j == 0:
                        vadd(lambda v, src=src, scl_ap=scl_ap, arow=arow:
                             v.tensor_scalar_mul(arow, src, scl_ap))
                    else:
                        vadd(lambda v, src=src, scl_ap=scl_ap, arow=arow:
                             v.scalar_tensor_tensor(
                                 arow, src, scl_ap, arow, ALU.mult, ALU.add))
                    r.wait("vector", "c", r.cur("c"))
                for il in range(4):
                    hp = h // 2
                    off = ((oc2 * KC + hp) * 4 + il) * 128 + sub * 64
                    vadd(lambda v, off=off, h=h, il=il, sub=sub, arow=arow:
                         v.tensor_scalar_mul(
                             inc_sb[64 * sub:64 * sub + 64, off:off + 64],
                             arow, float(bd[h] ** il)))
                    r.wait("vector", "c", r.cur("c"))
        inc_done = r.cur("c")

        attn_pk = AR.tile(128, KC * BS1, F32)
        attng = AR.tile(128, KC * BS1, BF16)
        sq64 = AR.tile(128, BS1, BF16)
        ybf256 = AR.tile(128, BS1, BF16)
        sqbf256 = AR.tile(128, BS1, BF16)
        y256 = AR.tile(128, KC * BS1, F32)
        op_t = [AR.tile(128, BS1, F32) for _ in range(2)]
        qp_t = [AR.tile(128, BS1, BF16) for _ in range(2)]
        gate16 = AR.tile(128, KC * BS1, BF16)
        ow_t = [AR.tile(128, KC * 128, BF16) for _ in range(2)]
        x2_t = [AR.tile(128, BS1, F32) for _ in range(2)]
        lnt256 = {
            "ones_row": ones_row,
            "m_row": sb("m_row2", [1, BS1], F32), "m2_row": sb("m2_row2", [1, BS1], F32),
            "var_row": sb("var_row2", [1, BS1], F32),
            "rstd_row": sb("rstd_row2", [1, BS1], F32),
            "mr_row": sb("mr_row2", [1, BS1], F32), "tmp": sb("tmpt2", [128, BS1], F32),
            "eps_col": lnt["eps_col"],
        }
        hp_used, x3st = {}, {}
        prev_x3_st = 0
        prev_ow = {}
        x2_used, y_prev = {}, {}
        gate_used = {}
        for chos in range(2):
            for blk in range(4):
                tix = chos * 4 + blk
                tok0 = chos * CHUNK + blk * BS1
                s2_mm = 0
                gsem = "gA" if tix % 2 == 0 else "gB"
                if tix >= 2:
                    r.wait("sync", "c", gate_used[tix - 2])
                r.add("sync", lambda q, tok0=tok0, gsem=gsem: q.dma_start(
                    out=gate16[:].rearrange("p (a t) -> p a t", a=KC),
                    in_=ap["gate"].rearrange("(a p) t -> p a t", p=128)
                    [:, :, tok0:tok0 + BS1]).then_inc(r.h(gsem), 16))
                g_ld = r.bump(gsem)
                for hp in range(KC):
                    it = tix * KC + hp
                    ob, qb = op_t[it % 2], qp_t[it % 2]
                    osem = "oA" if it % 2 == 0 else "oB"
                    qsem = "qA" if it % 2 == 0 else "qB"
                    if it >= 2:
                        r.wait("sync", "c", hp_used[it - 2])

                    def ldh(q, hp=hp, tok0=tok0, ob=ob, qb=qb,
                            osem=osem, qsem=qsem):
                        q.dma_start(out=ob[:], in_=ap["Opart"][128 * hp:128 * (hp + 1),
                                                               tok0:tok0 + BS1]) \
                            .then_inc(r.h(osem), 16)
                        q.dma_start(out=qb[:], in_=ap["qp"][128 * hp:128 * (hp + 1),
                                                            tok0:tok0 + BS1]) \
                            .then_inc(r.h(qsem), 16)
                    r.add("sync", ldh)
                    o_ld = r.bump(osem)
                    q_ld = r.bump(qsem)
                    r.wait("tensor", qsem, q_ld)
                    if it == 0:
                        r.wait("tensor", "c", inc_done)
                    if it >= 1:
                        r.wait("tensor", "c", attn_ev)
                    off = ((chos * KC + hp) * 4 + blk) * 128

                    def mmc(t, off=off, qb=qb):
                        t.matmul(PB[0][:, 0:BS1], inc_sb[:, off:off + 128],
                                 qb[:], start=True, stop=True) \
                            .then_inc(r.h("pe"), 1)
                    r.add("tensor", mmc)
                    corr_mm = r.bump("pe", 1)
                    r.wait("vector", "pe", corr_mm)
                    r.wait("vector", osem, o_ld)
                    r.wait("vector", "c", r.cur("c"))
                    attn_ev = vadd(lambda v, hp=hp, ob=ob: v.tensor_add(
                        attn_pk[:, hp * BS1:(hp + 1) * BS1], ob[:], PB[0][:, 0:BS1]))
                    r.wait("scalar", "c", attn_ev)
                    if s2_mm:
                        r.wait("scalar", "pe", s2_mm)
                    sqe = sadd(lambda s, hp=hp: s.activation(
                        sq64[:], attn_pk[:, hp * BS1:(hp + 1) * BS1], AF.Square))
                    r.wait("tensor", "ev", sqe)
                    if it == 0:
                        r.wait("tensor", "c", ones_done)

                    def mms(t, hp=hp, tix=tix):
                        t.matmul(PB[1][0:1, 0:BS1], ones_col_bf[:], sq64[:],
                                 start=hp == 0, stop=hp == KC - 1) \
                            .then_inc(r.h("pe"), 1)
                    r.add("tensor", mms)
                    s2_mm = r.bump("pe", 1)
                    hp_used[it] = attn_ev
                r.wait("scalar", "pe", s2_mm)
                rr1 = sadd(lambda s: s.activation(
                    lnt256["var_row"][:], PB[1][0:1, 0:BS1], AF.Identity,
                    scale=1.0 / D))
                r.wait("scalar", "ev", rr1)
                rr2 = sadd(lambda s: s.activation(
                    lnt256["var_row"][:], lnt256["var_row"][:], AF.Sqrt,
                    bias=lnt["eps_col"][:]))
                r.wait("vector", "ev", rr2)
                r.wait("vector", "c", r.cur("c"))
                rr3 = vadd(lambda v: v.reciprocal(
                    lnt256["rstd_row"][:], lnt256["var_row"][:]))
                r.wait("tensor", "c", rr3)
                r.add("tensor", lambda t: t.matmul(
                    PB[1][:, 0:BS1], ones_row[:], lnt256["rstd_row"][:],
                    start=True, stop=True).then_inc(r.h("pe"), 1))
                rbc = r.bump("pe", 1)
                r.wait("vector", "pe", rbc)
                ag_ev = 0
                for hp in range(KC):
                    r.wait("vector", "c", r.cur("c"))
                    t1 = vadd(lambda v, hp=hp: v.tensor_mul(
                        lnt256["tmp"][:], attn_pk[:, hp * BS1:(hp + 1) * BS1],
                        PB[1][:, 0:BS1]))
                    r.wait("vector", "c", t1)
                    if hp == 0:
                        r.wait("vector", gsem, g_ld)
                    ag_ev = vadd(lambda v, hp=hp: v.scalar_tensor_tensor(
                        attng[:, hp * BS1:(hp + 1) * BS1], lnt256["tmp"][:],
                        nw_sb[:, hp:hp + 1],
                        gate16[:, hp * BS1:(hp + 1) * BS1], ALU.mult, ALU.mult))
                gate_used[tix] = ag_ev
                y_evs = 0
                s12_mm = 0
                for oc in range(KC):
                    it2 = tix * KC + oc
                    wb = ow_t[it2 % 2]
                    wsem = "wA" if it2 % 2 == 0 else "wB"
                    if it2 >= 2:
                        r.wait("gpsimd", "pe", prev_ow[it2 - 2])
                    r.add("gpsimd", lambda g, oc=oc, wb=wb, wsem=wsem: g.dma_start(
                        out=wb[:], in_=ap["outw"][oc]).then_inc(r.h(wsem), 16))
                    owl = r.bump(wsem)
                    xb = x2_t[it2 % 2]
                    xsem = "xA" if it2 % 2 == 0 else "xB"
                    if it2 >= 2:
                        r.wait("sync", "c", x2_used[it2 - 2])
                    r.add("sync", lambda q, oc=oc, tok0=tok0, xb=xb, xsem=xsem:
                          q.dma_start(out=xb[:],
                                      in_=ap["x2"][:, oc * TPC + tok0: oc * TPC + tok0 + BS1])
                          .then_inc(r.h(xsem), 16))
                    x2l = r.bump(xsem)
                    bank = PB[2 + it2 % 2]
                    r.wait("tensor", wsem, owl)
                    r.wait("tensor", "c", ag_ev)
                    if it2 >= 2:
                        r.wait("tensor", "c", y_prev.get(it2 - 2, 0))
                    for kc in range(KC):
                        def mmo(t, oc=oc, kc=kc, wb=wb, bank=bank,
                                last=kc == KC - 1):
                            i = t.matmul(bank[:, 0:BS1],
                                         wb[:, kc * 128:(kc + 1) * 128],
                                         attng[:, kc * BS1:(kc + 1) * BS1],
                                         start=kc == 0, stop=last)
                            if last:
                                i.then_inc(r.h("pe"), 1)
                        r.add("tensor", mmo)
                    prev_ow[it2] = omm = r.bump("pe", 1)
                    r.wait("vector", "pe", omm)
                    r.wait("vector", xsem, x2l)
                    if tix >= 1 and oc == 0:
                        psem, pval = x3st[tix - 1]
                        r.wait("vector", psem, pval)
                    r.wait("vector", "c", r.cur("c"))
                    x2_used[it2] = y_prev[it2] = vadd(
                        lambda v, oc=oc, bank=bank, xb=xb: v.tensor_add(
                            y256[:, oc * BS1:(oc + 1) * BS1], bank[:, 0:BS1], xb[:]))
                    r.wait("scalar", "c", y_prev[it2])
                    if s12_mm:
                        r.wait("scalar", "pe", s12_mm)
                    sadd(lambda s, oc=oc: s.activation(
                        ybf256[:], y256[:, oc * BS1:(oc + 1) * BS1], AF.Identity))
                    sq2e = sadd(lambda s, oc=oc: s.activation(
                        sqbf256[:], y256[:, oc * BS1:(oc + 1) * BS1], AF.Square))
                    r.wait("tensor", "ev", sq2e)
                    r.add("tensor", lambda t, oc=oc: t.matmul(
                        PB[4][0:1, 0:BS1], ones_col_bf[:], ybf256[:],
                        start=oc == 0, stop=oc == KC - 1))

                    def mm2o(t, oc=oc):
                        t.matmul(PB[5][0:1, 0:BS1], ones_col_bf[:], sqbf256[:],
                                 start=oc == 0, stop=oc == KC - 1) \
                            .then_inc(r.h("pe"), 1)
                    r.add("tensor", mm2o)
                    s12_mm = r.bump("pe", 1)
                r.wait("scalar", "pe", s12_mm)
                r.wait("vector", "pe", s12_mm)
                emit_ln_epilogue(nc, r, lnt256, y256,
                                 lambda oc: y256[:, oc * BS1:(oc + 1) * BS1],
                                 PB, 4, 5, tok_w=BS1)
                x3_ln = r.cur("c")
                r.wait("sync", "c", x3_ln)
                x3sem = "x3A" if tix % 2 == 0 else "x3B"
                for oc in range(KC):
                    r.add("sync", lambda q, oc=oc, tok0=tok0, x3sem=x3sem: q.dma_start(
                        out=ap["x3"][:, oc * TPC + tok0: oc * TPC + tok0 + BS1],
                        in_=y256[:, oc * BS1:(oc + 1) * BS1]).then_inc(r.h(x3sem), 16))
                    r.bump(x3sem)
                x3st[tix] = (x3sem, r.cur(x3sem))
        x3_all_A = r.cur("x3A")
        x3_all_B = r.cur("x3B")
        AR.reset(m0)

        x3bf = AR.tile(128, KC * TT, BF16)
        gu = AR.tile(128, NFC * TT, BF16)
        yf = AR.tile(128, KC * TT, F32)
        wq_sb = [AR.tile(128, KC * 128, BF16) for _ in range(4)]
        wd_sb = [AR.tile(128, KC * 128, BF16) for _ in range(2)]
        x3_sl = [AR.tile(128, TT, F32) for _ in range(2)]
        silu1 = AR.tile(128, TT, BF16)
        ybff = AR.tile(128, TT, BF16)
        sqbff = AR.tile(128, TT, BF16)

        x3u, wqu, wdu, x3su = {}, {}, {}, {}
        guq_mm, dmm_done = {}, {}
        prev_out_st = 0
        yp = {}
        for ti in range(NTT):
            csem = "uA" if ti % 2 == 0 else "uB"
            if ti >= 2:
                r.wait("gpsimd", "pe", x3u[ti - 2])
            if ti == 0:
                r.wait("gpsimd", "x3A", x3_all_A)
                r.wait("gpsimd", "x3B", x3_all_B)
                r.wait("gpsimd", "pe", r.cur("pe"))
                r.wait("gpsimd", "c", r.cur("c"))
                r.wait("gpsimd", "ev", r.cur("ev"))
            r.add("gpsimd", lambda g, ti=ti, csem=csem: g.dma_start(
                out=x3bf[:].rearrange("p (a t) -> p a t", a=KC),
                in_=ap["x3"].rearrange("p (a t) -> p a t", a=KC)
                [:, :, ti * TT:(ti + 1) * TT]).then_inc(r.h(csem), 16))
            x3ld = r.bump(csem)
            gu_evs = {}
            sle_evs = {}
            for fc in range(NFC):
                fi = ti * NFC + fc
                wg = wq_sb[2 * (fi % 2)]
                wu = wq_sb[2 * (fi % 2) + 1]
                fsem = "fA" if fi % 2 == 0 else "fB"
                if fi >= 2:
                    r.wait("gpsimd", "pe", guq_mm[fi - 2])

                def ldw(g, fc=fc, wg=wg, wu=wu, fsem=fsem):
                    g.dma_start(out=wg[:], in_=ap["wgT"][fc]).then_inc(r.h(fsem), 16)
                    g.dma_start(out=wu[:], in_=ap["wuT"][fc]).then_inc(r.h(fsem), 16)
                r.add("gpsimd", ldw)
                wld = r.bump(fsem, 32)
                r.wait("vector", fsem, wld)
                if fi >= 2:
                    r.wait("vector", "pe", guq_mm.get(fi - 2, 0))
                r.wait("vector", "c", r.cur("c"))
                d1 = vadd(lambda v, wg=wg: v.tensor_scalar_sub(wg[:], wg[:], 128.0))
                r.wait("vector", "c", d1)
                d2 = vadd(lambda v, wu=wu: v.tensor_scalar_sub(wu[:], wu[:], 128.0))
                r.wait("tensor", "c", d2)
                r.wait("tensor", csem, x3ld)
                if fi >= 1:
                    r.wait("tensor", "ev", sle_evs.get(fi - 1, 0))
                    r.wait("tensor", "c", gu_evs.get(fi - 1, 0))
                for bi, wz in ((0, wg), (1, wu)):
                    bank = PB[bi]
                    for kc in range(KC):
                        def mmq(t, kc=kc, wz=wz, bank=bank, last=kc == KC - 1):
                            i = t.matmul(bank[:], wz[:, kc * 128:(kc + 1) * 128],
                                         x3bf[:, kc * TT:(kc + 1) * TT],
                                         start=kc == 0, stop=last)
                            if last:
                                i.then_inc(r.h("pe"), 1)
                        r.add("tensor", mmq)
                    r.bump("pe", 1)
                guq_mm[fi] = qmm = r.cur("pe")
                r.wait("scalar", "pe", qmm)
                if fi >= 1:
                    r.wait("scalar", "c", gu_v)
                sle = sadd(lambda s, fc=fc: s.activation(
                    silu1[:], PB[0][:], AF.Silu, scale=sg_sb[:, fc:fc + 1]))
                sle_evs[fi] = sle
                r.wait("vector", "ev", sle)
                r.wait("vector", "c", r.cur("c"))
                gu_v = vadd(lambda v, fc=fc: v.scalar_tensor_tensor(
                    gu[:, fc * TT:(fc + 1) * TT], PB[1][:], su_sb[:, fc:fc + 1],
                    silu1[:], ALU.mult, ALU.mult))
                gu_evs[fi] = gu_v
                wqu[fi] = gu_v
            down_mm = {}
            ybank_free2 = 0
            for ocg in range(2):
                for fc in range(NFC):
                    fi2 = (ti * 2 + ocg) * NFC + fc
                    wd = wd_sb[fi2 % 2]
                    dsem = "dA" if fi2 % 2 == 0 else "dB"
                    if fi2 >= 2:
                        r.wait("gpsimd", "pe", dmm_done[fi2 - 2])
                    r.add("gpsimd", lambda g, fc=fc, wd=wd, dsem=dsem: g.dma_start(
                        out=wd[:], in_=ap["wdfc"][fc]).then_inc(r.h(dsem), 16))
                    dld = r.bump(dsem)
                    r.wait("vector", dsem, dld)
                    if fi2 >= 2:
                        r.wait("vector", "pe", dmm_done.get(fi2 - 2, 0))
                    r.wait("vector", "c", r.cur("c"))
                    d3 = vadd(lambda v, wd=wd: v.tensor_scalar_sub(wd[:], wd[:], 128.0))
                    wdu[fi2] = d3
                    r.wait("tensor", "c", d3)
                    if fc == 0:
                        r.wait("tensor", "c", gu_v)
                        if ocg == 1:
                            r.wait("tensor", "c", ybank_free2)
                        elif ti >= 1:
                            r.wait("tensor", "c", ln_done_prev)
                    for oc8 in range(8):
                        oc = ocg * 8 + oc8

                        def mmd2(t, fc=fc, oc=oc, oc8=oc8, wd=wd,
                                 inc=oc8 == 7, last=fc == NFC - 1):
                            i = t.matmul(PB[oc8][:],
                                         wd[:, oc * 128:(oc + 1) * 128],
                                         gu[:, fc * TT:(fc + 1) * TT],
                                         start=fc == 0, stop=last)
                            if inc:
                                i.then_inc(r.h("pe"), 1)
                        r.add("tensor", mmd2)
                    dmm_done[fi2] = r.bump("pe", 1)
                dend = r.cur("pe")
                for oc8 in range(8):
                    oc = ocg * 8 + oc8
                    gx = ti * KC + oc
                    xb = x3_sl[gx % 2]
                    xsem2 = "r3A" if gx % 2 == 0 else "r3B"
                    if gx >= 2:
                        r.wait("sync", "c", x3su[gx - 2])
                    r.add("sync", lambda q, oc=oc, ti=ti, xb=xb, xsem2=xsem2:
                          q.dma_start(out=xb[:],
                                      in_=ap["x3"][:, oc * TPC + ti * TT: oc * TPC + ti * TT + TT])
                          .then_inc(r.h(xsem2), 16))
                    x3l = r.bump(xsem2)
                    r.wait("vector", "pe", dend)
                    r.wait("vector", xsem2, x3l)
                    if prev_out_st:
                        r.wait("vector", "st", prev_out_st)
                    r.wait("vector", "c", r.cur("c"))
                    x3su[gx] = yp[gx] = vadd(
                        lambda v, oc=oc, oc8=oc8, xb=xb: v.scalar_tensor_tensor(
                            yf[:, oc * TT:(oc + 1) * TT], PB[oc8][:],
                            sd_sb[:, oc:oc + 1], xb[:], ALU.mult, ALU.add))
                ybank_free2 = r.cur("c")
            y_done2 = r.cur("c")
            s12f = 0
            for oc in range(KC):
                r.wait("scalar", "c", y_done2)
                if s12f:
                    r.wait("scalar", "pe", s12f)
                sadd(lambda s, oc=oc: s.activation(
                    ybff[:], yf[:, oc * TT:(oc + 1) * TT], AF.Identity))
                sqf = sadd(lambda s, oc=oc: s.activation(
                    sqbff[:], yf[:, oc * TT:(oc + 1) * TT], AF.Square))
                r.wait("tensor", "ev", sqf)
                if oc == 0 and ti == 0:
                    r.wait("tensor", "c", ones_done)
                r.add("tensor", lambda t, oc=oc: t.matmul(
                    PB[4][0:1, 0:TT], ones_col_bf[:], ybff[:],
                    start=oc == 0, stop=oc == KC - 1))

                def mm2f(t, oc=oc):
                    t.matmul(PB[5][0:1, 0:TT], ones_col_bf[:], sqbff[:],
                             start=oc == 0, stop=oc == KC - 1) \
                        .then_inc(r.h("pe"), 1)
                r.add("tensor", mm2f)
                s12f = r.bump("pe", 1)
            r.wait("scalar", "pe", s12f)
            r.wait("vector", "pe", s12f)
            emit_ln_epilogue(nc, r, lnt, yf,
                             lambda oc: yf[:, oc * TT:(oc + 1) * TT], PB, 4, 5)
            ln_done_prev = r.cur("c")
            r.wait("sync", "c", ln_done_prev)
            for oc in range(KC):
                r.add("sync", lambda q, oc=oc, ti=ti: q.dma_start(
                    out=ap["out"][:, oc * TPC + ti * TT: oc * TPC + ti * TT + TT],
                    in_=yf[:, oc * TT:(oc + 1) * TT]).then_inc(r.h("st"), 16))
                r.bump("st")
            prev_out_st = r.cur("st")
            x3u[ti] = r.cur("pe")
        r.wait("sync", "st", r.cur("st"))
        r.run(block)
    return nc


def kernel(**inputs):
    pc1 = prep_launch1(inputs)
    res1 = run_bass_kernel_spmd(build_launch1(), pc1, core_ids=list(range(8)))
    l1out = [res1.results[c] for c in range(8)]

    pc2 = prep_launch2(inputs, l1out)
    res2 = run_bass_kernel_spmd(build_launch2(debug=False, stages=4), pc2,
                                core_ids=list(range(8)))
    l2out = [res2.results[c] for c in range(8)]
    l2bout = l2out

    pc3 = prep_launch3(inputs, l2out, l2bout)
    res3 = run_bass_kernel_spmd(build_launch3(), pc3, core_ids=list(range(8)))

    out_fm = np.zeros((B, D, S), np.float32)
    for c in range(8):
        b = c // 4
        ti = _own_token_index(c)
        o = res3.results[c]["out"]
        out_fm[b][:, ti] = o.reshape(128, KC, TPC).transpose(1, 0, 2).reshape(D, TPC)
    return np.ascontiguousarray(out_fm.transpose(0, 2, 1))


# version 03
# speedup vs baseline: 1.1702x; 1.1664x over previous
import math
from contextlib import ExitStack

import numpy as np
import ml_dtypes

import concourse.bass as bass
import concourse.mybir as mybir
from concourse.bass_utils import run_bass_kernel_spmd

F32 = mybir.dt.float32
BF16 = mybir.dt.bfloat16
I32 = mybir.dt.int32
AF = mybir.ActivationFunctionType
ALU = mybir.AluOpType
AX = mybir.AxisListType

B, S, D = 2, 8192, 2048
NH, HD = 16, 128
TPC = 2048
CHUNK = 1024
NTT = 4
TT = 512
KC = D // 128

H1, D1, BS1 = 32, 64, 256
NL = 32
F0 = 5632
NE, R0 = 8, 16
LORA_SCALE = 2.0
NFC = F0 // 128

NEG_BIG = -1.0e30
ENGINES = ("sync", "tensor", "vector", "scalar", "gpsimd")


def _chunks_of(core):
    j = core % 4
    return (j, 7 - j)


def _own_token_index(core):
    ja, jb = _chunks_of(core)
    return np.r_[ja * CHUNK:(ja + 1) * CHUNK, jb * CHUNK:(jb + 1) * CHUNK]


def _rope_tables():
    inv_freq = 1.0 / 10000.0 ** (np.arange(0, HD, 2, dtype=np.float64) / HD)
    t = np.arange(S, dtype=np.float64)
    freq = np.einsum("s,f->sf", t, inv_freq)
    emb = np.concatenate([freq, freq], axis=-1)
    A = np.sin(emb).T.astype(np.float32)
    Bt = np.cos(emb).T.astype(np.float32)
    return A, Bt


def _l1_decay():
    base = 1.0 / 2.0 ** (8.0 / H1)
    rate = base ** (np.arange(H1, dtype=np.float64) + 1.0)
    factor = 1.0 + 1e-5
    slope = rate * factor
    r = np.arange(BS1, dtype=np.float64) + 1.0
    qd = np.exp(-slope[:, None] * r[None, :])
    kd = np.exp(-slope[:, None] * (BS1 - r[None, :]))
    dd = r[:, None] - r[None, :]
    ddm = np.where(dd >= 0, np.exp(-slope[:, None, None] * dd[None]), 0.0)
    bd = np.exp(-slope * BS1)
    return (qd.astype(np.float32), kd.astype(np.float32),
            ddm.astype(np.float32), bd)


def _oc_tiles(w):
    din, n = w.shape
    kcn = din // 128
    t = w.reshape(kcn, 128, n // 128, 128)
    return np.ascontiguousarray(t.transpose(2, 1, 0, 3).reshape(n // 128, 128, kcn * 128))


def _pack_kc_free(m):
    din, t = m.shape
    kcn = din // 128
    return np.ascontiguousarray(
        m.reshape(kcn, 128, t).transpose(1, 0, 2).reshape(128, kcn * t))


def _cols2d(v):
    return np.ascontiguousarray(v.reshape(-1, 128).T)


class Rec:

    def __init__(self, nc, es):
        self.nc = nc
        self.es = es
        self.ops = {k: [] for k in ENGINES}
        self.sems = {}

    def sem(self, name):
        h = self.es.enter_context(self.nc.semaphore(name))
        s = [h, 0]
        self.sems[name] = s
        return name

    def bump(self, name, k=16):
        self.sems[name][1] += k
        return self.sems[name][1]

    def cur(self, name):
        return self.sems[name][1]

    def h(self, name):
        return self.sems[name][0]

    def add(self, eng, fn):
        self.ops[eng].append(fn)

    def wait(self, eng, name, val):
        h = self.sems[name][0]
        self.ops[eng].append(lambda e, h=h, v=val: e.wait_ge(h, v))

    def run(self, block):
        o = self.ops

        @block.sync
        def _(q):
            for f in o["sync"]:
                f(q)

        @block.tensor
        def _(t):
            for f in o["tensor"]:
                f(t)

        @block.vector
        def _(v):
            for f in o["vector"]:
                f(v)

        @block.scalar
        def _(s):
            for f in o["scalar"]:
                f(s)

        @block.gpsimd
        def _(g):
            for f in o["gpsimd"]:
                f(g)


class Arena:

    DTS = {F32: 4, BF16: 2, I32: 4}

    def __init__(self, nc, es, name, nbytes):
        self.hdl = es.enter_context(nc.sbuf_tensor(name, [128, nbytes], mybir.dt.uint8))
        self.base = self.hdl.ap()
        self.off = 0
        self.nbytes = nbytes

    def tile(self, p, w, dt):
        sz = w * self.DTS[dt]
        off = (self.off + 31) & ~31
        assert off + sz <= self.nbytes, f"arena overflow {off + sz} > {self.nbytes}"
        self.off = off + sz
        return self.base[0:p, off:off + sz].bitcast(dt)

    def mark(self):
        return self.off

    def reset(self, mark):
        self.off = mark


def prep_launch1(inp):
    A, Bt = _rope_tables()
    x = np.asarray(inp["L0_attn_x"])
    x_fm = np.ascontiguousarray(x.transpose(0, 2, 1))

    w_qkv = np.asarray(inp["L0_attn_w_qkv"])
    b_qkv = np.asarray(inp["L0_attn_b_qkv"])
    wqk = np.empty((D, 2 * D), np.float32)
    bqk = np.empty(2 * D, np.float32)
    for h in range(NH):
        wqk[:, 256 * h:256 * h + 128] = w_qkv[:, 384 * h:384 * h + 128]
        wqk[:, 256 * h + 128:256 * h + 256] = w_qkv[:, 384 * h + 128:384 * h + 256]
        bqk[256 * h:256 * h + 128] = b_qkv[384 * h:384 * h + 128]
        bqk[256 * h + 128:256 * h + 256] = b_qkv[384 * h + 128:384 * h + 256]
    wv = np.concatenate([w_qkv[:, 384 * h + 256:384 * h + 384] for h in range(NH)], 1)
    bv = np.concatenate([b_qkv[384 * h + 256:384 * h + 384] for h in range(NH)])

    prot = np.zeros((128, 128), np.float32)
    prot[np.arange(64), np.arange(64) + 64] = 1.0
    prot[np.arange(64) + 64, np.arange(64)] = -1.0
    wqk_t = _oc_tiles(wqk)
    wv_t = _oc_tiles(wv).reshape(4, 4, 128, KC * 128)
    wv_t = _oc_tiles(wv).reshape(16, 128, KC * 128)
    wv_g = np.empty((4, 128, KC * 512), np.float32)
    for vc in range(4):
        for sub in range(4):
            for kc in range(KC):
                wv_g[vc][:, kc * 512 + sub * 128:kc * 512 + (sub + 1) * 128] = \
                    wv_t[vc * 4 + sub][:, kc * 128:(kc + 1) * 128]

    pc = []
    for c in range(8):
        b = c // 4
        ti = _own_token_index(c)
        pc.append({
            "x_p": _pack_kc_free(np.ascontiguousarray(x_fm[b][:, ti])),
            "wqk": wqk_t,
            "wv": wv_g,
            "bqk": _cols2d(bqk),
            "bv": np.ascontiguousarray(bv[None, :]),
            "ropeA": np.ascontiguousarray(A[:, ti]),
            "ropeB": np.ascontiguousarray(Bt[:, ti]),
            "Prot": prot,
        })
    return pc


def build_launch1():
    nc = bass.Bass("TRN2", target_bir_lowering=False)
    ap = {
        "x_p": nc.dram_tensor("x_p", [128, KC * TPC], F32, kind="ExternalInput").ap(),
        "wqk": nc.dram_tensor("wqk", [32, 128, KC * 128], F32, kind="ExternalInput").ap(),
        "wv": nc.dram_tensor("wv", [4, 128, KC * 512], F32, kind="ExternalInput").ap(),
        "bqk": nc.dram_tensor("bqk", [128, 32], F32, kind="ExternalInput").ap(),
        "bv": nc.dram_tensor("bv", [1, D], F32, kind="ExternalInput").ap(),
        "ropeA": nc.dram_tensor("ropeA", [128, TPC], F32, kind="ExternalInput").ap(),
        "ropeB": nc.dram_tensor("ropeB", [128, TPC], F32, kind="ExternalInput").ap(),
        "Prot": nc.dram_tensor("Prot", [128, 128], F32, kind="ExternalInput").ap(),
        "Qr": nc.dram_tensor("Qr", [D, TPC], BF16, kind="ExternalOutput").ap(),
        "Kr": nc.dram_tensor("Kr", [D, TPC], BF16, kind="ExternalOutput").ap(),
        "Vt": nc.dram_tensor("Vt", [TPC, D], BF16, kind="ExternalOutput").ap(),
    }

    es = ExitStack()
    with es:
        sb = lambda name, shape, dt: es.enter_context(nc.sbuf_tensor(name, shape, dt))
        psum = lambda name, shape: es.enter_context(nc.psum_tensor(name, shape, F32))
        x_bf = sb("x_bf", [128, KC * TPC], BF16)
        ropeA = sb("ropeA_sb", [128, TPC], F32)
        ropeB = sb("ropeB_sb", [128, TPC], F32)
        bqk_sb = sb("bqk_sb", [128, 32], F32)
        bv_sb = sb("bv_sb", [1, D], F32)
        bvb = sb("bvb_sb", [128, D], F32)
        ones_row = sb("ones_row", [1, 128], F32)
        wqk_sb = [sb(f"wqk{i}", [128, KC * 128], BF16) for i in range(2)]
        wv_sb = [sb(f"wv{i}", [128, KC * 512], BF16) for i in range(2)]
        qkraw = [sb(f"qkraw{i}", [128, TT], BF16) for i in range(2)]
        prot_sb = sb("prot_sb", [128, 128], BF16)
        ma = [sb(f"ma{i}", [128, TT], F32) for i in range(2)]
        qk_bf = [sb(f"qkbf{i}", [128, TT], BF16) for i in range(2)]
        v_bf = [sb(f"vbf{i}", [128, TT], BF16) for i in range(2)]
        qk_ps = [psum(f"qkps{i}", [128, TT]) for i in range(2)]
        rot_ps = [psum(f"rotps{i}", [128, TT]) for i in range(2)]
        v_ps = [psum(f"vps{i}", [128, TT]) for i in range(2)]
        bvb_ps = psum("bvbps", [128, TT])

        r = Rec(nc, es)
        for s in ("in", "w0", "w1", "v0", "v1", "pe", "ev", "rp", "st", "c", "gin"):
            r.sem(s)
        block = es.enter_context(nc.Block())

        r.add("gpsimd", lambda g: g.dma_start(out=x_bf[:], in_=ap["x_p"])
              .then_inc(r.h("gin"), 16))
        r.bump("gin")
        r.add("gpsimd", lambda g: g.dma_start(out=prot_sb[:], in_=ap["Prot"])
              .then_inc(r.h("gin"), 16))
        r.bump("gin")
        for t, dst in (("ropeA", ropeA), ("ropeB", ropeB), ("bqk", bqk_sb),
                       ("bv", bv_sb)):
            r.add("sync", lambda q, t=t, dst=dst: q.dma_start(
                out=dst[:], in_=ap[t]).then_inc(r.h("in"), 16))
            r.bump("in")
        in_done = r.cur("in")
        gin_done = r.cur("gin")

        r.wait("vector", "in", in_done)
        r.wait("vector", "gin", gin_done)
        r.add("vector", lambda v: v.memset(ones_row[:], 1.0).then_inc(r.h("c"), 1))
        ones_done = r.bump("c", 1)
        r.wait("tensor", "c", ones_done)
        bvb_ev = None
        for i in range(4):
            if i >= 1:
                r.wait("tensor", "c", bvb_ev)
            r.add("tensor", lambda t, i=i: t.matmul(
                bvb_ps[:], ones_row[:],
                bv_sb[:, i * 512:(i + 1) * 512],
                start=True, stop=True).then_inc(r.h("pe"), 1))
            bvb_mm = r.bump("pe", 1)
            r.wait("vector", "pe", bvb_mm)
            r.add("vector", lambda v, i=i: v.tensor_copy(
                bvb[:, i * 512:(i + 1) * 512], bvb_ps[:]).then_inc(r.h("c"), 1))
            bvb_ev = r.bump("c", 1)

        ev_done, rp_done, st_done, pe_after_oc = {}, {}, {}, {}
        for oc in range(32):
            wbuf = wqk_sb[oc % 2]
            if oc >= 2:
                r.wait("gpsimd", "pe", pe_after_oc[oc - 2])
            wsem = "w0" if oc % 2 == 0 else "w1"
            r.add("gpsimd", lambda g, oc=oc, wbuf=wbuf, wsem=wsem: g.dma_start(
                out=wbuf[:], in_=ap["wqk"][oc]).then_inc(r.h(wsem), 16))
            w_done = r.bump(wsem)
            for ti in range(NTT):
                idx = oc * NTT + ti
                pbuf = qk_ps[idx % 2]
                r.wait("tensor", wsem, w_done)
                if idx >= 2:
                    r.wait("tensor", "ev", ev_done[idx - 2])
                if idx == 0:
                    r.wait("tensor", "in", in_done)
                    r.wait("tensor", "gin", gin_done)
                for kc in range(KC):
                    last = kc == KC - 1
                    def mm(t, oc=oc, ti=ti, kc=kc, pbuf=pbuf, wbuf=wbuf, last=last):
                        i = t.matmul(pbuf[:], wbuf[:, kc * 128:(kc + 1) * 128],
                                     x_bf[:, kc * TPC + ti * TT: kc * TPC + ti * TT + TT],
                                     start=(kc == 0), stop=last)
                        if last:
                            i.then_inc(r.h("pe"), 1)
                    r.add("tensor", mm)
                mm_done = r.bump("pe", 1)
                rbuf = qkraw[idx % 2]
                r.wait("scalar", "pe", mm_done)
                if idx >= 2:
                    r.wait("scalar", "rp", rp_done[idx - 2])
                r.add("scalar", lambda s, oc=oc, pbuf=pbuf, rbuf=rbuf: s.activation(
                    rbuf[:], pbuf[:], AF.Identity,
                    bias=bqk_sb[:, oc:oc + 1]).then_inc(r.h("ev"), 1))
                ev_done[idx] = r.bump("ev", 1)
                rotp = rot_ps[idx % 2]
                r.wait("tensor", "ev", ev_done[idx])
                if idx >= 2:
                    r.wait("tensor", "rp", rp_done[idx - 2])
                r.add("tensor", lambda t, rbuf=rbuf, rotp=rotp: t.matmul(
                    rotp[:], prot_sb[:], rbuf[:], start=True, stop=True)
                    .then_inc(r.h("pe"), 1))
                rot_done = r.bump("pe", 1)
                tok0 = ti * TT
                mabuf, obuf = ma[idx % 2], qk_bf[idx % 2]
                r.wait("vector", "ev", ev_done[idx])
                if idx == 0:
                    r.wait("vector", "in", in_done)
                if idx >= 2:
                    r.wait("vector", "st", st_done[idx - 2])
                r.add("vector", lambda v, rbuf=rbuf, mabuf=mabuf, tok0=tok0:
                      v.tensor_mul(mabuf[:], rbuf[:], ropeA[:, tok0:tok0 + TT]))
                r.wait("vector", "pe", rot_done)
                r.add("vector", lambda v, rotp=rotp, mabuf=mabuf, tok0=tok0:
                      v.tensor_mul(rotp[:], rotp[:], ropeB[:, tok0:tok0 + TT]))
                r.add("vector", lambda v, mabuf=mabuf, rotp=rotp, obuf=obuf:
                      v.tensor_add(obuf[:], mabuf[:], rotp[:])
                      .then_inc(r.h("rp"), 1))
                rp_done[idx] = r.bump("rp", 1)
                h = oc // 2
                dst = ap["Qr"] if oc % 2 == 0 else ap["Kr"]
                r.wait("sync", "rp", rp_done[idx])
                r.add("sync", lambda q, dst=dst, h=h, tok0=tok0, obuf=obuf:
                      q.dma_start(out=dst[128 * h:128 * h + 128, tok0:tok0 + TT],
                                  in_=obuf[:]).then_inc(r.h("st"), 16))
                st_done[idx] = r.bump("st")
            pe_after_oc[oc] = r.cur("pe")

        vev_done, vst_done, pe_after_vc = {}, {}, {}
        for vc in range(4):
            wbuf = wv_sb[vc % 2]
            if vc >= 2:
                r.wait("gpsimd", "pe", pe_after_vc[vc - 2])
            wsem = "v0" if vc % 2 == 0 else "v1"
            r.add("gpsimd", lambda g, vc=vc, wbuf=wbuf, wsem=wsem: g.dma_start(
                out=wbuf[:], in_=ap["wv"][vc]).then_inc(r.h(wsem), 16))
            w_done = r.bump(wsem)
            for tt in range(TPC // 128):
                idx = vc * (TPC // 128) + tt
                pbuf = v_ps[idx % 2]
                r.wait("tensor", wsem, w_done)
                if idx >= 2:
                    r.wait("tensor", "c", vev_done[idx - 2])
                for kc in range(KC):
                    last = kc == KC - 1
                    def mm(t, vc=vc, tt=tt, kc=kc, pbuf=pbuf, wbuf=wbuf, last=last):
                        i = t.matmul(pbuf[:],
                                     x_bf[:, kc * TPC + tt * 128: kc * TPC + (tt + 1) * 128],
                                     wbuf[:, kc * 512:(kc + 1) * 512],
                                     start=(kc == 0), stop=last)
                        if last:
                            i.then_inc(r.h("pe"), 1)
                    r.add("tensor", mm)
                mm_done = r.bump("pe", 1)
                obuf = v_bf[idx % 2]
                r.wait("vector", "pe", mm_done)
                if idx == 0:
                    r.wait("vector", "c", bvb_ev)
                if idx >= 2:
                    r.wait("vector", "st", vst_done[idx - 2])
                r.add("vector", lambda v, vc=vc, pbuf=pbuf, obuf=obuf:
                      v.tensor_add(obuf[:], pbuf[:], bvb[:, vc * 512:(vc + 1) * 512])
                      .then_inc(r.h("c"), 1))
                vev_done[idx] = r.bump("c", 1)
                r.wait("sync", "c", vev_done[idx])
                r.add("sync", lambda q, vc=vc, tt=tt, obuf=obuf: q.dma_start(
                    out=ap["Vt"][tt * 128:(tt + 1) * 128, vc * 512:(vc + 1) * 512],
                    in_=obuf[:]).then_inc(r.h("st"), 16))
                vst_done[idx] = r.bump("st")
            pe_after_vc[vc] = r.cur("pe")

        r.wait("sync", "st", r.cur("st"))
        r.run(block)

    return nc


def kernel(**inputs):
    raise NotImplementedError("launches 2 and 3 under construction")


def prep_launch2(inp, l1out):
    x = np.asarray(inp["L0_attn_x"])
    x_fm = np.ascontiguousarray(x.transpose(0, 2, 1))

    w_o = np.asarray(inp["L0_attn_w_o"])
    b_o = np.asarray(inp["L0_attn_b_o"])
    wo_t = _oc_tiles(w_o)

    KhA = np.zeros((2, 8 * 128, NH * CHUNK), ml_dtypes.bfloat16)
    VhA = np.zeros((2, 8 * 128, NH * CHUNK), ml_dtypes.bfloat16)
    QhA = np.zeros((8, 2 * 128, NH * CHUNK), ml_dtypes.bfloat16)
    for c in range(8):
        b = c // 4
        ja, jb = _chunks_of(c)
        Kr, Vt, Qr = l1out[c]["Kr"], l1out[c]["Vt"], l1out[c]["Qr"]
        for qc, ch in ((0, ja), (1, jb)):
            for h in range(NH):
                kt = Kr[128 * h:128 * h + 128, qc * CHUNK:(qc + 1) * CHUNK]
                KhA[b, ch * 128:(ch + 1) * 128, h * CHUNK:(h + 1) * CHUNK] = kt
                QhA[c, qc * 128:(qc + 1) * 128, h * CHUNK:(h + 1) * CHUNK] = \
                    Qr[128 * h:128 * h + 128, qc * CHUNK:(qc + 1) * CHUNK]
                vblk = Vt[qc * CHUNK:(qc + 1) * CHUNK, 128 * h:128 * h + 128]
                VhA[b, ch * 128:(ch + 1) * 128, h * CHUNK:(h + 1) * CHUNK] = \
                    vblk.reshape(8, 128, 128).transpose(1, 0, 2).reshape(128, CHUNK)

    masks = np.zeros((4, 128, 512), np.float32)
    for m in range(4):
        for p in range(128):
            masks[m, p, 128 * m + p + 1:] = NEG_BIG

    ident = np.eye(128, dtype=np.float32)
    pp = np.arange(128, dtype=np.int32)
    ffn = prep_ffn0(inp)
    l1a = prep_launch2b(inp, [None] * 8)
    for d in l1a:
        d.pop("x2", None)
        d.pop("ident", None)
    pc = []
    for c in range(8):
        b = c // 4
        ja, jb = _chunks_of(c)
        plan = [(ja, 0), (jb, 1)]
        plan += [(kv, 0) for kv in range(ja)]
        plan += [(kv, 1) for kv in range(jb)]
        assert len(plan) == 9
        kvidx = np.stack(
            [(kv * 128 + pp) * 2 + hf for kv, _ in plan for hf in range(2)], 1)
        qhidx = np.stack(
            [(qc * 128 + pp) * 2 + hf for _, qc in plan for hf in range(2)], 1)
        nidx = np.stack([qc * 128 + pp for _, qc in plan], 1)
        oidx = np.zeros((128, 9 * NH), np.int32)
        for u, (_, qc) in enumerate(plan):
            for h in range(NH):
                oidx[:, u * NH + h] = qc * NH * 128 + h * 128 + pp
        pc.append({
            "x_p": _pack_kc_free(np.ascontiguousarray(x_fm[b][:, _own_token_index(c)])),
            "QhA": np.ascontiguousarray(
                QhA[c].reshape(2 * 128, 2, 8 * CHUNK).reshape(2 * 128 * 2, 8 * CHUNK)),
            "KhA": np.ascontiguousarray(
                KhA[b].reshape(8 * 128, 2, 8 * CHUNK).reshape(8 * 128 * 2, 8 * CHUNK)),
            "VhA": np.ascontiguousarray(
                VhA[b].reshape(8 * 128, 2, 8 * CHUNK).reshape(8 * 128 * 2, 8 * CHUNK)),
            "kvidx": kvidx, "qhidx": qhidx, "nidx": nidx, "oidx": oidx,
            "masks": masks, "ident": ident,
            "wo": wo_t, "bo": _cols2d(b_o),
            **ffn, **l1a[c],
        })
    return pc


def build_launch2(debug=False, stages=4):
    nc = bass.Bass("TRN2", target_bir_lowering=False)
    ap = {
        "x_p": nc.dram_tensor("x_p", [128, KC * TPC], F32, kind="ExternalInput").ap(),
        "QhA": nc.dram_tensor("QhA", [2 * 256, NH * CHUNK // 2], BF16, kind="ExternalInput").ap(),
        "KhA": nc.dram_tensor("KhA", [8 * 256, NH * CHUNK // 2], BF16, kind="ExternalInput").ap(),
        "VhA": nc.dram_tensor("VhA", [8 * 256, NH * CHUNK // 2], BF16, kind="ExternalInput").ap(),
        "kvidx": nc.dram_tensor("kvidx", [128, 18], I32, kind="ExternalInput").ap(),
        "qhidx": nc.dram_tensor("qhidx", [128, 18], I32, kind="ExternalInput").ap(),
        "nidx": nc.dram_tensor("nidx", [128, 9], I32, kind="ExternalInput").ap(),
        "oidx": nc.dram_tensor("oidx", [128, 9 * NH], I32, kind="ExternalInput").ap(),
        "masks": nc.dram_tensor("masks", [4, 128, 512], F32, kind="ExternalInput").ap(),
        "ident": nc.dram_tensor("ident", [128, 128], F32, kind="ExternalInput").ap(),
        "wo": nc.dram_tensor("wo", [KC, 128, KC * 128], F32, kind="ExternalInput").ap(),
        "bo": nc.dram_tensor("bo", [128, KC], F32, kind="ExternalInput").ap(),
        "x1": nc.dram_tensor("x1", [128, KC * TPC], F32,
                             kind="ExternalOutput" if debug else "Internal").ap(),
        "x2": nc.dram_tensor("x2", [128, KC * TPC], F32, kind="ExternalOutput").ap(),
        "Odram": nc.dram_tensor("Odram", [2, NH, 128, 1032], F32,
                                kind="ExternalOutput" if debug else "Internal").ap(),
        "gw": nc.dram_tensor("gw", [128, KC * 8], F32, kind="ExternalInput").ap(),
        "w13": nc.dram_tensor("w13", [NFC, 128, 2 * KC * 128], F32, kind="ExternalInput").ap(),
        "b13": nc.dram_tensor("b13", [NFC, 48, NE * 128], F32, kind="ExternalInput").ap(),
        "a13": nc.dram_tensor("a13", [NE, 128, KC * 64], F32, kind="ExternalInput").ap(),
        "a2t": nc.dram_tensor("a2t", [128, NE * NFC * 16], F32, kind="ExternalInput").ap(),
        "b2so": nc.dram_tensor("b2so", [128, 2 * KC * 128], F32, kind="ExternalInput").ap(),
        "w2fc": nc.dram_tensor("w2fc", [NFC, 128, KC * 128], F32, kind="ExternalInput").ap(),
        "wqk1": nc.dram_tensor("wqk1", [32, 128, KC * 128], F32, kind="ExternalInput").ap(),
        "wv1": nc.dram_tensor("wv1", [4, 128, KC * 512], F32, kind="ExternalInput").ap(),
        "gwt": nc.dram_tensor("gwt", [16, 128, KC * 128], F32, kind="ExternalInput").ap(),
        "qd": nc.dram_tensor("qd", [H1, 64, BS1], F32, kind="ExternalInput").ap(),
        "kd": nc.dram_tensor("kd", [H1, 64, BS1], F32, kind="ExternalInput").ap(),
        "ddT": nc.dram_tensor("ddT", [H1, 128, 2 * BS1], F32, kind="ExternalInput").ap(),
        "qp": nc.dram_tensor("qp", [H1 * 64, TPC], BF16, kind="ExternalOutput").ap(),
        "Opart": nc.dram_tensor("Opart", [H1 * 64, TPC], F32, kind="ExternalOutput").ap(),
        "Cch": nc.dram_tensor("Cch", [2, H1, 64, 64], F32, kind="ExternalOutput").ap(),
        "gate": nc.dram_tensor("gate", [D, TPC], BF16, kind="ExternalOutput").ap(),
        "qk1": nc.dram_tensor("qk1", [2 * H1 * 64, TPC], BF16, kind="Internal").ap(),
        "v1": nc.dram_tensor("v1", [TPC, H1 * 64], BF16, kind="Internal").ap(),
    }

    es = ExitStack()
    with es:
        sb = lambda name, shape, dt: es.enter_context(nc.sbuf_tensor(name, shape, dt))
        PALL = es.enter_context(nc.psum_tensor("pall", [128, 4096], F32))
        PB = [PALL.ap()[:, i * 512:(i + 1) * 512] for i in range(8)]
        PBF = PALL.ap().bitcast(BF16)
        AR = Arena(nc, es, "arena", 188 * 1024)
        r = Rec(nc, es)
        for sname in ("in", "pe", "ev", "rp", "st", "c", "acc", "gin",
                      "pA", "pB", "gA", "gB", "gN",
                      "oA", "oB", "xA", "xB", "wA", "wB",
                      "c0s", "c1s", "aA", "aB", "fA", "fB", "bA", "bB",
                      "dA", "dB", "wrS", "wrT",
                      "w0", "w1", "v0", "v1s", "s0", "s1",
                      "q0", "q1", "k0", "k1", "dd0", "dd1", "vb0", "vb1"):
            r.sem(sname)
        block = es.enter_context(nc.Block())

        m0 = AR.mark()
        fl = emit_flash(nc, es, r, ap, sb, PB, AR, PALL, PBF)
        AR.reset(m0)
        if stages >= 2:
            w = emit_wo_ln(nc, es, r, ap, sb, PB, fl, AR)
            fl["x1_done"] = w["x1_done"]
        AR.reset(m0)
        if stages >= 3:
            f3 = emit_ffn0(nc, es, r, ap, sb, PB, fl, AR)
        AR.reset(m0)
        if stages >= 4:
            emit_l1a(nc, es, r, ap, sb, PB, AR, f3["x2_done"],
                     fl["ident_bf"], fl["ident_f32"],
                     pbf4=PBF[:, 4 * 1024:5 * 1024])
        r.wait("sync", "st", r.cur("st"))
        for pn in ("s0", "s1"):
            if pn in r.sems:
                r.wait("sync", pn, r.cur(pn))
        r.wait("gpsimd", "acc", r.cur("acc"))
        r.run(block)
    return nc


def emit_flash(nc, es, r, ap, sb, PB, AR, PALL, PBF):
    from concourse.bass import IndirectOffsetOnAxis
    ones_row = sb("f_onesr", [1, 128], F32)
    ones_col = sb("f_onesc", [128, 1], F32)

    masks_sb = AR.tile(128, 4 * 512, BF16)
    ident_bf = sb("f_identbf", [128, 128], BF16)
    ident_f32 = sb("f_identf", [128, 128], F32)
    kvidx_sb = sb("f_kvidx", [128, 18], I32)
    qhidx_sb = sb("f_qhidx", [128, 18], I32)
    nidx_sb = sb("f_nidx", [128, 9], I32)
    oidx_sb = sb("f_oidx", [128, 9 * NH], I32)
    negb_qc = [sb(f"f_negb{i}", [128, 128], F32) for i in range(2)]
    negb_u = sb("f_negbu", [128, 128], F32)
    maxk_sb = sb("f_maxk", [1, NH], F32)


    scl = sb("f_scl", [128, 1], F32)
    bcol = sb("f_bcol", [128, 1], F32)
    HGRP = NH // 2 * CHUNK
    kt_big = [AR.tile(128, HGRP, BF16) for i in range(2)]
    vt_big = [AR.tile(128, HGRP, BF16) for i in range(2)]
    qt_big = [AR.tile(128, HGRP, BF16) for i in range(2)]
    knorm = AR.tile(1, 8 * CHUNK, F32)
    k2 = AR.tile(128, CHUNK, F32)
    qnrow = AR.tile(1, CHUNK, F32)
    zeros_t = AR.tile(128, 1032, F32)
    oacc = [AR.tile(128, 1032, F32) for i in range(2)]
    p_sb2 = [AR.tile(128, CHUNK, BF16) for _ in range(2)]
    pt_sb2 = [AR.tile(128, CHUNK, BF16) for _ in range(2)]
    den_sb = sb("f_den", [128, 2], F32)
    negb_dram = nc.dram_tensor("negb_dram", [2 * 128, 128], F32, kind="Internal").ap()

    for t, dst in (("kvidx", kvidx_sb), ("qhidx", qhidx_sb), ("nidx", nidx_sb),
                   ("oidx", oidx_sb), ("ident", ident_f32)):
        r.add("sync", lambda q, t=t, dst=dst: q.dma_start(
            out=dst[:], in_=ap[t]).then_inc(r.h("in"), 16))
        r.bump("in")
    r.add("gpsimd", lambda g: g.dma_start(out=ident_bf[:], in_=ap["ident"])
          .then_inc(r.h("gin"), 16))
    r.bump("gin")
    for m in range(4):
        r.add("gpsimd", lambda g, m=m: g.dma_start(
            out=masks_sb[:, m * 512:(m + 1) * 512], in_=ap["masks"][m])
            .then_inc(r.h("gin"), 16))
        r.bump("gin")
    in_done = r.cur("in")
    gin_done = r.cur("gin")
    r.add("vector", lambda v: v.memset(ones_row[:], 1.0))
    r.add("vector", lambda v: v.memset(ones_col[:], 1.0))
    r.add("vector", lambda v: v.memset(zeros_t[:], 0.0).then_inc(r.h("c"), 1))
    zi = r.bump("c", 1)

    r.wait("gpsimd", "c", zi)
    for qc in range(2):
        for h in range(NH):
            r.add("gpsimd", lambda g, qc=qc, h=h: g.dma_start(
                out=ap["Odram"][qc, h], in_=zeros_t[:]).then_inc(r.h("acc"), 16))
            r.bump("acc")
    ozero_done = r.cur("acc")

    pre_done = {}
    knmm_prev = 0
    r.wait("gpsimd", "in", in_done)
    r.wait("gpsimd", "gin", gin_done)
    for h in range(NH):
        for ch in range(8):
            idx = h * 8 + ch
            buf = kt_big[idx % 2]
            if idx >= 2:
                r.wait("gpsimd", "c", pre_done[idx - 2])
            ksem = "pA" if idx % 2 == 0 else "pB"
            r.add("gpsimd", lambda g, h=h, ch=ch, buf=buf, ksem=ksem: g.dma_start(
                out=buf[:, 0:CHUNK],
                in_=ap["KhA"].rearrange("(r two) n -> r two n", two=2)
                    [ch * 128:(ch + 1) * 128, h // 8,
                     (h % 8) * CHUNK:(h % 8 + 1) * CHUNK])
                .then_inc(r.h(ksem), 16))
            kld = r.bump(ksem)
            r.wait("scalar", ksem, kld)
            if idx >= 2:
                r.wait("scalar", "c", pre_done[idx - 2])
            if idx >= 1:
                r.wait("scalar", "pe", knmm_prev)
            r.add("scalar", lambda s, buf=buf: s.activation(
                k2[:], buf[:, 0:CHUNK], AF.Square).then_inc(r.h("ev"), 1))
            sq_done = r.bump("ev", 1)
            r.wait("tensor", "ev", sq_done)
            if idx >= 1:
                r.wait("tensor", "c", pre_done[idx - 1])
            for t4 in range(2):
                bank = PB[5 + t4]
                r.add("tensor", lambda t, t4=t4, bank=bank: t.matmul(
                    bank[0:1, :], ones_col[:], k2[:, t4 * 512:(t4 + 1) * 512],
                    start=True, stop=True).then_inc(r.h("pe"), 1))
                knmm = r.bump("pe", 1)
                r.wait("vector", "pe", knmm)
                r.add("vector", lambda v, ch=ch, t4=t4, bank=bank: v.tensor_copy(
                    knorm[0:1, ch * CHUNK + t4 * 512: ch * CHUNK + (t4 + 1) * 512],
                    bank[0:1, :]).then_inc(r.h("c"), 1))
                pre_done[idx] = r.bump("c", 1)
            knmm_prev = knmm
        r.wait("vector", "c", r.cur("c"))
        r.add("vector", lambda v, h=h: v.reduce_max(
            maxk_sb[0:1, h:h + 1], knorm[0:1, :], axis=AX.X).then_inc(r.h("c"), 1))
        pre_done[h * 8 + 7] = r.bump("c", 1)
    maxk_done = r.cur("c")

    nb_done = {}
    cmm_prev = 0
    qn_prev = 0
    bc_prev = 0
    r.wait("tensor", "c", maxk_done)
    for h in range(NH):
        if bc_prev:
            r.wait("tensor", "c", bc_prev)
        r.add("tensor", lambda t, h=h: t.matmul(
            PB[7][:, 0:1], ones_row[:], maxk_sb[0:1, h:h + 1],
            start=True, stop=True).then_inc(r.h("pe"), 1))
        mk = r.bump("pe", 1)
        r.wait("vector", "pe", mk)
        r.add("vector", lambda v: v.tensor_copy(scl[:], PB[7][:, 0:1])
              .then_inc(r.h("c"), 1))
        sc_done = r.bump("c", 1)
        for qc in range(2):
            idx = h * 2 + qc
            buf = qt_big[idx % 2]
            if idx >= 2:
                r.wait("gpsimd", "c", nb_done[idx - 2])
            else:
                r.wait("gpsimd", "c", maxk_done)
            ksem = "pA" if idx % 2 == 0 else "pB"
            r.add("gpsimd", lambda g, h=h, qc=qc, buf=buf, ksem=ksem: g.dma_start(
                out=buf[:, 0:CHUNK],
                in_=ap["QhA"].rearrange("(r two) n -> r two n", two=2)
                    [qc * 128:(qc + 1) * 128, h // 8,
                     (h % 8) * CHUNK:(h % 8 + 1) * CHUNK])
                .then_inc(r.h(ksem), 16))
            qld = r.bump(ksem)
            r.wait("scalar", ksem, qld)
            if idx >= 2:
                r.wait("scalar", "c", nb_done[idx - 2])
            r.wait("scalar", "pe", knmm_prev)
            r.add("scalar", lambda s, buf=buf: s.activation(
                k2[:], buf[:, 0:CHUNK], AF.Square).then_inc(r.h("ev"), 1))
            q2_done = r.bump("ev", 1)
            r.wait("tensor", "ev", q2_done)
            r.wait("tensor", "c", qn_prev)
            for t4 in range(2):
                bank = PB[5 + t4]
                r.add("tensor", lambda t, t4=t4, bank=bank: t.matmul(
                    bank[0:1, :], ones_col[:], k2[:, t4 * 512:(t4 + 1) * 512],
                    start=True, stop=True).then_inc(r.h("pe"), 1))
                qmm = r.bump("pe", 1)
                r.wait("vector", "pe", qmm)
                if cmm_prev:
                    r.wait("vector", "pe", cmm_prev)
                r.add("vector", lambda v, t4=t4, bank=bank: v.tensor_copy(
                    qnrow[0:1, t4 * 512:(t4 + 1) * 512], bank[0:1, :])
                    .then_inc(r.h("c"), 1))
                r.bump("c", 1)
            knmm_prev = qmm
            qn_prev = r.cur("c")
            qn_done = r.cur("c")
            r.wait("tensor", "c", qn_done)
            for rr in range(8):
                if bc_prev:
                    r.wait("tensor", "c", bc_prev)
                r.add("tensor", lambda t, rr=rr: t.matmul(
                    PB[7][:, 1:2], qnrow[0:1, rr * 128:(rr + 1) * 128],
                    ones_row[0:1, 0:1],
                    start=True, stop=True).then_inc(r.h("pe"), 1))
                cmm = r.bump("pe", 1)
                cmm_prev = cmm
                r.wait("vector", "pe", cmm)
                r.wait("vector", "c", sc_done)
                r.wait("vector", "c", r.cur("c"))
                r.add("vector", lambda v: v.tensor_mul(
                    bcol[:], PB[7][:, 1:2], scl[:]).then_inc(r.h("c"), 1))
                bc_done = r.bump("c", 1)
                bc_prev = bc_done
                r.wait("scalar", "c", bc_done)
                r.add("scalar", lambda s: s.activation(
                    bcol[:], bcol[:], AF.Sqrt, scale=1.0 / 128.0)
                    .then_inc(r.h("ev"), 1))
                sq2 = r.bump("ev", 1)
                r.wait("vector", "ev", sq2)
                r.add("vector", lambda v, qc=qc, h=h, rr=rr: v.tensor_scalar_mul(
                    negb_qc[qc][:, h * 8 + rr: h * 8 + rr + 1], bcol[:], -1.0)
                    .then_inc(r.h("c"), 1))
                nb_done[idx] = r.bump("c", 1)
    negb_done = r.cur("c")
    r.wait("sync", "c", negb_done)
    for qc in range(2):
        r.add("sync", lambda q, qc=qc: q.dma_start(
            out=negb_dram[qc * 128:(qc + 1) * 128, :], in_=negb_qc[qc][:])
            .then_inc(r.h("st"), 16))
        r.bump("st")
    negb_st = r.cur("st")

    SR = [PALL.ap()[:, 0:1024], PALL.ap()[:, 1024:2048]]
    PTB = [PBF[:, 4 * 1024:4 * 1024 + 512], PBF[:, 5 * 1024:5 * 1024 + 512]]
    OB = PB[6]
    half_pe_done = {}
    oacc_dma_done = {}
    s_free = {0: 0, 1: 0}
    p_free = {0: 0, 1: 0}
    pt_bank_free = {0: 0, 1: 0}
    pt_sb_free = {0: 0, 1: 0}
    den_free = {0: 0, 1: 0}
    o_free = 0
    negbu_used = 0
    r.wait("gpsimd", "st", negb_st)
    for u in range(9):
        diag = u < 2
        if negbu_used:
            r.wait("gpsimd", "ev", negbu_used)
        r.add("gpsimd", lambda g, u=u: g.indirect_dma_start(
            negb_u[:], None, negb_dram,
            IndirectOffsetOnAxis(nidx_sb[:, u:u + 1], 0)).then_inc(r.h("gN"), 16))
        nb_ld = r.bump("gN")
        for half in range(2):
            hit = u * 2 + half
            kb, vb, qb = kt_big[hit % 2], vt_big[hit % 2], qt_big[hit % 2]
            if hit >= 2:
                r.wait("gpsimd", "pe", half_pe_done[hit - 2])
            uc = u * 2 + half
            gsem = "gA" if hit % 2 == 0 else "gB"

            def lds(g, uc=uc, kb=kb, vb=vb, qb=qb, gsem=gsem):
                g.indirect_dma_start(kb[:], None, ap["KhA"],
                                     IndirectOffsetOnAxis(kvidx_sb[:, uc:uc + 1], 0)) \
                    .then_inc(r.h(gsem), 16)
                g.indirect_dma_start(vb[:], None, ap["VhA"],
                                     IndirectOffsetOnAxis(kvidx_sb[:, uc:uc + 1], 0)) \
                    .then_inc(r.h(gsem), 16)
                g.indirect_dma_start(qb[:], None, ap["QhA"],
                                     IndirectOffsetOnAxis(qhidx_sb[:, uc:uc + 1], 0)) \
                    .then_inc(r.h(gsem), 16)
            r.add("gpsimd", lds)
            ld_done = r.bump(gsem, 48)
            r.wait("tensor", gsem, ld_done)
            if half == 0:
                r.wait("tensor", "gN", nb_ld)
            for hh in range(NH // 2):
                h = half * (NH // 2) + hh
                it = u * NH + h
                ob = oacc[it % 2]
                hb = hh * CHUNK
                exp_ev = {}
                for rr in range(9):
                    if rr < 8:
                        par = rr % 2
                        khs = [0] if (diag and rr < 4) else [0, 1]
                        nkh = len(khs)
                        mask_done = 0
                        for kh in khs:
                            bank = SR[par][:, kh * 512:(kh + 1) * 512]
                            if s_free[par]:
                                r.wait("tensor", "ev", s_free[par])
                            r.add("tensor", lambda t, qb=qb, kb=kb, hb=hb, rr=rr, kh=kh, bank=bank:
                                  t.matmul(bank, qb[:, hb + rr * 128: hb + (rr + 1) * 128],
                                           kb[:, hb + kh * 512: hb + (kh + 1) * 512],
                                           start=True, stop=True).then_inc(r.h("pe"), 1))
                            qk_done = r.bump("pe", 1)
                            if diag and kh == rr // 4:
                                r.wait("vector", "pe", qk_done)
                                r.add("vector", lambda v, rr=rr, bank=bank:
                                      v.tensor_add(bank, bank,
                                                   masks_sb[:, (rr % 4) * 512:(rr % 4 + 1) * 512])
                                      .then_inc(r.h("rp"), 1))
                                mask_done = r.bump("rp", 1)
                        r.wait("scalar", "pe", qk_done)
                        if mask_done:
                            r.wait("scalar", "rp", mask_done)
                        if hh == 0 and rr == 0:
                            r.wait("scalar", "gN", nb_ld)
                        if p_free[par]:
                            r.wait("scalar", "pe", p_free[par])
                        if den_free[par]:
                            r.wait("scalar", "c", den_free[par])
                        r.add("scalar", lambda s, h=h, rr=rr, nkh=nkh, par=par:
                              s.activation(p_sb2[par][:, 0:nkh * 512], SR[par][:, 0:nkh * 512],
                                           AF.Exp, scale=1.0 / math.sqrt(128.0),
                                           bias=negb_u[:, h * 8 + rr: h * 8 + rr + 1],
                                           accum_out=den_sb[:, par:par + 1])
                              .then_inc(r.h("ev"), 1))
                        s_free[par] = exp_ev[rr] = r.bump("ev", 1)
                    if rr >= 1:
                        pr = rr - 1
                        par = pr % 2
                        khs = [0] if (diag and pr < 4) else [0, 1]
                        r.wait("tensor", "ev", exp_ev[pr])
                        for kh in khs:
                            if pt_bank_free[kh]:
                                r.wait("tensor", "c", pt_bank_free[kh])
                            for t4 in range(4):
                                def tr(t, kh=kh, t4=t4, par=par):
                                    i = t.transpose(
                                        PTB[kh][:, t4 * 128:(t4 + 1) * 128],
                                        p_sb2[par][:, kh * 512 + t4 * 128: kh * 512 + (t4 + 1) * 128],
                                        ident_bf[:])
                                    if t4 == 3:
                                        i.then_inc(r.h("pe"), 1)
                                r.add("tensor", tr)
                            p_free[par] = trd = r.bump("pe", 1)
                            r.wait("vector", "pe", trd)
                            if pt_sb_free[par]:
                                r.wait("vector", "pe", pt_sb_free[par])
                            r.add("vector", lambda v, kh=kh, par=par: v.tensor_copy(
                                pt_sb2[par][:, kh * 512:(kh + 1) * 512], PTB[kh][:, 0:512])
                                .then_inc(r.h("c"), 1))
                            pt_bank_free[kh] = r.bump("c", 1)
                        r.wait("tensor", "c", pt_bank_free[khs[-1]])
                        if o_free:
                            r.wait("tensor", "c", o_free)
                        for kh in khs:
                            for t4 in range(4):
                                first = kh == khs[0] and t4 == 0
                                last = kh == khs[-1] and t4 == 3
                                def av(t, kh=kh, t4=t4, vb=vb, hb=hb, par=par,
                                       first=first, last=last):
                                    i = t.matmul(OB[:, 0:128],
                                                 pt_sb2[par][:, kh * 512 + t4 * 128: kh * 512 + (t4 + 1) * 128],
                                                 vb[:, hb + (kh * 4 + t4) * 128: hb + (kh * 4 + t4 + 1) * 128],
                                                 start=first, stop=last)
                                    if last:
                                        i.then_inc(r.h("pe"), 1)
                                r.add("tensor", av)
                        pt_sb_free[par] = avd = r.bump("pe", 1)
                        r.wait("vector", "ev", exp_ev[pr])
                        if it >= 2:
                            r.wait("vector", "acc", oacc_dma_done[it - 2])
                        else:
                            r.wait("vector", "acc", ozero_done)
                        r.add("vector", lambda v, pr=pr, ob=ob, par=par: v.tensor_copy(
                            ob[:, 1024 + pr:1024 + pr + 1], den_sb[:, par:par + 1])
                            .then_inc(r.h("c"), 1))
                        den_free[par] = r.bump("c", 1)
                        r.wait("vector", "pe", avd)
                        r.add("vector", lambda v, pr=pr, ob=ob: v.tensor_copy(
                            ob[:, pr * 128:(pr + 1) * 128], OB[:, 0:128])
                            .then_inc(r.h("c"), 1))
                        o_free = r.bump("c", 1)
                oacc_filled = o_free
                r.wait("gpsimd", "c", oacc_filled)
                if it >= 1:
                    r.wait("gpsimd", "acc", oacc_dma_done[it - 1])
                else:
                    r.wait("gpsimd", "acc", ozero_done)
                r.add("gpsimd", lambda g, u=u, h=h, ob=ob: g.indirect_dma_start(
                    ap["Odram"].rearrange("a h p t -> (a h p) t"),
                    IndirectOffsetOnAxis(oidx_sb[:, u * NH + h: u * NH + h + 1], 0),
                    ob[:], None, compute_op=ALU.add).then_inc(r.h("acc"), 16))
                oacc_dma_done[it] = r.bump("acc")
            half_pe_done[hit] = r.cur("pe")
        negbu_used = r.cur("ev")

    return {
        "ones_row": ones_row, "ones_col": ones_col, "ident_bf": ident_bf,
        "ident_f32": ident_f32, "flash_acc_done": r.cur("acc"),
        "flash_pe_done": r.cur("pe"), "negb_done": negb_done,
        "PBF2": PBF[:, 2 * 1024:3 * 1024],
    }


def emit_ln_epilogue(nc, r, sb_tiles, y_full, x1_slices, PB, s1b, s2b,
                     tok_w=512, nchunks=KC, eps=1e-5, dim=None):
    dim = dim or (nchunks * 128)
    ones_row = sb_tiles["ones_row"]
    m_row, m2_row = sb_tiles["m_row"], sb_tiles["m2_row"]
    var_row, rstd_row, mr_row = sb_tiles["var_row"], sb_tiles["rstd_row"], sb_tiles["mr_row"]
    tmp = sb_tiles["tmp"]

    r.add("scalar", lambda s: s.activation(
        m_row[:], PB[s1b][0:1, 0:tok_w], AF.Identity, scale=1.0 / dim)
        .then_inc(r.h("ev"), 1))
    mdone = r.bump("ev", 1)
    r.wait("vector", "ev", mdone)
    r.add("vector", lambda v: v.tensor_mul(m2_row[:], m_row[:], m_row[:])
          .then_inc(r.h("c"), 1))
    r.wait("vector", "c", r.bump("c", 1))
    r.add("vector", lambda v: v.scalar_tensor_tensor(
        var_row[:], PB[s2b][0:1, 0:tok_w], 1.0 / dim, m2_row[:],
        ALU.mult, ALU.subtract).then_inc(r.h("c"), 1))
    vdone = r.bump("c", 1)
    r.wait("scalar", "c", vdone)
    r.add("scalar", lambda s: s.activation(
        var_row[:], var_row[:], AF.Sqrt, bias=sb_tiles["eps_col"][:])
        .then_inc(r.h("ev"), 1))
    sdone = r.bump("ev", 1)
    r.wait("vector", "ev", sdone)
    r.add("vector", lambda v: v.reciprocal(rstd_row[:], var_row[:])
          .then_inc(r.h("c"), 1))
    r.wait("vector", "c", r.bump("c", 1))
    r.add("vector", lambda v: v.tensor_mul(mr_row[:], m_row[:], rstd_row[:])
          .then_inc(r.h("c"), 1))
    rows_done = r.bump("c", 1)
    r.wait("tensor", "c", rows_done)
    r.add("tensor", lambda t: t.matmul(PB[s1b][:, 0:tok_w], ones_row[:],
                                       rstd_row[:], start=True, stop=True)
          .then_inc(r.h("pe"), 1))
    r.add("tensor", lambda t: t.matmul(PB[s2b][:, 0:tok_w], ones_row[:],
                                       mr_row[:], start=True, stop=True)
          .then_inc(r.h("pe"), 1))
    r.bump("pe", 1)
    bc_done = r.bump("pe", 1)
    r.wait("vector", "pe", bc_done)
    for oc in range(nchunks):
        r.wait("vector", "c", r.cur("c"))
        r.add("vector", lambda v, oc=oc: v.tensor_mul(
            tmp[:], y_full[:, oc * tok_w:(oc + 1) * tok_w], PB[s1b][:, 0:tok_w])
            .then_inc(r.h("c"), 1))
        r.wait("vector", "c", r.bump("c", 1))
        r.add("vector", lambda v, oc=oc: v.tensor_sub(
            x1_slices(oc), tmp[:], PB[s2b][:, 0:tok_w]).then_inc(r.h("c"), 1))
        r.bump("c", 1)
    return r.cur("c")


def emit_wo_ln(nc, es, r, ap, sb, PB, fl, AR):
    ones_row, ones_col = fl["ones_row"], fl["ones_col"]
    ident_bf = fl["ident_bf"]
    ot_sb = AR.tile(128, NH * TPC, BF16)
    od_sb = [AR.tile(128, 1032, F32) for i in range(2)]
    recip_sb = sb("w_recip", [128, 8], F32)
    obf = sb("w_obf", [128, 128], BF16)
    wo_sb = [AR.tile(128, KC * 128, BF16) for i in range(2)]
    bo_sb = sb("w_bo", [128, KC], F32)
    x_sl = [AR.tile(128, TT, F32) for i in range(2)]
    y_full = AR.tile(128, KC * TT, F32)
    sq_bf = AR.tile(128, TT, BF16)
    y_bf = AR.tile(128, TT, BF16)
    ones_col_bf = sb("w_onescbf", [128, 1], BF16)
    lnt = {
        "ones_row": ones_row,
        "m_row": AR.tile(1, TT, F32), "m2_row": AR.tile(1, TT, F32),
        "var_row": AR.tile(1, TT, F32), "rstd_row": AR.tile(1, TT, F32),
        "mr_row": AR.tile(1, TT, F32), "tmp": AR.tile(128, TT, F32),
        "eps_col": sb("w_eps", [1, 1], F32),
    }

    r.add("vector", lambda v: v.memset(lnt["eps_col"][:], 1e-5))
    r.add("vector", lambda v: v.memset(ones_col_bf[:], 1.0).then_inc(r.h("c"), 1))
    onesbf_done = r.bump("c", 1)
    r.add("sync", lambda q: q.dma_start(out=bo_sb[:], in_=ap["bo"])
          .then_inc(r.h("in"), 16))
    bo_done = r.bump("in")

    flash_done = fl["flash_acc_done"]
    r.wait("sync", "acc", flash_done)
    r.wait("vector", "pe", fl["flash_pe_done"])
    r.wait("vector", "acc", flash_done)
    ot_done = {}
    od_used = {}
    obf_free = 0
    tr_free = 0
    for qc in range(2):
        for h in range(NH):
            it = qc * NH + h
            buf = od_sb[it % 2]
            if it >= 2:
                r.wait("sync", "ev", od_used[it - 2])
            osem = "oA" if it % 2 == 0 else "oB"
            r.add("sync", lambda q, qc=qc, h=h, buf=buf, osem=osem: q.dma_start(
                out=buf[:], in_=ap["Odram"][qc, h]).then_inc(r.h(osem), 16))
            old = r.bump(osem)
            r.wait("vector", osem, old)
            r.add("vector", lambda v, buf=buf: v.tensor_scalar_add(
                recip_sb[:], buf[:, 1024:1032], 1e-30).then_inc(r.h("c"), 1))
            r.wait("vector", "c", r.bump("c", 1))
            r.add("vector", lambda v: v.reciprocal(recip_sb[:], recip_sb[:])
                  .then_inc(r.h("c"), 1))
            rec_done = r.bump("c", 1)
            for rr in range(8):
                r.wait("scalar", "c", rec_done)
                if obf_free:
                    r.wait("scalar", "pe", obf_free)
                r.add("scalar", lambda s, buf=buf, rr=rr: s.activation(
                    obf[:], buf[:, rr * 128:(rr + 1) * 128], AF.Identity,
                    scale=recip_sb[:, rr:rr + 1]).then_inc(r.h("ev"), 1))
                ob_done = r.bump("ev", 1)
                od_used[it] = ob_done
                r.wait("tensor", "ev", ob_done)
                if tr_free:
                    r.wait("tensor", "c", tr_free)
                r.add("tensor", lambda t: t.transpose(
                    fl["PBF2"][:, 0:128], obf[:], ident_bf[:])
                    .then_inc(r.h("pe"), 1))
                obf_free = trd = r.bump("pe", 1)
                r.wait("vector", "pe", trd)
                col = h * TPC + qc * CHUNK + rr * 128
                r.add("vector", lambda v, col=col: v.tensor_copy(
                    ot_sb[:, col:col + 128], fl["PBF2"][:, 0:128])
                    .then_inc(r.h("c"), 1))
                tr_free = r.bump("c", 1)
            ot_done[it] = tr_free
    ot_all = tr_free

    st_prev = 0
    wo_used, x_used, y_done_all = {}, {}, {}
    for ti in range(NTT):
        y_done = {}
        s12_done = 0
        for oc in range(KC):
            gidx = ti * KC + oc
            wbuf = wo_sb[gidx % 2]
            if gidx >= 2:
                r.wait("gpsimd", "pe", wo_used[gidx - 2])
            wsem = "wA" if gidx % 2 == 0 else "wB"
            r.add("gpsimd", lambda g, oc=oc, wbuf=wbuf, wsem=wsem: g.dma_start(
                out=wbuf[:], in_=ap["wo"][oc]).then_inc(r.h(wsem), 16))
            wdone = r.bump(wsem)
            xbuf = x_sl[gidx % 2]
            if gidx >= 2:
                r.wait("sync", "c", x_used[gidx - 2])
            xsem = "xA" if gidx % 2 == 0 else "xB"
            r.add("sync", lambda q, oc=oc, ti=ti, xbuf=xbuf, xsem=xsem: q.dma_start(
                out=xbuf[:], in_=ap["x_p"][:, oc * TPC + ti * TT: oc * TPC + ti * TT + TT])
                .then_inc(r.h(xsem), 16))
            xdone = r.bump(xsem)
            bank = PB[gidx % 2]
            r.wait("tensor", wsem, wdone)
            if gidx == 0:
                r.wait("tensor", "c", ot_all)
            if gidx >= 2:
                r.wait("tensor", "c", y_done_all.get(gidx - 2, 0))
            for kc in range(KC):
                last = kc == KC - 1
                def mm(t, oc=oc, ti=ti, kc=kc, wbuf=wbuf, bank=bank, last=last):
                    i = t.matmul(bank[:], wbuf[:, kc * 128:(kc + 1) * 128],
                                 ot_sb[:, kc * TPC + ti * TT: kc * TPC + ti * TT + TT],
                                 start=(kc == 0), stop=last)
                    if last:
                        i.then_inc(r.h("pe"), 1)
                r.add("tensor", mm)
            wo_used[gidx] = mm_done = r.bump("pe", 1)
            r.wait("vector", "pe", mm_done)
            r.wait("vector", xsem, xdone)
            if gidx == 0:
                r.wait("vector", "in", bo_done)
            if st_prev:
                r.wait("vector", "st", st_prev)
            r.add("vector", lambda v, oc=oc, bank=bank, xbuf=xbuf:
                  v.scalar_tensor_tensor(
                      y_full[:, oc * TT:(oc + 1) * TT], bank[:],
                      bo_sb[:, oc:oc + 1], xbuf[:], ALU.add, ALU.add)
                  .then_inc(r.h("c"), 1))
            x_used[gidx] = y_done_all[gidx] = yd = r.bump("c", 1)
            r.wait("scalar", "c", yd)
            if s12_done:
                r.wait("scalar", "pe", s12_done)
            r.add("scalar", lambda s, oc=oc: s.activation(
                y_bf[:], y_full[:, oc * TT:(oc + 1) * TT], AF.Identity)
                .then_inc(r.h("ev"), 1))
            r.bump("ev", 1)
            r.add("scalar", lambda s, oc=oc: s.activation(
                sq_bf[:], y_full[:, oc * TT:(oc + 1) * TT], AF.Square)
                .then_inc(r.h("ev"), 1))
            sqd = r.bump("ev", 1)
            r.wait("tensor", "ev", sqd)
            if gidx == 0:
                r.wait("tensor", "c", onesbf_done)
            r.add("tensor", lambda t, oc=oc: t.matmul(
                PB[6][0:1, 0:TT], ones_col_bf[:], y_bf[:],
                start=(oc == 0), stop=(oc == KC - 1)))
            def mm2(t, oc=oc):
                i = t.matmul(PB[7][0:1, 0:TT], ones_col_bf[:], sq_bf[:],
                             start=(oc == 0), stop=(oc == KC - 1))
                i.then_inc(r.h("pe"), 1)
            r.add("tensor", mm2)
            s12_done = r.bump("pe", 1)
        r.wait("scalar", "pe", s12_done)
        r.wait("vector", "pe", s12_done)
        emit_ln_epilogue(nc, r, lnt, y_full,
                         lambda oc: y_full[:, oc * TT:(oc + 1) * TT], PB, 6, 7)
        ln_done = r.cur("c")
        r.wait("sync", "c", ln_done)
        for oc in range(KC):
            r.add("sync", lambda q, oc=oc, ti=ti: q.dma_start(
                out=ap["x1"][:, oc * TPC + ti * TT: oc * TPC + ti * TT + TT],
                in_=y_full[:, oc * TT:(oc + 1) * TT]).then_inc(r.h("st"), 16))
            r.bump("st")
        st_prev = r.cur("st")
    return {"x1_done": st_prev}


def prep_ffn0(inp):
    gate_w = np.asarray(inp["L0_ffn_gate_w"])
    w1 = np.asarray(inp["L0_ffn_w1"])
    w3 = np.asarray(inp["L0_ffn_w3"])
    w2 = np.asarray(inp["L0_ffn_w2"])
    a1 = np.asarray(inp["L0_ffn_a1"])
    b1 = np.asarray(inp["L0_ffn_b1"])
    a3 = np.asarray(inp["L0_ffn_a3"])
    b3 = np.asarray(inp["L0_ffn_b3"])
    a2 = np.asarray(inp["L0_ffn_a2"])
    b2 = np.asarray(inp["L0_ffn_b2"])

    gw = _pack_kc_free(np.ascontiguousarray(gate_w.T))
    w1t = _oc_tiles(np.ascontiguousarray(w1.T))
    w3t = _oc_tiles(np.ascontiguousarray(w3.T))
    w13 = np.concatenate([w1t, w3t], axis=2)
    b13 = np.zeros((NFC, 48, NE * 128), np.float32)
    for fc in range(NFC):
        for e in range(NE):
            b13[fc, 0:16, e * 128:(e + 1) * 128] = b1[e].T[:, fc * 128:(fc + 1) * 128]
            b13[fc, 32:48, e * 128:(e + 1) * 128] = b3[e].T[:, fc * 128:(fc + 1) * 128]
    a13 = np.zeros((NE, 128, KC * 64), np.float32)
    for e in range(NE):
        for kc in range(KC):
            a13[e][:, kc * 64:kc * 64 + 16] = a1[e][:, kc * 128:(kc + 1) * 128].T
            a13[e][:, kc * 64 + 32:kc * 64 + 48] = a3[e][:, kc * 128:(kc + 1) * 128].T
    a2t = np.zeros((NE, NFC, 128, 16), np.float32)
    for e in range(NE):
        for fc in range(NFC):
            a2t[e, fc] = a2[e][:, fc * 128:(fc + 1) * 128].T
    b2s = np.zeros((2, 128, D), np.float32)
    for e in range(NE):
        g = e // 2
        b2s[e % 2, 32 * g:32 * g + 16, :] = b2[e].T
    b2so = np.stack([_oc_tiles(b2s[0]), _oc_tiles(b2s[1])])
    w2t = _oc_tiles(np.ascontiguousarray(w2.T))
    w2fc = np.zeros((NFC, 128, KC * 128), np.float32)
    for fc in range(NFC):
        for oc in range(KC):
            w2fc[fc][:, oc * 128:(oc + 1) * 128] = w2t[oc][:, fc * 128:(fc + 1) * 128]
    a2th = np.ascontiguousarray(
        a2t.transpose(2, 0, 1, 3).reshape(128, NE * NFC * 16))
    b2sh = np.ascontiguousarray(
        b2so.transpose(2, 0, 1, 3).reshape(128, 2 * KC * 128))
    return {"gw": gw, "w13": w13, "b13": b13, "a13": a13, "a2t": a2th,
            "b2so": b2sh, "w2fc": w2fc}


def emit_ffn0(nc, es, r, ap, sb, PB, fl, AR):
    ones_row, ones_col, ident_bf, ident_f32 = (
        fl["ones_row"], fl["ones_col"], fl["ident_bf"], fl["ident_f32"])
    x1bf = AR.tile(128, KC * TT, BF16)
    acc = AR.tile(128, NFC * TT, BF16)
    y_full = AR.tile(128, KC * TT, F32)
    w13_sb = [AR.tile(128, 2 * KC * 128, BF16) for _ in range(2)]
    w2_sb = [AR.tile(128, KC * 128, BF16) for _ in range(2)]
    a13_sb = [AR.tile(128, 2 * KC * 64, BF16) for _ in range(2)]
    a2t_sb = AR.tile(128, NE * NFC * 16, BF16)
    b2s_sb = AR.tile(128, 2 * KC * 128, BF16)
    gw_sb = AR.tile(128, KC * 8, BF16)
    wb_sb = AR.tile(128, NE * TT, BF16)
    u_sb = AR.tile(64, NE * TT, BF16)
    b13_sb = [AR.tile(48, NE * 128, BF16) for _ in range(2)]
    zbf = AR.tile(128, 2 * TT, BF16)
    lg = sb("n_lg", [8, TT], F32)
    ltok = sb("n_ltok", [128, 8], F32)
    m1 = sb("n_m1", [128, 1], F32)
    ecol = sb("n_e", [128, 8], F32)
    ind1 = sb("n_ind1", [128, 8], F32)
    em = sb("n_em", [128, 8], F32)
    m2c = sb("n_m2", [128, 1], F32)
    wun = sb("n_wun", [128, 8], F32)
    dcol = sb("n_den", [128, 1], F32)
    wtok = sb("n_wtok", [128, 8], F32)
    wrow = sb("n_wrow", [8, TT], F32)
    wrow0 = [sb(f"n_wrow0{i}", [1, TT], F32) for i in range(2)]
    silu_sb = AR.tile(128, TT, BF16)
    act_sb = AR.tile(128, TT, BF16)
    t2_sb = AR.tile(128, TT, BF16)
    x_sl = [AR.tile(128, TT, F32) for i in range(2)]
    sqn_bf = AR.tile(128, TT, BF16)
    yn_bf = AR.tile(128, TT, BF16)
    lnt = {
        "ones_row": ones_row,
        "m_row": AR.tile(1, TT, F32), "m2_row": AR.tile(1, TT, F32),
        "var_row": AR.tile(1, TT, F32), "rstd_row": AR.tile(1, TT, F32),
        "mr_row": AR.tile(1, TT, F32), "tmp": AR.tile(128, TT, F32),
        "eps_col": sb("n_eps", [1, 1], F32),
    }
    ocbf = sb("n_ocbf", [128, 1], BF16)
    zrow_bf = sb("n_zrow", [1, 640], BF16)

    def vw(val):
        r.wait("vector", "c", val)

    def vadd(fn):
        r.add("vector", lambda v, fn=fn: fn(v).then_inc(r.h("c"), 1))
        return r.bump("c", 1)

    def sadd(fn):
        r.add("scalar", lambda s, fn=fn: fn(s).then_inc(r.h("ev"), 1))
        return r.bump("ev", 1)

    r.add("vector", lambda v: v.memset(lnt["eps_col"][:], 1e-5))
    r.add("vector", lambda v: v.memset(zrow_bf[:], 0.0))
    onesbf2 = vadd(lambda v: v.memset(ocbf[:], 1.0))

    r.wait("gpsimd", "st", fl["x1_done"])
    r.add("gpsimd", lambda g: g.dma_start(out=gw_sb[:], in_=ap["gw"])
          .then_inc(r.h("gin"), 16))
    r.bump("gin")
    r.add("gpsimd", lambda g: g.dma_start(
        out=b2s_sb[:], in_=ap["b2so"]).then_inc(r.h("gin"), 16))
    r.bump("gin")
    r.add("gpsimd", lambda g: g.dma_start(
        out=a2t_sb[:], in_=ap["a2t"]).then_inc(r.h("gin"), 16))
    r.bump("gin")
    ffn_gin = r.cur("gin")

    x1_done = fl["x1_done"]
    prev_store = 0
    y_ln_prev = 0
    x_used, w13_used, b13_used, a13_used, w2_used = {}, {}, {}, {}, {}
    x1bf_used = {}
    for ti in range(NTT):
        xs = "c0s" if ti % 2 == 0 else "c1s"
        if ti >= 2:
            r.wait("gpsimd", "pe", x1bf_used[ti - 2])
        if ti == 0:
            r.wait("gpsimd", "st", x1_done)
        r.add("gpsimd", lambda g, ti=ti, xs=xs: g.dma_start(
            out=x1bf[:].rearrange("p (a t) -> p a t", a=KC),
            in_=ap["x1"].rearrange("p (a t) -> p a t", a=KC)
            [:, :, ti * TT:(ti + 1) * TT]).then_inc(r.h(xs), 16))
        x1bf_ld = r.bump(xs)
        r.wait("tensor", xs, x1bf_ld)
        if ti == 0:
            r.wait("tensor", "gin", ffn_gin)
        if ti >= 1:
            r.wait("tensor", "c", y_ln_prev)
        for kc in range(KC):
            def mmg(t, kc=kc, last=kc == KC - 1):
                i = t.matmul(PB[0][0:8, :], gw_sb[:, kc * 8:(kc + 1) * 8],
                             x1bf[:, kc * TT:(kc + 1) * TT],
                             start=kc == 0, stop=last)
                if last:
                    i.then_inc(r.h("pe"), 1)
            r.add("tensor", mmg)
        gmm = r.bump("pe", 1)
        r.wait("vector", "pe", gmm)
        vw(r.cur("c"))
        lg_ev = vadd(lambda v: v.tensor_copy(lg[:], PB[0][0:8, :]))
        wrow_parts = []
        for sfour in range(4):
            r.wait("tensor", "c", lg_ev)
            if wrow_parts:
                r.wait("tensor", "c", wrow_parts[-1])
            r.add("tensor", lambda t, sfour=sfour: t.transpose(
                PB[1][:, 0:8], lg[:, sfour * 128:(sfour + 1) * 128],
                ident_f32[0:8, 0:8]).then_inc(r.h("pe"), 1))
            ltr = r.bump("pe", 1)
            r.wait("vector", "pe", ltr)
            vw(r.cur("c"))
            lt_ev = vadd(lambda v: v.tensor_copy(ltok[:], PB[1][:, 0:8]))
            vw(lt_ev)
            e1 = vadd(lambda v: v.reduce_max(m1[:], ltok[:], axis=AX.X))
            vw(e1)
            e2 = vadd(lambda v: v.tensor_scalar_mul(m1[:], m1[:], -1.0))
            r.wait("scalar", "c", e2)
            e3 = sadd(lambda s: s.activation(ecol[:], ltok[:], AF.Exp, bias=m1[:]))
            r.wait("vector", "ev", e3)
            e4 = vadd(lambda v: v.tensor_single_scalar(ind1[:], ecol[:], 1.0, ALU.is_ge))
            vw(e4)
            e5 = vadd(lambda v: v.tensor_scalar(ind1[:], ind1[:], -1.0, 1.0,
                                                ALU.mult, ALU.add))
            vw(e5)
            e6 = vadd(lambda v: v.tensor_mul(em[:], ecol[:], ind1[:]))
            vw(e6)
            e7 = vadd(lambda v: v.reduce_max(m2c[:], em[:], axis=AX.X))
            vw(e7)
            e8 = vadd(lambda v: v.tensor_single_scalar(wun[:], ecol[:], m2c[:], ALU.is_ge))
            vw(e8)
            e9 = vadd(lambda v: v.tensor_mul(wun[:], wun[:], ecol[:]))
            vw(e9)
            e10 = vadd(lambda v: v.reduce_sum(dcol[:], wun[:], axis=AX.X))
            vw(e10)
            e11 = vadd(lambda v: v.reciprocal(dcol[:], dcol[:]))
            vw(e11)
            e12 = vadd(lambda v: v.tensor_scalar_mul(wtok[:], wun[:], dcol[:]))
            r.wait("tensor", "c", e12)
            r.add("tensor", lambda t, sfour=sfour: t.transpose(
                PB[1][0:8, 0:128], wtok[:], ident_f32[:]).then_inc(r.h("pe"), 1))
            wtr = r.bump("pe", 1)
            r.wait("vector", "pe", wtr)
            vw(r.cur("c"))
            wrow_parts.append(vadd(lambda v, sfour=sfour: v.tensor_copy(
                wrow[:, sfour * 128:(sfour + 1) * 128], PB[1][0:8, 0:128])))
        wb_evs = []
        wmm_by_buf = {0: 0, 1: 0}
        for e in range(NE):
            wsel = "wrS" if e % 2 == 0 else "wrT"
            wbuf0 = wrow0[e % 2]
            r.wait("sync", "c", wrow_parts[-1])
            if wmm_by_buf[e % 2]:
                r.wait("sync", "pe", wmm_by_buf[e % 2])
            r.add("sync", lambda q, e=e, wbuf0=wbuf0: q.dma_start(
                out=wbuf0[:], in_=wrow[e:e + 1, :]).then_inc(
                r.h("wrS" if e % 2 == 0 else "wrT"), 16))
            wr0 = r.bump(wsel)
            r.wait("tensor", wsel, wr0)
            if wb_evs:
                r.wait("tensor", "c", wb_evs[-1])
            r.add("tensor", lambda t, wbuf0=wbuf0: t.matmul(
                PB[0][:, 0:TT], ones_row[:], wbuf0[:],
                start=True, stop=True).then_inc(r.h("pe"), 1))
            wmm = r.bump("pe", 1)
            wmm_by_buf[e % 2] = wmm
            r.wait("vector", "pe", wmm)
            vw(r.cur("c"))
            wb_evs.append(vadd(lambda v, e=e: v.tensor_copy(
                wb_sb[:, e * TT:(e + 1) * TT], PB[0][:, 0:TT])))
        wb_done = wb_evs[-1]

        for p in range(4):
            pidx = ti * 4 + p
            abuf = a13_sb[pidx % 2]
            asem = "aA" if pidx % 2 == 0 else "aB"
            if pidx >= 2:
                r.wait("gpsimd", "pe", a13_used[pidx - 2])

            def lda(g, p=p, abuf=abuf, asem=asem):
                g.dma_start(out=abuf[:, 0:KC * 64], in_=ap["a13"][2 * p]) \
                    .then_inc(r.h(asem), 16)
                g.dma_start(out=abuf[:, KC * 64:2 * KC * 64], in_=ap["a13"][2 * p + 1]) \
                    .then_inc(r.h(asem), 16)
            r.add("gpsimd", lda)
            ald = r.bump(asem, 32)
            r.wait("tensor", asem, ald)
            if p == 0 and ti >= 1:
                r.wait("tensor", "c", y_ln_prev)
            if pidx >= 1:
                r.wait("tensor", "c", u_ev_prev)
            for e_in in range(2):
                for kc in range(KC):
                    def mmu(t, p=p, e_in=e_in, kc=kc, abuf=abuf,
                            last=kc == KC - 1):
                        i = t.matmul(
                            PB[2][64 * e_in:64 * e_in + 64, :],
                            abuf[:, e_in * KC * 64 + kc * 64: e_in * KC * 64 + (kc + 1) * 64],
                            x1bf[:, kc * TT:(kc + 1) * TT],
                            start=kc == 0, stop=last,
                            tile_position=(0, 64 * e_in))
                        if last:
                            i.then_inc(r.h("pe"), 1)
                    r.add("tensor", mmu)
                r.bump("pe", 1)
            umm = r.cur("pe")
            a13_used[pidx] = umm
            r.wait("vector", "pe", umm)
            vw(r.cur("c"))
            vadd(lambda v, p=p: v.tensor_scalar_mul(
                u_sb[:, (2 * p) * TT:(2 * p + 1) * TT], PB[2][0:64, :], LORA_SCALE))
            vw(r.cur("c"))
            u_ev_prev = vadd(lambda v, p=p: v.tensor_scalar_mul(
                u_sb[:, (2 * p + 1) * TT:(2 * p + 2) * TT], PB[2][64:128, :], LORA_SCALE))
        u_done = u_ev_prev

        if ti >= 1:
            r.wait("tensor", "c", y_ln_prev)
        r.wait("tensor", "c", u_done)
        r.wait("tensor", "c", onesbf2)
        for zb in (7, 2):
            def zmm(t, zb=zb):
                t.matmul(PB[zb][:], zrow_bf[0:1, 0:128], zrow_bf[0:1, 128:640],
                         start=True, stop=False).then_inc(r.h("pe"), 1)
            r.add("tensor", zmm)
            r.bump("pe", 1)
        zclear = r.cur("pe")

        h_free = {3: 0, 5: 0, 4: 0, 6: 0}
        base_free = {3: 0, 4: 0}
        t2_used = 0
        silu_free = 0
        act_free = 0
        prev_accv = 0
        for fc in range(NFC):
            fidx = ti * NFC + fc
            wbuf = w13_sb[fidx % 2]
            fsem = "fA" if fidx % 2 == 0 else "fB"
            if fidx >= 2:
                r.wait("gpsimd", "pe", w13_used[fidx - 2])
            r.add("gpsimd", lambda g, fc=fc, wbuf=wbuf, fsem=fsem: g.dma_start(
                out=wbuf[:], in_=ap["w13"][fc]).then_inc(r.h(fsem), 16))
            wld = r.bump(fsem)
            bbuf = b13_sb[fidx % 2]
            bsem = "bA" if fidx % 2 == 0 else "bB"
            if fidx >= 2:
                r.wait("gpsimd", "pe", b13_used[fidx - 2])
            r.add("gpsimd", lambda g, fc=fc, bbuf=bbuf, bsem=bsem: g.dma_start(
                out=bbuf[:], in_=ap["b13"][fc]).then_inc(r.h(bsem), 16))
            bld = r.bump(bsem)
            r.wait("tensor", fsem, wld)
            for bi, bank in ((0, 3), (1, 4)):
                if base_free[bank]:
                    r.wait("tensor", "ev", base_free[bank])
                for kc in range(KC):
                    def mmb(t, bi=bi, kc=kc, bank=bank, wbuf=wbuf,
                            last=kc == KC - 1):
                        i = t.matmul(
                            PB[bank][:],
                            wbuf[:, bi * KC * 128 + kc * 128: bi * KC * 128 + (kc + 1) * 128],
                            x1bf[:, kc * TT:(kc + 1) * TT],
                            start=kc == 0, stop=last)
                        if last:
                            i.then_inc(r.h("pe"), 1)
                    r.add("tensor", mmb)
                r.bump("pe", 1)
            bmm = r.cur("pe")
            r.wait("scalar", "pe", bmm)
            if silu_free:
                r.wait("scalar", "c", silu_free)
            b1ev = sadd(lambda s: s.activation(silu_sb[:], PB[3][:], AF.Identity))
            b3ev = sadd(lambda s: s.activation(act_sb[:], PB[4][:], AF.Identity))
            base_free[3] = base_free[4] = b3ev
            for e in range(NE):
                h1b = 3 if e % 2 == 0 else 5
                h3b = 4 if e % 2 == 0 else 6
                r.wait("tensor", "ev", b3ev)
                r.wait("tensor", bsem, bld)
                if fc == 0 and e <= 1 and ti >= 1:
                    r.wait("tensor", "c", y_ln_prev)
                if h_free[h1b]:
                    r.wait("tensor", "ev", h_free[h1b])
                if e == 0:
                    r.wait("tensor", "c", u_done)
                    r.wait("tensor", "c", wb_done)

                def mmh(t, e=e, h1b=h1b, h3b=h3b, bbuf=bbuf):
                    t.matmul(PB[h1b][:], ident_bf[:], silu_sb[:],
                             start=True, stop=False)
                    i = t.matmul(PB[h1b][:], bbuf[0:16, e * 128:(e + 1) * 128],
                                 u_sb[0:16, e * TT:(e + 1) * TT],
                                 start=False, stop=True)
                    i.then_inc(r.h("pe"), 1)
                    t.matmul(PB[h3b][:], ident_bf[:], act_sb[:],
                             start=True, stop=False)
                    i2 = t.matmul(PB[h3b][:], bbuf[32:48, e * 128:(e + 1) * 128],
                                  u_sb[32:48, e * TT:(e + 1) * TT],
                                  start=False, stop=True)
                    i2.then_inc(r.h("pe"), 1)
                r.add("tensor", mmh)
                r.bump("pe", 1)
                hmm_done = r.bump("pe", 1)
                r.wait("scalar", "pe", hmm_done)
                if t2_used:
                    r.wait("scalar", "pe", t2_used)
                if prev_accv:
                    r.wait("scalar", "c", prev_accv)
                sl = sadd(lambda s, h1b=h1b: s.activation(
                    t2_sb[:], PB[h1b][:], AF.Silu))
                h_free[h1b] = sl
                r.wait("vector", "ev", sl)
                vw(r.cur("c"))
                a1v = vadd(lambda v, h3b=h3b: v.tensor_mul(
                    t2_sb[:], t2_sb[:], PB[h3b][:]))
                h_free[h3b] = a1v
                act_free = a1v
                vw(a1v)
                t2v = vadd(lambda v, e=e: v.tensor_mul(
                    t2_sb[:], t2_sb[:], wb_sb[:, e * TT:(e + 1) * TT]))
                vw(t2v)
                if e == 0:
                    accv = vadd(lambda v, fc=fc: v.tensor_copy(
                        acc[:, fc * TT:(fc + 1) * TT], t2_sb[:]))
                else:
                    accv = vadd(lambda v, fc=fc: v.tensor_add(
                        acc[:, fc * TT:(fc + 1) * TT],
                        acc[:, fc * TT:(fc + 1) * TT], t2_sb[:]))
                zbank = 7 if e % 2 == 0 else 2
                grp = e // 2
                r.wait("tensor", "c", t2v)

                def mmz(t, e=e, fc=fc, zbank=zbank, grp=grp):
                    t.matmul(PB[zbank][32 * grp:32 * grp + 16, :],
                             a2t_sb[:, (e * NFC + fc) * 16:(e * NFC + fc + 1) * 16],
                             t2_sb[:],
                             start=False, stop=False,
                             tile_position=(0, 32 * grp)).then_inc(r.h("pe"), 1)
                r.add("tensor", mmz)
                zmm = r.bump("pe", 1)
                t2_used = zmm
                prev_accv = accv
                silu_free = accv
            w13_used[fidx] = r.cur("pe")
            b13_used[fidx] = r.cur("pe")
        for zb in (7, 2):
            def zcl(t, zb=zb):
                t.matmul(PB[zb][:], zrow_bf[0:1, 0:128], zrow_bf[0:1, 128:640],
                         start=False, stop=True).then_inc(r.h("pe"), 1)
            r.add("tensor", zcl)
            r.bump("pe", 1)
        r.wait("vector", "pe", r.cur("pe"))
        vw(r.cur("c"))
        zev1 = vadd(lambda v: v.tensor_scalar_mul(zbf[:, 0:TT], PB[7][:], LORA_SCALE))
        vw(zev1)
        zev2 = vadd(lambda v: v.tensor_scalar_mul(zbf[:, TT:2 * TT], PB[2][:], LORA_SCALE))
        z_done = zev2

        down_mm = {}
        for ocg in range(2):
            for fc in range(NFC):
                fidx2 = (ti * 2 + ocg) * NFC + fc
                wbuf = w2_sb[fidx2 % 2]
                wsem2 = "dA" if fidx2 % 2 == 0 else "dB"
                if fidx2 >= 2:
                    r.wait("gpsimd", "pe", w2_used[fidx2 - 2])
                r.add("gpsimd", lambda g, fc=fc, wbuf=wbuf, wsem2=wsem2: g.dma_start(
                    out=wbuf[:], in_=ap["w2fc"][fc]).then_inc(r.h(wsem2), 16))
                w2ld = r.bump(wsem2)
                r.wait("tensor", wsem2, w2ld)
                if fc == 0:
                    r.wait("tensor", "c", z_done)
                    if ocg == 0 and ti >= 1:
                        r.wait("tensor", "c", y_ln_done)
                    if ocg == 1:
                        r.wait("tensor", "c", ybank_free)
                for oc8 in range(8):
                    oc = ocg * 8 + oc8

                    def mmd(t, fc=fc, oc=oc, oc8=oc8, wbuf=wbuf,
                            inc=oc8 == 7):
                        i = t.matmul(PB[oc8][:],
                                     wbuf[:, oc * 128:(oc + 1) * 128],
                                     acc[:, fc * TT:(fc + 1) * TT],
                                     start=fc == 0, stop=False)
                        if inc:
                            i.then_inc(r.h("pe"), 1)
                    r.add("tensor", mmd)
                w2_used[fidx2] = r.bump("pe", 1)
            for oc8 in range(8):
                oc = ocg * 8 + oc8

                def mmb2(t, oc=oc, oc8=oc8):
                    t.matmul(PB[oc8][:], b2s_sb[:, oc * 128:(oc + 1) * 128],
                             zbf[:, 0:TT], start=False, stop=False)
                    t.matmul(PB[oc8][:],
                             b2s_sb[:, (KC + oc) * 128:(KC + oc + 1) * 128],
                             zbf[:, TT:2 * TT], start=False, stop=True) \
                        .then_inc(r.h("pe"), 1)
                r.add("tensor", mmb2)
                down_mm[oc] = r.bump("pe", 1)
            for oc8 in range(8):
                oc = ocg * 8 + oc8
                gx = ti * KC + oc
                xbuf = x_sl[gx % 2]
                xsem = "xA" if gx % 2 == 0 else "xB"
                if gx >= 2:
                    r.wait("sync", "c", x_used[gx - 2])
                r.add("sync", lambda q, oc=oc, ti=ti, xbuf=xbuf, xsem=xsem: q.dma_start(
                    out=xbuf[:],
                    in_=ap["x1"][:, oc * TPC + ti * TT: oc * TPC + ti * TT + TT])
                    .then_inc(r.h(xsem), 16))
                xd = r.bump(xsem)
                r.wait("vector", "pe", down_mm[oc])
                r.wait("vector", xsem, xd)
                if prev_store:
                    r.wait("vector", "st", prev_store)
                vw(r.cur("c"))
                x_used[gx] = vadd(lambda v, oc=oc, oc8=oc8, xbuf=xbuf: v.tensor_add(
                    y_full[:, oc * TT:(oc + 1) * TT], PB[oc8][:], xbuf[:]))
            ybank_free = r.cur("c")
        y_done = r.cur("c")
        s12p = 0
        for oc in range(KC):
            r.wait("scalar", "c", y_done)
            if s12p:
                r.wait("scalar", "pe", s12p)
            sadd(lambda s, oc=oc: s.activation(
                yn_bf[:], y_full[:, oc * TT:(oc + 1) * TT], AF.Identity))
            sqd = sadd(lambda s, oc=oc: s.activation(
                sqn_bf[:], y_full[:, oc * TT:(oc + 1) * TT], AF.Square))
            r.wait("tensor", "ev", sqd)
            if oc == 0:
                r.wait("tensor", "c", onesbf2)
            r.add("tensor", lambda t, oc=oc: t.matmul(
                PB[5][0:1, 0:TT], ocbf[:], yn_bf[:],
                start=oc == 0, stop=oc == KC - 1))

            def mm2n(t, oc=oc):
                i = t.matmul(PB[6][0:1, 0:TT], ocbf[:], sqn_bf[:],
                             start=oc == 0, stop=oc == KC - 1)
                i.then_inc(r.h("pe"), 1)
            r.add("tensor", mm2n)
            s12p = r.bump("pe", 1)
        r.wait("scalar", "pe", s12p)
        r.wait("vector", "pe", s12p)
        emit_ln_epilogue(nc, r, lnt, y_full,
                         lambda oc: y_full[:, oc * TT:(oc + 1) * TT], PB, 5, 6)
        y_ln_done = r.cur("c")
        r.wait("sync", "c", y_ln_done)
        for oc in range(KC):
            r.add("sync", lambda q, oc=oc, ti=ti: q.dma_start(
                out=ap["x2"][:, oc * TPC + ti * TT: oc * TPC + ti * TT + TT],
                in_=y_full[:, oc * TT:(oc + 1) * TT]).then_inc(r.h("st"), 16))
            r.bump("st")
        prev_store = r.cur("st")
        y_ln_prev = y_ln_done
        x1bf_used[ti] = r.cur("pe")
    return {"x2_done": prev_store}


def prep_launch2b(inp, x2_by_core):
    qkv_w = np.asarray(inp["L1_attn_qkv_w"])
    gate_w = np.asarray(inp["L1_attn_gate_w"])
    qkv_wT = np.ascontiguousarray(qkv_w.T)
    wqk1 = np.empty((D, 2 * H1 * D1), np.float32)
    wv1 = np.empty((D, H1 * D1), np.float32)
    for h in range(H1):
        wqk1[:, 128 * h:128 * h + 64] = qkv_wT[:, 192 * h:192 * h + 64]
        wqk1[:, 128 * h + 64:128 * h + 128] = qkv_wT[:, 192 * h + 64:192 * h + 128]
        wv1[:, 64 * h:64 * h + 64] = qkv_wT[:, 192 * h + 128:192 * h + 192]
    wqk1_t = _oc_tiles(wqk1)
    wv1_t = _oc_tiles(wv1).reshape(16, 128, KC * 128)
    wv1_g = np.empty((4, 128, KC * 512), np.float32)
    for vc in range(4):
        for sub in range(4):
            for kc in range(KC):
                wv1_g[vc][:, kc * 512 + sub * 128:kc * 512 + (sub + 1) * 128] = \
                    wv1_t[vc * 4 + sub][:, kc * 128:(kc + 1) * 128]
    gwt = _oc_tiles(np.ascontiguousarray(gate_w.T))

    qd, kd, ddm, bd = _l1_decay()
    qd_e = np.ascontiguousarray(np.broadcast_to(qd[:, None, :], (H1, 64, BS1)))
    kd_e = np.ascontiguousarray(np.broadcast_to(kd[:, None, :], (H1, 64, BS1)))
    ddTf = ddm.transpose(0, 2, 1)
    ddT = np.ascontiguousarray(
        ddTf.reshape(H1, 2, 128, BS1).transpose(0, 2, 1, 3).reshape(H1, 128, 2 * BS1))
    ident = np.eye(128, dtype=np.float32)

    pc = []
    for c in range(8):
        d = {
            "wqk1": wqk1_t, "wv1": wv1_g, "gwt": gwt,
            "qd": qd_e, "kd": kd_e, "ddT": ddT, "ident": ident,
        }
        if x2_by_core[c] is not None:
            d["x2"] = x2_by_core[c]
        pc.append(d)
    return pc


def emit_l1a(nc, es, r, ap, sb, PB, AR, entry_st, ident_bf, ident_f32, pbf4=None):
    if pbf4 is None:
        pbf4 = PB[4].ap().bitcast(BF16)
    x_bf = AR.tile(128, KC * TPC, BF16)
    wbuf2 = [AR.tile(128, KC * 128, BF16) for i in range(2)]
    wvbuf = [AR.tile(128, KC * 512, BF16) for i in range(2)]
    out_bf = [AR.tile(128, TT, BF16) for i in range(2)]

    r.wait("gpsimd", "st", entry_st)
    r.wait("gpsimd", "pe", r.cur("pe"))
    r.wait("gpsimd", "c", r.cur("c"))
    r.wait("gpsimd", "ev", r.cur("ev"))
    r.add("gpsimd", lambda g: g.dma_start(out=x_bf[:], in_=ap["x2"])
          .then_inc(r.h("gin"), 16))
    r.bump("gin")
    gin_done = r.cur("gin")

    ev_done, st_done, pe_after = {}, {}, {}
    gidx = 0
    for kind, nocs, wsrc, dst in (("qk", 32, "wqk1", "qk1"),
                                  ("gate", 16, "gwt", "gate")):
        for oc in range(nocs):
            wbuf = wbuf2[gidx % 2]
            wsem = "w0" if gidx % 2 == 0 else "w1"
            if gidx >= 2:
                r.wait("gpsimd", "pe", pe_after[gidx - 2])
            r.add("gpsimd", lambda g, wsrc=wsrc, oc=oc, wbuf=wbuf, wsem=wsem:
                  g.dma_start(out=wbuf[:], in_=ap[wsrc][oc])
                  .then_inc(r.h(wsem), 16))
            w_done = r.bump(wsem)
            for ti in range(NTT):
                idx = gidx * NTT + ti
                pbuf = PB[idx % 2]
                r.wait("tensor", wsem, w_done)
                if idx >= 2:
                    r.wait("tensor", "ev", ev_done[idx - 2])
                if idx == 0:
                    r.wait("tensor", "gin", gin_done)
                for kc in range(KC):
                    def mm(t, oc=oc, ti=ti, kc=kc, pbuf=pbuf, wbuf=wbuf,
                           last=kc == KC - 1):
                        i = t.matmul(pbuf[:], wbuf[:, kc * 128:(kc + 1) * 128],
                                     x_bf[:, kc * TPC + ti * TT: kc * TPC + ti * TT + TT],
                                     start=kc == 0, stop=last)
                        if last:
                            i.then_inc(r.h("pe"), 1)
                    r.add("tensor", mm)
                mm_done = r.bump("pe", 1)
                obuf = out_bf[idx % 2]
                r.wait("scalar", "pe", mm_done)
                ssem = "s0" if idx % 2 == 0 else "s1"
                if idx >= 2:
                    r.wait("scalar", ssem, st_done[idx - 2])
                fn = AF.Silu if kind == "qk" else AF.Sigmoid
                r.add("scalar", lambda s, pbuf=pbuf, obuf=obuf, fn=fn:
                      s.activation(obuf[:], pbuf[:], fn).then_inc(r.h("ev"), 1))
                ev_done[idx] = r.bump("ev", 1)
                r.wait("sync", "ev", ev_done[idx])
                r.add("sync", lambda q, dst=dst, oc=oc, ti=ti, obuf=obuf, ssem=ssem:
                      q.dma_start(out=ap[dst][128 * oc:128 * (oc + 1),
                                              ti * TT:(ti + 1) * TT],
                                  in_=obuf[:]).then_inc(r.h(ssem), 16))
                st_done[idx] = r.bump(ssem)
            pe_after[gidx] = r.cur("pe")
            gidx += 1
    vev, vst, vpe = {}, {}, {}
    for vc in range(4):
        wbuf = wvbuf[vc % 2]
        wsem = "v0" if vc % 2 == 0 else "v1s"
        if vc >= 2:
            r.wait("gpsimd", "pe", vpe[vc - 2])
        r.add("gpsimd", lambda g, vc=vc, wbuf=wbuf, wsem=wsem: g.dma_start(
            out=wbuf[:], in_=ap["wv1"][vc]).then_inc(r.h(wsem), 16))
        w_done = r.bump(wsem)
        for tt in range(TPC // 128):
            idx = vc * (TPC // 128) + tt
            pbuf = PB[2 + idx % 2]
            r.wait("tensor", wsem, w_done)
            if idx >= 2:
                r.wait("tensor", "ev", vev[idx - 2])
            for kc in range(KC):
                def mmv(t, vc=vc, tt=tt, kc=kc, pbuf=pbuf, wbuf=wbuf,
                        last=kc == KC - 1):
                    i = t.matmul(pbuf[:],
                                 x_bf[:, kc * TPC + tt * 128: kc * TPC + (tt + 1) * 128],
                                 wbuf[:, kc * 512:(kc + 1) * 512],
                                 start=kc == 0, stop=last)
                    if last:
                        i.then_inc(r.h("pe"), 1)
                r.add("tensor", mmv)
            mm_done = r.bump("pe", 1)
            obuf = out_bf[idx % 2]
            r.wait("scalar", "pe", mm_done)
            ssem = "s0" if idx % 2 == 0 else "s1"
            if idx >= 2:
                r.wait("scalar", ssem, vst[idx - 2])
            else:
                r.wait("scalar", ssem, r.cur(ssem))
            r.add("scalar", lambda s, pbuf=pbuf, obuf=obuf:
                  s.activation(obuf[:], pbuf[:], AF.Silu).then_inc(r.h("ev"), 1))
            vev[idx] = r.bump("ev", 1)
            r.wait("sync", "ev", vev[idx])
            r.add("sync", lambda q, vc=vc, tt=tt, obuf=obuf, ssem=ssem: q.dma_start(
                out=ap["v1"][tt * 128:(tt + 1) * 128, vc * 512:(vc + 1) * 512],
                in_=obuf[:]).then_inc(r.h(ssem), 16))
            vst[idx] = r.bump(ssem)
        vpe[vc] = r.cur("pe")
    proj_s0 = r.cur("s0")
    proj_s1 = r.cur("s1")
    projections_pe = r.cur("pe")

    qtile = [AR.tile(64, CHUNK, BF16) for i in range(2)]
    ktile = [AR.tile(64, CHUNK, BF16) for i in range(2)]
    vtile = [AR.tile(128, 8 * 64, BF16) for i in range(2)]
    qd_sb = [AR.tile(64, BS1, F32) for i in range(2)]
    kd_sb = [AR.tile(64, BS1, F32) for i in range(2)]
    dd_sb = [AR.tile(128, 2 * BS1, F32) for i in range(2)]
    qp_st = AR.tile(64, CHUNK, BF16)
    op_st = AR.tile(64, CHUNK, F32)
    kp_bf = AR.tile(64, BS1, BF16)
    ktr = AR.tile(128, 2 * 64, BF16)
    p1_sb = AR.tile(128, 2 * BS1, BF16)
    s_f32 = AR.tile(64, 64, F32)
    s_bf = AR.tile(64, 64, BF16)
    cch_st = AR.tile(64, 64, F32)

    def vadd(fn):
        r.add("vector", lambda v, fn=fn: fn(v).then_inc(r.h("c"), 1))
        return r.bump("c", 1)

    r.wait("sync", "s0", proj_s0)
    r.wait("sync", "s1", proj_s1)
    hdec_used = {}
    qk_used = {}
    dd_done = {}
    for h in range(H1):
        hb = h % 2
        dsem = "dd0" if hb == 0 else "dd1"
        if h >= 2:
            r.wait("sync", "pe", hdec_used[h - 2])

        def ldd(q, h=h, hb=hb, dsem=dsem):
            q.dma_start(out=qd_sb[hb][:], in_=ap["qd"][h]).then_inc(r.h(dsem), 16)
            q.dma_start(out=kd_sb[hb][:], in_=ap["kd"][h]).then_inc(r.h(dsem), 16)
            q.dma_start(out=dd_sb[hb][:], in_=ap["ddT"][h]).then_inc(r.h(dsem), 16)
        r.add("sync", ldd)
        dd_done[h] = r.bump(dsem, 48)
        for chos in range(2):
            it = h * 2 + chos
            tb = it % 2
            qsem = "q0" if tb == 0 else "q1"
            vsem = "vb0" if tb == 0 else "vb1"
            if it >= 2:
                r.wait("sync", "pe", qk_used[it - 2])

            def ldq(q, h=h, chos=chos, tb=tb, qsem=qsem, vsem=vsem):
                q.dma_start(out=qtile[tb][:],
                            in_=ap["qk1"][128 * h:128 * h + 64,
                                          chos * CHUNK:(chos + 1) * CHUNK]) \
                    .then_inc(r.h(qsem), 16)
                q.dma_start(out=ktile[tb][:],
                            in_=ap["qk1"][128 * h + 64:128 * h + 128,
                                          chos * CHUNK:(chos + 1) * CHUNK]) \
                    .then_inc(r.h(qsem), 16)
                q.dma_start(out=vtile[tb][:].rearrange("p (sq f) -> p sq f", sq=8),
                            in_=ap["v1"][chos * CHUNK:(chos + 1) * CHUNK,
                                         64 * h:64 * (h + 1)]
                            .rearrange("(sq p) f -> p sq f", p=128)) \
                    .then_inc(r.h(vsem), 16)
            r.add("sync", ldq)
            q_done = r.bump(qsem, 32)
            v_done = r.bump(vsem)
            r.wait("vector", qsem, q_done)
            r.wait("vector", vsem, v_done)
            r.wait("vector", dsem, dd_done[h])
            if it >= 1:
                r.wait("vector", "st", qp_op_st)
            vw_c = vadd(lambda v: v.memset(s_f32[:], 0.0))
            sbf_c = vadd(lambda v: v.memset(s_bf[:], 0.0))
            if it == 0:
                p_prev = {0: 0, 1: 0}
                o_ev_c = 0
                s_upd_c = 0
            for blk in range(4):
                q_sl = qtile[tb][:, blk * BS1:(blk + 1) * BS1]
                k_sl = ktile[tb][:, blk * BS1:(blk + 1) * BS1]
                if blk >= 1:
                    r.wait("vector", "pe", inter_done)
                qp_c = vadd(lambda v, q_sl=q_sl, hb=hb, blk=blk: v.tensor_mul(
                    qp_st[:, blk * BS1:(blk + 1) * BS1], q_sl, qd_sb[hb][:]))
                kp_c = vadd(lambda v, k_sl=k_sl, hb=hb: v.tensor_mul(
                    kp_bf[:], k_sl, kd_sb[hb][:]))
                r.wait("tensor", "c", kp_c)
                if blk >= 1 or it >= 1:
                    r.wait("tensor", "c", ktr_ev if it + blk else 0)
                for ci in range(2):
                    def trk(t, ci=ci):
                        i = t.transpose(
                            pbf4[0:128, ci * 64:(ci + 1) * 64],
                            kp_bf[:, ci * 128:(ci + 1) * 128], ident_bf[0:64, 0:64])
                        if ci == 1:
                            i.then_inc(r.h("pe"), 1)
                    r.add("tensor", trk)
                ktr_pe = r.bump("pe", 1)
                r.wait("vector", "pe", ktr_pe)
                vw2 = r.cur("c")
                r.wait("vector", "c", vw2)
                ktr_ev = vadd(lambda v: v.tensor_copy(
                    ktr[:], pbf4[0:128, 0:128]))
                r.wait("tensor", "c", ktr_ev)
                if s_upd_c:
                    r.wait("tensor", "c", s_upd_c)
                for ci in range(2):
                    def mmc(t, ci=ci, blk=blk, tb=tb):
                        i = t.matmul(PB[2][0:64, 0:64],
                                     ktr[:, ci * 64:(ci + 1) * 64],
                                     vtile[tb][:, (2 * blk + ci) * 64:(2 * blk + ci + 1) * 64],
                                     start=ci == 0, stop=ci == 1)
                        if ci == 1:
                            i.then_inc(r.h("pe"), 1)
                    r.add("tensor", mmc)
                c_mm = r.bump("pe", 1)
                p_evs = []
                for ci in range(2):
                    bank = PB[ci]
                    if p_prev[ci]:
                        r.wait("tensor", "c", p_prev[ci])
                    def mma(t, ci=ci, bank=bank, k_sl=k_sl, q_sl=q_sl):
                        t.matmul(bank[0:128, 0:BS1],
                                 k_sl[:, ci * 128:(ci + 1) * 128], q_sl,
                                 start=True, stop=True).then_inc(r.h("pe"), 1)
                    r.add("tensor", mma)
                    aw_mm = r.bump("pe", 1)
                    r.wait("vector", "pe", aw_mm)
                    r.wait("vector", "c", r.cur("c"))
                    p_evs.append(vadd(lambda v, ci=ci, bank=bank, hb=hb: v.tensor_mul(
                        p1_sb[:, ci * BS1:(ci + 1) * BS1], bank[0:128, 0:BS1],
                        dd_sb[hb][:, ci * BS1:(ci + 1) * BS1])))
                p_prev = {0: p_evs[0], 1: p_evs[1]}
                r.wait("tensor", "c", p_evs[1])
                if o_ev_c:
                    r.wait("tensor", "c", o_ev_c)
                r.wait("tensor", "c", sbf_c)

                def mmav(t, blk=blk, tb=tb):
                    t.matmul(PB[3][0:64, 0:BS1],
                             vtile[tb][:, (2 * blk) * 64:(2 * blk + 1) * 64],
                             p1_sb[:, 0:BS1], start=True, stop=False)
                    t.matmul(PB[3][0:64, 0:BS1],
                             vtile[tb][:, (2 * blk + 1) * 64:(2 * blk + 2) * 64],
                             p1_sb[:, BS1:2 * BS1], start=False, stop=False)
                    t.matmul(PB[3][0:64, 0:BS1], s_bf[:],
                             qp_st[:, blk * BS1:(blk + 1) * BS1],
                             start=False, stop=True).then_inc(r.h("pe"), 1)
                r.add("tensor", mmav)
                inter_done = r.bump("pe", 1)
                r.wait("vector", "pe", inter_done)
                r.wait("vector", "c", r.cur("c"))
                o_ev_c = vadd(lambda v, blk=blk: v.tensor_copy(
                    op_st[:, blk * BS1:(blk + 1) * BS1], PB[3][0:64, 0:BS1]))
                r.wait("vector", "pe", c_mm)
                s_upd_c = vadd(lambda v, h=h: v.scalar_tensor_tensor(
                    s_f32[:], s_f32[:], float(BD1[h]), PB[2][0:64, 0:64],
                    ALU.mult, ALU.add))
                r.wait("vector", "c", s_upd_c)
                sbf_c = vadd(lambda v: v.tensor_copy(s_bf[:], s_f32[:]))
            qk_used[it] = r.cur("pe")
            hdec_used[h] = r.cur("pe")
            r.wait("sync", "c", sbf_c)
            r.wait("sync", "c", o_ev_c)

            def stq(q, h=h, chos=chos):
                q.dma_start(out=ap["qp"][64 * h:64 * (h + 1),
                                         chos * CHUNK:(chos + 1) * CHUNK],
                            in_=qp_st[:]).then_inc(r.h("st"), 16)
                q.dma_start(out=ap["Opart"][64 * h:64 * (h + 1),
                                            chos * CHUNK:(chos + 1) * CHUNK],
                            in_=op_st[:]).then_inc(r.h("st"), 16)
                q.dma_start(out=ap["Cch"][chos, h], in_=s_f32[:]) \
                    .then_inc(r.h("st"), 16)
            r.add("sync", stq)
            qp_op_st = r.bump("st", 48)

    return {"l1a_st": r.cur("st")}


def build_launch2b():
    nc = bass.Bass("TRN2", target_bir_lowering=False)
    ap = {
        "x2": nc.dram_tensor("x2", [128, KC * TPC], F32, kind="ExternalInput").ap(),
        "wqk1": nc.dram_tensor("wqk1", [32, 128, KC * 128], F32, kind="ExternalInput").ap(),
        "wv1": nc.dram_tensor("wv1", [4, 128, KC * 512], F32, kind="ExternalInput").ap(),
        "gwt": nc.dram_tensor("gwt", [16, 128, KC * 128], F32, kind="ExternalInput").ap(),
        "qd": nc.dram_tensor("qd", [H1, 64, BS1], F32, kind="ExternalInput").ap(),
        "kd": nc.dram_tensor("kd", [H1, 64, BS1], F32, kind="ExternalInput").ap(),
        "ddT": nc.dram_tensor("ddT", [H1, 128, 2 * BS1], F32, kind="ExternalInput").ap(),
        "ident": nc.dram_tensor("ident", [128, 128], F32, kind="ExternalInput").ap(),
        "qp": nc.dram_tensor("qp", [H1 * 64, TPC], BF16, kind="ExternalOutput").ap(),
        "Opart": nc.dram_tensor("Opart", [H1 * 64, TPC], F32, kind="ExternalOutput").ap(),
        "Cch": nc.dram_tensor("Cch", [2, H1, 64, 64], F32, kind="ExternalOutput").ap(),
        "gate": nc.dram_tensor("gate", [D, TPC], BF16, kind="ExternalOutput").ap(),
        "qk1": nc.dram_tensor("qk1", [2 * H1 * 64, TPC], BF16, kind="Internal").ap(),
        "v1": nc.dram_tensor("v1", [TPC, H1 * 64], BF16, kind="Internal").ap(),
    }

    es = ExitStack()
    with es:
        sb = lambda name, shape, dt: es.enter_context(nc.sbuf_tensor(name, shape, dt))
        PB = [es.enter_context(nc.psum_tensor(f"bank{i}", [128, 512], F32))
              for i in range(8)]
        r = Rec(nc, es)
        for sname in ("in", "pe", "ev", "rp", "st", "c", "gin",
                      "w0", "w1", "v0", "v1s", "s0", "s1",
                      "q0", "q1", "k0", "k1", "dd0", "dd1", "vb0", "vb1"):
            r.sem(sname)
        block = es.enter_context(nc.Block())

        x_bf = sb("x_bf", [128, KC * TPC], BF16)
        ident_bf = sb("ident_bf", [128, 128], BF16)
        wbuf2 = [sb(f"wb{i}", [128, KC * 128], BF16) for i in range(2)]
        wvbuf = [sb(f"wvb_{i}", [128, KC * 512], BF16) for i in range(2)]
        out_bf = [sb(f"obf{i}", [128, TT], BF16) for i in range(2)]

        r.add("gpsimd", lambda g: g.dma_start(out=x_bf[:], in_=ap["x2"])
              .then_inc(r.h("gin"), 16))
        r.bump("gin")
        r.add("gpsimd", lambda g: g.dma_start(out=ident_bf[:], in_=ap["ident"])
              .then_inc(r.h("gin"), 16))
        r.bump("gin")
        gin_done = r.cur("gin")

        ev_done, st_done, pe_after = {}, {}, {}
        gidx = 0
        for kind, nocs, wsrc, dst in (("qk", 32, "wqk1", "qk1"),
                                      ("gate", 16, "gwt", "gate")):
            for oc in range(nocs):
                wbuf = wbuf2[gidx % 2]
                wsem = "w0" if gidx % 2 == 0 else "w1"
                if gidx >= 2:
                    r.wait("gpsimd", "pe", pe_after[gidx - 2])
                r.add("gpsimd", lambda g, wsrc=wsrc, oc=oc, wbuf=wbuf, wsem=wsem:
                      g.dma_start(out=wbuf[:], in_=ap[wsrc][oc])
                      .then_inc(r.h(wsem), 16))
                w_done = r.bump(wsem)
                for ti in range(NTT):
                    idx = gidx * NTT + ti
                    pbuf = PB[idx % 2]
                    r.wait("tensor", wsem, w_done)
                    if idx >= 2:
                        r.wait("tensor", "ev", ev_done[idx - 2])
                    if idx == 0:
                        r.wait("tensor", "gin", gin_done)
                    for kc in range(KC):
                        def mm(t, oc=oc, ti=ti, kc=kc, pbuf=pbuf, wbuf=wbuf,
                               last=kc == KC - 1):
                            i = t.matmul(pbuf[:], wbuf[:, kc * 128:(kc + 1) * 128],
                                         x_bf[:, kc * TPC + ti * TT: kc * TPC + ti * TT + TT],
                                         start=kc == 0, stop=last)
                            if last:
                                i.then_inc(r.h("pe"), 1)
                        r.add("tensor", mm)
                    mm_done = r.bump("pe", 1)
                    obuf = out_bf[idx % 2]
                    r.wait("scalar", "pe", mm_done)
                    ssem = "s0" if idx % 2 == 0 else "s1"
                    if idx >= 2:
                        r.wait("scalar", ssem, st_done[idx - 2])
                    fn = AF.Silu if kind == "qk" else AF.Sigmoid
                    r.add("scalar", lambda s, pbuf=pbuf, obuf=obuf, fn=fn:
                          s.activation(obuf[:], pbuf[:], fn).then_inc(r.h("ev"), 1))
                    ev_done[idx] = r.bump("ev", 1)
                    r.wait("sync", "ev", ev_done[idx])
                    r.add("sync", lambda q, dst=dst, oc=oc, ti=ti, obuf=obuf, ssem=ssem:
                          q.dma_start(out=ap[dst][128 * oc:128 * (oc + 1),
                                                  ti * TT:(ti + 1) * TT],
                                      in_=obuf[:]).then_inc(r.h(ssem), 16))
                    st_done[idx] = r.bump(ssem)
                pe_after[gidx] = r.cur("pe")
                gidx += 1
        vev, vst, vpe = {}, {}, {}
        for vc in range(4):
            wbuf = wvbuf[vc % 2]
            wsem = "v0" if vc % 2 == 0 else "v1s"
            if vc >= 2:
                r.wait("gpsimd", "pe", vpe[vc - 2])
            r.add("gpsimd", lambda g, vc=vc, wbuf=wbuf, wsem=wsem: g.dma_start(
                out=wbuf[:], in_=ap["wv1"][vc]).then_inc(r.h(wsem), 16))
            w_done = r.bump(wsem)
            for tt in range(TPC // 128):
                idx = vc * (TPC // 128) + tt
                pbuf = PB[2 + idx % 2]
                r.wait("tensor", wsem, w_done)
                if idx >= 2:
                    r.wait("tensor", "ev", vev[idx - 2])
                for kc in range(KC):
                    def mmv(t, vc=vc, tt=tt, kc=kc, pbuf=pbuf, wbuf=wbuf,
                            last=kc == KC - 1):
                        i = t.matmul(pbuf[:],
                                     x_bf[:, kc * TPC + tt * 128: kc * TPC + (tt + 1) * 128],
                                     wbuf[:, kc * 512:(kc + 1) * 512],
                                     start=kc == 0, stop=last)
                        if last:
                            i.then_inc(r.h("pe"), 1)
                    r.add("tensor", mmv)
                mm_done = r.bump("pe", 1)
                obuf = out_bf[idx % 2]
                r.wait("scalar", "pe", mm_done)
                ssem = "s0" if idx % 2 == 0 else "s1"
                if idx >= 2:
                    r.wait("scalar", ssem, vst[idx - 2])
                else:
                    r.wait("scalar", ssem, r.cur(ssem))
                r.add("scalar", lambda s, pbuf=pbuf, obuf=obuf:
                      s.activation(obuf[:], pbuf[:], AF.Silu).then_inc(r.h("ev"), 1))
                vev[idx] = r.bump("ev", 1)
                r.wait("sync", "ev", vev[idx])
                r.add("sync", lambda q, vc=vc, tt=tt, obuf=obuf, ssem=ssem: q.dma_start(
                    out=ap["v1"][tt * 128:(tt + 1) * 128, vc * 512:(vc + 1) * 512],
                    in_=obuf[:]).then_inc(r.h(ssem), 16))
                vst[idx] = r.bump(ssem)
            vpe[vc] = r.cur("pe")
        proj_s0 = r.cur("s0")
        proj_s1 = r.cur("s1")
        projections_pe = r.cur("pe")

        qtile = [sb(f"qt{i}", [64, CHUNK], BF16) for i in range(2)]
        ktile = [sb(f"kt{i}", [64, CHUNK], BF16) for i in range(2)]
        vtile = [sb(f"vt{i}", [128, 8 * 64], BF16) for i in range(2)]
        qd_sb = [sb(f"qd{i}", [64, BS1], F32) for i in range(2)]
        kd_sb = [sb(f"kd{i}", [64, BS1], F32) for i in range(2)]
        dd_sb = [sb(f"dd{i}", [128, 2 * BS1], F32) for i in range(2)]
        qp_st = sb("qp_st", [64, CHUNK], BF16)
        op_st = sb("op_st", [64, CHUNK], F32)
        kp_bf = sb("kp_bf", [64, BS1], BF16)
        ktr = sb("ktr", [128, 2 * 64], BF16)
        p1_sb = sb("p1_sb", [128, 2 * BS1], BF16)
        s_f32 = sb("s_f32", [64, 64], F32)
        s_bf = sb("s_bf", [64, 64], BF16)
        cch_st = sb("cch_st", [64, 64], F32)

        def vadd(fn):
            r.add("vector", lambda v, fn=fn: fn(v).then_inc(r.h("c"), 1))
            return r.bump("c", 1)

        r.wait("sync", "s0", proj_s0)
        r.wait("sync", "s1", proj_s1)
        hdec_used = {}
        qk_used = {}
        dd_done = {}
        for h in range(H1):
            hb = h % 2
            dsem = "dd0" if hb == 0 else "dd1"
            if h >= 2:
                r.wait("sync", "pe", hdec_used[h - 2])

            def ldd(q, h=h, hb=hb, dsem=dsem):
                q.dma_start(out=qd_sb[hb][:], in_=ap["qd"][h]).then_inc(r.h(dsem), 16)
                q.dma_start(out=kd_sb[hb][:], in_=ap["kd"][h]).then_inc(r.h(dsem), 16)
                q.dma_start(out=dd_sb[hb][:], in_=ap["ddT"][h]).then_inc(r.h(dsem), 16)
            r.add("sync", ldd)
            dd_done[h] = r.bump(dsem, 48)
            for chos in range(2):
                it = h * 2 + chos
                tb = it % 2
                qsem = "q0" if tb == 0 else "q1"
                vsem = "vb0" if tb == 0 else "vb1"
                if it >= 2:
                    r.wait("sync", "pe", qk_used[it - 2])

                def ldq(q, h=h, chos=chos, tb=tb, qsem=qsem, vsem=vsem):
                    q.dma_start(out=qtile[tb][:],
                                in_=ap["qk1"][128 * h:128 * h + 64,
                                              chos * CHUNK:(chos + 1) * CHUNK]) \
                        .then_inc(r.h(qsem), 16)
                    q.dma_start(out=ktile[tb][:],
                                in_=ap["qk1"][128 * h + 64:128 * h + 128,
                                              chos * CHUNK:(chos + 1) * CHUNK]) \
                        .then_inc(r.h(qsem), 16)
                    q.dma_start(out=vtile[tb][:].rearrange("p (sq f) -> p sq f", sq=8),
                                in_=ap["v1"][chos * CHUNK:(chos + 1) * CHUNK,
                                             64 * h:64 * (h + 1)]
                                .rearrange("(sq p) f -> p sq f", p=128)) \
                        .then_inc(r.h(vsem), 16)
                r.add("sync", ldq)
                q_done = r.bump(qsem, 32)
                v_done = r.bump(vsem)
                r.wait("vector", qsem, q_done)
                r.wait("vector", vsem, v_done)
                r.wait("vector", dsem, dd_done[h])
                if it >= 1:
                    r.wait("vector", "st", qp_op_st)
                vw_c = vadd(lambda v: v.memset(s_f32[:], 0.0))
                sbf_c = vadd(lambda v: v.memset(s_bf[:], 0.0))
                if it == 0:
                    p_prev = {0: 0, 1: 0}
                    o_ev_c = 0
                    s_upd_c = 0
                for blk in range(4):
                    q_sl = qtile[tb][:, blk * BS1:(blk + 1) * BS1]
                    k_sl = ktile[tb][:, blk * BS1:(blk + 1) * BS1]
                    if blk >= 1:
                        r.wait("vector", "pe", inter_done)
                    qp_c = vadd(lambda v, q_sl=q_sl, hb=hb, blk=blk: v.tensor_mul(
                        qp_st[:, blk * BS1:(blk + 1) * BS1], q_sl, qd_sb[hb][:]))
                    kp_c = vadd(lambda v, k_sl=k_sl, hb=hb: v.tensor_mul(
                        kp_bf[:], k_sl, kd_sb[hb][:]))
                    r.wait("tensor", "c", kp_c)
                    if blk >= 1 or it >= 1:
                        r.wait("tensor", "c", ktr_ev if it + blk else 0)
                    for ci in range(2):
                        def trk(t, ci=ci):
                            i = t.transpose(
                                PB[4].ap().bitcast(BF16)[0:128, ci * 64:(ci + 1) * 64],
                                kp_bf[:, ci * 128:(ci + 1) * 128], ident_bf[0:64, 0:64])
                            if ci == 1:
                                i.then_inc(r.h("pe"), 1)
                        r.add("tensor", trk)
                    ktr_pe = r.bump("pe", 1)
                    r.wait("vector", "pe", ktr_pe)
                    vw2 = r.cur("c")
                    r.wait("vector", "c", vw2)
                    ktr_ev = vadd(lambda v: v.tensor_copy(
                        ktr[:], PB[4].ap().bitcast(BF16)[0:128, 0:128]))
                    r.wait("tensor", "c", ktr_ev)
                    if s_upd_c:
                        r.wait("tensor", "c", s_upd_c)
                    for ci in range(2):
                        def mmc(t, ci=ci, blk=blk, tb=tb):
                            i = t.matmul(PB[2][0:64, 0:64],
                                         ktr[:, ci * 64:(ci + 1) * 64],
                                         vtile[tb][:, (2 * blk + ci) * 64:(2 * blk + ci + 1) * 64],
                                         start=ci == 0, stop=ci == 1)
                            if ci == 1:
                                i.then_inc(r.h("pe"), 1)
                        r.add("tensor", mmc)
                    c_mm = r.bump("pe", 1)
                    p_evs = []
                    for ci in range(2):
                        bank = PB[ci]
                        if p_prev[ci]:
                            r.wait("tensor", "c", p_prev[ci])
                        def mma(t, ci=ci, bank=bank, k_sl=k_sl, q_sl=q_sl):
                            t.matmul(bank[0:128, 0:BS1],
                                     k_sl[:, ci * 128:(ci + 1) * 128], q_sl,
                                     start=True, stop=True).then_inc(r.h("pe"), 1)
                        r.add("tensor", mma)
                        aw_mm = r.bump("pe", 1)
                        r.wait("vector", "pe", aw_mm)
                        r.wait("vector", "c", r.cur("c"))
                        p_evs.append(vadd(lambda v, ci=ci, bank=bank, hb=hb: v.tensor_mul(
                            p1_sb[:, ci * BS1:(ci + 1) * BS1], bank[0:128, 0:BS1],
                            dd_sb[hb][:, ci * BS1:(ci + 1) * BS1])))
                    p_prev = {0: p_evs[0], 1: p_evs[1]}
                    r.wait("tensor", "c", p_evs[1])
                    if o_ev_c:
                        r.wait("tensor", "c", o_ev_c)
                    r.wait("tensor", "c", sbf_c)

                    def mmav(t, blk=blk, tb=tb):
                        t.matmul(PB[3][0:64, 0:BS1],
                                 vtile[tb][:, (2 * blk) * 64:(2 * blk + 1) * 64],
                                 p1_sb[:, 0:BS1], start=True, stop=False)
                        t.matmul(PB[3][0:64, 0:BS1],
                                 vtile[tb][:, (2 * blk + 1) * 64:(2 * blk + 2) * 64],
                                 p1_sb[:, BS1:2 * BS1], start=False, stop=False)
                        t.matmul(PB[3][0:64, 0:BS1], s_bf[:],
                                 qp_st[:, blk * BS1:(blk + 1) * BS1],
                                 start=False, stop=True).then_inc(r.h("pe"), 1)
                    r.add("tensor", mmav)
                    inter_done = r.bump("pe", 1)
                    r.wait("vector", "pe", inter_done)
                    r.wait("vector", "c", r.cur("c"))
                    o_ev_c = vadd(lambda v, blk=blk: v.tensor_copy(
                        op_st[:, blk * BS1:(blk + 1) * BS1], PB[3][0:64, 0:BS1]))
                    r.wait("vector", "pe", c_mm)
                    s_upd_c = vadd(lambda v, h=h: v.scalar_tensor_tensor(
                        s_f32[:], s_f32[:], float(BD1[h]), PB[2][0:64, 0:64],
                        ALU.mult, ALU.add))
                    r.wait("vector", "c", s_upd_c)
                    sbf_c = vadd(lambda v: v.tensor_copy(s_bf[:], s_f32[:]))
                qk_used[it] = r.cur("pe")
                hdec_used[h] = r.cur("pe")
                r.wait("sync", "c", sbf_c)
                r.wait("sync", "c", o_ev_c)

                def stq(q, h=h, chos=chos):
                    q.dma_start(out=ap["qp"][64 * h:64 * (h + 1),
                                             chos * CHUNK:(chos + 1) * CHUNK],
                                in_=qp_st[:]).then_inc(r.h("st"), 16)
                    q.dma_start(out=ap["Opart"][64 * h:64 * (h + 1),
                                                chos * CHUNK:(chos + 1) * CHUNK],
                                in_=op_st[:]).then_inc(r.h("st"), 16)
                    q.dma_start(out=ap["Cch"][chos, h], in_=s_f32[:]) \
                        .then_inc(r.h("st"), 16)
                r.add("sync", stq)
                qp_op_st = r.bump("st", 48)
        r.wait("sync", "st", r.cur("st"))
        r.run(block)
    return nc


BD1 = _l1_decay()[3]


def prep_launch3(inp, l2out, l2bout):
    out_w = np.asarray(inp["L1_attn_out_w"])
    norm_w = np.asarray(inp["L1_attn_norm_w"])
    wq_gate = np.asarray(inp["L1_ffn_wq_gate"])
    s_gate = np.asarray(inp["L1_ffn_s_gate"])
    wq_up = np.asarray(inp["L1_ffn_wq_up"])
    s_up = np.asarray(inp["L1_ffn_s_up"])
    wq_down = np.asarray(inp["L1_ffn_wq_down"])
    s_down = np.asarray(inp["L1_ffn_s_down"])

    outw_t = _oc_tiles(np.ascontiguousarray(out_w.T))
    wgT = _oc_tiles(np.ascontiguousarray(wq_gate.T.astype(np.int32)))
    wuT = _oc_tiles(np.ascontiguousarray(wq_up.T.astype(np.int32)))
    wdT = np.ascontiguousarray(wq_down.T.astype(np.int32))
    wdt_oc = _oc_tiles(wdT)
    wdfc = np.zeros((NFC, 128, KC * 128), np.int32)
    for fc in range(NFC):
        for oc in range(KC):
            wdfc[fc][:, oc * 128:(oc + 1) * 128] = wdt_oc[oc][:, fc * 128:(fc + 1) * 128]

    _, _, _, bd = _l1_decay()
    CgT = np.zeros((2, 64, 8 * H1 * 64), np.float32)
    for cc in range(8):
        b = cc // 4
        for chos in range(2):
            absj = _chunks_of(cc)[chos]
            for h in range(H1):
                CgT[b][:, (absj * H1 + h) * 64:(absj * H1 + h + 1) * 64] = \
                    l2bout[cc]["Cch"][chos, h]
    pc = []
    for c in range(8):
        b = c // 4
        wcols = np.zeros((64, 2 * 8 * H1), np.float32)
        for oc2 in range(2):
            C = _chunks_of(c)[oc2]
            for j in range(8):
                if j < C:
                    for h in range(H1):
                        wcols[:, oc2 * 8 * H1 + j * H1 + h] = bd[h] ** (4 * (C - 1 - j))
        pc.append({
            "x2": l2out[c]["x2"],
            "Opart": l2bout[c]["Opart"],
            "qp": l2bout[c]["qp"],
            "gate": l2bout[c]["gate"],
            "Cg": CgT[b],
            "wcols": wcols,
            "nwp": _cols2d(norm_w),
            "outw": outw_t,
            "wgT": wgT, "wuT": wuT, "wdfc": wdfc,
            "sg": _cols2d(s_gate), "su": _cols2d(s_up), "sd": _cols2d(s_down),
        })
    return pc


def build_launch3():
    nc = bass.Bass("TRN2", target_bir_lowering=False)
    ap = {
        "x2": nc.dram_tensor("x2", [128, KC * TPC], F32, kind="ExternalInput").ap(),
        "Opart": nc.dram_tensor("Opart", [H1 * 64, TPC], F32, kind="ExternalInput").ap(),
        "qp": nc.dram_tensor("qp", [H1 * 64, TPC], BF16, kind="ExternalInput").ap(),
        "gate": nc.dram_tensor("gate", [D, TPC], BF16, kind="ExternalInput").ap(),
        "Cg": nc.dram_tensor("Cg", [64, 8 * H1 * 64], F32, kind="ExternalInput").ap(),
        "wcols": nc.dram_tensor("wcols", [64, 2 * 8 * H1], F32, kind="ExternalInput").ap(),
        "nwp": nc.dram_tensor("nwp", [128, KC], F32, kind="ExternalInput").ap(),
        "outw": nc.dram_tensor("outw", [KC, 128, KC * 128], F32, kind="ExternalInput").ap(),
        "wgT": nc.dram_tensor("wgT", [NFC, 128, KC * 128], I32, kind="ExternalInput").ap(),
        "wuT": nc.dram_tensor("wuT", [NFC, 128, KC * 128], I32, kind="ExternalInput").ap(),
        "wdfc": nc.dram_tensor("wdfc", [NFC, 128, KC * 128], I32, kind="ExternalInput").ap(),
        "sg": nc.dram_tensor("sg", [128, NFC], F32, kind="ExternalInput").ap(),
        "su": nc.dram_tensor("su", [128, NFC], F32, kind="ExternalInput").ap(),
        "sd": nc.dram_tensor("sd", [128, KC], F32, kind="ExternalInput").ap(),
        "out": nc.dram_tensor("out", [128, KC * TPC], F32, kind="ExternalOutput").ap(),
        "x3": nc.dram_tensor("x3", [128, KC * TPC], F32, kind="Internal").ap(),
        "gu_d": nc.dram_tensor("gu_d", [F0, TPC], BF16, kind="Internal").ap(),
    }

    es = ExitStack()
    with es:
        sb = lambda name, shape, dt: es.enter_context(nc.sbuf_tensor(name, shape, dt))
        PB = [es.enter_context(nc.psum_tensor(f"bank{i}", [128, 512], F32))
              for i in range(8)]
        AR = Arena(nc, es, "arena", 168 * 1024)
        r = Rec(nc, es)
        for sname in ("in", "pe", "ev", "rp", "st", "c", "gin",
                      "oA", "oB", "qA", "qB", "gA", "gB", "wA", "wB",
                      "xA", "xB", "fA", "fB", "uA", "uB", "dA", "dB",
                      "x3A", "x3B", "r3A", "r3B"):
            r.sem(sname)
        block = es.enter_context(nc.Block())

        ones_row = sb("ones_row", [1, 128], F32)
        ones_col = sb("ones_col", [128, 1], F32)
        ones_col_bf = sb("ones_cbf", [128, 1], BF16)
        nw_sb = sb("nw_sb", [128, KC], F32)
        sg_sb = sb("sg_sb", [128, NFC], F32)
        su_sb = sb("su_sb", [128, NFC], F32)
        sd_sb = sb("sd_sb", [128, KC], F32)
        lnt = {
            "ones_row": ones_row,
            "m_row": sb("m_row", [1, TT], F32), "m2_row": sb("m2_row", [1, TT], F32),
            "var_row": sb("var_row", [1, TT], F32), "rstd_row": sb("rstd_row", [1, TT], F32),
            "mr_row": sb("mr_row", [1, TT], F32), "tmp": sb("tmpt", [128, TT], F32),
            "eps_col": sb("eps_col", [1, 1], F32),
        }

        def vadd(fn):
            r.add("vector", lambda v, fn=fn: fn(v).then_inc(r.h("c"), 1))
            return r.bump("c", 1)

        def sadd(fn):
            r.add("scalar", lambda s, fn=fn: fn(s).then_inc(r.h("ev"), 1))
            return r.bump("ev", 1)

        r.add("vector", lambda v: v.memset(ones_row[:], 1.0))
        r.add("vector", lambda v: v.memset(ones_col[:], 1.0))
        r.add("vector", lambda v: v.memset(lnt["eps_col"][:], 1e-5))
        ones_done = vadd(lambda v: v.memset(ones_col_bf[:], 1.0))
        for t, dst in (("nwp", nw_sb), ("sg", sg_sb), ("su", su_sb), ("sd", sd_sb)):
            r.add("sync", lambda q, t=t, dst=dst: q.dma_start(
                out=dst[:], in_=ap[t]).then_inc(r.h("in"), 16))
            r.bump("in")
        const_in = r.cur("in")

        _, _, _, bd = _l1_decay()

        m0 = AR.mark()
        cg_sb = AR.tile(128, 8 * H1 * 64, F32)
        wc_sb = sb("wc_sb", [128, 2 * 8 * H1], F32)
        accd = sb("accd", [128, 64], F32)
        inc_sb = AR.tile(128, 2 * KC * 4 * 128, BF16)
        r.add("sync", lambda q: q.dma_start(out=cg_sb[0:64, :], in_=ap["Cg"])
              .then_inc(r.h("in"), 16))
        r.bump("in")
        r.add("sync", lambda q: q.dma_start(out=cg_sb[64:128, :], in_=ap["Cg"])
              .then_inc(r.h("in"), 16))
        r.bump("in")
        r.add("vector", lambda v: v.memset(inc_sb[:], 0.0).then_inc(r.h("c"), 1))
        r.bump("c", 1)
        r.add("sync", lambda q: q.dma_start(out=wc_sb[0:64, :], in_=ap["wcols"])
              .then_inc(r.h("in"), 16))
        r.bump("in")
        r.add("sync", lambda q: q.dma_start(out=wc_sb[64:128, :], in_=ap["wcols"])
              .then_inc(r.h("in"), 16))
        cg_in = r.bump("in")
        r.wait("vector", "in", cg_in)
        for oc2 in range(2):
            for h in range(H1):
                sub = h % 2
                arow = accd[64 * sub:64 * sub + 64, :]
                for j in range(8):
                    col = oc2 * 8 * H1 + j * H1 + h
                    src = cg_sb[64 * sub:64 * sub + 64,
                                (j * H1 + h) * 64:(j * H1 + h + 1) * 64]
                    scl_ap = wc_sb[64 * sub:64 * sub + 64, col:col + 1]
                    if j == 0:
                        vadd(lambda v, src=src, scl_ap=scl_ap, arow=arow:
                             v.tensor_scalar_mul(arow, src, scl_ap))
                    else:
                        vadd(lambda v, src=src, scl_ap=scl_ap, arow=arow:
                             v.scalar_tensor_tensor(
                                 arow, src, scl_ap, arow, ALU.mult, ALU.add))
                    r.wait("vector", "c", r.cur("c"))
                for il in range(4):
                    hp = h // 2
                    off = ((oc2 * KC + hp) * 4 + il) * 128 + sub * 64
                    vadd(lambda v, off=off, h=h, il=il, sub=sub, arow=arow:
                         v.tensor_scalar_mul(
                             inc_sb[64 * sub:64 * sub + 64, off:off + 64],
                             arow, float(bd[h] ** il)))
                    r.wait("vector", "c", r.cur("c"))
        inc_done = r.cur("c")

        attn_pk = AR.tile(128, KC * BS1, F32)
        attng = AR.tile(128, KC * BS1, BF16)
        sq64 = AR.tile(128, BS1, BF16)
        ybf256 = AR.tile(128, BS1, BF16)
        sqbf256 = AR.tile(128, BS1, BF16)
        y256 = AR.tile(128, KC * BS1, F32)
        op_t = [AR.tile(128, BS1, F32) for _ in range(2)]
        qp_t = [AR.tile(128, BS1, BF16) for _ in range(2)]
        gate16 = AR.tile(128, KC * BS1, BF16)
        ow_t = [AR.tile(128, KC * 128, BF16) for _ in range(2)]
        x2_t = [AR.tile(128, BS1, F32) for _ in range(2)]
        lnt256 = {
            "ones_row": ones_row,
            "m_row": sb("m_row2", [1, BS1], F32), "m2_row": sb("m2_row2", [1, BS1], F32),
            "var_row": sb("var_row2", [1, BS1], F32),
            "rstd_row": sb("rstd_row2", [1, BS1], F32),
            "mr_row": sb("mr_row2", [1, BS1], F32), "tmp": sb("tmpt2", [128, BS1], F32),
            "eps_col": lnt["eps_col"],
        }
        hp_used, x3st = {}, {}
        prev_x3_st = 0
        prev_ow = {}
        x2_used, y_prev = {}, {}
        gate_used = {}
        for chos in range(2):
            for blk in range(4):
                tix = chos * 4 + blk
                tok0 = chos * CHUNK + blk * BS1
                s2_mm = 0
                gsem = "gA" if tix % 2 == 0 else "gB"
                if tix >= 2:
                    r.wait("sync", "c", gate_used[tix - 2])
                r.add("sync", lambda q, tok0=tok0, gsem=gsem: q.dma_start(
                    out=gate16[:].rearrange("p (a t) -> p a t", a=KC),
                    in_=ap["gate"].rearrange("(a p) t -> p a t", p=128)
                    [:, :, tok0:tok0 + BS1]).then_inc(r.h(gsem), 16))
                g_ld = r.bump(gsem)
                for hp in range(KC):
                    it = tix * KC + hp
                    ob, qb = op_t[it % 2], qp_t[it % 2]
                    osem = "oA" if it % 2 == 0 else "oB"
                    qsem = "qA" if it % 2 == 0 else "qB"
                    if it >= 2:
                        r.wait("sync", "c", hp_used[it - 2])

                    def ldh(q, hp=hp, tok0=tok0, ob=ob, qb=qb,
                            osem=osem, qsem=qsem):
                        q.dma_start(out=ob[:], in_=ap["Opart"][128 * hp:128 * (hp + 1),
                                                               tok0:tok0 + BS1]) \
                            .then_inc(r.h(osem), 16)
                        q.dma_start(out=qb[:], in_=ap["qp"][128 * hp:128 * (hp + 1),
                                                            tok0:tok0 + BS1]) \
                            .then_inc(r.h(qsem), 16)
                    r.add("sync", ldh)
                    o_ld = r.bump(osem)
                    q_ld = r.bump(qsem)
                    r.wait("tensor", qsem, q_ld)
                    if it == 0:
                        r.wait("tensor", "c", inc_done)
                    if it >= 1:
                        r.wait("tensor", "c", attn_ev)
                    off = ((chos * KC + hp) * 4 + blk) * 128

                    def mmc(t, off=off, qb=qb):
                        t.matmul(PB[0][:, 0:BS1], inc_sb[:, off:off + 128],
                                 qb[:], start=True, stop=True) \
                            .then_inc(r.h("pe"), 1)
                    r.add("tensor", mmc)
                    corr_mm = r.bump("pe", 1)
                    r.wait("vector", "pe", corr_mm)
                    r.wait("vector", osem, o_ld)
                    r.wait("vector", "c", r.cur("c"))
                    attn_ev = vadd(lambda v, hp=hp, ob=ob: v.tensor_add(
                        attn_pk[:, hp * BS1:(hp + 1) * BS1], ob[:], PB[0][:, 0:BS1]))
                    r.wait("scalar", "c", attn_ev)
                    if s2_mm:
                        r.wait("scalar", "pe", s2_mm)
                    sqe = sadd(lambda s, hp=hp: s.activation(
                        sq64[:], attn_pk[:, hp * BS1:(hp + 1) * BS1], AF.Square))
                    r.wait("tensor", "ev", sqe)
                    if it == 0:
                        r.wait("tensor", "c", ones_done)

                    def mms(t, hp=hp, tix=tix):
                        t.matmul(PB[1][0:1, 0:BS1], ones_col_bf[:], sq64[:],
                                 start=hp == 0, stop=hp == KC - 1) \
                            .then_inc(r.h("pe"), 1)
                    r.add("tensor", mms)
                    s2_mm = r.bump("pe", 1)
                    hp_used[it] = attn_ev
                r.wait("scalar", "pe", s2_mm)
                rr1 = sadd(lambda s: s.activation(
                    lnt256["var_row"][:], PB[1][0:1, 0:BS1], AF.Identity,
                    scale=1.0 / D))
                r.wait("scalar", "ev", rr1)
                rr2 = sadd(lambda s: s.activation(
                    lnt256["var_row"][:], lnt256["var_row"][:], AF.Sqrt,
                    bias=lnt["eps_col"][:]))
                r.wait("vector", "ev", rr2)
                r.wait("vector", "c", r.cur("c"))
                rr3 = vadd(lambda v: v.reciprocal(
                    lnt256["rstd_row"][:], lnt256["var_row"][:]))
                r.wait("tensor", "c", rr3)
                r.add("tensor", lambda t: t.matmul(
                    PB[1][:, 0:BS1], ones_row[:], lnt256["rstd_row"][:],
                    start=True, stop=True).then_inc(r.h("pe"), 1))
                rbc = r.bump("pe", 1)
                r.wait("vector", "pe", rbc)
                ag_ev = 0
                for hp in range(KC):
                    r.wait("vector", "c", r.cur("c"))
                    t1 = vadd(lambda v, hp=hp: v.tensor_mul(
                        lnt256["tmp"][:], attn_pk[:, hp * BS1:(hp + 1) * BS1],
                        PB[1][:, 0:BS1]))
                    r.wait("vector", "c", t1)
                    if hp == 0:
                        r.wait("vector", gsem, g_ld)
                    ag_ev = vadd(lambda v, hp=hp: v.scalar_tensor_tensor(
                        attng[:, hp * BS1:(hp + 1) * BS1], lnt256["tmp"][:],
                        nw_sb[:, hp:hp + 1],
                        gate16[:, hp * BS1:(hp + 1) * BS1], ALU.mult, ALU.mult))
                gate_used[tix] = ag_ev
                y_evs = 0
                s12_mm = 0
                for oc in range(KC):
                    it2 = tix * KC + oc
                    wb = ow_t[it2 % 2]
                    wsem = "wA" if it2 % 2 == 0 else "wB"
                    if it2 >= 2:
                        r.wait("gpsimd", "pe", prev_ow[it2 - 2])
                    r.add("gpsimd", lambda g, oc=oc, wb=wb, wsem=wsem: g.dma_start(
                        out=wb[:], in_=ap["outw"][oc]).then_inc(r.h(wsem), 16))
                    owl = r.bump(wsem)
                    xb = x2_t[it2 % 2]
                    xsem = "xA" if it2 % 2 == 0 else "xB"
                    if it2 >= 2:
                        r.wait("sync", "c", x2_used[it2 - 2])
                    r.add("sync", lambda q, oc=oc, tok0=tok0, xb=xb, xsem=xsem:
                          q.dma_start(out=xb[:],
                                      in_=ap["x2"][:, oc * TPC + tok0: oc * TPC + tok0 + BS1])
                          .then_inc(r.h(xsem), 16))
                    x2l = r.bump(xsem)
                    bank = PB[2 + it2 % 2]
                    r.wait("tensor", wsem, owl)
                    r.wait("tensor", "c", ag_ev)
                    if it2 >= 2:
                        r.wait("tensor", "c", y_prev.get(it2 - 2, 0))
                    for kc in range(KC):
                        def mmo(t, oc=oc, kc=kc, wb=wb, bank=bank,
                                last=kc == KC - 1):
                            i = t.matmul(bank[:, 0:BS1],
                                         wb[:, kc * 128:(kc + 1) * 128],
                                         attng[:, kc * BS1:(kc + 1) * BS1],
                                         start=kc == 0, stop=last)
                            if last:
                                i.then_inc(r.h("pe"), 1)
                        r.add("tensor", mmo)
                    prev_ow[it2] = omm = r.bump("pe", 1)
                    r.wait("vector", "pe", omm)
                    r.wait("vector", xsem, x2l)
                    if tix >= 1 and oc == 0:
                        psem, pval = x3st[tix - 1]
                        r.wait("vector", psem, pval)
                    r.wait("vector", "c", r.cur("c"))
                    x2_used[it2] = y_prev[it2] = vadd(
                        lambda v, oc=oc, bank=bank, xb=xb: v.tensor_add(
                            y256[:, oc * BS1:(oc + 1) * BS1], bank[:, 0:BS1], xb[:]))
                    r.wait("scalar", "c", y_prev[it2])
                    if s12_mm:
                        r.wait("scalar", "pe", s12_mm)
                    sadd(lambda s, oc=oc: s.activation(
                        ybf256[:], y256[:, oc * BS1:(oc + 1) * BS1], AF.Identity))
                    sq2e = sadd(lambda s, oc=oc: s.activation(
                        sqbf256[:], y256[:, oc * BS1:(oc + 1) * BS1], AF.Square))
                    r.wait("tensor", "ev", sq2e)
                    r.add("tensor", lambda t, oc=oc: t.matmul(
                        PB[4][0:1, 0:BS1], ones_col_bf[:], ybf256[:],
                        start=oc == 0, stop=oc == KC - 1))

                    def mm2o(t, oc=oc):
                        t.matmul(PB[5][0:1, 0:BS1], ones_col_bf[:], sqbf256[:],
                                 start=oc == 0, stop=oc == KC - 1) \
                            .then_inc(r.h("pe"), 1)
                    r.add("tensor", mm2o)
                    s12_mm = r.bump("pe", 1)
                r.wait("scalar", "pe", s12_mm)
                r.wait("vector", "pe", s12_mm)
                emit_ln_epilogue(nc, r, lnt256, y256,
                                 lambda oc: y256[:, oc * BS1:(oc + 1) * BS1],
                                 PB, 4, 5, tok_w=BS1)
                x3_ln = r.cur("c")
                r.wait("sync", "c", x3_ln)
                x3sem = "x3A" if tix % 2 == 0 else "x3B"
                for oc in range(KC):
                    r.add("sync", lambda q, oc=oc, tok0=tok0, x3sem=x3sem: q.dma_start(
                        out=ap["x3"][:, oc * TPC + tok0: oc * TPC + tok0 + BS1],
                        in_=y256[:, oc * BS1:(oc + 1) * BS1]).then_inc(r.h(x3sem), 16))
                    r.bump(x3sem)
                x3st[tix] = (x3sem, r.cur(x3sem))
        x3_all_A = r.cur("x3A")
        x3_all_B = r.cur("x3B")
        AR.reset(m0)

        x3bf = AR.tile(128, KC * TT, BF16)
        gu = AR.tile(128, NFC * TT, BF16)
        yf = AR.tile(128, KC * TT, F32)
        wq_sb = [AR.tile(128, KC * 128, BF16) for _ in range(4)]
        wd_sb = [AR.tile(128, KC * 128, BF16) for _ in range(2)]
        x3_sl = [AR.tile(128, TT, F32) for _ in range(2)]
        silu1 = AR.tile(128, TT, BF16)
        ybff = AR.tile(128, TT, BF16)
        sqbff = AR.tile(128, TT, BF16)

        x3u, wqu, wdu, x3su = {}, {}, {}, {}
        guq_mm, dmm_done = {}, {}
        prev_out_st = 0
        yp = {}
        for ti in range(NTT):
            csem = "uA" if ti % 2 == 0 else "uB"
            if ti >= 2:
                r.wait("gpsimd", "pe", x3u[ti - 2])
            if ti == 0:
                r.wait("gpsimd", "x3A", x3_all_A)
                r.wait("gpsimd", "x3B", x3_all_B)
                r.wait("gpsimd", "pe", r.cur("pe"))
                r.wait("gpsimd", "c", r.cur("c"))
                r.wait("gpsimd", "ev", r.cur("ev"))
            r.add("gpsimd", lambda g, ti=ti, csem=csem: g.dma_start(
                out=x3bf[:].rearrange("p (a t) -> p a t", a=KC),
                in_=ap["x3"].rearrange("p (a t) -> p a t", a=KC)
                [:, :, ti * TT:(ti + 1) * TT]).then_inc(r.h(csem), 16))
            x3ld = r.bump(csem)
            gu_evs = {}
            sle_evs = {}
            for fc in range(NFC):
                fi = ti * NFC + fc
                wg = wq_sb[2 * (fi % 2)]
                wu = wq_sb[2 * (fi % 2) + 1]
                fsem = "fA" if fi % 2 == 0 else "fB"
                if fi >= 2:
                    r.wait("gpsimd", "pe", guq_mm[fi - 2])

                def ldw(g, fc=fc, wg=wg, wu=wu, fsem=fsem):
                    g.dma_start(out=wg[:], in_=ap["wgT"][fc]).then_inc(r.h(fsem), 16)
                    g.dma_start(out=wu[:], in_=ap["wuT"][fc]).then_inc(r.h(fsem), 16)
                r.add("gpsimd", ldw)
                wld = r.bump(fsem, 32)
                r.wait("vector", fsem, wld)
                if fi >= 2:
                    r.wait("vector", "pe", guq_mm.get(fi - 2, 0))
                r.wait("vector", "c", r.cur("c"))
                d1 = vadd(lambda v, wg=wg: v.tensor_scalar_sub(wg[:], wg[:], 128.0))
                r.wait("vector", "c", d1)
                d2 = vadd(lambda v, wu=wu: v.tensor_scalar_sub(wu[:], wu[:], 128.0))
                r.wait("tensor", "c", d2)
                r.wait("tensor", csem, x3ld)
                if fi >= 1:
                    r.wait("tensor", "ev", sle_evs.get(fi - 1, 0))
                    r.wait("tensor", "c", gu_evs.get(fi - 1, 0))
                for bi, wz in ((0, wg), (1, wu)):
                    bank = PB[bi]
                    for kc in range(KC):
                        def mmq(t, kc=kc, wz=wz, bank=bank, last=kc == KC - 1):
                            i = t.matmul(bank[:], wz[:, kc * 128:(kc + 1) * 128],
                                         x3bf[:, kc * TT:(kc + 1) * TT],
                                         start=kc == 0, stop=last)
                            if last:
                                i.then_inc(r.h("pe"), 1)
                        r.add("tensor", mmq)
                    r.bump("pe", 1)
                guq_mm[fi] = qmm = r.cur("pe")
                r.wait("scalar", "pe", qmm)
                if fi >= 1:
                    r.wait("scalar", "c", gu_v)
                sle = sadd(lambda s, fc=fc: s.activation(
                    silu1[:], PB[0][:], AF.Silu, scale=sg_sb[:, fc:fc + 1]))
                sle_evs[fi] = sle
                r.wait("vector", "ev", sle)
                r.wait("vector", "c", r.cur("c"))
                gu_v = vadd(lambda v, fc=fc: v.scalar_tensor_tensor(
                    gu[:, fc * TT:(fc + 1) * TT], PB[1][:], su_sb[:, fc:fc + 1],
                    silu1[:], ALU.mult, ALU.mult))
                gu_evs[fi] = gu_v
                wqu[fi] = gu_v
            down_mm = {}
            ybank_free2 = 0
            for ocg in range(2):
                for fc in range(NFC):
                    fi2 = (ti * 2 + ocg) * NFC + fc
                    wd = wd_sb[fi2 % 2]
                    dsem = "dA" if fi2 % 2 == 0 else "dB"
                    if fi2 >= 2:
                        r.wait("gpsimd", "pe", dmm_done[fi2 - 2])
                    r.add("gpsimd", lambda g, fc=fc, wd=wd, dsem=dsem: g.dma_start(
                        out=wd[:], in_=ap["wdfc"][fc]).then_inc(r.h(dsem), 16))
                    dld = r.bump(dsem)
                    r.wait("vector", dsem, dld)
                    if fi2 >= 2:
                        r.wait("vector", "pe", dmm_done.get(fi2 - 2, 0))
                    r.wait("vector", "c", r.cur("c"))
                    d3 = vadd(lambda v, wd=wd: v.tensor_scalar_sub(wd[:], wd[:], 128.0))
                    wdu[fi2] = d3
                    r.wait("tensor", "c", d3)
                    if fc == 0:
                        r.wait("tensor", "c", gu_v)
                        if ocg == 1:
                            r.wait("tensor", "c", ybank_free2)
                        elif ti >= 1:
                            r.wait("tensor", "c", ln_done_prev)
                    for oc8 in range(8):
                        oc = ocg * 8 + oc8

                        def mmd2(t, fc=fc, oc=oc, oc8=oc8, wd=wd,
                                 inc=oc8 == 7, last=fc == NFC - 1):
                            i = t.matmul(PB[oc8][:],
                                         wd[:, oc * 128:(oc + 1) * 128],
                                         gu[:, fc * TT:(fc + 1) * TT],
                                         start=fc == 0, stop=last)
                            if inc:
                                i.then_inc(r.h("pe"), 1)
                        r.add("tensor", mmd2)
                    dmm_done[fi2] = r.bump("pe", 1)
                dend = r.cur("pe")
                for oc8 in range(8):
                    oc = ocg * 8 + oc8
                    gx = ti * KC + oc
                    xb = x3_sl[gx % 2]
                    xsem2 = "r3A" if gx % 2 == 0 else "r3B"
                    if gx >= 2:
                        r.wait("sync", "c", x3su[gx - 2])
                    r.add("sync", lambda q, oc=oc, ti=ti, xb=xb, xsem2=xsem2:
                          q.dma_start(out=xb[:],
                                      in_=ap["x3"][:, oc * TPC + ti * TT: oc * TPC + ti * TT + TT])
                          .then_inc(r.h(xsem2), 16))
                    x3l = r.bump(xsem2)
                    r.wait("vector", "pe", dend)
                    r.wait("vector", xsem2, x3l)
                    if prev_out_st:
                        r.wait("vector", "st", prev_out_st)
                    r.wait("vector", "c", r.cur("c"))
                    x3su[gx] = yp[gx] = vadd(
                        lambda v, oc=oc, oc8=oc8, xb=xb: v.scalar_tensor_tensor(
                            yf[:, oc * TT:(oc + 1) * TT], PB[oc8][:],
                            sd_sb[:, oc:oc + 1], xb[:], ALU.mult, ALU.add))
                ybank_free2 = r.cur("c")
            y_done2 = r.cur("c")
            s12f = 0
            for oc in range(KC):
                r.wait("scalar", "c", y_done2)
                if s12f:
                    r.wait("scalar", "pe", s12f)
                sadd(lambda s, oc=oc: s.activation(
                    ybff[:], yf[:, oc * TT:(oc + 1) * TT], AF.Identity))
                sqf = sadd(lambda s, oc=oc: s.activation(
                    sqbff[:], yf[:, oc * TT:(oc + 1) * TT], AF.Square))
                r.wait("tensor", "ev", sqf)
                if oc == 0 and ti == 0:
                    r.wait("tensor", "c", ones_done)
                r.add("tensor", lambda t, oc=oc: t.matmul(
                    PB[4][0:1, 0:TT], ones_col_bf[:], ybff[:],
                    start=oc == 0, stop=oc == KC - 1))

                def mm2f(t, oc=oc):
                    t.matmul(PB[5][0:1, 0:TT], ones_col_bf[:], sqbff[:],
                             start=oc == 0, stop=oc == KC - 1) \
                        .then_inc(r.h("pe"), 1)
                r.add("tensor", mm2f)
                s12f = r.bump("pe", 1)
            r.wait("scalar", "pe", s12f)
            r.wait("vector", "pe", s12f)
            emit_ln_epilogue(nc, r, lnt, yf,
                             lambda oc: yf[:, oc * TT:(oc + 1) * TT], PB, 4, 5)
            ln_done_prev = r.cur("c")
            r.wait("sync", "c", ln_done_prev)
            for oc in range(KC):
                r.add("sync", lambda q, oc=oc, ti=ti: q.dma_start(
                    out=ap["out"][:, oc * TPC + ti * TT: oc * TPC + ti * TT + TT],
                    in_=yf[:, oc * TT:(oc + 1) * TT]).then_inc(r.h("st"), 16))
                r.bump("st")
            prev_out_st = r.cur("st")
            x3u[ti] = r.cur("pe")
        r.wait("sync", "st", r.cur("st"))
        r.run(block)
    return nc


def kernel(**inputs):
    pc1 = prep_launch1(inputs)
    res1 = run_bass_kernel_spmd(build_launch1(), pc1, core_ids=list(range(8)))
    l1out = [res1.results[c] for c in range(8)]

    pc2 = prep_launch2(inputs, l1out)
    res2 = run_bass_kernel_spmd(build_launch2(debug=False, stages=4), pc2,
                                core_ids=list(range(8)))
    l2out = [res2.results[c] for c in range(8)]
    l2bout = l2out

    pc3 = prep_launch3(inputs, l2out, l2bout)
    res3 = run_bass_kernel_spmd(build_launch3(), pc3, core_ids=list(range(8)))

    out_fm = np.zeros((B, D, S), np.float32)
    for c in range(8):
        b = c // 4
        ti = _own_token_index(c)
        o = res3.results[c]["out"]
        out_fm[b][:, ti] = o.reshape(128, KC, TPC).transpose(1, 0, 2).reshape(D, TPC)
    return np.ascontiguousarray(out_fm.transpose(0, 2, 1))
